# version 28
# baseline (speedup 1.0000x reference)
import math
import numpy as np
import ml_dtypes
import concourse.bass as bass
import concourse.mybir as mybir
from concourse.bass_utils import run_bass_kernel_spmd

F32 = mybir.dt.float32
BF16 = mybir.dt.bfloat16
AF = mybir.ActivationFunctionType
OP = mybir.AluOpType
AX = mybir.AxisListType

T = 2048
D = 1024
NDC = 8
NFC = 8
NE = 8
DH = 64
EPS = 1e-5

DEBUG = False

ENGS = ("pe", "v", "s", "g", "y")
QUEUES = ("qy", "qg")


class Sch:

    def __init__(self):
        self.seq = {k: [] for k in ENGS}
        self.n = {k: 0 for k in ENGS}
        self.nd = {k: 0 for k in QUEUES}
        self.wm = {k: {} for k in ENGS}

    def w(self, eng, sem, val):
        if not val or sem == eng or val <= self.wm[eng].get(sem, 0):
            return
        self.wm[eng][sem] = val
        self.seq[eng].append(("w", sem, val))

    def we(self, eng, et):
        if et is not None:
            self.w(eng, et[0], et[1])

    def o(self, eng, fn, chk=False):
        self.seq[eng].append(("o", fn, chk))
        if chk:
            self.n[eng] += 1
            return self.n[eng]
        return None

    def chk(self, eng):
        for idx in range(len(self.seq[eng]) - 1, -1, -1):
            it = self.seq[eng][idx]
            if it[0] == "o":
                if not it[2]:
                    self.seq[eng][idx] = ("o", it[1], True)
                    self.n[eng] += 1
                return self.n[eng]
            if it[0] == "d":
                raise RuntimeError("chk would land on a dma")
        raise RuntimeError("no op to chk")

    def d(self, eng, q, fn):
        self.seq[eng].append(("d", fn, q))
        self.nd[q] += 1
        return self.nd[q] * 16

    def replay(self, eng_obj, name, sems):
        for it in self.seq[name]:
            if it[0] == "w":
                eng_obj.wait_ge(sems[it[1]], it[2])
            elif it[0] == "o":
                inst = it[1](eng_obj)
                if it[2]:
                    inst.then_inc(sems[name], 1)
            else:
                it[1](eng_obj).then_inc(sems[it[2]], 16)


def build():
    nc = bass.Bass("TRN2", target_bir_lowering=False)

    dx = nc.dram_tensor("x", [D, T], F32, kind="ExternalInput").ap()
    dwq = nc.dram_tensor("wq", [D, D], F32, kind="ExternalInput").ap()
    dwk = nc.dram_tensor("wk", [D, D], F32, kind="ExternalInput").ap()
    dwv = nc.dram_tensor("wv", [D, D], F32, kind="ExternalInput").ap()
    dwo = nc.dram_tensor("wo", [D, D], F32, kind="ExternalInput").ap()
    dw1 = nc.dram_tensor("w1", [NE, D, D], F32, kind="ExternalInput").ap()
    dw3 = nc.dram_tensor("w3", [NE, D, D], F32, kind="ExternalInput").ap()
    dw2 = nc.dram_tensor("w2", [NE, D, D], F32, kind="ExternalInput").ap()
    dsw1 = nc.dram_tensor("sw1", [D, D], F32, kind="ExternalInput").ap()
    dsw3 = nc.dram_tensor("sw3", [D, D], F32, kind="ExternalInput").ap()
    dsw2 = nc.dram_tensor("sw2", [D, D], F32, kind="ExternalInput").ap()
    dwmod = nc.dram_tensor("wmod", [D, 6144], F32, kind="ExternalInput").ap()
    dcc = nc.dram_tensor("cc", [128, 8], F32, kind="ExternalInput").ap()
    dqkp = nc.dram_tensor("qkp", [128, 6], F32, kind="ExternalInput").ap()
    dbias = nc.dram_tensor("biases", [128, 160], F32, kind="ExternalInput").ap()
    dgatet = nc.dram_tensor("gatet", [128, 64], F32, kind="ExternalInput").ap()
    db2r = nc.dram_tensor("b2r", [8, 1024], F32, kind="ExternalInput").ap()
    dallones = nc.dram_tensor("callones", [128, 128], F32, kind="ExternalInput").ap()
    dblockones = nc.dram_tensor("cblockones", [128, 128], F32, kind="ExternalInput").ap()
    dident = nc.dram_tensor("cident", [128, 128], F32, kind="ExternalInput").ap()
    dmask = nc.dram_tensor("cmask", [128, 128], F32, kind="ExternalInput").ap()
    dsel2 = nc.dram_tensor("csel2", [2, 128], F32, kind="ExternalInput").ap()
    dones1 = nc.dram_tensor("cones1", [1, 128], F32, kind="ExternalInput").ap()
    didentf = nc.dram_tensor("cidentf", [128, 128], F32, kind="ExternalInput").ap()
    depsr = nc.dram_tensor("cepsr", [1, 1024], F32, kind="ExternalInput").ap()
    desel = nc.dram_tensor("cesel", [8, 1024], F32, kind="ExternalInput").ap()
    dout = nc.dram_tensor("out", [D, T], F32, kind="ExternalOutput").ap()
    if DEBUG:
        dbg_par = nc.dram_tensor("dbg_par", [128, 64], F32, kind="ExternalOutput").ap()
        dbg_h = nc.dram_tensor("dbg_h", [128, NDC * T], BF16, kind="ExternalOutput").ap()
        dbg_qkv = nc.dram_tensor("dbg_qkv", [128, 8192], BF16, kind="ExternalOutput").ap()
        dbg_ao = nc.dram_tensor("dbg_ao", [128, NDC * T], BF16, kind="ExternalOutput").ap()
        dbg_x1 = nc.dram_tensor("dbg_x1", [128, NDC * T], F32, kind="ExternalOutput").ap()
        dbg_crow = nc.dram_tensor("dbg_crow", [8, T], BF16, kind="ExternalOutput").ap()
        dbg_h2 = nc.dram_tensor("dbg_h2", [128, NDC * T], BF16, kind="ExternalOutput").ap()
        dbg_ao2 = nc.dram_tensor("dbg_ao2", [128, T], BF16, kind="ExternalOutput").ap()
        dbg_lc = nc.dram_tensor("dbg_lc", [128, 32], F32, kind="ExternalOutput").ap()
        dbg_pt = nc.dram_tensor("dbg_pt", [128, T], BF16, kind="ExternalOutput").ap()
        dbg_isge = nc.dram_tensor("dbg_isge", [128, 16], F32, kind="ExternalOutput").ap()
        dbg_sc = nc.dram_tensor("dbg_sc", [128, 2048], F32, kind="ExternalOutput").ap()
        dbg_nm = nc.dram_tensor("dbg_nm", [128, 4], F32, kind="ExternalOutput").ap()
        dbg_pr = nc.dram_tensor("dbg_pr", [128, 2048], BF16, kind="ExternalOutput").ap()

    X = nc.alloc_sbuf_tensor("X", [128, NDC * T], F32)
    HH = nc.alloc_sbuf_tensor("HH", [128, NDC * T], BF16)
    AO = nc.alloc_sbuf_tensor("AO", [128, NDC * T], BF16)
    WB = nc.alloc_sbuf_tensor("WB", [128, 2 * 8192], BF16)
    WORK = nc.alloc_sbuf_tensor("WORK", [128, 8192], BF16)
    SCA = nc.alloc_sbuf_tensor("SCA", [128, T], BF16)
    SCB = nc.alloc_sbuf_tensor("SCB", [128, T], BF16)
    WVS = nc.alloc_sbuf_tensor("WVS", [128, 2048], BF16)
    KP1 = nc.alloc_sbuf_tensor("KP1", [128, 2048], BF16)
    CROW = nc.alloc_sbuf_tensor("CROW", [8, T], BF16)
    QKP = nc.alloc_sbuf_tensor("QKP", [128, 6], F32)
    CC = nc.alloc_sbuf_tensor("CC", [128, 8], F32)
    SCS = nc.alloc_sbuf_tensor("SCS", [128, 8], F32)
    BIA = nc.alloc_sbuf_tensor("BIA", [128, 160], F32)
    PAR = nc.alloc_sbuf_tensor("PAR", [128, 64], F32)
    GATET = nc.alloc_sbuf_tensor("GATET", [128, 64], BF16)
    B2R = nc.alloc_sbuf_tensor("B2R", [8, 1024], BF16)
    ONES = nc.alloc_sbuf_tensor("ONES", [128, 128], BF16)
    BONES = nc.alloc_sbuf_tensor("BONES", [128, 128], BF16)
    IDENT = nc.alloc_sbuf_tensor("IDENT", [128, 128], BF16)
    MASK = nc.alloc_sbuf_tensor("MASK", [128, 128], BF16)
    SEL2 = nc.alloc_sbuf_tensor("SEL2", [2, 128], BF16)
    ONES1 = nc.alloc_sbuf_tensor("ONES1", [1, 128], BF16)
    IDENTF = nc.alloc_sbuf_tensor("IDENTF", [128, 128], F32)
    EPSR = nc.alloc_sbuf_tensor("EPSR", [1, 1024], BF16)
    ESEL = nc.alloc_sbuf_tensor("ESEL", [8, 1024], BF16)
    NM = nc.alloc_sbuf_tensor("NM", [128, 8], F32)
    LC = nc.alloc_sbuf_tensor("LC", [128, 32], F32)
    GST = nc.alloc_sbuf_tensor("GST", [128, 64], F32)
    JNK = nc.alloc_sbuf_tensor("JNK", [128, 4], F32)

    PA = nc.alloc_psum_tensor("PA", [128, 2048], F32)
    PB = nc.alloc_psum_tensor("PB", [128, 2048], F32)

    S = Sch()
    aof = AO.ap().bitcast(F32)

    def xs(dc, lo=0, hi=T):
        return X[:, dc * T + lo: dc * T + hi]

    def hs(dc, lo=0, hi=T):
        return HH[:, dc * T + lo: dc * T + hi]

    def aos(dc, lo=0, hi=T):
        return AO[:, dc * T + lo: dc * T + hi]

    tok_x = None
    tok_smalls = None
    for sb_t, dr in ((QKP, dqkp), (CC, dcc), (BIA, dbias), (IDENTF, didentf)):
        tok_smalls = S.d("y", "qy", lambda e, a=sb_t, b=dr: e.dma_start(out=a[:], in_=b[:]))
    tok_gatet = S.d("g", "qg", lambda e: e.dma_start(out=GATET[:], in_=dgatet[:]))
    tok_consts = None
    for sb_t, dr in ((ONES, dallones), (BONES, dblockones), (IDENT, dident),
                     (MASK, dmask), (SEL2, dsel2), (ESEL, desel), (B2R, db2r),
                     (ONES1, dones1), (EPSR, depsr)):
        tok_consts = S.d("g", "qg", lambda e, a=sb_t, b=dr: e.dma_start(out=a[:], in_=b[:]))

    def wload(dst_ap, dram_ap):
        return lambda e: e.dma_start(
            out=dst_ap.rearrange("p (c o) -> p c o", c=8),
            in_=dram_ap.rearrange("(c p) o -> p c o", p=128))

    R0 = WB[:, 0:8192]
    R1 = WB[:, 8192:16384]
    RW = WORK[:, 0:8192]
    tok_wq = S.d("g", "qg", wload(R0, dwq))
    tok_wk = S.d("g", "qg", wload(R1, dwk))

    def wv_piece(fc):
        pc = fc % 2
        return S.d("g", "qg", lambda e, fc=fc, pc=pc: e.dma_start(
            out=WVS[:, pc * 1024:(pc + 1) * 1024].rearrange("p (c o) -> p c o", c=8),
            in_=dwv.rearrange("(c p) o -> p c o", p=128)[:, :, fc * 128:(fc + 1) * 128]))

    tok_wv = [wv_piece(0), wv_piece(1)]

    S.w("s", "qy", tok_smalls)
    chk_scs = S.o("s", lambda e: e.activation(SCS[:], CC[:], AF.Silu), chk=True)

    NPC = 12
    PW = 6144 // NPC
    modp = PB[:, 2000:2048]
    ROWB = nc.alloc_sbuf_tensor("ROWB", [1, 512], F32)
    piece_tok = [None] * NPC
    piece_done = [None] * NPC
    ev_tok = None
    ev_toks = []
    tr_toks = []
    first_tr = True
    for p in range(NPC):
        if p >= 2:
            S.w("y", "pe", piece_done[p - 2])
        piece_tok[p] = S.d("y", "qy", lambda e, p=p: e.dma_start(
            out=aof[:, (p % 2) * 4096:(p % 2) * 4096 + 4096].rearrange(
                "p (c o) -> p c o", c=8),
            in_=dwmod.rearrange("(c p) o -> p c o", p=128)[:, :, p * PW:(p + 1) * PW]))
        if p == 3:
            tok_x = S.d("y", "qy", lambda e: e.dma_start(
                out=X[:].rearrange("p (c t) -> p c t", c=NDC),
                in_=dx.rearrange("(c p) t -> p c t", p=128)))
        S.w("pe", "qy", piece_tok[p])
        if p == 0:
            S.w("pe", "s", chk_scs)
        modrow = PB[0:1, (p % 2) * 512:(p % 2) * 512 + 512]
        if len(ev_toks) >= 2:
            S.w("pe", "v", ev_toks[-2])
        for kc in range(8):
            S.o("pe", lambda e, p=p, kc=kc, modrow=modrow: e.matmul(
                modrow, SCS[:, kc:kc + 1],
                aof[:, (p % 2) * 4096 + kc * PW: (p % 2) * 4096 + kc * PW + PW],
                start=(kc == 0), stop=(kc == 7)))
        cmm = S.chk("pe")
        piece_done[p] = cmm
        S.w("v", "pe", cmm)
        if tr_toks:
            S.w("v", "pe", tr_toks[-1])
        ev_tok = S.o("v", lambda e, modrow=modrow: e.tensor_copy(ROWB[:], modrow),
                     chk=True)
        ev_toks.append(ev_tok)
        S.w("pe", "v", ev_tok)
        for cb in range(4):
            col = p * 4 + cb
            S.o("pe", lambda e, cb=cb, col=col, first=first_tr: e.matmul(
                modp[:, col:col + 1], ROWB[0:1, cb * 128:(cb + 1) * 128],
                IDENTF[0:1, 0:1], start=first, stop=(col == 47),
                is_transpose=True))
            first_tr = False
        tr_toks.append(S.chk("pe"))
    piece_done[NPC - 1] = tr_toks[-1]
    chk_modmm = piece_done[-1]

    S.w("v", "pe", tr_toks[3])
    S.w("v", "qy", tok_smalls)

    def asm1(e):
        e.tensor_copy(PAR[:, 0:8], modp[:, 0:8])
        return e.tensor_scalar_add(PAR[:, 8:16], modp[:, 8:16], 1.0)
    chk_par1 = S.o("v", asm1, chk=True)
    S.w("v", "pe", chk_modmm)

    def asm(e):
        e.tensor_copy(PAR[:, 24:32], modp[:, 24:32])
        e.tensor_scalar_add(PAR[:, 32:40], modp[:, 32:40], 1.0)
        A, B = 0, 32
        for base, src in ((A, 16), (B, 40)):
            e.tensor_single_scalar(GST[:, base:base + 8], modp[:, src:src + 8],
                                   0.0, OP.is_ge)
        for base in (A, B):
            e.tensor_scalar_mul(GST[:, base + 8:base + 16], GST[:, base:base + 8], 2.0)
        for base in (A, B):
            e.tensor_single_scalar(GST[:, base + 8:base + 16],
                                   GST[:, base + 8:base + 16], 1.0, OP.subtract)
        for base, src in ((A, 16), (B, 40)):
            e.tensor_scalar_mul(GST[:, base + 16:base + 24], modp[:, src:src + 8], -1.0)
        for base in (A, B):
            e.tensor_scalar_max(GST[:, base + 16:base + 24],
                                GST[:, base + 16:base + 24], 0.25)
        for base, src in ((A, 16), (B, 40)):
            e.tensor_scalar_max(GST[:, base + 24:base + 32], modp[:, src:src + 8], 0.25)
        for base in (A, B):
            e.tensor_max(GST[:, base + 16:base + 24], GST[:, base + 16:base + 24],
                         GST[:, base + 24:base + 32])
        for base, dst in ((A, 16), (B, 40)):
            e.tensor_mul(PAR[:, dst:dst + 8], GST[:, base + 8:base + 16],
                         GST[:, base + 16:base + 24])
        e.tensor_mul(PAR[:, 48:56], PAR[:, 16:24], BIA[:, 144:152])
        if DEBUG:
            e.tensor_single_scalar(GST[:, 32:40], modp[:, 16:24], 0.0, OP.is_ge)
            e.tensor_scalar_mul(GST[:, 40:48], GST[:, 32:40], 2.0)
            e.tensor_single_scalar(GST[:, 40:48], GST[:, 40:48], 1.0, OP.subtract)
        return e.tensor_mul(PAR[:, 56:64], PAR[:, 40:48], BIA[:, 152:160])
    chk_par = S.o("v", asm, chk=True)
    S.w("s", "v", chk_par)

    ln_stats_free = [None]

    def layernorm(sc_col, bi_col, t_slice, h2f32_slice, h2_cb, pre_waits, lnscr):
        for eng in ("pe", "v", "s"):
            for et in pre_waits:
                S.we(eng, et)
        S.w("v", "qy", tok_x)
        S.w("s", "qy", tok_x)
        last_mm = None
        for dc in range(NDC):
            if last_mm is not None:
                S.w("v", "pe", last_mm)
                S.w("s", "pe", last_mm)
            ccp = S.o("v", lambda e, dc=dc: e.tensor_copy(SCA[:], xs(dc)), chk=True)
            csq = S.o("s", lambda e, dc=dc: e.activation(SCB[:], xs(dc), AF.Square),
                      chk=True)
            S.w("pe", "v", ccp)
            S.w("pe", "s", csq)
            S.w("pe", "qg", tok_consts)
            for tc in range(4):
                S.o("pe", lambda e, dc=dc, tc=tc: e.matmul(
                    PA[:, tc * 512:(tc + 1) * 512], ONES[:],
                    SCA[:, tc * 512:(tc + 1) * 512],
                    start=(dc == 0), stop=(dc == NDC - 1)))
                S.o("pe", lambda e, dc=dc, tc=tc: e.matmul(
                    PB[:, tc * 512:(tc + 1) * 512], ONES[:],
                    SCB[:, tc * 512:(tc + 1) * 512],
                    start=(dc == 0), stop=False))
            last_mm = S.chk("pe")
        for tc in range(4):
            S.o("pe", lambda e, tc=tc: e.matmul(
                PB[:, tc * 512:(tc + 1) * 512], ONES1[0:1, :],
                EPSR[0:1, 512:1024], start=False, stop=True))
        last_mm = S.chk("pe")
        S.w("v", "pe", last_mm)
        S.o("v", lambda e: e.tensor_scalar_mul(SCA[:], PA[:], 1.0 / D))
        ct1 = S.o("v", lambda e: e.tensor_tensor(SCB[:], SCA[:], PA[:], OP.mult),
                  chk=True)
        crstd = None
        cln_prev = None
        for h in (0, 1):
            if cln_prev is not None:
                S.w("v", "s", cln_prev)
            ct2 = S.o("v", lambda e, h=h: e.scalar_tensor_tensor(
                lnscr[:, 0:1024], SCB[:, h * 1024:(h + 1) * 1024], -1.0,
                PB[:, h * 1024:(h + 1) * 1024], OP.mult, OP.add), chk=True)
            S.w("s", "v", ct2)
            cln_prev = S.o("s", lambda e: e.activation(
                lnscr[:, 1024:2048], lnscr[:, 0:1024], AF.Ln), chk=True)
            crstd = S.o("s", lambda e, h=h: e.activation(
                SCB[:, h * 1024:(h + 1) * 1024], lnscr[:, 1024:2048], AF.Exp,
                scale=-0.5, bias=QKP[:, 5:6]), chk=True)
            ln_stats_free[0] = ("v", ct2)
        h_chks = []
        S.w("v", "s", crstd)
        cu = None
        prev_gate_tok = None
        for dc in range(NDC):
            if h_chks:
                S.w("v", "s", h_chks[-1][1] or h_chks[-1][0])
            S.o("v", lambda e, dc=dc: e.scalar_tensor_tensor(
                SCA[:], PA[:], -1.0 / D, xs(dc), OP.mult, OP.add))
            cu = S.o("v", lambda e, t=t_slice: e.tensor_tensor(t, SCA[:], SCB[:],
                                                               OP.mult), chk=True)
            S.w("s", "v", cu)
            ch = S.o("s", lambda e, dc=dc, t=t_slice: e.activation(
                hs(dc), t, AF.Identity, bias=PAR[:, bi_col + dc:bi_col + dc + 1],
                scale=PAR[:, sc_col + dc:sc_col + dc + 1]), chk=True)
            ch2 = None
            if h2f32_slice is not None:
                if prev_gate_tok is not None:
                    S.w("s", "pe", prev_gate_tok)
                ch2 = S.o("s", lambda e, dc=dc, t=t_slice: e.activation(
                    h2f32_slice, t, AF.Identity, bias=PAR[:, bi_col + dc:bi_col + dc + 1],
                    scale=PAR[:, sc_col + dc:sc_col + dc + 1]), chk=True)
                prev_gate_tok = h2_cb(dc, ch2)
            h_chks.append((ch, ch2, dc))
        return h_chks, cu

    if DEBUG:
        S.w("y", "v", chk_par)
        S.d("y", "qy", lambda e: e.dma_start(out=dbg_par, in_=PAR[:]))
        S.d("y", "qy", lambda e: e.dma_start(out=dbg_isge, in_=GST[:, 32:48]))

    workf = WORK.ap().bitcast(F32)

    def ln1_halved():
        chks = [[None, None] for _ in range(NDC)]
        S.w("v", "qy", tok_x)
        S.w("s", "qy", tok_x)
        S.w("s", "v", chk_par1)
        last_mm = None
        last_u = None
        prev_ch = None
        for th in (0, 1):
            tlo = th * 1024
            for dc in range(NDC):
                if last_mm is not None:
                    S.w("v", "pe", last_mm)
                    S.w("s", "pe", last_mm)
                ccp = S.o("v", lambda e, dc=dc, tlo=tlo: e.tensor_copy(
                    SCA[:, 0:1024], xs(dc, tlo, tlo + 1024)), chk=True)
                csq = S.o("s", lambda e, dc=dc, tlo=tlo: e.activation(
                    SCB[:, 0:1024], xs(dc, tlo, tlo + 1024), AF.Square), chk=True)
                S.w("pe", "v", ccp)
                S.w("pe", "s", csq)
                S.w("pe", "qg", tok_consts)
                if th == 1 and dc == 0:
                    S.w("pe", "v", last_u)
                for tc in (0, 1):
                    S.o("pe", lambda e, tc=tc, dc=dc: e.matmul(
                        PA[:, tc * 512:(tc + 1) * 512], ONES[:],
                        SCA[:, tc * 512:(tc + 1) * 512],
                        start=(dc == 0), stop=(dc == NDC - 1)))
                    S.o("pe", lambda e, tc=tc, dc=dc: e.matmul(
                        PA[:, 1024 + tc * 512:1024 + (tc + 1) * 512], ONES[:],
                        SCB[:, tc * 512:(tc + 1) * 512],
                        start=(dc == 0), stop=False))
                last_mm = S.chk("pe")
            for tc in (0, 1):
                S.o("pe", lambda e, tc=tc: e.matmul(
                    PA[:, 1024 + tc * 512:1024 + (tc + 1) * 512], ONES1[0:1, :],
                    EPSR[0:1, 512:1024], start=False, stop=True))
            last_mm = S.chk("pe")
            S.w("v", "pe", last_mm)
            S.o("v", lambda e: e.tensor_scalar_mul(SCA[:, 0:1024], PA[:, 0:1024],
                                                   1.0 / D))
            S.o("v", lambda e: e.tensor_tensor(SCB[:, 0:1024], SCA[:, 0:1024],
                                               PA[:, 0:1024], OP.mult))
            ct2 = S.o("v", lambda e: e.scalar_tensor_tensor(
                workf[:, 1024:2048], SCB[:, 0:1024], -1.0, PA[:, 1024:2048],
                OP.mult, OP.add), chk=True)
            S.w("s", "v", ct2)
            S.o("s", lambda e: e.activation(workf[:, 2048:3072], workf[:, 1024:2048],
                                            AF.Ln))
            crstd = S.o("s", lambda e: e.activation(
                SCB[:, 0:1024], workf[:, 2048:3072], AF.Exp,
                scale=-0.5, bias=QKP[:, 5:6]), chk=True)
            S.w("v", "s", crstd)
            for dc in range(NDC):
                if prev_ch is not None:
                    S.w("v", "s", prev_ch)
                S.o("v", lambda e, dc=dc, tlo=tlo: e.scalar_tensor_tensor(
                    SCA[:, 0:1024], PA[:, 0:1024], -1.0 / D,
                    xs(dc, tlo, tlo + 1024), OP.mult, OP.add))
                cu = S.o("v", lambda e: e.tensor_tensor(
                    WORK[:, 0:1024], SCA[:, 0:1024], SCB[:, 0:1024], OP.mult),
                    chk=True)
                last_u = cu
                S.w("s", "v", cu)
                prev_ch = S.o("s", lambda e, dc=dc, tlo=tlo: e.activation(
                    hs(dc, tlo, tlo + 1024), WORK[:, 0:1024], AF.Identity,
                    bias=PAR[:, 0 + dc:0 + dc + 1],
                    scale=PAR[:, 8 + dc:8 + dc + 1]), chk=True)
                chks[dc][th] = prev_ch
        return chks, last_u

    ln1_hchks, ln1_pa_free = ln1_halved()

    QR, KR, VR, VT = 0, 2048, 4096, 6144
    TRA = PB[:, 1024:1536].bitcast(BF16)
    TRB = PB[:, 1536:2048].bitcast(BF16)
    AVP = PB[:, 1024:1152]
    AVP2 = PB[:, 1536:1664]
    trfree = [None, None]

    halves = [(PB, 0), (PA, 0), (PA, 1024)]
    half_free = [None, None, None]
    hctr = [0]
    tr_last = [None]
    av_last = [None]
    norm_tok = [None]
    ctm_k = [None]
    prev_cbc = [None]
    prev_ctl = [None]
    scb_pe_free = [None]
    kp1_free = [None]
    ao_guard = [False]
    last_q_mm = [None]
    last_k_mm = [None]

    ckp1z = S.o("v", lambda e: e.memset(KP1.ap()[0:64, 0:2048], 0.0), chk=True)

    for fc in range(NFC):
        chk_evac = {}
        for mat, psum, wreg, wtok, dst in (("q", PA, R0, tok_wq, QR),
                                           ("k", PB, R1, tok_wk, KR)):
            S.w("pe", "qg", wtok)
            if fc == 0:
                S.w("pe", "v", ln1_pa_free)
            else:
                if mat == "q":
                    S.we("pe", half_free[1])
                    S.we("pe", half_free[2])
                else:
                    S.we("pe", half_free[0])
            for kc in range(8):
                if fc == 0:
                    S.w("pe", "s", ln1_hchks[kc][0])
                    S.w("pe", "s", ln1_hchks[kc][1])
                for tc in range(4):
                    S.o("pe", lambda e, psum=psum, wreg=wreg, tc=tc, kc=kc, fc=fc:
                        e.matmul(
                            psum[:, tc * 512:(tc + 1) * 512],
                            wreg[:, kc * 1024 + fc * 128: kc * 1024 + fc * 128 + 128],
                            hs(kc, tc * 512, (tc + 1) * 512),
                            start=(kc == 0), stop=(kc == 7)))
            cm = S.chk("pe")
            if fc == NFC - 1:
                (last_q_mm if mat == "q" else last_k_mm)[0] = cm
            S.w("v", "pe", cm)
            chk_evac[mat] = S.o("v", lambda e, psum=psum, dst=dst: e.tensor_copy(
                WORK[:, dst:dst + 2048], psum[:]), chk=True)
        S.w("pe", "v", chk_evac["q"])
        S.w("pe", "qg", tok_wv[fc])
        pc = fc % 2
        for kc in range(8):
            for tc in range(4):
                S.o("pe", lambda e, tc=tc, kc=kc, pc=pc: e.matmul(
                    PA[:, tc * 512:(tc + 1) * 512],
                    WVS[:, pc * 1024 + kc * 128: pc * 1024 + kc * 128 + 128],
                    hs(kc, tc * 512, (tc + 1) * 512),
                    start=(kc == 0), stop=(kc == 7)))
        cmv = S.chk("pe")
        if fc + 2 < NFC:
            S.w("g", "pe", cmv)
            tok_wv.append(wv_piece(fc + 2))
        S.w("v", "pe", cmv)
        chk_evac["v"] = S.o("v", lambda e: e.tensor_copy(WORK[:, VR:VR + 2048], PA[:]),
                            chk=True)

        vt_half = [None, None]
        S.w("pe", "v", chk_evac["v"])
        vt_done = None
        for rnd in range(2):
            trv = TRA if rnd == 0 else TRB
            S.we("pe", trfree[rnd])
            for j8 in range(8):
                j = rnd * 8 + j8
                S.o("pe", lambda e, j=j, j8=j8, trv=trv: e.matmul(
                    trv[:, j8 * 128:(j8 + 1) * 128],
                    WORK[:, VR + j * 128: VR + (j + 1) * 128], IDENT[:],
                    start=(j8 == 0), stop=(j8 == 7), is_transpose=True))
            ct = S.chk("pe")
            vt_done = ct
            S.w("v", "pe", ct)
            tok = ("v", S.o("v", lambda e, rnd=rnd, trv=trv: e.tensor_copy(
                WORK[:, VT + rnd * 1024: VT + (rnd + 1) * 1024], trv[:]), chk=True))
            trfree[rnd] = tok
            vt_half[rnd] = tok

        pa_free = ("v", chk_evac["v"])
        pb_free = ("v", chk_evac["k"])
        qraw = WORK[:, QR:QR + 2048]
        kraw = WORK[:, KR:KR + 2048]
        scaf = SCA.ap().bitcast(F32)
        S.we("pe", pa_free)
        for tc in range(4):
            S.o("pe", lambda e, tc=tc: e.matmul(
                PA[:, tc * 512:(tc + 1) * 512], BONES[:],
                qraw[:, tc * 512:(tc + 1) * 512], start=True, stop=True))
        csq_ = S.chk("pe")
        S.we("pe", pb_free)
        S.we("pe", vt_half[0])
        S.we("pe", vt_half[1])
        for tc in range(4):
            S.o("pe", lambda e, tc=tc: e.matmul(
                PB[:, tc * 512:(tc + 1) * 512], BONES[:],
                kraw[:, tc * 512:(tc + 1) * 512], start=True, stop=True))
        csk_ = S.chk("pe")
        S.w("v", "pe", csq_)
        cuq = S.o("v", lambda e: e.scalar_tensor_tensor(
            qraw, PA[:], -1.0 / DH, qraw, OP.mult, OP.add), chk=True)
        S.w("v", "pe", csk_)
        cuk = S.o("v", lambda e: e.scalar_tensor_tensor(
            kraw, PB[:], -1.0 / DH, kraw, OP.mult, OP.add), chk=True)
        S.w("s", "v", cuq)
        S.we("s", prev_cbc[0])
        cxq = S.o("s", lambda e: e.activation(SCA[:], qraw, AF.Square), chk=True)
        S.w("s", "v", cuk)
        S.we("s", prev_ctl[0])
        S.we("s", scb_pe_free[0])
        cxk = S.o("s", lambda e: e.activation(SCB[:], kraw, AF.Square), chk=True)
        S.w("pe", "s", cxq)
        S.w("pe", "v", cuq)
        for tc in range(4):
            S.o("pe", lambda e, tc=tc: e.matmul(
                PA[:, tc * 512:(tc + 1) * 512], BONES[:],
                SCA[:, tc * 512:(tc + 1) * 512], start=True, stop=False))
            S.o("pe", lambda e, tc=tc: e.matmul(
                PA[:, tc * 512:(tc + 1) * 512], ONES1[0:1, :],
                EPSR[0:1, 0:512], start=False, stop=True))
        cvq = S.chk("pe")
        S.w("pe", "s", cxk)
        S.w("pe", "v", cuk)
        S.we("pe", vt_half[0])
        S.we("pe", vt_half[1])
        for tc in range(4):
            S.o("pe", lambda e, tc=tc: e.matmul(
                PB[:, tc * 512:(tc + 1) * 512], BONES[:],
                SCB[:, tc * 512:(tc + 1) * 512], start=True, stop=False))
            S.o("pe", lambda e, tc=tc: e.matmul(
                PB[:, tc * 512:(tc + 1) * 512], ONES1[0:1, :],
                EPSR[0:1, 0:512], start=False, stop=True))
        cvk = S.chk("pe")
        qact_half = [None, None]
        kact_half = [None, None]
        kz_half = [None, None]
        scb_user = None
        ln_k = [None, None]
        ln_q = [None, None]
        S.w("s", "pe", cvk)
        for h in (0, 1):
            lo, hi = h * 1024, (h + 1) * 1024
            ln_k[h] = S.o("s", lambda e, lo=lo, hi=hi: e.activation(
                scaf[:, 0:1024], PB[:, lo:hi], AF.Ln), chk=True)
            ce = S.o("s", lambda e, lo=lo, hi=hi: e.activation(
                SCB[:, lo:hi], scaf[:, 0:1024], AF.Exp,
                scale=-0.5, bias=QKP[:, 4:5]), chk=True)
            S.w("v", "s", ce)
            ctmk = S.o("v", lambda e, lo=lo, hi=hi: e.tensor_tensor(
                WORK[:, KR + lo:KR + hi], WORK[:, KR + lo:KR + hi],
                SCB[:, lo:hi], OP.mult), chk=True)
            if h == 1:
                ctm_k[0] = ctmk
            S.w("s", "v", ctmk)
            S.w("s", "v", ckp1z)
            S.we("s", kp1_free[0])
            ck1 = S.o("s", lambda e, lo=lo, hi=hi: e.activation(
                KP1.ap()[64:128, lo:hi], WORK.ap()[64:128, KR + lo:KR + hi],
                AF.Identity, bias=QKP.ap()[64:128, 3:4],
                scale=QKP.ap()[64:128, 2:3]), chk=True)
            ck0 = S.o("s", lambda e, lo=lo, hi=hi: e.activation(
                WORK.ap()[0:64, KR + lo:KR + hi], WORK.ap()[0:64, KR + lo:KR + hi],
                AF.Identity, bias=QKP.ap()[0:64, 3:4],
                scale=QKP.ap()[0:64, 2:3]), chk=True)
            kact_half[h] = ("s", ck0)
            S.w("v", "s", ck1)
            kz_half[h] = ("v", S.o("v", lambda e, lo=lo, hi=hi: e.memset(
                WORK.ap()[64:128, KR + lo:KR + hi], 0.0), chk=True))
            ln_q[h] = S.o("s", lambda e, lo=lo, hi=hi: e.activation(
                scaf[:, 0:1024], PA[:, lo:hi], AF.Ln), chk=True)
            ce = S.o("s", lambda e, lo=lo, hi=hi: e.activation(
                SCB[:, lo:hi], scaf[:, 0:1024], AF.Exp,
                scale=-0.5, bias=QKP[:, 4:5]), chk=True)
            S.w("v", "s", ce)
            ctmq = S.o("v", lambda e, lo=lo, hi=hi: e.tensor_tensor(
                WORK[:, QR + lo:QR + hi], WORK[:, QR + lo:QR + hi],
                SCB[:, lo:hi], OP.mult), chk=True)
            scb_user = ("v", ctmq)
            S.w("s", "v", ctmq)
            cq = S.o("s", lambda e, lo=lo, hi=hi: e.activation(
                WORK[:, QR + lo:QR + hi], WORK[:, QR + lo:QR + hi], AF.Identity,
                bias=QKP[:, 1:2], scale=QKP[:, 0:1]), chk=True)
            qact_half[h] = ("s", cq)
        half_free[0] = ("s", ln_k[0])
        half_free[1] = ("s", ln_q[0])
        half_free[2] = ("s", ln_q[1])
        hctr[0] += (3 - hctr[0] % 3) % 3
        PRB = VR

        if DEBUG and fc == 0:
            S.we("y", qact_half[1])
            S.we("y", kact_half[1])
            S.we("y", tr_last[0])
            _tok = S.d("y", "qy", lambda e: e.dma_start(out=dbg_qkv, in_=WORK[:]))
            S.w("s", "qy", _tok)

        NT = 32
        pbufs = [WORK[:, VR:VR + 2048], SCB[:, 0:2048], SCA[:, 0:2048]]
        probs_free = [("pe", vt_done), scb_user, None]
        lc_free = [None, None]
        st = [dict() for _ in range(NT)]

        def emit_A(t):
            i, hh = divmod(t, 2)
            qoff = 64 * hh
            kv = (i + 1) * 128
            nh = 1 if kv <= 1024 else 2
            par = t % 2
            hf = i // 8
            S.we("pe", qact_half[hf])
            for h2 in range(hf + 1):
                S.we("pe", kact_half[h2])
                S.we("pe", kz_half[h2])
            hids = []
            csc_h = []
            for hx in range(nh):
                hid = hctr[0] % 3
                hctr[0] += 1
                hids.append(hid)
                S.we("pe", half_free[hid])
                psum, poff = halves[hid]
                lo = hx * 1024
                hi = min(kv, lo + 1024)
                off = lo
                while off < hi:
                    n = min(512, hi - off)
                    kop = (WORK[:, KR + off: KR + off + n] if hh == 0
                           else KP1[:, off:off + n])
                    S.o("pe", lambda e, psum=psum, poff=poff, off=off, lo=lo,
                        n=n, i=i, kop=kop: e.matmul(
                        psum[:, poff + off - lo: poff + off - lo + n],
                        WORK[:, QR + i * 128: QR + (i + 1) * 128],
                        kop, start=True, stop=True))
                    off += n
                csc_h.append(S.chk("pe"))
            st[t].update(csc=csc_h[-1], csc_h=csc_h, hids=hids, nh=nh, hh=hh, i=i)

        def emit_B(t):
            i, hh = st[t]["i"], st[t]["hh"]
            kv = (i + 1) * 128
            nh = st[t]["nh"]
            hids = st[t]["hids"]
            par = t % 2
            csc_h = st[t]["csc_h"]
            if t >= 2:
                S.w("v", "s", st[t - 2]["exp"])
            if nh == 2:
                psum, poff = halves[hids[0]]
                S.w("v", "pe", csc_h[0])
                S.o("v", lambda e, psum=psum, poff=poff, par=par: e.reduce_max(
                    NM[:, 4 * par:4 * par + 1], psum[:, poff:poff + 1024],
                    axis=AX.X, negate=True))
            S.w("v", "pe", csc_h[-1])
            dpsum, dpoff = halves[hids[-1]]
            drel = (i * 128) % 1024
            S.o("v", lambda e, dpsum=dpsum, dpoff=dpoff, drel=drel: e.tensor_tensor(
                dpsum[:, dpoff + drel: dpoff + drel + 128],
                dpsum[:, dpoff + drel: dpoff + drel + 128], MASK[:], OP.add))
            hx = nh - 1
            psum, poff = halves[hids[-1]]
            w = min(kv - hx * 1024, 1024)
            S.o("v", lambda e, psum=psum, poff=poff, w=w, hx=hx, par=par:
                e.reduce_max(
                    NM[:, 4 * par + hx:4 * par + hx + 1],
                    psum[:, poff:poff + w], axis=AX.X, negate=True))
            S.o("v", lambda e: e.tensor_copy(JNK[:, 0:2], QKP[:, 0:2]))
            if nh == 2:
                cnm = S.o("v", lambda e, par=par: e.tensor_tensor(
                    NM[:, 4 * par:4 * par + 1], NM[:, 4 * par:4 * par + 1],
                    NM[:, 4 * par + 1:4 * par + 2], OP.min), chk=True)
            else:
                cnm = S.chk("v")
            st[t]["cnm"] = cnm

        def emit_C(t):
            i, hh = st[t]["i"], st[t]["hh"]
            kv = (i + 1) * 128
            nh = st[t]["nh"]
            hids = st[t]["hids"]
            par = t % 2
            pbi = t % 3
            cnm = st[t]["cnm"]
            S.w("s", "v", cnm)
            S.we("s", probs_free[pbi])
            S.we("s", lc_free[par])
            chk_exp = None
            for hx, hid in enumerate(hids):
                psum, poff = halves[hid]
                w = min(kv - hx * 1024, 1024)
                chk_exp = S.o("s", lambda e, psum=psum, poff=poff, w=w, hx=hx,
                              par=par, pbi=pbi: e.activation(
                                  pbufs[pbi][:, hx * 1024: hx * 1024 + w],
                                  psum[:, poff:poff + w], AF.Exp,
                                  bias=NM[:, 4 * par:4 * par + 1],
                                  accum_out=NM[:, 4 * par + 2 + hx:4 * par + 3 + hx]),
                              chk=True)
                half_free[hid] = ("s", chk_exp)
            st[t]["exp"] = chk_exp

        def emit_D(tq, rnd):
            hh1, i1 = st[tq]["hh"], st[tq]["i"]
            prb = pbufs[tq % 3]
            ntr = i1 + 1
            nrnd = (ntr + 7) // 8
            if rnd >= nrnd:
                return
            if rnd == 0:
                S.w("pe", "s", st[tq]["exp"])
                S.we("pe", trfree[0])
                st[tq]["tr_evac"] = [None, None]
            else:
                S.we("pe", st[tq]["tr_evac"][0])
            lo8, hi8 = rnd * 8, min(ntr, rnd * 8 + 8)
            for j in range(lo8, hi8):
                S.o("pe", lambda e, j=j, lo8=lo8, hi8=hi8, prb=prb:
                    e.matmul(
                        TRA[:, (j - lo8) * 128:(j - lo8 + 1) * 128],
                        prb[:, j * 128:(j + 1) * 128], IDENT[:],
                        start=(j == lo8), stop=(j == hi8 - 1),
                        is_transpose=True))
            ctr = S.chk("pe")
            S.w("s", "pe", ctr)
            tok = ("s", S.o("s", lambda e, lo8=lo8, hi8=hi8, prb=prb: e.copy(
                prb[:, lo8 * 128:hi8 * 128],
                TRA[:, 0:(hi8 - lo8) * 128]), chk=True))
            st[tq]["tr_evac"][rnd] = tok
            if rnd == nrnd - 1:
                trfree[0] = tok

        def emit_EF(tq):
            hh1, i1 = st[tq]["hh"], st[tq]["i"]
            qoff1 = 64 * hh1
            par1 = tq % 2
            pbi = tq % 3
            prb = pbufs[pbi]
            ntr = i1 + 1
            nrnd = (ntr + 7) // 8
            avp = AVP2
            tr_evac = st[tq]["tr_evac"]
            S.we("pe", av_last[0])
            S.we("pe", vt_half[1])
            tp = (0, 64) if hh1 == 1 else None
            for rnd in range(nrnd):
                S.we("pe", tr_evac[rnd])
                lo8, hi8 = rnd * 8, min(ntr, rnd * 8 + 8)
                for j in range(lo8, hi8):
                    S.o("pe", lambda e, j=j, i1=i1, qoff1=qoff1, tp=tp, avp=avp,
                        prb=prb: e.matmul(
                            avp[qoff1:qoff1 + 64, 0:128],
                            WORK[:, VT + j * 128 + qoff1: VT + j * 128 + qoff1 + 64],
                            prb[:, j * 128:(j + 1) * 128],
                            start=(j == 0), stop=(j == i1), tile_position=tp))
            cav = S.chk("pe")
            st[tq]["cav"] = cav
            probs_free[pbi] = ("pe", cav)
            S.w("v", "s", st[tq]["exp"])
            if st[tq]["nh"] == 2:
                clc = S.o("v", lambda e, i1=i1, hh1=hh1, par1=par1: e.tensor_tensor(
                    LC[:, 2 * i1 + hh1: 2 * i1 + hh1 + 1],
                    NM[:, 4 * par1 + 2:4 * par1 + 3],
                    NM[:, 4 * par1 + 3:4 * par1 + 4], OP.add), chk=True)
            else:
                clc = S.o("v", lambda e, i1=i1, hh1=hh1, par1=par1: e.tensor_copy(
                    LC[:, 2 * i1 + hh1: 2 * i1 + hh1 + 1],
                    NM[:, 4 * par1 + 2:4 * par1 + 3]), chk=True)
            lc_free[par1] = ("v", clc)
            S.w("v", "pe", cav)
            av_last[0] = ("v", S.o("v", lambda e, fc=fc, i1=i1, qoff1=qoff1,
                                   avp=avp: e.tensor_copy(
                                       AO.ap()[qoff1:qoff1 + 64,
                                               fc * T + i1 * 128: fc * T + (i1 + 1) * 128],
                                       avp[qoff1:qoff1 + 64, 0:128]), chk=True))

        emit_A(0)
        emit_B(0)
        emit_C(0)
        for t in range(NT):
            if t + 1 < NT:
                emit_A(t + 1)
                emit_B(t + 1)
            emit_D(t, 0)
            if t >= 1:
                emit_EF(t - 1)
            if t + 1 < NT:
                emit_C(t + 1)
            emit_D(t, 1)
        emit_EF(NT - 1)
        scb_pe_free[0] = probs_free[1]
        kp1_free[0] = ("pe", st[NT - 1]["csc"])

        S.o("v", lambda e: e.reciprocal(LC[:], LC[:]))
        S.o("v", lambda e: e.tensor_copy(JNK[:, 0:2], QKP[:, 0:2]))
        clb = S.o("v", lambda e: e.tensor_copy(SCB[:, 0:32], LC[:]), chk=True)
        S.w("pe", "v", clb)
        S.we("pe", av_last[0])
        S.we("pe", half_free[0])
        S.we("pe", trfree[0])
        S.we("pe", trfree[1])
        pbbf = PB.ap().bitcast(BF16)
        for i in range(16):
            S.o("pe", lambda e, i=i, pbbf=pbbf: e.matmul(
                pbbf[0:2, i * 128:(i + 1) * 128], SCB[:, 2 * i:2 * i + 2], IDENT[:],
                start=(i % 8 == 0), stop=(i % 8 == 7), is_transpose=True))
        ctl = S.chk("pe")
        S.w("v", "pe", ctl)
        cle = S.o("v", lambda e, pbbf=pbbf: e.tensor_copy(
            SCA.ap()[0:2, 0:2048], pbbf[0:2, 0:2048]), chk=True)
        S.w("pe", "v", cle)
        for t16 in range(16):
            S.o("pe", lambda e, t16=t16: e.matmul(
                PB[:, t16 * 128:(t16 + 1) * 128], SEL2[:],
                SCA.ap()[0:2, t16 * 128:(t16 + 1) * 128],
                start=(t16 % 4 == 0), stop=(t16 % 4 == 3)))
        cbc = S.chk("pe")
        S.w("v", "pe", cbc)
        cnorm = S.o("v", lambda e, fc=fc: e.tensor_tensor(
            aos(fc), aos(fc), PB[:], OP.mult), chk=True)
        norm_tok[0] = ("v", cnorm)
        half_free[0] = ("v", cnorm)
        trfree[0] = ("v", cnorm)
        trfree[1] = ("v", cnorm)
        prev_cbc[0] = ("pe", cbc)
        prev_ctl[0] = ("pe", ctl)

    S.w("g", "pe", last_q_mm[0])
    tok_wo = S.d("g", "qg", wload(R0, dwo))

    S.w("pe", "qg", tok_wo)
    S.we("pe", norm_tok[0])
    wo_cadd = []
    wo_cts = []
    last_wo_mm = None
    wo_tbufs = [SCA, SCB, None]
    for dc in range(NDC):
        psum = PA if dc % 2 == 0 else PB
        if dc >= 2:
            S.w("pe", "s", wo_cts[dc - 2])
        for kc in range(8):
            for tc in range(4):
                S.o("pe", lambda e, psum=psum, dc=dc, tc=tc, kc=kc: e.matmul(
                    psum[:, tc * 512:(tc + 1) * 512],
                    R0[:, kc * 1024 + dc * 128: kc * 1024 + dc * 128 + 128],
                    aos(kc, tc * 512, (tc + 1) * 512),
                    start=(kc == 0), stop=(kc == 7)))
        cwo = S.chk("pe")
        last_wo_mm = cwo
        S.w("s", "pe", cwo)
        if dc >= 3:
            S.w("s", "v", wo_cadd[dc - 3])
        tbuf = [SCA[:, 0:2048], SCB[:, 0:2048], WORK[:, 0:2048]][dc % 3]
        cts = S.o("s", lambda e, psum=psum, dc=dc, tbuf=tbuf: e.activation(
            tbuf, psum[:], AF.Identity,
            bias=PAR[:, 48 + dc:48 + dc + 1],
            scale=PAR[:, 16 + dc:16 + dc + 1]), chk=True)
        wo_cts.append(cts)
        S.w("v", "s", cts)
        cadd = S.o("v", lambda e, dc=dc, tbuf=tbuf: e.tensor_tensor(
            xs(dc), xs(dc), tbuf, OP.add), chk=True)
        wo_cadd.append(cadd)
    wo_done = wo_cadd[-1]

    if DEBUG:
        S.we("y", norm_tok[0])
        _tok = S.d("y", "qy", lambda e: e.dma_start(out=dbg_ao, in_=AO[:]))
        S.w("v", "qy", _tok)
        S.w("s", "qy", _tok)
        S.w("y", "v", wo_done)
        _tok = S.d("y", "qy", lambda e: e.dma_start(out=dbg_x1, in_=X[:]))
        S.w("v", "qy", _tok)

    ln2_chks, ln2_pa_free = layernorm(32, 24, AO[:, 0:2048], None, None,
                                      pre_waits=[("v", wo_done),
                                                 ("s", wo_cts[-1])],
                                      lnscr=aof[:, 3072:5120])

    S.we("pe", ln_stats_free[0])
    for dc in range(NDC):
        S.w("pe", "s", ln2_chks[dc][0])
        S.w("pe", "qg", tok_gatet)
        for tc in range(4):
            S.o("pe", lambda e, dc=dc, tc=tc: e.matmul(
                PB[0:8, tc * 512:(tc + 1) * 512],
                GATET[:, dc * 8:(dc + 1) * 8],
                hs(dc, tc * 512, (tc + 1) * 512),
                start=(dc == 0), stop=(dc == NDC - 1)))
    cgm = S.chk("pe")
    S.w("v", "pe", cgm)
    cgev = S.o("v", lambda e: e.tensor_copy(SCA.ap()[0:8, 0:2048], PB[0:8, 0:2048]),
               chk=True)
    S.w("pe", "v", cgev)
    S.w("pe", "v", ln2_pa_free)
    pabf = PA.ap().bitcast(BF16)
    for t16 in range(16):
        S.o("pe", lambda e, t16=t16, pabf=pabf: e.matmul(
            pabf[0:128, t16 * 8:(t16 + 1) * 8],
            SCA.ap()[0:8, t16 * 128:(t16 + 1) * 128], IDENT.ap()[0:8, 0:8],
            start=(t16 == 0), stop=(t16 == 15), is_transpose=True))
    gate_done = S.chk("pe")

    sbf = SCB.ap().bitcast(F32)
    gex = sbf[:, 128:256]
    g3 = pabf[0:128, 0:128].rearrange("p (g k) -> p g k", k=8)
    ge3 = gex.rearrange("p (g k) -> p g k", k=8)

    def bc16(ap):
        return ap.rearrange("p (g o) -> p g o", o=1).broadcast_to([128, 16, 8])

    S.w("v", "pe", gate_done)
    S.o("v", lambda e: e.reduce_max(
        GST[:, 0:16].rearrange("p (g o) -> p g o", o=1), g3, axis=AX.X))
    S.o("v", lambda e: e.tensor_copy(JNK[:, 0:2], QKP[:, 0:2]))
    cgs = S.o("v", lambda e: e.tensor_tensor(ge3, g3, bc16(GST[:, 0:16]), OP.subtract),
              chk=True)
    S.w("s", "v", cgs)
    cge = S.o("s", lambda e: e.activation(gex, gex, AF.Exp), chk=True)
    S.w("v", "s", cge)
    S.o("v", lambda e: e.reduce_sum(
        GST[:, 16:32].rearrange("p (g o) -> p g o", o=1), ge3, axis=AX.X))
    S.o("v", lambda e: e.reduce_max(
        GST[:, 0:16].rearrange("p (g o) -> p g o", o=1), ge3, axis=AX.X))
    S.o("v", lambda e: e.tensor_copy(JNK[:, 0:2], QKP[:, 0:2]))
    S.o("v", lambda e: e.tensor_tensor(
        sbf[:, 256:384].rearrange("p (g k) -> p g k", k=8), ge3, bc16(GST[:, 0:16]),
        OP.is_ge))
    S.o("v", lambda e: e.scalar_tensor_tensor(
        sbf[:, 256:384], sbf[:, 256:384], -1e30, gex, OP.mult, OP.add))
    S.o("v", lambda e: e.reduce_max(
        GST[:, 32:48].rearrange("p (g o) -> p g o", o=1),
        sbf[:, 256:384].rearrange("p (g k) -> p g k", k=8), axis=AX.X))
    S.o("v", lambda e: e.tensor_copy(JNK[:, 0:2], QKP[:, 0:2]))
    S.o("v", lambda e: e.tensor_tensor(
        sbf[:, 256:384].rearrange("p (g k) -> p g k", k=8), ge3, bc16(GST[:, 32:48]),
        OP.is_ge))
    S.o("v", lambda e: e.tensor_tensor(gex, gex, sbf[:, 256:384], OP.mult))
    S.o("v", lambda e: e.reciprocal(GST[:, 48:64], GST[:, 16:32]))
    S.o("v", lambda e: e.tensor_copy(JNK[:, 0:2], QKP[:, 0:2]))
    S.o("v", lambda e: e.tensor_tensor(ge3, ge3, bc16(GST[:, 48:64]), OP.mult))
    ccb = S.o("v", lambda e: e.tensor_copy(SCA[:, 0:128], gex), chk=True)
    S.w("pe", "v", ccb)
    for t16 in range(16):
        S.o("pe", lambda e, t16=t16, pabf=pabf: e.matmul(
            pabf[0:8, t16 * 128:(t16 + 1) * 128], SCA[:, t16 * 8:(t16 + 1) * 8],
            IDENT[:], start=(t16 % 8 == 0), stop=(t16 % 8 == 7), is_transpose=True))
    ctc = S.chk("pe")
    S.w("v", "pe", ctc)
    crow_done = S.o("v", lambda e, pabf=pabf: e.tensor_copy(
        CROW[:], pabf[0:8, 0:2048]), chk=True)

    regs = [R1, RW, R0]
    reg_free = [("pe", last_k_mm[0]), ("v", wo_cadd[5]), ("s", wo_cts[-1])]
    wtoks = {}
    drams = {"w1": dw1, "w3": dw3, "w2": dw2}
    sdrams = {"w1": dsw1, "w3": dsw3, "w2": dsw2}

    def sched_wload(m_idx, e_idx, mat):
        ridx = (3 * e_idx + m_idx) % 3
        reg = regs[ridx]
        S.we("g", reg_free[ridx])
        if e_idx < NE:
            src = drams[mat].rearrange("e (c p) o -> e p c o", p=128)[e_idx]
        else:
            src = sdrams[mat].rearrange("(c p) o -> p c o", p=128)
        wtoks[(e_idx, mat)] = S.d("g", "qg", lambda e, reg=reg, src=src: e.dma_start(
            out=reg.rearrange("p (c o) -> p c o", c=8), in_=src))

    for m_idx, mat in enumerate(("w1", "w3", "w2")):
        sched_wload(m_idx, 0, mat)

    pa_free_ffn = ("v", crow_done)
    pb_free_ffn = ("v", cgev)
    he_read = ("s", ln2_chks[-1][0])
    prev_csil = None
    cw2 = None
    stts = []

    for e_idx in range(NE + 1):
        is_sh = e_idx == NE
        rW1 = regs[(3 * e_idx + 0) % 3]
        rW3 = regs[(3 * e_idx + 1) % 3]
        rW2 = regs[(3 * e_idx + 2) % 3]
        if not is_sh:
            S.w("pe", "v", crow_done)
            S.we("pe", pa_free_ffn)
            for tc in range(4):
                S.o("pe", lambda e, e_idx=e_idx, tc=tc: e.matmul(
                    PA[:, tc * 512:(tc + 1) * 512],
                    ESEL[:, e_idx * 128:(e_idx + 1) * 128],
                    CROW[:, tc * 512:(tc + 1) * 512], start=True, stop=True))
            ccb2 = S.chk("pe")
            S.w("v", "pe", ccb2)
            cbev = S.o("v", lambda e: e.tensor_copy(SCB[:], PA[:]), chk=True)
            pa_free_ffn = ("v", cbev)
        he_done = None
        for ic in range(8):
            bcol1 = (e_idx * 8 + ic) if not is_sh else (128 + ic)
            bcol3 = (64 + e_idx * 8 + ic) if not is_sh else (136 + ic)
            S.w("pe", "qg", wtoks[(e_idx, "w1")])
            S.we("pe", pa_free_ffn)
            for kc in range(8):
                if e_idx == 0 and ic == 0:
                    S.w("pe", "s", ln2_chks[kc][0])
                for tc in range(4):
                    S.o("pe", lambda e, rW1=rW1, ic=ic, tc=tc, kc=kc: e.matmul(
                        PA[:, tc * 512:(tc + 1) * 512],
                        rW1[:, kc * 1024 + ic * 128: kc * 1024 + ic * 128 + 128],
                        hs(kc, tc * 512, (tc + 1) * 512),
                        start=(kc == 0), stop=(kc == 7)))
            ch1 = S.chk("pe")
            S.w("s", "pe", ch1)
            if prev_csil is not None and he_done is not None:
                S.w("s", "v", he_done)
            csil = S.o("s", lambda e, bcol1=bcol1: e.activation(
                SCA[:], PA[:], AF.Silu, bias=BIA[:, bcol1:bcol1 + 1]), chk=True)
            prev_csil = csil
            pa_free_ffn = ("s", csil)
            S.w("pe", "qg", wtoks[(e_idx, "w3")])
            S.we("pe", pb_free_ffn)
            for kc in range(8):
                for tc in range(4):
                    S.o("pe", lambda e, rW3=rW3, ic=ic, tc=tc, kc=kc: e.matmul(
                        PB[:, tc * 512:(tc + 1) * 512],
                        rW3[:, kc * 1024 + ic * 128: kc * 1024 + ic * 128 + 128],
                        hs(kc, tc * 512, (tc + 1) * 512),
                        start=(kc == 0), stop=(kc == 7)))
            ch3 = S.chk("pe")
            S.w("v", "pe", ch3)
            S.w("v", "s", csil)
            S.we("v", he_read)
            chm = S.o("v", lambda e, ic=ic, bcol3=bcol3: e.scalar_tensor_tensor(
                aos(ic), PB[:], BIA[:, bcol3:bcol3 + 1], SCA[:], OP.add, OP.mult),
                chk=True)
            pb_free_ffn = ("v", chm)
            if not is_sh:
                chm = S.o("v", lambda e, ic=ic: e.tensor_tensor(
                    aos(ic), aos(ic), SCB[:], OP.mult), chk=True)
            he_done = chm
        psfree = {0: pa_free_ffn, 1: pb_free_ffn}
        sh_cts = []
        for dc in range(NDC):
            par = dc % 2
            psum = PA if par == 0 else PB
            S.w("pe", "qg", wtoks[(e_idx, "w2")])
            S.w("pe", "v", he_done)
            S.we("pe", psfree[par])
            for kc in range(8):
                for tc in range(4):
                    S.o("pe", lambda e, rW2=rW2, psum=psum, dc=dc, tc=tc, kc=kc:
                        e.matmul(
                            psum[:, tc * 512:(tc + 1) * 512],
                            rW2[:, kc * 1024 + dc * 128: kc * 1024 + dc * 128 + 128],
                            aos(kc, tc * 512, (tc + 1) * 512),
                            start=(kc == 0), stop=(kc == 7 and not is_sh)))
            if is_sh:
                S.w("pe", "qg", tok_consts)
                S.w("pe", "v", crow_done)
                for tc in range(4):
                    S.o("pe", lambda e, psum=psum, dc=dc, tc=tc: e.matmul(
                        psum[:, tc * 512:(tc + 1) * 512],
                        B2R[0:8, dc * 128:(dc + 1) * 128],
                        CROW[0:8, tc * 512:(tc + 1) * 512],
                        start=False, stop=True))
            cw2 = S.chk("pe")
            if dc == NDC - 1:
                he_read = ("pe", cw2)
            if not is_sh:
                S.w("v", "pe", cw2)
                cst = S.o("v", lambda e, psum=psum, dc=dc: e.scalar_tensor_tensor(
                    xs(dc), psum[:], PAR[:, 40 + dc:40 + dc + 1], xs(dc),
                    OP.mult, OP.add), chk=True)
            else:
                S.w("s", "pe", cw2)
                if dc >= 2:
                    S.w("s", "v", stts[-2])
                tbuf = SCA if par == 0 else SCB
                ct = S.o("s", lambda e, psum=psum, dc=dc, tbuf=tbuf: e.activation(
                    tbuf[:], psum[:], AF.Identity,
                    bias=PAR[:, 56 + dc:56 + dc + 1],
                    scale=PAR[:, 40 + dc:40 + dc + 1]), chk=True)
                sh_cts.append(ct)
                S.w("v", "s", ct)
                cst = S.o("v", lambda e, dc=dc, tbuf=tbuf: e.tensor_tensor(
                    xs(dc), xs(dc), tbuf[:], OP.add), chk=True)
            psfree[par] = ("v", cst)
            stts.append(cst)
        pa_free_ffn = psfree[0]
        pb_free_ffn = psfree[1]
        reg_free[(3 * e_idx + 0) % 3] = ("pe", ch1)
        reg_free[(3 * e_idx + 1) % 3] = ("pe", ch3)
        reg_free[(3 * e_idx + 2) % 3] = ("pe", cw2)
        if e_idx < NE:
            for m_idx, mat in enumerate(("w1", "w3", "w2")):
                sched_wload(m_idx, e_idx + 1, mat)

    for t in stts[-NDC:]:
        S.w("y", "v", t)
    tok_store = S.d("y", "qy", lambda e: e.dma_start(
        out=dout.rearrange("(c p) t -> p c t", p=128),
        in_=X[:].rearrange("p (c t) -> p c t", c=NDC)))
    S.w("y", "qy", tok_store)

    with (
        nc.allow_low_precision("bf16 intermediates are deliberate; fp32 where it matters"),
        nc.semaphore("sem_pe") as sem_pe,
        nc.semaphore("sem_v") as sem_v,
        nc.semaphore("sem_s") as sem_s,
        nc.semaphore("sem_g") as sem_g,
        nc.semaphore("sem_y") as sem_y,
        nc.semaphore("sem_qy") as sem_qy,
        nc.semaphore("sem_qg") as sem_qg,
        nc.Block() as block,
    ):
        sems = {"pe": sem_pe, "v": sem_v, "s": sem_s, "g": sem_g, "y": sem_y,
                "qy": sem_qy, "qg": sem_qg}

        @block.tensor
        def _(eng):
            S.replay(eng, "pe", sems)

        @block.vector
        def _(eng):
            S.replay(eng, "v", sems)

        @block.scalar
        def _(eng):
            S.replay(eng, "s", sems)

        @block.gpsimd
        def _(eng):
            S.replay(eng, "g", sems)

        @block.sync
        def _(eng):
            S.replay(eng, "y", sems)

    return nc


_CACHE = {}
_LAST_RES = None


def kernel(**inputs):
    N = 8
    x = np.asarray(inputs["L0_attn_x"])
    f32 = np.float32

    def c_(a):
        return np.ascontiguousarray(a, dtype=f32)

    wq = c_(np.asarray(inputs["L0_attn_Wq"]).T)
    wk = c_(np.asarray(inputs["L0_attn_Wk"]).T)
    wv = c_(np.asarray(inputs["L0_attn_Wv"]).T)
    wo = c_(np.asarray(inputs["L0_attn_Wo"]).T)
    w1 = c_(inputs["L0_ffn_w1"])
    w3 = c_(inputs["L0_ffn_w3"])
    w2 = c_(inputs["L0_ffn_w2"])
    sw1 = c_(inputs["L0_ffn_sw1"])
    sw3 = c_(inputs["L0_ffn_sw3"])
    sw2 = c_(inputs["L0_ffn_sw2"])
    wmod = c_(inputs["L0_wmod"])
    cfull = np.asarray(inputs["L0_c"], dtype=f32)

    gq = np.asarray(inputs["L0_attn_gq"], dtype=f32)
    bq = np.asarray(inputs["L0_attn_bq"], dtype=f32)
    gk = np.asarray(inputs["L0_attn_gk"], dtype=f32)
    bk = np.asarray(inputs["L0_attn_bk"], dtype=f32)
    qkp = c_(np.stack([np.tile(gq, 2), np.tile(bq, 2),
                       np.tile(gk, 2), np.tile(bk, 2),
                       np.full(128, 0.5 * np.log(64.0), dtype=f32),
                       np.full(128, 0.5 * np.log(1024.0), dtype=f32)], axis=1))

    b1 = np.asarray(inputs["L0_ffn_b1"], dtype=f32)
    b3 = np.asarray(inputs["L0_ffn_b3"], dtype=f32)
    b2 = c_(inputs["L0_ffn_b2"])
    sb1 = np.asarray(inputs["L0_ffn_sb1"], dtype=f32)
    sb3 = np.asarray(inputs["L0_ffn_sb3"], dtype=f32)
    sb2 = np.asarray(inputs["L0_ffn_sb2"], dtype=f32)
    bo = np.asarray(inputs["L0_attn_bo"], dtype=f32)
    biases = np.zeros((128, 160), dtype=f32)
    biases[:, 0:64] = b1.reshape(8, 8, 128).transpose(2, 0, 1).reshape(128, 64)
    biases[:, 64:128] = b3.reshape(8, 8, 128).transpose(2, 0, 1).reshape(128, 64)
    biases[:, 128:136] = sb1.reshape(8, 128).T
    biases[:, 136:144] = sb3.reshape(8, 128).T
    biases[:, 144:152] = bo.reshape(8, 128).T
    biases[:, 152:160] = sb2.reshape(8, 128).T
    biases = c_(biases)

    gatet = c_(np.asarray(inputs["L0_ffn_gate_w"], dtype=f32).T
               .reshape(8, 128, 8).transpose(1, 0, 2).reshape(128, 64))

    allones = np.ones((128, 128), dtype=f32)
    blockones = np.zeros((128, 128), dtype=f32)
    blockones[0:64, 0:64] = 1.0
    blockones[64:128, 64:128] = 1.0
    ident = np.eye(128, dtype=f32)
    maskd = np.triu(np.full((128, 128), -1e30, dtype=f32), 1)
    sel2 = np.zeros((2, 128), dtype=f32)
    sel2[0, 0:64] = 1.0
    sel2[1, 64:128] = 1.0
    esel = np.zeros((8, 1024), dtype=f32)
    for e in range(8):
        esel[e, e * 128:(e + 1) * 128] = 1.0
    ones1 = np.ones((1, 128), dtype=f32)
    epsr = np.zeros((1, 1024), dtype=f32)
    epsr[0, 0:512] = 64 * 1e-5
    epsr[0, 512:1024] = 1024 * 1e-5

    if "nc" not in _CACHE:
        _CACHE["nc"] = build()
    nc = _CACHE["nc"]

    in_maps = []
    for i in range(N):
        in_maps.append({
            "x": c_(x[i].T),
            "wq": wq, "wk": wk, "wv": wv, "wo": wo,
            "w1": w1, "w3": w3, "w2": w2,
            "sw1": sw1, "sw3": sw3, "sw2": sw2,
            "wmod": wmod,
            "cc": c_(cfull[i].reshape(8, 128).T),
            "qkp": qkp, "biases": biases, "gatet": gatet, "b2r": b2,
            "callones": allones, "cblockones": blockones, "cident": ident,
            "cmask": maskd, "csel2": sel2, "cesel": esel,
            "cones1": ones1, "cepsr": epsr, "cidentf": ident,
        })
    res = run_bass_kernel_spmd(nc, in_maps, core_ids=list(range(N)))
    global _LAST_RES
    _LAST_RES = res
    out = np.stack([np.ascontiguousarray(r["out"].T) for r in res.results], axis=0)
    return out.astype(np.float32)


# version 29
# speedup vs baseline: 1.0567x; 1.0567x over previous
import math
import numpy as np
import ml_dtypes
import concourse.bass as bass
import concourse.mybir as mybir
from concourse.bass_utils import run_bass_kernel_spmd

F32 = mybir.dt.float32
BF16 = mybir.dt.bfloat16
AF = mybir.ActivationFunctionType
OP = mybir.AluOpType
AX = mybir.AxisListType

T = 2048
D = 1024
NDC = 8
NFC = 8
NE = 8
DH = 64
EPS = 1e-5

DEBUG = False

ENGS = ("pe", "v", "s", "g", "y")
QUEUES = ("qy", "qg")


class Sch:

    def __init__(self):
        self.seq = {k: [] for k in ENGS}
        self.n = {k: 0 for k in ENGS}
        self.nd = {k: 0 for k in QUEUES}
        self.wm = {k: {} for k in ENGS}

    def w(self, eng, sem, val):
        if not val or sem == eng or val <= self.wm[eng].get(sem, 0):
            return
        self.wm[eng][sem] = val
        self.seq[eng].append(("w", sem, val))

    def we(self, eng, et):
        if et is not None:
            self.w(eng, et[0], et[1])

    def o(self, eng, fn, chk=False):
        self.seq[eng].append(("o", fn, chk))
        if chk:
            self.n[eng] += 1
            return self.n[eng]
        return None

    def chk(self, eng):
        for idx in range(len(self.seq[eng]) - 1, -1, -1):
            it = self.seq[eng][idx]
            if it[0] == "o":
                if not it[2]:
                    self.seq[eng][idx] = ("o", it[1], True)
                    self.n[eng] += 1
                return self.n[eng]
            if it[0] == "d":
                raise RuntimeError("chk would land on a dma")
        raise RuntimeError("no op to chk")

    def d(self, eng, q, fn):
        self.seq[eng].append(("d", fn, q))
        self.nd[q] += 1
        return self.nd[q] * 16

    def replay(self, eng_obj, name, sems):
        for it in self.seq[name]:
            if it[0] == "w":
                eng_obj.wait_ge(sems[it[1]], it[2])
            elif it[0] == "o":
                inst = it[1](eng_obj)
                if it[2]:
                    inst.then_inc(sems[name], 1)
            else:
                it[1](eng_obj).then_inc(sems[it[2]], 16)


def build():
    nc = bass.Bass("TRN2", target_bir_lowering=False)

    dx = nc.dram_tensor("x", [D, T], F32, kind="ExternalInput").ap()
    dwq = nc.dram_tensor("wq", [D, D], F32, kind="ExternalInput").ap()
    dwk = nc.dram_tensor("wk", [D, D], F32, kind="ExternalInput").ap()
    dwv = nc.dram_tensor("wv", [D, D], F32, kind="ExternalInput").ap()
    dwo = nc.dram_tensor("wo", [D, D], F32, kind="ExternalInput").ap()
    dw1 = nc.dram_tensor("w1", [NE, D, D], F32, kind="ExternalInput").ap()
    dw3 = nc.dram_tensor("w3", [NE, D, D], F32, kind="ExternalInput").ap()
    dw2 = nc.dram_tensor("w2", [NE, D, D], F32, kind="ExternalInput").ap()
    dsw1 = nc.dram_tensor("sw1", [D, D], F32, kind="ExternalInput").ap()
    dsw3 = nc.dram_tensor("sw3", [D, D], F32, kind="ExternalInput").ap()
    dsw2 = nc.dram_tensor("sw2", [D, D], F32, kind="ExternalInput").ap()
    dwmod = nc.dram_tensor("wmod", [D, 6144], F32, kind="ExternalInput").ap()
    dcc = nc.dram_tensor("cc", [128, 8], F32, kind="ExternalInput").ap()
    dqkp = nc.dram_tensor("qkp", [128, 6], F32, kind="ExternalInput").ap()
    dbias = nc.dram_tensor("biases", [128, 160], F32, kind="ExternalInput").ap()
    dgatet = nc.dram_tensor("gatet", [128, 64], F32, kind="ExternalInput").ap()
    db2r = nc.dram_tensor("b2r", [8, 1024], F32, kind="ExternalInput").ap()
    dallones = nc.dram_tensor("callones", [128, 128], F32, kind="ExternalInput").ap()
    dblockones = nc.dram_tensor("cblockones", [128, 128], F32, kind="ExternalInput").ap()
    dident = nc.dram_tensor("cident", [128, 128], F32, kind="ExternalInput").ap()
    dmask = nc.dram_tensor("cmask", [128, 128], F32, kind="ExternalInput").ap()
    dsel2 = nc.dram_tensor("csel2", [2, 128], F32, kind="ExternalInput").ap()
    dones1 = nc.dram_tensor("cones1", [1, 128], F32, kind="ExternalInput").ap()
    didentf = nc.dram_tensor("cidentf", [128, 128], F32, kind="ExternalInput").ap()
    depsr = nc.dram_tensor("cepsr", [1, 1024], F32, kind="ExternalInput").ap()
    desel = nc.dram_tensor("cesel", [8, 1024], F32, kind="ExternalInput").ap()
    dout = nc.dram_tensor("out", [D, T], F32, kind="ExternalOutput").ap()
    if DEBUG:
        dbg_par = nc.dram_tensor("dbg_par", [128, 64], F32, kind="ExternalOutput").ap()
        dbg_h = nc.dram_tensor("dbg_h", [128, NDC * T], BF16, kind="ExternalOutput").ap()
        dbg_qkv = nc.dram_tensor("dbg_qkv", [128, 8192], BF16, kind="ExternalOutput").ap()
        dbg_ao = nc.dram_tensor("dbg_ao", [128, NDC * T], BF16, kind="ExternalOutput").ap()
        dbg_x1 = nc.dram_tensor("dbg_x1", [128, NDC * T], F32, kind="ExternalOutput").ap()
        dbg_crow = nc.dram_tensor("dbg_crow", [8, T], BF16, kind="ExternalOutput").ap()
        dbg_h2 = nc.dram_tensor("dbg_h2", [128, NDC * T], BF16, kind="ExternalOutput").ap()
        dbg_ao2 = nc.dram_tensor("dbg_ao2", [128, T], BF16, kind="ExternalOutput").ap()
        dbg_lc = nc.dram_tensor("dbg_lc", [128, 32], F32, kind="ExternalOutput").ap()
        dbg_pt = nc.dram_tensor("dbg_pt", [128, T], BF16, kind="ExternalOutput").ap()
        dbg_isge = nc.dram_tensor("dbg_isge", [128, 16], F32, kind="ExternalOutput").ap()
        dbg_sc = nc.dram_tensor("dbg_sc", [128, 2048], F32, kind="ExternalOutput").ap()
        dbg_nm = nc.dram_tensor("dbg_nm", [128, 4], F32, kind="ExternalOutput").ap()
        dbg_pr = nc.dram_tensor("dbg_pr", [128, 2048], BF16, kind="ExternalOutput").ap()

    X = nc.alloc_sbuf_tensor("X", [128, NDC * T], F32)
    HH = nc.alloc_sbuf_tensor("HH", [128, NDC * T], BF16)
    AO = nc.alloc_sbuf_tensor("AO", [128, NDC * T], BF16)
    WB = nc.alloc_sbuf_tensor("WB", [128, 2 * 8192], BF16)
    WORK = nc.alloc_sbuf_tensor("WORK", [128, 8192], BF16)
    SCA = nc.alloc_sbuf_tensor("SCA", [128, T], BF16)
    SCB = nc.alloc_sbuf_tensor("SCB", [128, T], BF16)
    WVS = nc.alloc_sbuf_tensor("WVS", [128, 2048], BF16)
    KP1 = nc.alloc_sbuf_tensor("KP1", [128, 2048], BF16)
    CROW = nc.alloc_sbuf_tensor("CROW", [8, T], BF16)
    QKP = nc.alloc_sbuf_tensor("QKP", [128, 6], F32)
    CC = nc.alloc_sbuf_tensor("CC", [128, 8], F32)
    SCS = nc.alloc_sbuf_tensor("SCS", [128, 8], F32)
    BIA = nc.alloc_sbuf_tensor("BIA", [128, 160], F32)
    PAR = nc.alloc_sbuf_tensor("PAR", [128, 64], F32)
    GATET = nc.alloc_sbuf_tensor("GATET", [128, 64], BF16)
    B2R = nc.alloc_sbuf_tensor("B2R", [8, 1024], BF16)
    ONES = nc.alloc_sbuf_tensor("ONES", [128, 128], BF16)
    BONES = nc.alloc_sbuf_tensor("BONES", [128, 128], BF16)
    IDENT = nc.alloc_sbuf_tensor("IDENT", [128, 128], BF16)
    MASK = nc.alloc_sbuf_tensor("MASK", [128, 128], BF16)
    SEL2 = nc.alloc_sbuf_tensor("SEL2", [2, 128], BF16)
    ONES1 = nc.alloc_sbuf_tensor("ONES1", [1, 128], BF16)
    IDENTF = nc.alloc_sbuf_tensor("IDENTF", [128, 128], F32)
    EPSR = nc.alloc_sbuf_tensor("EPSR", [1, 1024], BF16)
    ESEL = nc.alloc_sbuf_tensor("ESEL", [8, 1024], BF16)
    NM = nc.alloc_sbuf_tensor("NM", [128, 8], F32)
    LC = nc.alloc_sbuf_tensor("LC", [128, 32], F32)
    GST = nc.alloc_sbuf_tensor("GST", [128, 64], F32)
    JNK = nc.alloc_sbuf_tensor("JNK", [128, 4], F32)

    PA = nc.alloc_psum_tensor("PA", [128, 2048], F32)
    PB = nc.alloc_psum_tensor("PB", [128, 2048], F32)

    S = Sch()
    aof = AO.ap().bitcast(F32)

    def xs(dc, lo=0, hi=T):
        return X[:, dc * T + lo: dc * T + hi]

    def hs(dc, lo=0, hi=T):
        return HH[:, dc * T + lo: dc * T + hi]

    def aos(dc, lo=0, hi=T):
        return AO[:, dc * T + lo: dc * T + hi]

    tok_x = None
    tok_smalls = None
    for sb_t, dr in ((QKP, dqkp), (CC, dcc), (BIA, dbias), (IDENTF, didentf)):
        tok_smalls = S.d("y", "qy", lambda e, a=sb_t, b=dr: e.dma_start(out=a[:], in_=b[:]))
    tok_gatet = S.d("g", "qg", lambda e: e.dma_start(out=GATET[:], in_=dgatet[:]))
    tok_consts = None
    for sb_t, dr in ((ONES, dallones), (BONES, dblockones), (IDENT, dident),
                     (MASK, dmask), (SEL2, dsel2), (ESEL, desel), (B2R, db2r),
                     (ONES1, dones1), (EPSR, depsr)):
        tok_consts = S.d("g", "qg", lambda e, a=sb_t, b=dr: e.dma_start(out=a[:], in_=b[:]))

    def wload(dst_ap, dram_ap):
        return lambda e: e.dma_start(
            out=dst_ap.rearrange("p (c o) -> p c o", c=8),
            in_=dram_ap.rearrange("(c p) o -> p c o", p=128))

    R0 = WB[:, 0:8192]
    R1 = WB[:, 8192:16384]
    RW = WORK[:, 0:8192]
    tok_wq = S.d("g", "qg", wload(R0, dwq))
    tok_wk = S.d("g", "qg", wload(R1, dwk))

    def wv_piece(fc):
        pc = fc % 2
        return S.d("g", "qg", lambda e, fc=fc, pc=pc: e.dma_start(
            out=WVS[:, pc * 1024:(pc + 1) * 1024].rearrange("p (c o) -> p c o", c=8),
            in_=dwv.rearrange("(c p) o -> p c o", p=128)[:, :, fc * 128:(fc + 1) * 128]))

    tok_wv = [wv_piece(0), wv_piece(1)]

    S.w("s", "qy", tok_smalls)
    chk_scs = S.o("s", lambda e: e.activation(SCS[:], CC[:], AF.Silu), chk=True)

    NPC = 12
    PW = 6144 // NPC
    modp = PB[:, 2000:2048]
    ROWB = nc.alloc_sbuf_tensor("ROWB", [1, 512], F32)
    piece_tok = [None] * NPC
    piece_done = [None] * NPC
    ev_tok = None
    ev_toks = []
    tr_toks = []
    first_tr = True
    for p in range(NPC):
        if p >= 2:
            S.w("y", "pe", piece_done[p - 2])
        piece_tok[p] = S.d("y", "qy", lambda e, p=p: e.dma_start(
            out=aof[:, (p % 2) * 4096:(p % 2) * 4096 + 4096].rearrange(
                "p (c o) -> p c o", c=8),
            in_=dwmod.rearrange("(c p) o -> p c o", p=128)[:, :, p * PW:(p + 1) * PW]))
        if p == 3:
            tok_x = S.d("y", "qy", lambda e: e.dma_start(
                out=X[:].rearrange("p (c t) -> p c t", c=NDC),
                in_=dx.rearrange("(c p) t -> p c t", p=128)))
        S.w("pe", "qy", piece_tok[p])
        if p == 0:
            S.w("pe", "s", chk_scs)
        modrow = PB[0:1, (p % 2) * 512:(p % 2) * 512 + 512]
        if len(ev_toks) >= 2:
            S.w("pe", "v", ev_toks[-2])
        for kc in range(8):
            S.o("pe", lambda e, p=p, kc=kc, modrow=modrow: e.matmul(
                modrow, SCS[:, kc:kc + 1],
                aof[:, (p % 2) * 4096 + kc * PW: (p % 2) * 4096 + kc * PW + PW],
                start=(kc == 0), stop=(kc == 7)))
        cmm = S.chk("pe")
        piece_done[p] = cmm
        S.w("v", "pe", cmm)
        if tr_toks:
            S.w("v", "pe", tr_toks[-1])
        ev_tok = S.o("v", lambda e, modrow=modrow: e.tensor_copy(ROWB[:], modrow),
                     chk=True)
        ev_toks.append(ev_tok)
        S.w("pe", "v", ev_tok)
        for cb in range(4):
            col = p * 4 + cb
            S.o("pe", lambda e, cb=cb, col=col, first=first_tr: e.matmul(
                modp[:, col:col + 1], ROWB[0:1, cb * 128:(cb + 1) * 128],
                IDENTF[0:1, 0:1], start=first, stop=(col == 47),
                is_transpose=True))
            first_tr = False
        tr_toks.append(S.chk("pe"))
    piece_done[NPC - 1] = tr_toks[-1]
    chk_modmm = piece_done[-1]

    S.w("v", "pe", tr_toks[3])
    S.w("v", "qy", tok_smalls)

    def asm1(e):
        e.tensor_copy(PAR[:, 0:8], modp[:, 0:8])
        return e.tensor_scalar_add(PAR[:, 8:16], modp[:, 8:16], 1.0)
    chk_par1 = S.o("v", asm1, chk=True)
    S.w("v", "pe", chk_modmm)

    def asm(e):
        e.tensor_copy(PAR[:, 24:32], modp[:, 24:32])
        e.tensor_scalar_add(PAR[:, 32:40], modp[:, 32:40], 1.0)
        A, B = 0, 32
        for base, src in ((A, 16), (B, 40)):
            e.tensor_single_scalar(GST[:, base:base + 8], modp[:, src:src + 8],
                                   0.0, OP.is_ge)
        for base in (A, B):
            e.tensor_scalar_mul(GST[:, base + 8:base + 16], GST[:, base:base + 8], 2.0)
        for base in (A, B):
            e.tensor_single_scalar(GST[:, base + 8:base + 16],
                                   GST[:, base + 8:base + 16], 1.0, OP.subtract)
        for base, src in ((A, 16), (B, 40)):
            e.tensor_scalar_mul(GST[:, base + 16:base + 24], modp[:, src:src + 8], -1.0)
        for base in (A, B):
            e.tensor_scalar_max(GST[:, base + 16:base + 24],
                                GST[:, base + 16:base + 24], 0.25)
        for base, src in ((A, 16), (B, 40)):
            e.tensor_scalar_max(GST[:, base + 24:base + 32], modp[:, src:src + 8], 0.25)
        for base in (A, B):
            e.tensor_max(GST[:, base + 16:base + 24], GST[:, base + 16:base + 24],
                         GST[:, base + 24:base + 32])
        for base, dst in ((A, 16), (B, 40)):
            e.tensor_mul(PAR[:, dst:dst + 8], GST[:, base + 8:base + 16],
                         GST[:, base + 16:base + 24])
        e.tensor_mul(PAR[:, 48:56], PAR[:, 16:24], BIA[:, 144:152])
        if DEBUG:
            e.tensor_single_scalar(GST[:, 32:40], modp[:, 16:24], 0.0, OP.is_ge)
            e.tensor_scalar_mul(GST[:, 40:48], GST[:, 32:40], 2.0)
            e.tensor_single_scalar(GST[:, 40:48], GST[:, 40:48], 1.0, OP.subtract)
        return e.tensor_mul(PAR[:, 56:64], PAR[:, 40:48], BIA[:, 152:160])
    chk_par = S.o("v", asm, chk=True)
    S.w("s", "v", chk_par)

    ln_stats_free = [None]

    def layernorm(sc_col, bi_col, t_slice, h2f32_slice, h2_cb, pre_waits, lnscr):
        for eng in ("pe", "v", "s"):
            for et in pre_waits:
                S.we(eng, et)
        S.w("v", "qy", tok_x)
        S.w("s", "qy", tok_x)
        last_mm = None
        for dc in range(NDC):
            if last_mm is not None:
                S.w("v", "pe", last_mm)
                S.w("s", "pe", last_mm)
            ccp = S.o("v", lambda e, dc=dc: e.tensor_copy(SCA[:], xs(dc)), chk=True)
            csq = S.o("s", lambda e, dc=dc: e.activation(SCB[:], xs(dc), AF.Square),
                      chk=True)
            S.w("pe", "v", ccp)
            S.w("pe", "s", csq)
            S.w("pe", "qg", tok_consts)
            for tc in range(4):
                S.o("pe", lambda e, dc=dc, tc=tc: e.matmul(
                    PA[:, tc * 512:(tc + 1) * 512], ONES[:],
                    SCA[:, tc * 512:(tc + 1) * 512],
                    start=(dc == 0), stop=(dc == NDC - 1)))
                S.o("pe", lambda e, dc=dc, tc=tc: e.matmul(
                    PB[:, tc * 512:(tc + 1) * 512], ONES[:],
                    SCB[:, tc * 512:(tc + 1) * 512],
                    start=(dc == 0), stop=False))
            last_mm = S.chk("pe")
        for tc in range(4):
            S.o("pe", lambda e, tc=tc: e.matmul(
                PB[:, tc * 512:(tc + 1) * 512], ONES1[0:1, :],
                EPSR[0:1, 512:1024], start=False, stop=True))
        last_mm = S.chk("pe")
        S.w("v", "pe", last_mm)
        S.o("v", lambda e: e.tensor_scalar_mul(SCA[:], PA[:], 1.0 / D))
        ct1 = S.o("v", lambda e: e.tensor_tensor(SCB[:], SCA[:], PA[:], OP.mult),
                  chk=True)
        crstd = None
        cln_prev = None
        for h in (0, 1):
            if cln_prev is not None:
                S.w("v", "s", cln_prev)
            ct2 = S.o("v", lambda e, h=h: e.scalar_tensor_tensor(
                lnscr[:, 0:1024], SCB[:, h * 1024:(h + 1) * 1024], -1.0,
                PB[:, h * 1024:(h + 1) * 1024], OP.mult, OP.add), chk=True)
            S.w("s", "v", ct2)
            cln_prev = S.o("s", lambda e: e.activation(
                lnscr[:, 1024:2048], lnscr[:, 0:1024], AF.Ln), chk=True)
            crstd = S.o("s", lambda e, h=h: e.activation(
                SCB[:, h * 1024:(h + 1) * 1024], lnscr[:, 1024:2048], AF.Exp,
                scale=-0.5, bias=QKP[:, 5:6]), chk=True)
            ln_stats_free[0] = ("v", ct2)
        h_chks = []
        S.w("v", "s", crstd)
        cu = None
        prev_gate_tok = None
        for dc in range(NDC):
            if h_chks:
                S.w("v", "s", h_chks[-1][1] or h_chks[-1][0])
            S.o("v", lambda e, dc=dc: e.scalar_tensor_tensor(
                SCA[:], PA[:], -1.0 / D, xs(dc), OP.mult, OP.add))
            cu = S.o("v", lambda e, t=t_slice: e.tensor_tensor(t, SCA[:], SCB[:],
                                                               OP.mult), chk=True)
            S.w("s", "v", cu)
            ch = S.o("s", lambda e, dc=dc, t=t_slice: e.activation(
                hs(dc), t, AF.Identity, bias=PAR[:, bi_col + dc:bi_col + dc + 1],
                scale=PAR[:, sc_col + dc:sc_col + dc + 1]), chk=True)
            ch2 = None
            if h2f32_slice is not None:
                if prev_gate_tok is not None:
                    S.w("s", "pe", prev_gate_tok)
                ch2 = S.o("s", lambda e, dc=dc, t=t_slice: e.activation(
                    h2f32_slice, t, AF.Identity, bias=PAR[:, bi_col + dc:bi_col + dc + 1],
                    scale=PAR[:, sc_col + dc:sc_col + dc + 1]), chk=True)
                prev_gate_tok = h2_cb(dc, ch2)
            h_chks.append((ch, ch2, dc))
        return h_chks, cu

    if DEBUG:
        S.w("y", "v", chk_par)
        S.d("y", "qy", lambda e: e.dma_start(out=dbg_par, in_=PAR[:]))
        S.d("y", "qy", lambda e: e.dma_start(out=dbg_isge, in_=GST[:, 32:48]))

    workf = WORK.ap().bitcast(F32)

    def ln1_halved():
        chks = [[None, None] for _ in range(NDC)]
        S.w("v", "qy", tok_x)
        S.w("s", "qy", tok_x)
        S.w("s", "v", chk_par1)
        last_mm = None
        last_u = None
        prev_ch = None
        for th in (0, 1):
            tlo = th * 1024
            for dc in range(NDC):
                if last_mm is not None:
                    S.w("v", "pe", last_mm)
                    S.w("s", "pe", last_mm)
                ccp = S.o("v", lambda e, dc=dc, tlo=tlo: e.tensor_copy(
                    SCA[:, 0:1024], xs(dc, tlo, tlo + 1024)), chk=True)
                csq = S.o("s", lambda e, dc=dc, tlo=tlo: e.activation(
                    SCB[:, 0:1024], xs(dc, tlo, tlo + 1024), AF.Square), chk=True)
                S.w("pe", "v", ccp)
                S.w("pe", "s", csq)
                S.w("pe", "qg", tok_consts)
                if th == 1 and dc == 0:
                    S.w("pe", "v", last_u)
                for tc in (0, 1):
                    S.o("pe", lambda e, tc=tc, dc=dc: e.matmul(
                        PA[:, tc * 512:(tc + 1) * 512], ONES[:],
                        SCA[:, tc * 512:(tc + 1) * 512],
                        start=(dc == 0), stop=(dc == NDC - 1)))
                    S.o("pe", lambda e, tc=tc, dc=dc: e.matmul(
                        PA[:, 1024 + tc * 512:1024 + (tc + 1) * 512], ONES[:],
                        SCB[:, tc * 512:(tc + 1) * 512],
                        start=(dc == 0), stop=False))
                last_mm = S.chk("pe")
            for tc in (0, 1):
                S.o("pe", lambda e, tc=tc: e.matmul(
                    PA[:, 1024 + tc * 512:1024 + (tc + 1) * 512], ONES1[0:1, :],
                    EPSR[0:1, 512:1024], start=False, stop=True))
            last_mm = S.chk("pe")
            S.w("v", "pe", last_mm)
            S.o("v", lambda e: e.tensor_scalar_mul(SCA[:, 0:1024], PA[:, 0:1024],
                                                   1.0 / D))
            S.o("v", lambda e: e.tensor_tensor(SCB[:, 0:1024], SCA[:, 0:1024],
                                               PA[:, 0:1024], OP.mult))
            ct2 = S.o("v", lambda e: e.scalar_tensor_tensor(
                workf[:, 1024:2048], SCB[:, 0:1024], -1.0, PA[:, 1024:2048],
                OP.mult, OP.add), chk=True)
            S.w("s", "v", ct2)
            S.o("s", lambda e: e.activation(workf[:, 2048:3072], workf[:, 1024:2048],
                                            AF.Ln))
            crstd = S.o("s", lambda e: e.activation(
                SCB[:, 0:1024], workf[:, 2048:3072], AF.Exp,
                scale=-0.5, bias=QKP[:, 5:6]), chk=True)
            S.w("v", "s", crstd)
            for dc in range(NDC):
                if prev_ch is not None:
                    S.w("v", "s", prev_ch)
                S.o("v", lambda e, dc=dc, tlo=tlo: e.scalar_tensor_tensor(
                    SCA[:, 0:1024], PA[:, 0:1024], -1.0 / D,
                    xs(dc, tlo, tlo + 1024), OP.mult, OP.add))
                cu = S.o("v", lambda e: e.tensor_tensor(
                    WORK[:, 0:1024], SCA[:, 0:1024], SCB[:, 0:1024], OP.mult),
                    chk=True)
                last_u = cu
                S.w("s", "v", cu)
                prev_ch = S.o("s", lambda e, dc=dc, tlo=tlo: e.activation(
                    hs(dc, tlo, tlo + 1024), WORK[:, 0:1024], AF.Identity,
                    bias=PAR[:, 0 + dc:0 + dc + 1],
                    scale=PAR[:, 8 + dc:8 + dc + 1]), chk=True)
                chks[dc][th] = prev_ch
        return chks, last_u

    ln1_hchks, ln1_pa_free = ln1_halved()

    QR, KR, VR, VT = 0, 2048, 4096, 6144
    TRA = PB[:, 1024:1536].bitcast(BF16)
    TRB = PB[:, 1536:2048].bitcast(BF16)
    AVP = PB[:, 1024:1152]
    AVP2 = PB[:, 1536:1664]
    trfree = [None, None]

    halves = [(PB, 0), (PA, 0), (PA, 1024)]
    half_free = [None, None, None]
    hctr = [0]
    tr_last = [None]
    av_last = [None]
    norm_tok = [None]
    ctm_k = [None]
    prev_cbc = [None]
    prev_ctl = [None]
    scb_pe_free = [None]
    kp1_free = [None]
    ao_guard = [False]
    last_q_mm = [None]
    last_k_mm = [None]

    ckp1z = S.o("v", lambda e: e.memset(KP1.ap()[0:64, 0:2048], 0.0), chk=True)

    for fc in range(NFC):
        chk_evac = {}
        for mat, psum, wreg, wtok, dst in (("q", PA, R0, tok_wq, QR),
                                           ("k", PB, R1, tok_wk, KR)):
            S.w("pe", "qg", wtok)
            if fc == 0:
                S.w("pe", "v", ln1_pa_free)
            else:
                if mat == "q":
                    S.we("pe", half_free[1])
                    S.we("pe", half_free[2])
                else:
                    S.we("pe", half_free[0])
            for kc in range(8):
                if fc == 0:
                    S.w("pe", "s", ln1_hchks[kc][0])
                    S.w("pe", "s", ln1_hchks[kc][1])
                for tc in range(4):
                    S.o("pe", lambda e, psum=psum, wreg=wreg, tc=tc, kc=kc, fc=fc:
                        e.matmul(
                            psum[:, tc * 512:(tc + 1) * 512],
                            wreg[:, kc * 1024 + fc * 128: kc * 1024 + fc * 128 + 128],
                            hs(kc, tc * 512, (tc + 1) * 512),
                            start=(kc == 0), stop=(kc == 7)))
            cm = S.chk("pe")
            if fc == NFC - 1:
                (last_q_mm if mat == "q" else last_k_mm)[0] = cm
            S.w("v", "pe", cm)
            chk_evac[mat] = S.o("v", lambda e, psum=psum, dst=dst: e.tensor_copy(
                WORK[:, dst:dst + 2048], psum[:]), chk=True)
        S.w("pe", "v", chk_evac["q"])
        S.w("pe", "qg", tok_wv[fc])
        pc = fc % 2
        for kc in range(8):
            for tc in range(4):
                S.o("pe", lambda e, tc=tc, kc=kc, pc=pc: e.matmul(
                    PA[:, tc * 512:(tc + 1) * 512],
                    WVS[:, pc * 1024 + kc * 128: pc * 1024 + kc * 128 + 128],
                    hs(kc, tc * 512, (tc + 1) * 512),
                    start=(kc == 0), stop=(kc == 7)))
        cmv = S.chk("pe")
        if fc + 2 < NFC:
            S.w("g", "pe", cmv)
            tok_wv.append(wv_piece(fc + 2))
        S.w("v", "pe", cmv)
        cev0 = S.o("v", lambda e: e.tensor_copy(
            WORK[:, VR:VR + 1024], PA[:, 0:1024]), chk=True)
        chk_evac["v"] = S.o("v", lambda e: e.tensor_copy(
            WORK[:, VR + 1024:VR + 2048], PA[:, 1024:2048]), chk=True)

        vt_half = [None, None]
        vt_done = None
        for rnd in range(2):
            trv = TRA if rnd == 0 else TRB
            S.w("pe", "v", cev0 if rnd == 0 else chk_evac["v"])
            S.we("pe", trfree[rnd])
            for j8 in range(8):
                j = rnd * 8 + j8
                S.o("pe", lambda e, j=j, j8=j8, trv=trv: e.matmul(
                    trv[:, j8 * 128:(j8 + 1) * 128],
                    WORK[:, VR + j * 128: VR + (j + 1) * 128], IDENT[:],
                    start=(j8 == 0), stop=(j8 == 7), is_transpose=True))
            ct = S.chk("pe")
            vt_done = ct
            S.w("v", "pe", ct)
            tok = ("v", S.o("v", lambda e, rnd=rnd, trv=trv: e.tensor_copy(
                WORK[:, VT + rnd * 1024: VT + (rnd + 1) * 1024], trv[:]), chk=True))
            trfree[rnd] = tok
            vt_half[rnd] = tok

        pa_free = ("v", chk_evac["v"])
        pb_free = ("v", chk_evac["k"])
        qraw = WORK[:, QR:QR + 2048]
        kraw = WORK[:, KR:KR + 2048]
        scaf = SCA.ap().bitcast(F32)
        S.we("pe", pa_free)
        for tc in range(4):
            S.o("pe", lambda e, tc=tc: e.matmul(
                PA[:, tc * 512:(tc + 1) * 512], BONES[:],
                qraw[:, tc * 512:(tc + 1) * 512], start=True, stop=True))
        csq_ = S.chk("pe")
        S.we("pe", pb_free)
        S.we("pe", vt_half[0])
        S.we("pe", vt_half[1])
        for tc in range(4):
            S.o("pe", lambda e, tc=tc: e.matmul(
                PB[:, tc * 512:(tc + 1) * 512], BONES[:],
                kraw[:, tc * 512:(tc + 1) * 512], start=True, stop=True))
        csk_ = S.chk("pe")
        S.w("v", "pe", csq_)
        cuq = S.o("v", lambda e: e.scalar_tensor_tensor(
            qraw, PA[:], -1.0 / DH, qraw, OP.mult, OP.add), chk=True)
        S.w("v", "pe", csk_)
        cuk = S.o("v", lambda e: e.scalar_tensor_tensor(
            kraw, PB[:], -1.0 / DH, kraw, OP.mult, OP.add), chk=True)
        S.w("s", "v", cuq)
        S.we("s", prev_cbc[0])
        cxq = S.o("s", lambda e: e.activation(SCA[:], qraw, AF.Square), chk=True)
        S.w("s", "v", cuk)
        S.we("s", prev_ctl[0])
        S.we("s", scb_pe_free[0])
        cxk = S.o("s", lambda e: e.activation(SCB[:], kraw, AF.Square), chk=True)
        S.w("pe", "s", cxq)
        S.w("pe", "v", cuq)
        for tc in range(4):
            S.o("pe", lambda e, tc=tc: e.matmul(
                PA[:, tc * 512:(tc + 1) * 512], BONES[:],
                SCA[:, tc * 512:(tc + 1) * 512], start=True, stop=False))
            S.o("pe", lambda e, tc=tc: e.matmul(
                PA[:, tc * 512:(tc + 1) * 512], ONES1[0:1, :],
                EPSR[0:1, 0:512], start=False, stop=True))
        cvq = S.chk("pe")
        S.w("pe", "s", cxk)
        S.w("pe", "v", cuk)
        S.we("pe", vt_half[0])
        S.we("pe", vt_half[1])
        for tc in range(4):
            S.o("pe", lambda e, tc=tc: e.matmul(
                PB[:, tc * 512:(tc + 1) * 512], BONES[:],
                SCB[:, tc * 512:(tc + 1) * 512], start=True, stop=False))
            S.o("pe", lambda e, tc=tc: e.matmul(
                PB[:, tc * 512:(tc + 1) * 512], ONES1[0:1, :],
                EPSR[0:1, 0:512], start=False, stop=True))
        cvk = S.chk("pe")
        qact_half = [None, None]
        kact_half = [None, None]
        kz_half = [None, None]
        scb_user = None
        ln_k = [None, None]
        ln_q = [None, None]
        S.w("s", "pe", cvk)
        for h in (0, 1):
            lo, hi = h * 1024, (h + 1) * 1024
            ln_k[h] = S.o("s", lambda e, lo=lo, hi=hi: e.activation(
                scaf[:, 0:1024], PB[:, lo:hi], AF.Ln), chk=True)
            ce = S.o("s", lambda e, lo=lo, hi=hi: e.activation(
                SCB[:, lo:hi], scaf[:, 0:1024], AF.Exp,
                scale=-0.5, bias=QKP[:, 4:5]), chk=True)
            S.w("v", "s", ce)
            ctmk = S.o("v", lambda e, lo=lo, hi=hi: e.tensor_tensor(
                WORK[:, KR + lo:KR + hi], WORK[:, KR + lo:KR + hi],
                SCB[:, lo:hi], OP.mult), chk=True)
            if h == 1:
                ctm_k[0] = ctmk
            S.w("s", "v", ctmk)
            S.w("s", "v", ckp1z)
            S.we("s", kp1_free[0])
            ck1 = S.o("s", lambda e, lo=lo, hi=hi: e.activation(
                KP1.ap()[64:128, lo:hi], WORK.ap()[64:128, KR + lo:KR + hi],
                AF.Identity, bias=QKP.ap()[64:128, 3:4],
                scale=QKP.ap()[64:128, 2:3]), chk=True)
            ck0 = S.o("s", lambda e, lo=lo, hi=hi: e.activation(
                WORK.ap()[0:64, KR + lo:KR + hi], WORK.ap()[0:64, KR + lo:KR + hi],
                AF.Identity, bias=QKP.ap()[0:64, 3:4],
                scale=QKP.ap()[0:64, 2:3]), chk=True)
            kact_half[h] = ("s", ck0)
            S.w("v", "s", ck1)
            kz_half[h] = ("v", S.o("v", lambda e, lo=lo, hi=hi: e.memset(
                WORK.ap()[64:128, KR + lo:KR + hi], 0.0), chk=True))
            ln_q[h] = S.o("s", lambda e, lo=lo, hi=hi: e.activation(
                scaf[:, 0:1024], PA[:, lo:hi], AF.Ln), chk=True)
            ce = S.o("s", lambda e, lo=lo, hi=hi: e.activation(
                SCB[:, lo:hi], scaf[:, 0:1024], AF.Exp,
                scale=-0.5, bias=QKP[:, 4:5]), chk=True)
            S.w("v", "s", ce)
            ctmq = S.o("v", lambda e, lo=lo, hi=hi: e.tensor_tensor(
                WORK[:, QR + lo:QR + hi], WORK[:, QR + lo:QR + hi],
                SCB[:, lo:hi], OP.mult), chk=True)
            scb_user = ("v", ctmq)
            S.w("s", "v", ctmq)
            cq = S.o("s", lambda e, lo=lo, hi=hi: e.activation(
                WORK[:, QR + lo:QR + hi], WORK[:, QR + lo:QR + hi], AF.Identity,
                bias=QKP[:, 1:2], scale=QKP[:, 0:1]), chk=True)
            qact_half[h] = ("s", cq)
        half_free[0] = ("s", ln_k[0])
        half_free[1] = ("s", ln_q[0])
        half_free[2] = ("s", ln_q[1])
        hctr[0] += (3 - hctr[0] % 3) % 3
        PRB = VR

        if DEBUG and fc == 0:
            S.we("y", qact_half[1])
            S.we("y", kact_half[1])
            S.we("y", tr_last[0])
            _tok = S.d("y", "qy", lambda e: e.dma_start(out=dbg_qkv, in_=WORK[:]))
            S.w("s", "qy", _tok)

        NT = 32
        pbufs = [WORK[:, VR:VR + 2048], SCB[:, 0:2048], SCA[:, 0:2048]]
        probs_free = [("pe", vt_done), scb_user, None]
        lc_free = [None, None]
        st = [dict() for _ in range(NT)]

        def emit_A(t):
            i, hh = divmod(t, 2)
            qoff = 64 * hh
            kv = (i + 1) * 128
            nh = 1 if kv <= 1024 else 2
            par = t % 2
            hf = i // 8
            S.we("pe", qact_half[hf])
            for h2 in range(hf + 1):
                S.we("pe", kact_half[h2])
                S.we("pe", kz_half[h2])
            hids = []
            csc_h = []
            for hx in range(nh):
                hid = hctr[0] % 3
                hctr[0] += 1
                hids.append(hid)
                S.we("pe", half_free[hid])
                psum, poff = halves[hid]
                lo = hx * 1024
                hi = min(kv, lo + 1024)
                off = lo
                while off < hi:
                    n = min(512, hi - off)
                    kop = (WORK[:, KR + off: KR + off + n] if hh == 0
                           else KP1[:, off:off + n])
                    S.o("pe", lambda e, psum=psum, poff=poff, off=off, lo=lo,
                        n=n, i=i, kop=kop: e.matmul(
                        psum[:, poff + off - lo: poff + off - lo + n],
                        WORK[:, QR + i * 128: QR + (i + 1) * 128],
                        kop, start=True, stop=True))
                    off += n
                csc_h.append(S.chk("pe"))
            st[t].update(csc=csc_h[-1], csc_h=csc_h, hids=hids, nh=nh, hh=hh, i=i)

        def emit_B(t):
            i, hh = st[t]["i"], st[t]["hh"]
            kv = (i + 1) * 128
            nh = st[t]["nh"]
            hids = st[t]["hids"]
            par = t % 2
            csc_h = st[t]["csc_h"]
            if t >= 2:
                S.w("v", "s", st[t - 2]["exp"])
            if nh == 2:
                psum, poff = halves[hids[0]]
                S.w("v", "pe", csc_h[0])
                S.o("v", lambda e, psum=psum, poff=poff, par=par: e.reduce_max(
                    NM[:, 4 * par:4 * par + 1], psum[:, poff:poff + 1024],
                    axis=AX.X, negate=True))
            S.w("v", "pe", csc_h[-1])
            dpsum, dpoff = halves[hids[-1]]
            drel = (i * 128) % 1024
            S.o("v", lambda e, dpsum=dpsum, dpoff=dpoff, drel=drel: e.tensor_tensor(
                dpsum[:, dpoff + drel: dpoff + drel + 128],
                dpsum[:, dpoff + drel: dpoff + drel + 128], MASK[:], OP.add))
            hx = nh - 1
            psum, poff = halves[hids[-1]]
            w = min(kv - hx * 1024, 1024)
            S.o("v", lambda e, psum=psum, poff=poff, w=w, hx=hx, par=par:
                e.reduce_max(
                    NM[:, 4 * par + hx:4 * par + hx + 1],
                    psum[:, poff:poff + w], axis=AX.X, negate=True))
            S.o("v", lambda e: e.tensor_copy(JNK[:, 0:2], QKP[:, 0:2]))
            if nh == 2:
                cnm = S.o("v", lambda e, par=par: e.tensor_tensor(
                    NM[:, 4 * par:4 * par + 1], NM[:, 4 * par:4 * par + 1],
                    NM[:, 4 * par + 1:4 * par + 2], OP.min), chk=True)
            else:
                cnm = S.chk("v")
            st[t]["cnm"] = cnm

        def emit_C(t):
            i, hh = st[t]["i"], st[t]["hh"]
            kv = (i + 1) * 128
            nh = st[t]["nh"]
            hids = st[t]["hids"]
            par = t % 2
            pbi = t % 3
            cnm = st[t]["cnm"]
            S.w("s", "v", cnm)
            S.we("s", probs_free[pbi])
            S.we("s", lc_free[par])
            chk_exp = None
            for hx, hid in enumerate(hids):
                psum, poff = halves[hid]
                w = min(kv - hx * 1024, 1024)
                chk_exp = S.o("s", lambda e, psum=psum, poff=poff, w=w, hx=hx,
                              par=par, pbi=pbi: e.activation(
                                  pbufs[pbi][:, hx * 1024: hx * 1024 + w],
                                  psum[:, poff:poff + w], AF.Exp,
                                  bias=NM[:, 4 * par:4 * par + 1],
                                  accum_out=NM[:, 4 * par + 2 + hx:4 * par + 3 + hx]),
                              chk=True)
                half_free[hid] = ("s", chk_exp)
            st[t]["exp"] = chk_exp

        def emit_D(tq, rnd):
            hh1, i1 = st[tq]["hh"], st[tq]["i"]
            prb = pbufs[tq % 3]
            ntr = i1 + 1
            nrnd = (ntr + 7) // 8
            if rnd >= nrnd:
                return
            if rnd == 0:
                S.w("pe", "s", st[tq]["exp"])
                S.we("pe", trfree[0])
                st[tq]["tr_evac"] = [None, None]
            else:
                S.we("pe", st[tq]["tr_evac"][0])
            lo8, hi8 = rnd * 8, min(ntr, rnd * 8 + 8)
            for j in range(lo8, hi8):
                S.o("pe", lambda e, j=j, lo8=lo8, hi8=hi8, prb=prb:
                    e.matmul(
                        TRA[:, (j - lo8) * 128:(j - lo8 + 1) * 128],
                        prb[:, j * 128:(j + 1) * 128], IDENT[:],
                        start=(j == lo8), stop=(j == hi8 - 1),
                        is_transpose=True))
            ctr = S.chk("pe")
            S.w("s", "pe", ctr)
            tok = ("s", S.o("s", lambda e, lo8=lo8, hi8=hi8, prb=prb: e.copy(
                prb[:, lo8 * 128:hi8 * 128],
                TRA[:, 0:(hi8 - lo8) * 128]), chk=True))
            st[tq]["tr_evac"][rnd] = tok
            if rnd == nrnd - 1:
                trfree[0] = tok

        def emit_EF(tq):
            hh1, i1 = st[tq]["hh"], st[tq]["i"]
            qoff1 = 64 * hh1
            par1 = tq % 2
            pbi = tq % 3
            prb = pbufs[pbi]
            ntr = i1 + 1
            nrnd = (ntr + 7) // 8
            avp = AVP2
            tr_evac = st[tq]["tr_evac"]
            S.we("pe", av_last[0])
            S.we("pe", vt_half[1])
            tp = (0, 64) if hh1 == 1 else None
            for rnd in range(nrnd):
                S.we("pe", tr_evac[rnd])
                lo8, hi8 = rnd * 8, min(ntr, rnd * 8 + 8)
                for j in range(lo8, hi8):
                    S.o("pe", lambda e, j=j, i1=i1, qoff1=qoff1, tp=tp, avp=avp,
                        prb=prb: e.matmul(
                            avp[qoff1:qoff1 + 64, 0:128],
                            WORK[:, VT + j * 128 + qoff1: VT + j * 128 + qoff1 + 64],
                            prb[:, j * 128:(j + 1) * 128],
                            start=(j == 0), stop=(j == i1), tile_position=tp))
            cav = S.chk("pe")
            st[tq]["cav"] = cav
            probs_free[pbi] = ("pe", cav)
            S.w("v", "s", st[tq]["exp"])
            if st[tq]["nh"] == 2:
                clc = S.o("v", lambda e, i1=i1, hh1=hh1, par1=par1: e.tensor_tensor(
                    LC[:, 2 * i1 + hh1: 2 * i1 + hh1 + 1],
                    NM[:, 4 * par1 + 2:4 * par1 + 3],
                    NM[:, 4 * par1 + 3:4 * par1 + 4], OP.add), chk=True)
            else:
                clc = S.o("v", lambda e, i1=i1, hh1=hh1, par1=par1: e.tensor_copy(
                    LC[:, 2 * i1 + hh1: 2 * i1 + hh1 + 1],
                    NM[:, 4 * par1 + 2:4 * par1 + 3]), chk=True)
            lc_free[par1] = ("v", clc)
            S.w("v", "pe", cav)
            av_last[0] = ("v", S.o("v", lambda e, fc=fc, i1=i1, qoff1=qoff1,
                                   avp=avp: e.tensor_copy(
                                       AO.ap()[qoff1:qoff1 + 64,
                                               fc * T + i1 * 128: fc * T + (i1 + 1) * 128],
                                       avp[qoff1:qoff1 + 64, 0:128]), chk=True))

        emit_A(0)
        emit_B(0)
        emit_C(0)
        for t in range(NT):
            if t + 1 < NT:
                emit_A(t + 1)
                emit_B(t + 1)
            emit_D(t, 0)
            if t >= 1:
                emit_EF(t - 1)
            if t + 1 < NT:
                emit_C(t + 1)
            emit_D(t, 1)
        emit_EF(NT - 1)
        scb_pe_free[0] = probs_free[1]
        kp1_free[0] = ("pe", st[NT - 1]["csc"])

        S.o("v", lambda e: e.reciprocal(LC[:], LC[:]))
        S.o("v", lambda e: e.tensor_copy(JNK[:, 0:2], QKP[:, 0:2]))
        clb = S.o("v", lambda e: e.tensor_copy(SCB[:, 0:32], LC[:]), chk=True)
        S.w("pe", "v", clb)
        S.we("pe", av_last[0])
        S.we("pe", half_free[0])
        S.we("pe", trfree[0])
        S.we("pe", trfree[1])
        pbbf = PB.ap().bitcast(BF16)
        for i in range(16):
            S.o("pe", lambda e, i=i, pbbf=pbbf: e.matmul(
                pbbf[0:2, i * 128:(i + 1) * 128], SCB[:, 2 * i:2 * i + 2], IDENT[:],
                start=(i % 8 == 0), stop=(i % 8 == 7), is_transpose=True))
        ctl = S.chk("pe")
        S.w("v", "pe", ctl)
        cle = S.o("v", lambda e, pbbf=pbbf: e.tensor_copy(
            SCA.ap()[0:2, 0:2048], pbbf[0:2, 0:2048]), chk=True)
        S.w("pe", "v", cle)
        for t16 in range(16):
            S.o("pe", lambda e, t16=t16: e.matmul(
                PB[:, t16 * 128:(t16 + 1) * 128], SEL2[:],
                SCA.ap()[0:2, t16 * 128:(t16 + 1) * 128],
                start=(t16 % 4 == 0), stop=(t16 % 4 == 3)))
        cbc = S.chk("pe")
        S.w("v", "pe", cbc)
        cnorm = S.o("v", lambda e, fc=fc: e.tensor_tensor(
            aos(fc), aos(fc), PB[:], OP.mult), chk=True)
        norm_tok[0] = ("v", cnorm)
        half_free[0] = ("v", cnorm)
        trfree[0] = ("v", cnorm)
        trfree[1] = ("v", cnorm)
        prev_cbc[0] = ("pe", cbc)
        prev_ctl[0] = ("pe", ctl)

    S.w("g", "pe", last_q_mm[0])
    tok_wo = S.d("g", "qg", wload(R0, dwo))

    S.w("pe", "qg", tok_wo)
    S.we("pe", norm_tok[0])
    wo_cadd = []
    wo_cts = []
    last_wo_mm = None
    wo_tbufs = [SCA, SCB, None]
    for dc in range(NDC):
        psum = PA if dc % 2 == 0 else PB
        if dc >= 2:
            S.w("pe", "s", wo_cts[dc - 2])
        for kc in range(8):
            for tc in range(4):
                S.o("pe", lambda e, psum=psum, dc=dc, tc=tc, kc=kc: e.matmul(
                    psum[:, tc * 512:(tc + 1) * 512],
                    R0[:, kc * 1024 + dc * 128: kc * 1024 + dc * 128 + 128],
                    aos(kc, tc * 512, (tc + 1) * 512),
                    start=(kc == 0), stop=(kc == 7)))
        cwo = S.chk("pe")
        last_wo_mm = cwo
        S.w("s", "pe", cwo)
        if dc >= 3:
            S.w("s", "v", wo_cadd[dc - 3])
        tbuf = [SCA[:, 0:2048], SCB[:, 0:2048], WORK[:, 0:2048]][dc % 3]
        cts = S.o("s", lambda e, psum=psum, dc=dc, tbuf=tbuf: e.activation(
            tbuf, psum[:], AF.Identity,
            bias=PAR[:, 48 + dc:48 + dc + 1],
            scale=PAR[:, 16 + dc:16 + dc + 1]), chk=True)
        wo_cts.append(cts)
        S.w("v", "s", cts)
        cadd = S.o("v", lambda e, dc=dc, tbuf=tbuf: e.tensor_tensor(
            xs(dc), xs(dc), tbuf, OP.add), chk=True)
        wo_cadd.append(cadd)
    wo_done = wo_cadd[-1]

    if DEBUG:
        S.we("y", norm_tok[0])
        _tok = S.d("y", "qy", lambda e: e.dma_start(out=dbg_ao, in_=AO[:]))
        S.w("v", "qy", _tok)
        S.w("s", "qy", _tok)
        S.w("y", "v", wo_done)
        _tok = S.d("y", "qy", lambda e: e.dma_start(out=dbg_x1, in_=X[:]))
        S.w("v", "qy", _tok)

    ln2_chks, ln2_pa_free = layernorm(32, 24, AO[:, 0:2048], None, None,
                                      pre_waits=[("v", wo_done),
                                                 ("s", wo_cts[-1])],
                                      lnscr=aof[:, 3072:5120])

    S.we("pe", ln_stats_free[0])
    for dc in range(NDC):
        S.w("pe", "s", ln2_chks[dc][0])
        S.w("pe", "qg", tok_gatet)
        for tc in range(4):
            S.o("pe", lambda e, dc=dc, tc=tc: e.matmul(
                PB[0:8, tc * 512:(tc + 1) * 512],
                GATET[:, dc * 8:(dc + 1) * 8],
                hs(dc, tc * 512, (tc + 1) * 512),
                start=(dc == 0), stop=(dc == NDC - 1)))
    cgm = S.chk("pe")
    S.w("v", "pe", cgm)
    cgev = S.o("v", lambda e: e.tensor_copy(SCA.ap()[0:8, 0:2048], PB[0:8, 0:2048]),
               chk=True)
    S.w("pe", "v", cgev)
    S.w("pe", "v", ln2_pa_free)
    pabf = PA.ap().bitcast(BF16)
    for t16 in range(16):
        S.o("pe", lambda e, t16=t16, pabf=pabf: e.matmul(
            pabf[0:128, t16 * 8:(t16 + 1) * 8],
            SCA.ap()[0:8, t16 * 128:(t16 + 1) * 128], IDENT.ap()[0:8, 0:8],
            start=(t16 == 0), stop=(t16 == 15), is_transpose=True))
    gate_done = S.chk("pe")

    sbf = SCB.ap().bitcast(F32)
    gex = sbf[:, 128:256]
    g3 = pabf[0:128, 0:128].rearrange("p (g k) -> p g k", k=8)
    ge3 = gex.rearrange("p (g k) -> p g k", k=8)

    def bc16(ap):
        return ap.rearrange("p (g o) -> p g o", o=1).broadcast_to([128, 16, 8])

    S.w("v", "pe", gate_done)
    S.o("v", lambda e: e.reduce_max(
        GST[:, 0:16].rearrange("p (g o) -> p g o", o=1), g3, axis=AX.X))
    S.o("v", lambda e: e.tensor_copy(JNK[:, 0:2], QKP[:, 0:2]))
    cgs = S.o("v", lambda e: e.tensor_tensor(ge3, g3, bc16(GST[:, 0:16]), OP.subtract),
              chk=True)
    S.w("s", "v", cgs)
    cge = S.o("s", lambda e: e.activation(gex, gex, AF.Exp), chk=True)
    S.w("v", "s", cge)
    S.o("v", lambda e: e.reduce_sum(
        GST[:, 16:32].rearrange("p (g o) -> p g o", o=1), ge3, axis=AX.X))
    S.o("v", lambda e: e.reduce_max(
        GST[:, 0:16].rearrange("p (g o) -> p g o", o=1), ge3, axis=AX.X))
    S.o("v", lambda e: e.tensor_copy(JNK[:, 0:2], QKP[:, 0:2]))
    S.o("v", lambda e: e.tensor_tensor(
        sbf[:, 256:384].rearrange("p (g k) -> p g k", k=8), ge3, bc16(GST[:, 0:16]),
        OP.is_ge))
    S.o("v", lambda e: e.scalar_tensor_tensor(
        sbf[:, 256:384], sbf[:, 256:384], -1e30, gex, OP.mult, OP.add))
    S.o("v", lambda e: e.reduce_max(
        GST[:, 32:48].rearrange("p (g o) -> p g o", o=1),
        sbf[:, 256:384].rearrange("p (g k) -> p g k", k=8), axis=AX.X))
    S.o("v", lambda e: e.tensor_copy(JNK[:, 0:2], QKP[:, 0:2]))
    S.o("v", lambda e: e.tensor_tensor(
        sbf[:, 256:384].rearrange("p (g k) -> p g k", k=8), ge3, bc16(GST[:, 32:48]),
        OP.is_ge))
    S.o("v", lambda e: e.tensor_tensor(gex, gex, sbf[:, 256:384], OP.mult))
    S.o("v", lambda e: e.reciprocal(GST[:, 48:64], GST[:, 16:32]))
    S.o("v", lambda e: e.tensor_copy(JNK[:, 0:2], QKP[:, 0:2]))
    S.o("v", lambda e: e.tensor_tensor(ge3, ge3, bc16(GST[:, 48:64]), OP.mult))
    ccb = S.o("v", lambda e: e.tensor_copy(SCA[:, 0:128], gex), chk=True)
    S.w("pe", "v", ccb)
    for t16 in range(16):
        S.o("pe", lambda e, t16=t16, pabf=pabf: e.matmul(
            pabf[0:8, t16 * 128:(t16 + 1) * 128], SCA[:, t16 * 8:(t16 + 1) * 8],
            IDENT[:], start=(t16 % 8 == 0), stop=(t16 % 8 == 7), is_transpose=True))
    ctc = S.chk("pe")
    S.w("v", "pe", ctc)
    crow_done = S.o("v", lambda e, pabf=pabf: e.tensor_copy(
        CROW[:], pabf[0:8, 0:2048]), chk=True)

    regs = [R1, RW, R0]
    reg_free = [("pe", last_k_mm[0]), ("v", wo_cadd[5]), ("s", wo_cts[-1])]
    wtoks = {}
    drams = {"w1": dw1, "w3": dw3, "w2": dw2}
    sdrams = {"w1": dsw1, "w3": dsw3, "w2": dsw2}

    def sched_wload(m_idx, e_idx, mat):
        ridx = (3 * e_idx + m_idx) % 3
        reg = regs[ridx]
        S.we("g", reg_free[ridx])
        if e_idx < NE:
            src = drams[mat].rearrange("e (c p) o -> e p c o", p=128)[e_idx]
        else:
            src = sdrams[mat].rearrange("(c p) o -> p c o", p=128)
        wtoks[(e_idx, mat)] = S.d("g", "qg", lambda e, reg=reg, src=src: e.dma_start(
            out=reg.rearrange("p (c o) -> p c o", c=8), in_=src))

    for m_idx, mat in enumerate(("w1", "w3", "w2")):
        sched_wload(m_idx, 0, mat)

    pa_free_ffn = ("v", crow_done)
    pb_free_ffn = ("v", cgev)
    he_read = ("s", ln2_chks[-1][0])
    prev_csil = None
    cw2 = None
    stts = []

    for e_idx in range(NE + 1):
        is_sh = e_idx == NE
        rW1 = regs[(3 * e_idx + 0) % 3]
        rW3 = regs[(3 * e_idx + 1) % 3]
        rW2 = regs[(3 * e_idx + 2) % 3]
        if not is_sh:
            S.w("pe", "v", crow_done)
            S.we("pe", pa_free_ffn)
            for tc in range(4):
                S.o("pe", lambda e, e_idx=e_idx, tc=tc: e.matmul(
                    PA[:, tc * 512:(tc + 1) * 512],
                    ESEL[:, e_idx * 128:(e_idx + 1) * 128],
                    CROW[:, tc * 512:(tc + 1) * 512], start=True, stop=True))
            ccb2 = S.chk("pe")
            S.w("v", "pe", ccb2)
            cbev = S.o("v", lambda e: e.tensor_copy(SCB[:], PA[:]), chk=True)
            pa_free_ffn = ("v", cbev)
        he_done = None
        for ic in range(8):
            bcol1 = (e_idx * 8 + ic) if not is_sh else (128 + ic)
            bcol3 = (64 + e_idx * 8 + ic) if not is_sh else (136 + ic)
            S.w("pe", "qg", wtoks[(e_idx, "w1")])
            S.we("pe", pa_free_ffn)
            for kc in range(8):
                if e_idx == 0 and ic == 0:
                    S.w("pe", "s", ln2_chks[kc][0])
                for tc in range(4):
                    S.o("pe", lambda e, rW1=rW1, ic=ic, tc=tc, kc=kc: e.matmul(
                        PA[:, tc * 512:(tc + 1) * 512],
                        rW1[:, kc * 1024 + ic * 128: kc * 1024 + ic * 128 + 128],
                        hs(kc, tc * 512, (tc + 1) * 512),
                        start=(kc == 0), stop=(kc == 7)))
            ch1 = S.chk("pe")
            S.w("s", "pe", ch1)
            if prev_csil is not None and he_done is not None:
                S.w("s", "v", he_done)
            csil = S.o("s", lambda e, bcol1=bcol1: e.activation(
                SCA[:], PA[:], AF.Silu, bias=BIA[:, bcol1:bcol1 + 1]), chk=True)
            prev_csil = csil
            pa_free_ffn = ("s", csil)
            S.w("pe", "qg", wtoks[(e_idx, "w3")])
            S.we("pe", pb_free_ffn)
            for kc in range(8):
                for tc in range(4):
                    S.o("pe", lambda e, rW3=rW3, ic=ic, tc=tc, kc=kc: e.matmul(
                        PB[:, tc * 512:(tc + 1) * 512],
                        rW3[:, kc * 1024 + ic * 128: kc * 1024 + ic * 128 + 128],
                        hs(kc, tc * 512, (tc + 1) * 512),
                        start=(kc == 0), stop=(kc == 7)))
            ch3 = S.chk("pe")
            S.w("v", "pe", ch3)
            S.w("v", "s", csil)
            S.we("v", he_read)
            chm = S.o("v", lambda e, ic=ic, bcol3=bcol3: e.scalar_tensor_tensor(
                aos(ic), PB[:], BIA[:, bcol3:bcol3 + 1], SCA[:], OP.add, OP.mult),
                chk=True)
            pb_free_ffn = ("v", chm)
            if not is_sh:
                chm = S.o("v", lambda e, ic=ic: e.tensor_tensor(
                    aos(ic), aos(ic), SCB[:], OP.mult), chk=True)
            he_done = chm
        psfree = {0: pa_free_ffn, 1: pb_free_ffn}
        sh_cts = []
        for dc in range(NDC):
            par = dc % 2
            psum = PA if par == 0 else PB
            S.w("pe", "qg", wtoks[(e_idx, "w2")])
            S.w("pe", "v", he_done)
            S.we("pe", psfree[par])
            for kc in range(8):
                for tc in range(4):
                    S.o("pe", lambda e, rW2=rW2, psum=psum, dc=dc, tc=tc, kc=kc:
                        e.matmul(
                            psum[:, tc * 512:(tc + 1) * 512],
                            rW2[:, kc * 1024 + dc * 128: kc * 1024 + dc * 128 + 128],
                            aos(kc, tc * 512, (tc + 1) * 512),
                            start=(kc == 0), stop=(kc == 7 and not is_sh)))
            if is_sh:
                S.w("pe", "qg", tok_consts)
                S.w("pe", "v", crow_done)
                for tc in range(4):
                    S.o("pe", lambda e, psum=psum, dc=dc, tc=tc: e.matmul(
                        psum[:, tc * 512:(tc + 1) * 512],
                        B2R[0:8, dc * 128:(dc + 1) * 128],
                        CROW[0:8, tc * 512:(tc + 1) * 512],
                        start=False, stop=True))
            cw2 = S.chk("pe")
            if dc == NDC - 1:
                he_read = ("pe", cw2)
            if not is_sh:
                S.w("v", "pe", cw2)
                cst = S.o("v", lambda e, psum=psum, dc=dc: e.scalar_tensor_tensor(
                    xs(dc), psum[:], PAR[:, 40 + dc:40 + dc + 1], xs(dc),
                    OP.mult, OP.add), chk=True)
            else:
                S.w("s", "pe", cw2)
                if dc >= 2:
                    S.w("s", "v", stts[-2])
                tbuf = SCA if par == 0 else SCB
                ct = S.o("s", lambda e, psum=psum, dc=dc, tbuf=tbuf: e.activation(
                    tbuf[:], psum[:], AF.Identity,
                    bias=PAR[:, 56 + dc:56 + dc + 1],
                    scale=PAR[:, 40 + dc:40 + dc + 1]), chk=True)
                sh_cts.append(ct)
                S.w("v", "s", ct)
                cst = S.o("v", lambda e, dc=dc, tbuf=tbuf: e.tensor_tensor(
                    xs(dc), xs(dc), tbuf[:], OP.add), chk=True)
            psfree[par] = ("v", cst)
            stts.append(cst)
        pa_free_ffn = psfree[0]
        pb_free_ffn = psfree[1]
        reg_free[(3 * e_idx + 0) % 3] = ("pe", ch1)
        reg_free[(3 * e_idx + 1) % 3] = ("pe", ch3)
        reg_free[(3 * e_idx + 2) % 3] = ("pe", cw2)
        if e_idx < NE:
            for m_idx, mat in enumerate(("w1", "w3", "w2")):
                sched_wload(m_idx, e_idx + 1, mat)

    for t in stts[-NDC:]:
        S.w("y", "v", t)
    tok_store = S.d("y", "qy", lambda e: e.dma_start(
        out=dout.rearrange("(c p) t -> p c t", p=128),
        in_=X[:].rearrange("p (c t) -> p c t", c=NDC)))
    S.w("y", "qy", tok_store)

    with (
        nc.allow_low_precision("bf16 intermediates are deliberate; fp32 where it matters"),
        nc.semaphore("sem_pe") as sem_pe,
        nc.semaphore("sem_v") as sem_v,
        nc.semaphore("sem_s") as sem_s,
        nc.semaphore("sem_g") as sem_g,
        nc.semaphore("sem_y") as sem_y,
        nc.semaphore("sem_qy") as sem_qy,
        nc.semaphore("sem_qg") as sem_qg,
        nc.Block() as block,
    ):
        sems = {"pe": sem_pe, "v": sem_v, "s": sem_s, "g": sem_g, "y": sem_y,
                "qy": sem_qy, "qg": sem_qg}

        @block.tensor
        def _(eng):
            S.replay(eng, "pe", sems)

        @block.vector
        def _(eng):
            S.replay(eng, "v", sems)

        @block.scalar
        def _(eng):
            S.replay(eng, "s", sems)

        @block.gpsimd
        def _(eng):
            S.replay(eng, "g", sems)

        @block.sync
        def _(eng):
            S.replay(eng, "y", sems)

    return nc


_CACHE = {}
_LAST_RES = None


def kernel(**inputs):
    N = 8
    x = np.asarray(inputs["L0_attn_x"])
    f32 = np.float32

    def c_(a):
        return np.ascontiguousarray(a, dtype=f32)

    wq = c_(np.asarray(inputs["L0_attn_Wq"]).T)
    wk = c_(np.asarray(inputs["L0_attn_Wk"]).T)
    wv = c_(np.asarray(inputs["L0_attn_Wv"]).T)
    wo = c_(np.asarray(inputs["L0_attn_Wo"]).T)
    w1 = c_(inputs["L0_ffn_w1"])
    w3 = c_(inputs["L0_ffn_w3"])
    w2 = c_(inputs["L0_ffn_w2"])
    sw1 = c_(inputs["L0_ffn_sw1"])
    sw3 = c_(inputs["L0_ffn_sw3"])
    sw2 = c_(inputs["L0_ffn_sw2"])
    wmod = c_(inputs["L0_wmod"])
    cfull = np.asarray(inputs["L0_c"], dtype=f32)

    gq = np.asarray(inputs["L0_attn_gq"], dtype=f32)
    bq = np.asarray(inputs["L0_attn_bq"], dtype=f32)
    gk = np.asarray(inputs["L0_attn_gk"], dtype=f32)
    bk = np.asarray(inputs["L0_attn_bk"], dtype=f32)
    qkp = c_(np.stack([np.tile(gq, 2), np.tile(bq, 2),
                       np.tile(gk, 2), np.tile(bk, 2),
                       np.full(128, 0.5 * np.log(64.0), dtype=f32),
                       np.full(128, 0.5 * np.log(1024.0), dtype=f32)], axis=1))

    b1 = np.asarray(inputs["L0_ffn_b1"], dtype=f32)
    b3 = np.asarray(inputs["L0_ffn_b3"], dtype=f32)
    b2 = c_(inputs["L0_ffn_b2"])
    sb1 = np.asarray(inputs["L0_ffn_sb1"], dtype=f32)
    sb3 = np.asarray(inputs["L0_ffn_sb3"], dtype=f32)
    sb2 = np.asarray(inputs["L0_ffn_sb2"], dtype=f32)
    bo = np.asarray(inputs["L0_attn_bo"], dtype=f32)
    biases = np.zeros((128, 160), dtype=f32)
    biases[:, 0:64] = b1.reshape(8, 8, 128).transpose(2, 0, 1).reshape(128, 64)
    biases[:, 64:128] = b3.reshape(8, 8, 128).transpose(2, 0, 1).reshape(128, 64)
    biases[:, 128:136] = sb1.reshape(8, 128).T
    biases[:, 136:144] = sb3.reshape(8, 128).T
    biases[:, 144:152] = bo.reshape(8, 128).T
    biases[:, 152:160] = sb2.reshape(8, 128).T
    biases = c_(biases)

    gatet = c_(np.asarray(inputs["L0_ffn_gate_w"], dtype=f32).T
               .reshape(8, 128, 8).transpose(1, 0, 2).reshape(128, 64))

    allones = np.ones((128, 128), dtype=f32)
    blockones = np.zeros((128, 128), dtype=f32)
    blockones[0:64, 0:64] = 1.0
    blockones[64:128, 64:128] = 1.0
    ident = np.eye(128, dtype=f32)
    maskd = np.triu(np.full((128, 128), -1e30, dtype=f32), 1)
    sel2 = np.zeros((2, 128), dtype=f32)
    sel2[0, 0:64] = 1.0
    sel2[1, 64:128] = 1.0
    esel = np.zeros((8, 1024), dtype=f32)
    for e in range(8):
        esel[e, e * 128:(e + 1) * 128] = 1.0
    ones1 = np.ones((1, 128), dtype=f32)
    epsr = np.zeros((1, 1024), dtype=f32)
    epsr[0, 0:512] = 64 * 1e-5
    epsr[0, 512:1024] = 1024 * 1e-5

    if "nc" not in _CACHE:
        _CACHE["nc"] = build()
    nc = _CACHE["nc"]

    in_maps = []
    for i in range(N):
        in_maps.append({
            "x": c_(x[i].T),
            "wq": wq, "wk": wk, "wv": wv, "wo": wo,
            "w1": w1, "w3": w3, "w2": w2,
            "sw1": sw1, "sw3": sw3, "sw2": sw2,
            "wmod": wmod,
            "cc": c_(cfull[i].reshape(8, 128).T),
            "qkp": qkp, "biases": biases, "gatet": gatet, "b2r": b2,
            "callones": allones, "cblockones": blockones, "cident": ident,
            "cmask": maskd, "csel2": sel2, "cesel": esel,
            "cones1": ones1, "cepsr": epsr, "cidentf": ident,
        })
    res = run_bass_kernel_spmd(nc, in_maps, core_ids=list(range(N)))
    global _LAST_RES
    _LAST_RES = res
    out = np.stack([np.ascontiguousarray(r["out"].T) for r in res.results], axis=0)
    return out.astype(np.float32)


# version 30
# speedup vs baseline: 1.0707x; 1.0132x over previous
import math
import numpy as np
import ml_dtypes
import concourse.bass as bass
import concourse.mybir as mybir
from concourse.bass_utils import run_bass_kernel_spmd

F32 = mybir.dt.float32
BF16 = mybir.dt.bfloat16
AF = mybir.ActivationFunctionType
OP = mybir.AluOpType
AX = mybir.AxisListType

T = 2048
D = 1024
NDC = 8
NFC = 8
NE = 8
DH = 64
EPS = 1e-5

DEBUG = False

ENGS = ("pe", "v", "s", "g", "y")
QUEUES = ("qy", "qg")


class Sch:

    def __init__(self):
        self.seq = {k: [] for k in ENGS}
        self.n = {k: 0 for k in ENGS}
        self.nd = {k: 0 for k in QUEUES}
        self.wm = {k: {} for k in ENGS}

    def w(self, eng, sem, val):
        if not val or sem == eng or val <= self.wm[eng].get(sem, 0):
            return
        self.wm[eng][sem] = val
        self.seq[eng].append(("w", sem, val))

    def we(self, eng, et):
        if et is not None:
            self.w(eng, et[0], et[1])

    def o(self, eng, fn, chk=False):
        self.seq[eng].append(("o", fn, chk))
        if chk:
            self.n[eng] += 1
            return self.n[eng]
        return None

    def chk(self, eng):
        for idx in range(len(self.seq[eng]) - 1, -1, -1):
            it = self.seq[eng][idx]
            if it[0] == "o":
                if not it[2]:
                    self.seq[eng][idx] = ("o", it[1], True)
                    self.n[eng] += 1
                return self.n[eng]
            if it[0] == "d":
                raise RuntimeError("chk would land on a dma")
        raise RuntimeError("no op to chk")

    def d(self, eng, q, fn):
        self.seq[eng].append(("d", fn, q))
        self.nd[q] += 1
        return self.nd[q] * 16

    def replay(self, eng_obj, name, sems):
        for it in self.seq[name]:
            if it[0] == "w":
                eng_obj.wait_ge(sems[it[1]], it[2])
            elif it[0] == "o":
                inst = it[1](eng_obj)
                if it[2]:
                    inst.then_inc(sems[name], 1)
            else:
                it[1](eng_obj).then_inc(sems[it[2]], 16)


def build():
    nc = bass.Bass("TRN2", target_bir_lowering=False)

    dx = nc.dram_tensor("x", [D, T], F32, kind="ExternalInput").ap()
    dwq = nc.dram_tensor("wq", [D, D], F32, kind="ExternalInput").ap()
    dwk = nc.dram_tensor("wk", [D, D], F32, kind="ExternalInput").ap()
    dwv = nc.dram_tensor("wv", [D, D], F32, kind="ExternalInput").ap()
    dwo = nc.dram_tensor("wo", [D, D], F32, kind="ExternalInput").ap()
    dw1 = nc.dram_tensor("w1", [NE, D, D], F32, kind="ExternalInput").ap()
    dw3 = nc.dram_tensor("w3", [NE, D, D], F32, kind="ExternalInput").ap()
    dw2 = nc.dram_tensor("w2", [NE, D, D], F32, kind="ExternalInput").ap()
    dsw1 = nc.dram_tensor("sw1", [D, D], F32, kind="ExternalInput").ap()
    dsw3 = nc.dram_tensor("sw3", [D, D], F32, kind="ExternalInput").ap()
    dsw2 = nc.dram_tensor("sw2", [D, D], F32, kind="ExternalInput").ap()
    dwmod = nc.dram_tensor("wmod", [D, 6144], F32, kind="ExternalInput").ap()
    dcc = nc.dram_tensor("cc", [128, 8], F32, kind="ExternalInput").ap()
    dqkp = nc.dram_tensor("qkp", [128, 6], F32, kind="ExternalInput").ap()
    dbias = nc.dram_tensor("biases", [128, 160], F32, kind="ExternalInput").ap()
    dgatet = nc.dram_tensor("gatet", [128, 64], F32, kind="ExternalInput").ap()
    db2r = nc.dram_tensor("b2r", [8, 1024], F32, kind="ExternalInput").ap()
    dallones = nc.dram_tensor("callones", [128, 128], F32, kind="ExternalInput").ap()
    dblockones = nc.dram_tensor("cblockones", [128, 128], F32, kind="ExternalInput").ap()
    dident = nc.dram_tensor("cident", [128, 128], F32, kind="ExternalInput").ap()
    dmask = nc.dram_tensor("cmask", [128, 128], F32, kind="ExternalInput").ap()
    dsel2 = nc.dram_tensor("csel2", [2, 128], F32, kind="ExternalInput").ap()
    dones1 = nc.dram_tensor("cones1", [1, 128], F32, kind="ExternalInput").ap()
    didentf = nc.dram_tensor("cidentf", [128, 128], F32, kind="ExternalInput").ap()
    depsr = nc.dram_tensor("cepsr", [1, 1024], F32, kind="ExternalInput").ap()
    desel = nc.dram_tensor("cesel", [8, 1024], F32, kind="ExternalInput").ap()
    dout = nc.dram_tensor("out", [D, T], F32, kind="ExternalOutput").ap()
    if DEBUG:
        dbg_par = nc.dram_tensor("dbg_par", [128, 64], F32, kind="ExternalOutput").ap()
        dbg_h = nc.dram_tensor("dbg_h", [128, NDC * T], BF16, kind="ExternalOutput").ap()
        dbg_qkv = nc.dram_tensor("dbg_qkv", [128, 8192], BF16, kind="ExternalOutput").ap()
        dbg_ao = nc.dram_tensor("dbg_ao", [128, NDC * T], BF16, kind="ExternalOutput").ap()
        dbg_x1 = nc.dram_tensor("dbg_x1", [128, NDC * T], F32, kind="ExternalOutput").ap()
        dbg_crow = nc.dram_tensor("dbg_crow", [8, T], BF16, kind="ExternalOutput").ap()
        dbg_h2 = nc.dram_tensor("dbg_h2", [128, NDC * T], BF16, kind="ExternalOutput").ap()
        dbg_ao2 = nc.dram_tensor("dbg_ao2", [128, T], BF16, kind="ExternalOutput").ap()
        dbg_lc = nc.dram_tensor("dbg_lc", [128, 32], F32, kind="ExternalOutput").ap()
        dbg_pt = nc.dram_tensor("dbg_pt", [128, T], BF16, kind="ExternalOutput").ap()
        dbg_isge = nc.dram_tensor("dbg_isge", [128, 16], F32, kind="ExternalOutput").ap()
        dbg_sc = nc.dram_tensor("dbg_sc", [128, 2048], F32, kind="ExternalOutput").ap()
        dbg_nm = nc.dram_tensor("dbg_nm", [128, 4], F32, kind="ExternalOutput").ap()
        dbg_pr = nc.dram_tensor("dbg_pr", [128, 2048], BF16, kind="ExternalOutput").ap()

    X = nc.alloc_sbuf_tensor("X", [128, NDC * T], F32)
    HH = nc.alloc_sbuf_tensor("HH", [128, NDC * T], BF16)
    AO = nc.alloc_sbuf_tensor("AO", [128, NDC * T], BF16)
    WB = nc.alloc_sbuf_tensor("WB", [128, 2 * 8192], BF16)
    WORK = nc.alloc_sbuf_tensor("WORK", [128, 8192], BF16)
    SCA = nc.alloc_sbuf_tensor("SCA", [128, T], BF16)
    SCB = nc.alloc_sbuf_tensor("SCB", [128, T], BF16)
    WVS = nc.alloc_sbuf_tensor("WVS", [128, 2048], BF16)
    KP1 = nc.alloc_sbuf_tensor("KP1", [128, 2048], BF16)
    CROW = nc.alloc_sbuf_tensor("CROW", [8, T], BF16)
    QKP = nc.alloc_sbuf_tensor("QKP", [128, 6], F32)
    CC = nc.alloc_sbuf_tensor("CC", [128, 8], F32)
    SCS = nc.alloc_sbuf_tensor("SCS", [128, 8], F32)
    BIA = nc.alloc_sbuf_tensor("BIA", [128, 160], F32)
    PAR = nc.alloc_sbuf_tensor("PAR", [128, 64], F32)
    GATET = nc.alloc_sbuf_tensor("GATET", [128, 64], BF16)
    B2R = nc.alloc_sbuf_tensor("B2R", [8, 1024], BF16)
    ONES = nc.alloc_sbuf_tensor("ONES", [128, 128], BF16)
    BONES = nc.alloc_sbuf_tensor("BONES", [128, 128], BF16)
    IDENT = nc.alloc_sbuf_tensor("IDENT", [128, 128], BF16)
    MASK = nc.alloc_sbuf_tensor("MASK", [128, 128], BF16)
    SEL2 = nc.alloc_sbuf_tensor("SEL2", [2, 128], BF16)
    ONES1 = nc.alloc_sbuf_tensor("ONES1", [1, 128], BF16)
    IDENTF = nc.alloc_sbuf_tensor("IDENTF", [128, 128], F32)
    EPSR = nc.alloc_sbuf_tensor("EPSR", [1, 1024], BF16)
    ESEL = nc.alloc_sbuf_tensor("ESEL", [8, 1024], BF16)
    NM = nc.alloc_sbuf_tensor("NM", [128, 8], F32)
    LC = nc.alloc_sbuf_tensor("LC", [128, 32], F32)
    GST = nc.alloc_sbuf_tensor("GST", [128, 64], F32)
    JNK = nc.alloc_sbuf_tensor("JNK", [128, 4], F32)

    PA = nc.alloc_psum_tensor("PA", [128, 2048], F32)
    PB = nc.alloc_psum_tensor("PB", [128, 2048], F32)

    S = Sch()
    aof = AO.ap().bitcast(F32)

    def xs(dc, lo=0, hi=T):
        return X[:, dc * T + lo: dc * T + hi]

    def hs(dc, lo=0, hi=T):
        return HH[:, dc * T + lo: dc * T + hi]

    def aos(dc, lo=0, hi=T):
        return AO[:, dc * T + lo: dc * T + hi]

    tok_x = None
    tok_smalls = None
    for sb_t, dr in ((QKP, dqkp), (CC, dcc), (BIA, dbias), (IDENTF, didentf)):
        tok_smalls = S.d("y", "qy", lambda e, a=sb_t, b=dr: e.dma_start(out=a[:], in_=b[:]))
    tok_gatet = S.d("g", "qg", lambda e: e.dma_start(out=GATET[:], in_=dgatet[:]))
    tok_consts = None
    for sb_t, dr in ((ONES, dallones), (BONES, dblockones), (IDENT, dident),
                     (MASK, dmask), (SEL2, dsel2), (ESEL, desel), (B2R, db2r),
                     (ONES1, dones1), (EPSR, depsr)):
        tok_consts = S.d("g", "qg", lambda e, a=sb_t, b=dr: e.dma_start(out=a[:], in_=b[:]))

    def wload(dst_ap, dram_ap):
        return lambda e: e.dma_start(
            out=dst_ap.rearrange("p (c o) -> p c o", c=8),
            in_=dram_ap.rearrange("(c p) o -> p c o", p=128))

    R0 = WB[:, 0:8192]
    R1 = WB[:, 8192:16384]
    RW = WORK[:, 0:8192]
    tok_wq = S.d("g", "qg", wload(R0, dwq))
    tok_wk = S.d("g", "qg", wload(R1, dwk))

    def wv_piece(fc):
        pc = fc % 2
        return S.d("g", "qg", lambda e, fc=fc, pc=pc: e.dma_start(
            out=WVS[:, pc * 1024:(pc + 1) * 1024].rearrange("p (c o) -> p c o", c=8),
            in_=dwv.rearrange("(c p) o -> p c o", p=128)[:, :, fc * 128:(fc + 1) * 128]))

    tok_wv = [wv_piece(0), wv_piece(1)]

    S.w("s", "qy", tok_smalls)
    chk_scs = S.o("s", lambda e: e.activation(SCS[:], CC[:], AF.Silu), chk=True)

    NPC = 12
    PW = 6144 // NPC
    modp = PB[:, 2000:2048]
    ROWB = nc.alloc_sbuf_tensor("ROWB", [1, 512], F32)
    piece_tok = [None] * NPC
    piece_done = [None] * NPC
    ev_tok = None
    ev_toks = []
    tr_toks = []
    first_tr = True
    for p in range(NPC):
        if p >= 2:
            S.w("y", "pe", piece_done[p - 2])
        piece_tok[p] = S.d("y", "qy", lambda e, p=p: e.dma_start(
            out=aof[:, (p % 2) * 4096:(p % 2) * 4096 + 4096].rearrange(
                "p (c o) -> p c o", c=8),
            in_=dwmod.rearrange("(c p) o -> p c o", p=128)[:, :, p * PW:(p + 1) * PW]))
        if p == 3:
            tok_x = S.d("y", "qy", lambda e: e.dma_start(
                out=X[:].rearrange("p (c t) -> p c t", c=NDC),
                in_=dx.rearrange("(c p) t -> p c t", p=128)))
        S.w("pe", "qy", piece_tok[p])
        if p == 0:
            S.w("pe", "s", chk_scs)
        modrow = PB[0:1, (p % 2) * 512:(p % 2) * 512 + 512]
        if len(ev_toks) >= 2:
            S.w("pe", "v", ev_toks[-2])
        for kc in range(8):
            S.o("pe", lambda e, p=p, kc=kc, modrow=modrow: e.matmul(
                modrow, SCS[:, kc:kc + 1],
                aof[:, (p % 2) * 4096 + kc * PW: (p % 2) * 4096 + kc * PW + PW],
                start=(kc == 0), stop=(kc == 7)))
        cmm = S.chk("pe")
        piece_done[p] = cmm
        S.w("v", "pe", cmm)
        if tr_toks:
            S.w("v", "pe", tr_toks[-1])
        ev_tok = S.o("v", lambda e, modrow=modrow: e.tensor_copy(ROWB[:], modrow),
                     chk=True)
        ev_toks.append(ev_tok)
        S.w("pe", "v", ev_tok)
        for cb in range(4):
            col = p * 4 + cb
            S.o("pe", lambda e, cb=cb, col=col, first=first_tr: e.matmul(
                modp[:, col:col + 1], ROWB[0:1, cb * 128:(cb + 1) * 128],
                IDENTF[0:1, 0:1], start=first, stop=(col == 47),
                is_transpose=True))
            first_tr = False
        tr_toks.append(S.chk("pe"))
    piece_done[NPC - 1] = tr_toks[-1]
    chk_modmm = piece_done[-1]

    S.w("v", "pe", tr_toks[3])
    S.w("v", "qy", tok_smalls)

    def asm1(e):
        e.tensor_copy(PAR[:, 0:8], modp[:, 0:8])
        return e.tensor_scalar_add(PAR[:, 8:16], modp[:, 8:16], 1.0)
    chk_par1 = S.o("v", asm1, chk=True)
    S.w("v", "pe", chk_modmm)

    def asm(e):
        e.tensor_copy(PAR[:, 24:32], modp[:, 24:32])
        e.tensor_scalar_add(PAR[:, 32:40], modp[:, 32:40], 1.0)
        A, B = 0, 32
        for base, src in ((A, 16), (B, 40)):
            e.tensor_single_scalar(GST[:, base:base + 8], modp[:, src:src + 8],
                                   0.0, OP.is_ge)
        for base in (A, B):
            e.tensor_scalar_mul(GST[:, base + 8:base + 16], GST[:, base:base + 8], 2.0)
        for base in (A, B):
            e.tensor_single_scalar(GST[:, base + 8:base + 16],
                                   GST[:, base + 8:base + 16], 1.0, OP.subtract)
        for base, src in ((A, 16), (B, 40)):
            e.tensor_scalar_mul(GST[:, base + 16:base + 24], modp[:, src:src + 8], -1.0)
        for base in (A, B):
            e.tensor_scalar_max(GST[:, base + 16:base + 24],
                                GST[:, base + 16:base + 24], 0.25)
        for base, src in ((A, 16), (B, 40)):
            e.tensor_scalar_max(GST[:, base + 24:base + 32], modp[:, src:src + 8], 0.25)
        for base in (A, B):
            e.tensor_max(GST[:, base + 16:base + 24], GST[:, base + 16:base + 24],
                         GST[:, base + 24:base + 32])
        for base, dst in ((A, 16), (B, 40)):
            e.tensor_mul(PAR[:, dst:dst + 8], GST[:, base + 8:base + 16],
                         GST[:, base + 16:base + 24])
        e.tensor_mul(PAR[:, 48:56], PAR[:, 16:24], BIA[:, 144:152])
        if DEBUG:
            e.tensor_single_scalar(GST[:, 32:40], modp[:, 16:24], 0.0, OP.is_ge)
            e.tensor_scalar_mul(GST[:, 40:48], GST[:, 32:40], 2.0)
            e.tensor_single_scalar(GST[:, 40:48], GST[:, 40:48], 1.0, OP.subtract)
        return e.tensor_mul(PAR[:, 56:64], PAR[:, 40:48], BIA[:, 152:160])
    chk_par = S.o("v", asm, chk=True)
    S.w("s", "v", chk_par)

    ln_stats_free = [None]

    def layernorm(sc_col, bi_col, t_slice, h2f32_slice, h2_cb, pre_waits, lnscr):
        for eng in ("pe", "v", "s"):
            for et in pre_waits:
                S.we(eng, et)
        S.w("v", "qy", tok_x)
        S.w("s", "qy", tok_x)
        last_mm = None
        for dc in range(NDC):
            if last_mm is not None:
                S.w("v", "pe", last_mm)
                S.w("s", "pe", last_mm)
            ccp = S.o("v", lambda e, dc=dc: e.tensor_copy(SCA[:], xs(dc)), chk=True)
            csq = S.o("s", lambda e, dc=dc: e.activation(SCB[:], xs(dc), AF.Square),
                      chk=True)
            S.w("pe", "v", ccp)
            S.w("pe", "s", csq)
            S.w("pe", "qg", tok_consts)
            for tc in range(4):
                S.o("pe", lambda e, dc=dc, tc=tc: e.matmul(
                    PA[:, tc * 512:(tc + 1) * 512], ONES[:],
                    SCA[:, tc * 512:(tc + 1) * 512],
                    start=(dc == 0), stop=(dc == NDC - 1)))
                S.o("pe", lambda e, dc=dc, tc=tc: e.matmul(
                    PB[:, tc * 512:(tc + 1) * 512], ONES[:],
                    SCB[:, tc * 512:(tc + 1) * 512],
                    start=(dc == 0), stop=False))
            last_mm = S.chk("pe")
        for tc in range(4):
            S.o("pe", lambda e, tc=tc: e.matmul(
                PB[:, tc * 512:(tc + 1) * 512], ONES1[0:1, :],
                EPSR[0:1, 512:1024], start=False, stop=True))
        last_mm = S.chk("pe")
        S.w("v", "pe", last_mm)
        S.o("v", lambda e: e.tensor_scalar_mul(SCA[:], PA[:], 1.0 / D))
        ct1 = S.o("v", lambda e: e.tensor_tensor(SCB[:], SCA[:], PA[:], OP.mult),
                  chk=True)
        crstd = None
        cln_prev = None
        for h in (0, 1):
            if cln_prev is not None:
                S.w("v", "s", cln_prev)
            ct2 = S.o("v", lambda e, h=h: e.scalar_tensor_tensor(
                lnscr[:, 0:1024], SCB[:, h * 1024:(h + 1) * 1024], -1.0,
                PB[:, h * 1024:(h + 1) * 1024], OP.mult, OP.add), chk=True)
            S.w("s", "v", ct2)
            cln_prev = S.o("s", lambda e: e.activation(
                lnscr[:, 1024:2048], lnscr[:, 0:1024], AF.Ln), chk=True)
            crstd = S.o("s", lambda e, h=h: e.activation(
                SCB[:, h * 1024:(h + 1) * 1024], lnscr[:, 1024:2048], AF.Exp,
                scale=-0.5, bias=QKP[:, 5:6]), chk=True)
            ln_stats_free[0] = ("v", ct2)
        h_chks = []
        S.w("v", "s", crstd)
        cu = None
        prev_gate_tok = None
        for dc in range(NDC):
            if h_chks:
                S.w("v", "s", h_chks[-1][1] or h_chks[-1][0])
            S.o("v", lambda e, dc=dc: e.scalar_tensor_tensor(
                SCA[:], PA[:], -1.0 / D, xs(dc), OP.mult, OP.add))
            cu = S.o("v", lambda e, t=t_slice: e.tensor_tensor(t, SCA[:], SCB[:],
                                                               OP.mult), chk=True)
            S.w("s", "v", cu)
            ch = S.o("s", lambda e, dc=dc, t=t_slice: e.activation(
                hs(dc), t, AF.Identity, bias=PAR[:, bi_col + dc:bi_col + dc + 1],
                scale=PAR[:, sc_col + dc:sc_col + dc + 1]), chk=True)
            ch2 = None
            if h2f32_slice is not None:
                if prev_gate_tok is not None:
                    S.w("s", "pe", prev_gate_tok)
                ch2 = S.o("s", lambda e, dc=dc, t=t_slice: e.activation(
                    h2f32_slice, t, AF.Identity, bias=PAR[:, bi_col + dc:bi_col + dc + 1],
                    scale=PAR[:, sc_col + dc:sc_col + dc + 1]), chk=True)
                prev_gate_tok = h2_cb(dc, ch2)
            h_chks.append((ch, ch2, dc))
        return h_chks, cu

    if DEBUG:
        S.w("y", "v", chk_par)
        S.d("y", "qy", lambda e: e.dma_start(out=dbg_par, in_=PAR[:]))
        S.d("y", "qy", lambda e: e.dma_start(out=dbg_isge, in_=GST[:, 32:48]))

    workf = WORK.ap().bitcast(F32)

    def ln1_halved():
        chks = [[None, None] for _ in range(NDC)]
        S.w("v", "qy", tok_x)
        S.w("s", "qy", tok_x)
        S.w("s", "v", chk_par1)
        last_mm = None
        last_u = None
        prev_ch = None
        for th in (0, 1):
            tlo = th * 1024
            for dc in range(NDC):
                if last_mm is not None:
                    S.w("v", "pe", last_mm)
                    S.w("s", "pe", last_mm)
                ccp = S.o("v", lambda e, dc=dc, tlo=tlo: e.tensor_copy(
                    SCA[:, 0:1024], xs(dc, tlo, tlo + 1024)), chk=True)
                csq = S.o("s", lambda e, dc=dc, tlo=tlo: e.activation(
                    SCB[:, 0:1024], xs(dc, tlo, tlo + 1024), AF.Square), chk=True)
                S.w("pe", "v", ccp)
                S.w("pe", "s", csq)
                S.w("pe", "qg", tok_consts)
                if th == 1 and dc == 0:
                    S.w("pe", "v", last_u)
                for tc in (0, 1):
                    S.o("pe", lambda e, tc=tc, dc=dc: e.matmul(
                        PA[:, tc * 512:(tc + 1) * 512], ONES[:],
                        SCA[:, tc * 512:(tc + 1) * 512],
                        start=(dc == 0), stop=(dc == NDC - 1)))
                    S.o("pe", lambda e, tc=tc, dc=dc: e.matmul(
                        PA[:, 1024 + tc * 512:1024 + (tc + 1) * 512], ONES[:],
                        SCB[:, tc * 512:(tc + 1) * 512],
                        start=(dc == 0), stop=False))
                last_mm = S.chk("pe")
            for tc in (0, 1):
                S.o("pe", lambda e, tc=tc: e.matmul(
                    PA[:, 1024 + tc * 512:1024 + (tc + 1) * 512], ONES1[0:1, :],
                    EPSR[0:1, 512:1024], start=False, stop=True))
            last_mm = S.chk("pe")
            S.w("v", "pe", last_mm)
            S.o("v", lambda e: e.tensor_scalar_mul(SCA[:, 0:1024], PA[:, 0:1024],
                                                   1.0 / D))
            S.o("v", lambda e: e.tensor_tensor(SCB[:, 0:1024], SCA[:, 0:1024],
                                               PA[:, 0:1024], OP.mult))
            ct2 = S.o("v", lambda e: e.scalar_tensor_tensor(
                workf[:, 1024:2048], SCB[:, 0:1024], -1.0, PA[:, 1024:2048],
                OP.mult, OP.add), chk=True)
            S.w("s", "v", ct2)
            S.o("s", lambda e: e.activation(workf[:, 2048:3072], workf[:, 1024:2048],
                                            AF.Ln))
            crstd = S.o("s", lambda e: e.activation(
                SCB[:, 0:1024], workf[:, 2048:3072], AF.Exp,
                scale=-0.5, bias=QKP[:, 5:6]), chk=True)
            S.w("v", "s", crstd)
            for dc in range(NDC):
                if prev_ch is not None:
                    S.w("v", "s", prev_ch)
                S.o("v", lambda e, dc=dc, tlo=tlo: e.scalar_tensor_tensor(
                    SCA[:, 0:1024], PA[:, 0:1024], -1.0 / D,
                    xs(dc, tlo, tlo + 1024), OP.mult, OP.add))
                cu = S.o("v", lambda e: e.tensor_tensor(
                    WORK[:, 0:1024], SCA[:, 0:1024], SCB[:, 0:1024], OP.mult),
                    chk=True)
                last_u = cu
                S.w("s", "v", cu)
                prev_ch = S.o("s", lambda e, dc=dc, tlo=tlo: e.activation(
                    hs(dc, tlo, tlo + 1024), WORK[:, 0:1024], AF.Identity,
                    bias=PAR[:, 0 + dc:0 + dc + 1],
                    scale=PAR[:, 8 + dc:8 + dc + 1]), chk=True)
                chks[dc][th] = prev_ch
        return chks, last_u

    ln1_hchks, ln1_pa_free = ln1_halved()

    QR, KR, VR, VT = 0, 2048, 4096, 6144
    TRA = PB[:, 1024:1536].bitcast(BF16)
    TRB = PB[:, 1536:2048].bitcast(BF16)
    AVP = PB[:, 1024:1152]
    AVP2 = PB[:, 1536:1664]
    trfree = [None, None]

    halves = [(PB, 0), (PA, 0), (PA, 1024)]
    half_free = [None, None, None]
    hctr = [0]
    tr_last = [None]
    av_last = [None]
    norm_tok = [None]
    ctm_k = [None]
    prev_cbc = [None]
    prev_ctl = [None]
    scb_pe_free = [None]
    kp1_free = [None]
    ao_guard = [False]
    last_q_mm = [None]
    last_k_mm = [None]

    ckp1z = S.o("v", lambda e: e.memset(KP1.ap()[0:64, 0:2048], 0.0), chk=True)

    for fc in range(NFC):
        chk_evac = {}
        for mat, psum, wreg, wtok, dst in (("q", PA, R0, tok_wq, QR),
                                           ("k", PB, R1, tok_wk, KR)):
            S.w("pe", "qg", wtok)
            if fc == 0:
                S.w("pe", "v", ln1_pa_free)
            else:
                if mat == "q":
                    S.we("pe", half_free[1])
                    S.we("pe", half_free[2])
                else:
                    S.we("pe", half_free[0])
            for kc in range(8):
                if fc == 0:
                    S.w("pe", "s", ln1_hchks[kc][0])
                    S.w("pe", "s", ln1_hchks[kc][1])
                for tc in range(4):
                    S.o("pe", lambda e, psum=psum, wreg=wreg, tc=tc, kc=kc, fc=fc:
                        e.matmul(
                            psum[:, tc * 512:(tc + 1) * 512],
                            wreg[:, kc * 1024 + fc * 128: kc * 1024 + fc * 128 + 128],
                            hs(kc, tc * 512, (tc + 1) * 512),
                            start=(kc == 0), stop=(kc == 7)))
            cm = S.chk("pe")
            if fc == NFC - 1:
                (last_q_mm if mat == "q" else last_k_mm)[0] = cm
            S.w("v", "pe", cm)
            chk_evac[mat] = S.o("v", lambda e, psum=psum, dst=dst: e.tensor_copy(
                WORK[:, dst:dst + 2048], psum[:]), chk=True)
        S.w("pe", "v", chk_evac["q"])
        S.w("pe", "qg", tok_wv[fc])
        pc = fc % 2
        for kc in range(8):
            for tc in range(4):
                S.o("pe", lambda e, tc=tc, kc=kc, pc=pc: e.matmul(
                    PA[:, tc * 512:(tc + 1) * 512],
                    WVS[:, pc * 1024 + kc * 128: pc * 1024 + kc * 128 + 128],
                    hs(kc, tc * 512, (tc + 1) * 512),
                    start=(kc == 0), stop=(kc == 7)))
        cmv = S.chk("pe")
        if fc + 2 < NFC:
            S.w("g", "pe", cmv)
            tok_wv.append(wv_piece(fc + 2))
        S.w("v", "pe", cmv)
        cev0 = S.o("v", lambda e: e.tensor_copy(
            WORK[:, VR:VR + 1024], PA[:, 0:1024]), chk=True)
        chk_evac["v"] = S.o("v", lambda e: e.tensor_copy(
            WORK[:, VR + 1024:VR + 2048], PA[:, 1024:2048]), chk=True)

        vt_half = [None, None]
        vt_done = None
        for rnd in range(2):
            trv = TRA if rnd == 0 else TRB
            S.w("pe", "v", cev0 if rnd == 0 else chk_evac["v"])
            S.we("pe", trfree[rnd])
            for j8 in range(8):
                j = rnd * 8 + j8
                S.o("pe", lambda e, j=j, j8=j8, trv=trv: e.matmul(
                    trv[:, j8 * 128:(j8 + 1) * 128],
                    WORK[:, VR + j * 128: VR + (j + 1) * 128], IDENT[:],
                    start=(j8 == 0), stop=(j8 == 7), is_transpose=True))
            ct = S.chk("pe")
            vt_done = ct
            S.w("v", "pe", ct)
            tok = ("v", S.o("v", lambda e, rnd=rnd, trv=trv: e.tensor_copy(
                WORK[:, VT + rnd * 1024: VT + (rnd + 1) * 1024], trv[:]), chk=True))
            trfree[rnd] = tok
            vt_half[rnd] = tok

        pa_free = ("v", chk_evac["v"])
        pb_free = ("v", chk_evac["k"])
        qraw = WORK[:, QR:QR + 2048]
        kraw = WORK[:, KR:KR + 2048]
        scaf = SCA.ap().bitcast(F32)
        S.we("pe", pa_free)
        for tc in range(4):
            S.o("pe", lambda e, tc=tc: e.matmul(
                PA[:, tc * 512:(tc + 1) * 512], BONES[:],
                qraw[:, tc * 512:(tc + 1) * 512], start=True, stop=True))
        csq_ = S.chk("pe")
        S.we("pe", pb_free)
        S.we("pe", vt_half[0])
        S.we("pe", vt_half[1])
        for tc in range(4):
            S.o("pe", lambda e, tc=tc: e.matmul(
                PB[:, tc * 512:(tc + 1) * 512], BONES[:],
                kraw[:, tc * 512:(tc + 1) * 512], start=True, stop=True))
        csk_ = S.chk("pe")
        S.w("v", "pe", csq_)
        cuq = S.o("v", lambda e: e.scalar_tensor_tensor(
            qraw, PA[:], -1.0 / DH, qraw, OP.mult, OP.add), chk=True)
        S.w("v", "pe", csk_)
        cuk = S.o("v", lambda e: e.scalar_tensor_tensor(
            kraw, PB[:], -1.0 / DH, kraw, OP.mult, OP.add), chk=True)
        S.w("s", "v", cuq)
        S.we("s", prev_cbc[0])
        cxq = S.o("s", lambda e: e.activation(SCA[:], qraw, AF.Square), chk=True)
        S.w("s", "v", cuk)
        S.we("s", prev_ctl[0])
        S.we("s", scb_pe_free[0])
        cxk = S.o("s", lambda e: e.activation(SCB[:], kraw, AF.Square), chk=True)
        S.w("pe", "s", cxq)
        S.w("pe", "v", cuq)
        for tc in range(4):
            S.o("pe", lambda e, tc=tc: e.matmul(
                PA[:, tc * 512:(tc + 1) * 512], BONES[:],
                SCA[:, tc * 512:(tc + 1) * 512], start=True, stop=False))
            S.o("pe", lambda e, tc=tc: e.matmul(
                PA[:, tc * 512:(tc + 1) * 512], ONES1[0:1, :],
                EPSR[0:1, 0:512], start=False, stop=True))
        cvq = S.chk("pe")
        S.w("pe", "s", cxk)
        S.w("pe", "v", cuk)
        S.we("pe", vt_half[0])
        S.we("pe", vt_half[1])
        for tc in range(4):
            S.o("pe", lambda e, tc=tc: e.matmul(
                PB[:, tc * 512:(tc + 1) * 512], BONES[:],
                SCB[:, tc * 512:(tc + 1) * 512], start=True, stop=False))
            S.o("pe", lambda e, tc=tc: e.matmul(
                PB[:, tc * 512:(tc + 1) * 512], ONES1[0:1, :],
                EPSR[0:1, 0:512], start=False, stop=True))
        cvk = S.chk("pe")
        qact_half = [None, None]
        kact_half = [None, None]
        kz_half = [None, None]
        scb_user = None
        ln_k = [None, None]
        ln_q = [None, None]
        S.w("s", "pe", cvk)
        for h in (0, 1):
            lo, hi = h * 1024, (h + 1) * 1024
            ln_k[h] = S.o("s", lambda e, lo=lo, hi=hi: e.activation(
                scaf[:, 0:1024], PB[:, lo:hi], AF.Ln), chk=True)
            ce = S.o("s", lambda e, lo=lo, hi=hi: e.activation(
                SCB[:, lo:hi], scaf[:, 0:1024], AF.Exp,
                scale=-0.5, bias=QKP[:, 4:5]), chk=True)
            S.w("v", "s", ce)
            ctmk = S.o("v", lambda e, lo=lo, hi=hi: e.tensor_tensor(
                WORK[:, KR + lo:KR + hi], WORK[:, KR + lo:KR + hi],
                SCB[:, lo:hi], OP.mult), chk=True)
            if h == 1:
                ctm_k[0] = ctmk
            S.w("s", "v", ctmk)
            S.w("s", "v", ckp1z)
            S.we("s", kp1_free[0])
            ck1 = S.o("s", lambda e, lo=lo, hi=hi: e.activation(
                KP1.ap()[64:128, lo:hi], WORK.ap()[64:128, KR + lo:KR + hi],
                AF.Identity, bias=QKP.ap()[64:128, 3:4],
                scale=QKP.ap()[64:128, 2:3]), chk=True)
            ck0 = S.o("s", lambda e, lo=lo, hi=hi: e.activation(
                WORK.ap()[0:64, KR + lo:KR + hi], WORK.ap()[0:64, KR + lo:KR + hi],
                AF.Identity, bias=QKP.ap()[0:64, 3:4],
                scale=QKP.ap()[0:64, 2:3]), chk=True)
            kact_half[h] = ("s", ck0)
            S.w("v", "s", ck1)
            kz_half[h] = ("v", S.o("v", lambda e, lo=lo, hi=hi: e.memset(
                WORK.ap()[64:128, KR + lo:KR + hi], 0.0), chk=True))
            ln_q[h] = S.o("s", lambda e, lo=lo, hi=hi: e.activation(
                scaf[:, 0:1024], PA[:, lo:hi], AF.Ln), chk=True)
            ce = S.o("s", lambda e, lo=lo, hi=hi: e.activation(
                SCB[:, lo:hi], scaf[:, 0:1024], AF.Exp,
                scale=-0.5, bias=QKP[:, 4:5]), chk=True)
            S.w("v", "s", ce)
            ctmq = S.o("v", lambda e, lo=lo, hi=hi: e.tensor_tensor(
                WORK[:, QR + lo:QR + hi], WORK[:, QR + lo:QR + hi],
                SCB[:, lo:hi], OP.mult), chk=True)
            scb_user = ("v", ctmq)
            S.w("s", "v", ctmq)
            cq = S.o("s", lambda e, lo=lo, hi=hi: e.activation(
                WORK[:, QR + lo:QR + hi], WORK[:, QR + lo:QR + hi], AF.Identity,
                bias=QKP[:, 1:2], scale=QKP[:, 0:1]), chk=True)
            qact_half[h] = ("s", cq)
        half_free[0] = ("s", ln_k[0])
        half_free[1] = ("s", ln_q[0])
        half_free[2] = ("s", ln_q[1])
        hctr[0] += (3 - hctr[0] % 3) % 3
        PRB = VR

        if DEBUG and fc == 0:
            S.we("y", qact_half[1])
            S.we("y", kact_half[1])
            S.we("y", tr_last[0])
            _tok = S.d("y", "qy", lambda e: e.dma_start(out=dbg_qkv, in_=WORK[:]))
            S.w("s", "qy", _tok)

        NT = 32
        pbufs = [WORK[:, VR:VR + 2048], SCB[:, 0:2048], SCA[:, 0:2048]]
        probs_free = [("pe", vt_done), scb_user, None]
        lc_free = [None, None]
        st = [dict() for _ in range(NT)]

        def emit_A(t):
            i, hh = divmod(t, 2)
            qoff = 64 * hh
            kv = (i + 1) * 128
            nh = 1 if kv <= 1024 else 2
            par = t % 2
            hf = i // 8
            S.we("pe", qact_half[hf])
            for h2 in range(hf + 1):
                S.we("pe", kact_half[h2])
                S.we("pe", kz_half[h2])
            hids = []
            csc_h = []
            for hx in range(nh):
                hid = hctr[0] % 3
                hctr[0] += 1
                hids.append(hid)
                S.we("pe", half_free[hid])
                psum, poff = halves[hid]
                lo = hx * 1024
                hi = min(kv, lo + 1024)
                off = lo
                while off < hi:
                    n = min(512, hi - off)
                    kop = (WORK[:, KR + off: KR + off + n] if hh == 0
                           else KP1[:, off:off + n])
                    S.o("pe", lambda e, psum=psum, poff=poff, off=off, lo=lo,
                        n=n, i=i, kop=kop: e.matmul(
                        psum[:, poff + off - lo: poff + off - lo + n],
                        WORK[:, QR + i * 128: QR + (i + 1) * 128],
                        kop, start=True, stop=True))
                    off += n
                csc_h.append(S.chk("pe"))
            st[t].update(csc=csc_h[-1], csc_h=csc_h, hids=hids, nh=nh, hh=hh, i=i)

        def emit_B(t):
            i, hh = st[t]["i"], st[t]["hh"]
            kv = (i + 1) * 128
            nh = st[t]["nh"]
            hids = st[t]["hids"]
            par = t % 2
            csc_h = st[t]["csc_h"]
            if t >= 2:
                S.w("v", "s", st[t - 2]["exp"])
            if nh == 2:
                psum, poff = halves[hids[0]]
                S.w("v", "pe", csc_h[0])
                S.o("v", lambda e, psum=psum, poff=poff, par=par: e.reduce_max(
                    NM[:, 4 * par:4 * par + 1], psum[:, poff:poff + 1024],
                    axis=AX.X, negate=True))
            S.w("v", "pe", csc_h[-1])
            dpsum, dpoff = halves[hids[-1]]
            drel = (i * 128) % 1024
            S.o("v", lambda e, dpsum=dpsum, dpoff=dpoff, drel=drel: e.tensor_tensor(
                dpsum[:, dpoff + drel: dpoff + drel + 128],
                dpsum[:, dpoff + drel: dpoff + drel + 128], MASK[:], OP.add))
            hx = nh - 1
            psum, poff = halves[hids[-1]]
            w = min(kv - hx * 1024, 1024)
            S.o("v", lambda e, psum=psum, poff=poff, w=w, hx=hx, par=par:
                e.reduce_max(
                    NM[:, 4 * par + hx:4 * par + hx + 1],
                    psum[:, poff:poff + w], axis=AX.X, negate=True))
            S.o("v", lambda e: e.tensor_copy(JNK[:, 0:2], QKP[:, 0:2]))
            if nh == 2:
                cnm = S.o("v", lambda e, par=par: e.tensor_tensor(
                    NM[:, 4 * par:4 * par + 1], NM[:, 4 * par:4 * par + 1],
                    NM[:, 4 * par + 1:4 * par + 2], OP.min), chk=True)
            else:
                cnm = S.chk("v")
            st[t]["cnm"] = cnm

        def emit_C(t):
            i, hh = st[t]["i"], st[t]["hh"]
            kv = (i + 1) * 128
            nh = st[t]["nh"]
            hids = st[t]["hids"]
            par = t % 2
            pbi = t % 3
            cnm = st[t]["cnm"]
            S.w("s", "v", cnm)
            S.we("s", probs_free[pbi])
            S.we("s", lc_free[par])
            chk_exp = None
            for hx, hid in enumerate(hids):
                psum, poff = halves[hid]
                w = min(kv - hx * 1024, 1024)
                chk_exp = S.o("s", lambda e, psum=psum, poff=poff, w=w, hx=hx,
                              par=par, pbi=pbi: e.activation(
                                  pbufs[pbi][:, hx * 1024: hx * 1024 + w],
                                  psum[:, poff:poff + w], AF.Exp,
                                  bias=NM[:, 4 * par:4 * par + 1],
                                  accum_out=NM[:, 4 * par + 2 + hx:4 * par + 3 + hx]),
                              chk=True)
                half_free[hid] = ("s", chk_exp)
            st[t]["exp"] = chk_exp

        def emit_D(tq, rnd):
            hh1, i1 = st[tq]["hh"], st[tq]["i"]
            prb = pbufs[tq % 3]
            ntr = i1 + 1
            nrnd = (ntr + 7) // 8
            if rnd >= nrnd:
                return
            if rnd == 0:
                S.w("pe", "s", st[tq]["exp"])
                S.we("pe", trfree[0])
                st[tq]["tr_evac"] = [None, None]
            else:
                S.we("pe", st[tq]["tr_evac"][0])
            lo8, hi8 = rnd * 8, min(ntr, rnd * 8 + 8)
            for j in range(lo8, hi8):
                S.o("pe", lambda e, j=j, lo8=lo8, hi8=hi8, prb=prb:
                    e.matmul(
                        TRA[:, (j - lo8) * 128:(j - lo8 + 1) * 128],
                        prb[:, j * 128:(j + 1) * 128], IDENT[:],
                        start=(j == lo8), stop=(j == hi8 - 1),
                        is_transpose=True))
            ctr = S.chk("pe")
            S.w("s", "pe", ctr)
            tok = ("s", S.o("s", lambda e, lo8=lo8, hi8=hi8, prb=prb: e.copy(
                prb[:, lo8 * 128:hi8 * 128],
                TRA[:, 0:(hi8 - lo8) * 128]), chk=True))
            st[tq]["tr_evac"][rnd] = tok
            if rnd == nrnd - 1:
                trfree[0] = tok

        def emit_EF(tq):
            hh1, i1 = st[tq]["hh"], st[tq]["i"]
            qoff1 = 64 * hh1
            par1 = tq % 2
            pbi = tq % 3
            prb = pbufs[pbi]
            ntr = i1 + 1
            nrnd = (ntr + 7) // 8
            avp = AVP2
            tr_evac = st[tq]["tr_evac"]
            S.we("pe", av_last[0])
            S.we("pe", vt_half[1])
            tp = (0, 64) if hh1 == 1 else None
            for rnd in range(nrnd):
                S.we("pe", tr_evac[rnd])
                lo8, hi8 = rnd * 8, min(ntr, rnd * 8 + 8)
                for j in range(lo8, hi8):
                    S.o("pe", lambda e, j=j, i1=i1, qoff1=qoff1, tp=tp, avp=avp,
                        prb=prb: e.matmul(
                            avp[qoff1:qoff1 + 64, 0:128],
                            WORK[:, VT + j * 128 + qoff1: VT + j * 128 + qoff1 + 64],
                            prb[:, j * 128:(j + 1) * 128],
                            start=(j == 0), stop=(j == i1), tile_position=tp))
            cav = S.chk("pe")
            st[tq]["cav"] = cav
            probs_free[pbi] = ("pe", cav)
            S.w("v", "s", st[tq]["exp"])
            if st[tq]["nh"] == 2:
                clc = S.o("v", lambda e, i1=i1, hh1=hh1, par1=par1: e.tensor_tensor(
                    LC[:, 2 * i1 + hh1: 2 * i1 + hh1 + 1],
                    NM[:, 4 * par1 + 2:4 * par1 + 3],
                    NM[:, 4 * par1 + 3:4 * par1 + 4], OP.add), chk=True)
            else:
                clc = S.o("v", lambda e, i1=i1, hh1=hh1, par1=par1: e.tensor_copy(
                    LC[:, 2 * i1 + hh1: 2 * i1 + hh1 + 1],
                    NM[:, 4 * par1 + 2:4 * par1 + 3]), chk=True)
            lc_free[par1] = ("v", clc)
            S.w("v", "pe", cav)
            av_last[0] = ("v", S.o("v", lambda e, fc=fc, i1=i1, qoff1=qoff1,
                                   avp=avp: e.tensor_copy(
                                       AO.ap()[qoff1:qoff1 + 64,
                                               fc * T + i1 * 128: fc * T + (i1 + 1) * 128],
                                       avp[qoff1:qoff1 + 64, 0:128]), chk=True))

        emit_A(0)
        emit_B(0)
        emit_C(0)
        for t in range(NT):
            if t + 1 < NT:
                emit_A(t + 1)
                emit_B(t + 1)
            emit_D(t, 0)
            if t >= 1:
                emit_EF(t - 1)
            if t + 1 < NT:
                emit_C(t + 1)
            emit_D(t, 1)
        emit_EF(NT - 1)
        scb_pe_free[0] = probs_free[1]
        kp1_free[0] = ("pe", st[NT - 1]["csc"])

        S.o("v", lambda e: e.reciprocal(LC[:], LC[:]))
        S.o("v", lambda e: e.tensor_copy(JNK[:, 0:2], QKP[:, 0:2]))
        clb = S.o("v", lambda e: e.tensor_copy(SCB[:, 0:32], LC[:]), chk=True)
        S.w("pe", "v", clb)
        S.we("pe", av_last[0])
        S.we("pe", half_free[0])
        S.we("pe", trfree[0])
        S.we("pe", trfree[1])
        pbbf = PB.ap().bitcast(BF16)
        for i in range(16):
            S.o("pe", lambda e, i=i, pbbf=pbbf: e.matmul(
                pbbf[0:2, i * 128:(i + 1) * 128], SCB[:, 2 * i:2 * i + 2], IDENT[:],
                start=(i % 8 == 0), stop=(i % 8 == 7), is_transpose=True))
        ctl = S.chk("pe")
        S.w("v", "pe", ctl)
        cle = S.o("v", lambda e, pbbf=pbbf: e.tensor_copy(
            SCA.ap()[0:2, 0:2048], pbbf[0:2, 0:2048]), chk=True)
        S.w("pe", "v", cle)
        for t16 in range(16):
            S.o("pe", lambda e, t16=t16: e.matmul(
                PB[:, t16 * 128:(t16 + 1) * 128], SEL2[:],
                SCA.ap()[0:2, t16 * 128:(t16 + 1) * 128],
                start=(t16 % 4 == 0), stop=(t16 % 4 == 3)))
        cbc = S.chk("pe")
        S.w("v", "pe", cbc)
        cnorm = S.o("v", lambda e, fc=fc: e.tensor_tensor(
            aos(fc), aos(fc), PB[:], OP.mult), chk=True)
        norm_tok[0] = ("v", cnorm)
        half_free[0] = ("v", cnorm)
        trfree[0] = ("v", cnorm)
        trfree[1] = ("v", cnorm)
        prev_cbc[0] = ("pe", cbc)
        prev_ctl[0] = ("pe", ctl)

    S.w("g", "pe", last_q_mm[0])
    tok_wo = S.d("g", "qg", wload(R0, dwo))

    S.w("pe", "qg", tok_wo)
    S.we("pe", norm_tok[0])
    wo_cadd = []
    wo_cts = []
    last_wo_mm = None
    wo_tbufs = [SCA, SCB, None]
    for dc in range(NDC):
        psum = PA if dc % 2 == 0 else PB
        if dc >= 2:
            S.w("pe", "s", wo_cts[dc - 2])
        for kc in range(8):
            for tc in range(4):
                S.o("pe", lambda e, psum=psum, dc=dc, tc=tc, kc=kc: e.matmul(
                    psum[:, tc * 512:(tc + 1) * 512],
                    R0[:, kc * 1024 + dc * 128: kc * 1024 + dc * 128 + 128],
                    aos(kc, tc * 512, (tc + 1) * 512),
                    start=(kc == 0), stop=(kc == 7)))
        cwo = S.chk("pe")
        last_wo_mm = cwo
        S.w("s", "pe", cwo)
        if dc >= 3:
            S.w("s", "v", wo_cadd[dc - 3])
        tbuf = [SCA[:, 0:2048], SCB[:, 0:2048], WORK[:, 0:2048]][dc % 3]
        cts = S.o("s", lambda e, psum=psum, dc=dc, tbuf=tbuf: e.activation(
            tbuf, psum[:], AF.Identity,
            bias=PAR[:, 48 + dc:48 + dc + 1],
            scale=PAR[:, 16 + dc:16 + dc + 1]), chk=True)
        wo_cts.append(cts)
        S.w("v", "s", cts)
        cadd = S.o("v", lambda e, dc=dc, tbuf=tbuf: e.tensor_tensor(
            xs(dc), xs(dc), tbuf, OP.add), chk=True)
        wo_cadd.append(cadd)
    wo_done = wo_cadd[-1]

    if DEBUG:
        S.we("y", norm_tok[0])
        _tok = S.d("y", "qy", lambda e: e.dma_start(out=dbg_ao, in_=AO[:]))
        S.w("v", "qy", _tok)
        S.w("s", "qy", _tok)
        S.w("y", "v", wo_done)
        _tok = S.d("y", "qy", lambda e: e.dma_start(out=dbg_x1, in_=X[:]))
        S.w("v", "qy", _tok)

    ln2_chks, ln2_pa_free = layernorm(32, 24, AO[:, 0:2048], None, None,
                                      pre_waits=[("v", wo_done),
                                                 ("s", wo_cts[-1])],
                                      lnscr=aof[:, 3072:5120])

    S.we("pe", ln_stats_free[0])
    for dc in range(NDC):
        S.w("pe", "s", ln2_chks[dc][0])
        S.w("pe", "qg", tok_gatet)
        for tc in range(4):
            S.o("pe", lambda e, dc=dc, tc=tc: e.matmul(
                PB[0:8, tc * 512:(tc + 1) * 512],
                GATET[:, dc * 8:(dc + 1) * 8],
                hs(dc, tc * 512, (tc + 1) * 512),
                start=(dc == 0), stop=(dc == NDC - 1)))
    cgm = S.chk("pe")
    S.w("v", "pe", cgm)
    cgev = S.o("v", lambda e: e.tensor_copy(SCA.ap()[0:8, 0:2048], PB[0:8, 0:2048]),
               chk=True)
    S.w("pe", "v", cgev)
    S.w("pe", "v", ln2_pa_free)
    pabf = PA.ap().bitcast(BF16)
    for t16 in range(16):
        S.o("pe", lambda e, t16=t16, pabf=pabf: e.matmul(
            pabf[0:128, t16 * 8:(t16 + 1) * 8],
            SCA.ap()[0:8, t16 * 128:(t16 + 1) * 128], IDENT.ap()[0:8, 0:8],
            start=(t16 == 0), stop=(t16 == 15), is_transpose=True))
    gate_done = S.chk("pe")

    sbf = SCB.ap().bitcast(F32)
    gex = sbf[:, 128:256]
    g3 = pabf[0:128, 0:128].rearrange("p (g k) -> p g k", k=8)
    ge3 = gex.rearrange("p (g k) -> p g k", k=8)

    def bc16(ap):
        return ap.rearrange("p (g o) -> p g o", o=1).broadcast_to([128, 16, 8])

    S.w("v", "pe", gate_done)
    S.o("v", lambda e: e.reduce_max(
        GST[:, 0:16].rearrange("p (g o) -> p g o", o=1), g3, axis=AX.X))
    S.o("v", lambda e: e.tensor_copy(JNK[:, 0:2], QKP[:, 0:2]))
    cgs = S.o("v", lambda e: e.tensor_tensor(ge3, g3, bc16(GST[:, 0:16]), OP.subtract),
              chk=True)
    S.w("s", "v", cgs)
    cge = S.o("s", lambda e: e.activation(gex, gex, AF.Exp), chk=True)
    S.w("v", "s", cge)
    S.o("v", lambda e: e.reduce_sum(
        GST[:, 16:32].rearrange("p (g o) -> p g o", o=1), ge3, axis=AX.X))
    S.o("v", lambda e: e.reduce_max(
        GST[:, 0:16].rearrange("p (g o) -> p g o", o=1), ge3, axis=AX.X))
    S.o("v", lambda e: e.tensor_copy(JNK[:, 0:2], QKP[:, 0:2]))
    S.o("v", lambda e: e.tensor_tensor(
        sbf[:, 256:384].rearrange("p (g k) -> p g k", k=8), ge3, bc16(GST[:, 0:16]),
        OP.is_ge))
    S.o("v", lambda e: e.scalar_tensor_tensor(
        sbf[:, 256:384], sbf[:, 256:384], -1e30, gex, OP.mult, OP.add))
    S.o("v", lambda e: e.reduce_max(
        GST[:, 32:48].rearrange("p (g o) -> p g o", o=1),
        sbf[:, 256:384].rearrange("p (g k) -> p g k", k=8), axis=AX.X))
    S.o("v", lambda e: e.tensor_copy(JNK[:, 0:2], QKP[:, 0:2]))
    S.o("v", lambda e: e.tensor_tensor(
        sbf[:, 256:384].rearrange("p (g k) -> p g k", k=8), ge3, bc16(GST[:, 32:48]),
        OP.is_ge))
    S.o("v", lambda e: e.tensor_tensor(gex, gex, sbf[:, 256:384], OP.mult))
    S.o("v", lambda e: e.reciprocal(GST[:, 48:64], GST[:, 16:32]))
    S.o("v", lambda e: e.tensor_copy(JNK[:, 0:2], QKP[:, 0:2]))
    S.o("v", lambda e: e.tensor_tensor(ge3, ge3, bc16(GST[:, 48:64]), OP.mult))
    ccb = S.o("v", lambda e: e.tensor_copy(SCA[:, 0:128], gex), chk=True)
    S.w("pe", "v", ccb)
    for t16 in range(16):
        S.o("pe", lambda e, t16=t16, pabf=pabf: e.matmul(
            pabf[0:8, t16 * 128:(t16 + 1) * 128], SCA[:, t16 * 8:(t16 + 1) * 8],
            IDENT[:], start=(t16 % 8 == 0), stop=(t16 % 8 == 7), is_transpose=True))
    ctc = S.chk("pe")
    S.w("v", "pe", ctc)
    crow_done = S.o("v", lambda e, pabf=pabf: e.tensor_copy(
        CROW[:], pabf[0:8, 0:2048]), chk=True)

    regs = [R1, RW, R0]
    reg_free = [("pe", last_k_mm[0]), ("v", wo_cadd[5]), ("s", wo_cts[-1])]
    wtoks = {}
    drams = {"w1": dw1, "w3": dw3, "w2": dw2}
    sdrams = {"w1": dsw1, "w3": dsw3, "w2": dsw2}

    def sched_wload(m_idx, e_idx, mat):
        ridx = (3 * e_idx + m_idx) % 3
        reg = regs[ridx]
        S.we("g", reg_free[ridx])
        if e_idx < NE:
            src = drams[mat].rearrange("e (c p) o -> e p c o", p=128)[e_idx]
        else:
            src = sdrams[mat].rearrange("(c p) o -> p c o", p=128)
        wtoks[(e_idx, mat)] = S.d("g", "qg", lambda e, reg=reg, src=src: e.dma_start(
            out=reg.rearrange("p (c o) -> p c o", c=8), in_=src))

    for m_idx, mat in enumerate(("w1", "w3", "w2")):
        sched_wload(m_idx, 0, mat)

    pa_free_ffn = ("v", crow_done)
    pb_free_ffn = ("v", cgev)
    he_read = ("s", ln2_chks[-1][0])
    prev_csil = None
    cw2 = None
    stts = []

    for e_idx in range(NE + 1):
        is_sh = e_idx == NE
        rW1 = regs[(3 * e_idx + 0) % 3]
        rW3 = regs[(3 * e_idx + 1) % 3]
        rW2 = regs[(3 * e_idx + 2) % 3]
        if not is_sh:
            S.w("pe", "v", crow_done)
            S.we("pe", pa_free_ffn)
            for tc in range(4):
                S.o("pe", lambda e, e_idx=e_idx, tc=tc: e.matmul(
                    PA[:, tc * 512:(tc + 1) * 512],
                    ESEL[:, e_idx * 128:(e_idx + 1) * 128],
                    CROW[:, tc * 512:(tc + 1) * 512], start=True, stop=True))
            ccb2 = S.chk("pe")
            S.w("v", "pe", ccb2)
            cbev0 = S.o("v", lambda e: e.tensor_copy(
                SCB[:, 0:1024], PA[:, 0:1024]), chk=True)
            cbev1 = S.o("v", lambda e: e.tensor_copy(
                SCB[:, 1024:2048], PA[:, 1024:2048]), chk=True)
            cb_half = [("v", cbev0), ("v", cbev1)]
            pa_free_ffn = ("v", cbev1)
        he_done = None
        he_toks = [None] * 8
        for ic in range(8):
            bcol1 = (e_idx * 8 + ic) if not is_sh else (128 + ic)
            bcol3 = (64 + e_idx * 8 + ic) if not is_sh else (136 + ic)
            S.w("pe", "qg", wtoks[(e_idx, "w1")])
            split_cb = (not is_sh) and ic == 0
            if not split_cb:
                S.we("pe", pa_free_ffn)
            for tc in range(4):
                if split_cb:
                    S.we("pe", cb_half[0] if tc < 2 else cb_half[1])
                for kc in range(8):
                    if e_idx == 0 and ic == 0:
                        S.w("pe", "s", ln2_chks[kc][0])
                    S.o("pe", lambda e, rW1=rW1, ic=ic, tc=tc, kc=kc: e.matmul(
                        PA[:, tc * 512:(tc + 1) * 512],
                        rW1[:, kc * 1024 + ic * 128: kc * 1024 + ic * 128 + 128],
                        hs(kc, tc * 512, (tc + 1) * 512),
                        start=(kc == 0), stop=(kc == 7)))
            ch1 = S.chk("pe")
            S.w("s", "pe", ch1)
            if prev_csil is not None and he_done is not None:
                S.w("s", "v", he_done)
            csil = S.o("s", lambda e, bcol1=bcol1: e.activation(
                SCA[:], PA[:], AF.Silu, bias=BIA[:, bcol1:bcol1 + 1]), chk=True)
            prev_csil = csil
            pa_free_ffn = ("s", csil)
            S.w("pe", "qg", wtoks[(e_idx, "w3")])
            S.we("pe", pb_free_ffn)
            for kc in range(8):
                for tc in range(4):
                    S.o("pe", lambda e, rW3=rW3, ic=ic, tc=tc, kc=kc: e.matmul(
                        PB[:, tc * 512:(tc + 1) * 512],
                        rW3[:, kc * 1024 + ic * 128: kc * 1024 + ic * 128 + 128],
                        hs(kc, tc * 512, (tc + 1) * 512),
                        start=(kc == 0), stop=(kc == 7)))
            ch3 = S.chk("pe")
            S.w("v", "pe", ch3)
            S.w("v", "s", csil)
            S.we("v", he_read)
            chm = S.o("v", lambda e, ic=ic, bcol3=bcol3: e.scalar_tensor_tensor(
                aos(ic), PB[:], BIA[:, bcol3:bcol3 + 1], SCA[:], OP.add, OP.mult),
                chk=True)
            pb_free_ffn = ("v", chm)
            if not is_sh:
                chm = S.o("v", lambda e, ic=ic: e.tensor_tensor(
                    aos(ic), aos(ic), SCB[:], OP.mult), chk=True)
            he_done = chm
            he_toks[ic] = ("v", chm)
        psfree = {0: pa_free_ffn, 1: pb_free_ffn}
        sh_cts = []
        for dc in range(NDC):
            par = dc % 2
            psum = PA if par == 0 else PB
            S.w("pe", "qg", wtoks[(e_idx, "w2")])
            S.we("pe", psfree[par])
            for kc in range(8):
                S.we("pe", he_toks[kc])
                for tc in range(4):
                    S.o("pe", lambda e, rW2=rW2, psum=psum, dc=dc, tc=tc, kc=kc:
                        e.matmul(
                            psum[:, tc * 512:(tc + 1) * 512],
                            rW2[:, kc * 1024 + dc * 128: kc * 1024 + dc * 128 + 128],
                            aos(kc, tc * 512, (tc + 1) * 512),
                            start=(kc == 0), stop=(kc == 7 and not is_sh)))
            if is_sh:
                S.w("pe", "qg", tok_consts)
                S.w("pe", "v", crow_done)
                for tc in range(4):
                    S.o("pe", lambda e, psum=psum, dc=dc, tc=tc: e.matmul(
                        psum[:, tc * 512:(tc + 1) * 512],
                        B2R[0:8, dc * 128:(dc + 1) * 128],
                        CROW[0:8, tc * 512:(tc + 1) * 512],
                        start=False, stop=True))
            cw2 = S.chk("pe")
            if dc == NDC - 1:
                he_read = ("pe", cw2)
            if not is_sh:
                S.w("v", "pe", cw2)
                cst = S.o("v", lambda e, psum=psum, dc=dc: e.scalar_tensor_tensor(
                    xs(dc), psum[:], PAR[:, 40 + dc:40 + dc + 1], xs(dc),
                    OP.mult, OP.add), chk=True)
            else:
                S.w("s", "pe", cw2)
                if dc >= 2:
                    S.w("s", "v", stts[-2])
                tbuf = SCA if par == 0 else SCB
                ct = S.o("s", lambda e, psum=psum, dc=dc, tbuf=tbuf: e.activation(
                    tbuf[:], psum[:], AF.Identity,
                    bias=PAR[:, 56 + dc:56 + dc + 1],
                    scale=PAR[:, 40 + dc:40 + dc + 1]), chk=True)
                sh_cts.append(ct)
                S.w("v", "s", ct)
                cst = S.o("v", lambda e, dc=dc, tbuf=tbuf: e.tensor_tensor(
                    xs(dc), xs(dc), tbuf[:], OP.add), chk=True)
            psfree[par] = ("v", cst)
            stts.append(cst)
        pa_free_ffn = psfree[0]
        pb_free_ffn = psfree[1]
        reg_free[(3 * e_idx + 0) % 3] = ("pe", ch1)
        reg_free[(3 * e_idx + 1) % 3] = ("pe", ch3)
        reg_free[(3 * e_idx + 2) % 3] = ("pe", cw2)
        if e_idx < NE:
            for m_idx, mat in enumerate(("w1", "w3", "w2")):
                sched_wload(m_idx, e_idx + 1, mat)

    for t in stts[-NDC:]:
        S.w("y", "v", t)
    tok_store = S.d("y", "qy", lambda e: e.dma_start(
        out=dout.rearrange("(c p) t -> p c t", p=128),
        in_=X[:].rearrange("p (c t) -> p c t", c=NDC)))
    S.w("y", "qy", tok_store)

    with (
        nc.allow_low_precision("bf16 intermediates are deliberate; fp32 where it matters"),
        nc.semaphore("sem_pe") as sem_pe,
        nc.semaphore("sem_v") as sem_v,
        nc.semaphore("sem_s") as sem_s,
        nc.semaphore("sem_g") as sem_g,
        nc.semaphore("sem_y") as sem_y,
        nc.semaphore("sem_qy") as sem_qy,
        nc.semaphore("sem_qg") as sem_qg,
        nc.Block() as block,
    ):
        sems = {"pe": sem_pe, "v": sem_v, "s": sem_s, "g": sem_g, "y": sem_y,
                "qy": sem_qy, "qg": sem_qg}

        @block.tensor
        def _(eng):
            S.replay(eng, "pe", sems)

        @block.vector
        def _(eng):
            S.replay(eng, "v", sems)

        @block.scalar
        def _(eng):
            S.replay(eng, "s", sems)

        @block.gpsimd
        def _(eng):
            S.replay(eng, "g", sems)

        @block.sync
        def _(eng):
            S.replay(eng, "y", sems)

    return nc


_CACHE = {}
_LAST_RES = None


def kernel(**inputs):
    N = 8
    x = np.asarray(inputs["L0_attn_x"])
    f32 = np.float32

    def c_(a):
        return np.ascontiguousarray(a, dtype=f32)

    wq = c_(np.asarray(inputs["L0_attn_Wq"]).T)
    wk = c_(np.asarray(inputs["L0_attn_Wk"]).T)
    wv = c_(np.asarray(inputs["L0_attn_Wv"]).T)
    wo = c_(np.asarray(inputs["L0_attn_Wo"]).T)
    w1 = c_(inputs["L0_ffn_w1"])
    w3 = c_(inputs["L0_ffn_w3"])
    w2 = c_(inputs["L0_ffn_w2"])
    sw1 = c_(inputs["L0_ffn_sw1"])
    sw3 = c_(inputs["L0_ffn_sw3"])
    sw2 = c_(inputs["L0_ffn_sw2"])
    wmod = c_(inputs["L0_wmod"])
    cfull = np.asarray(inputs["L0_c"], dtype=f32)

    gq = np.asarray(inputs["L0_attn_gq"], dtype=f32)
    bq = np.asarray(inputs["L0_attn_bq"], dtype=f32)
    gk = np.asarray(inputs["L0_attn_gk"], dtype=f32)
    bk = np.asarray(inputs["L0_attn_bk"], dtype=f32)
    qkp = c_(np.stack([np.tile(gq, 2), np.tile(bq, 2),
                       np.tile(gk, 2), np.tile(bk, 2),
                       np.full(128, 0.5 * np.log(64.0), dtype=f32),
                       np.full(128, 0.5 * np.log(1024.0), dtype=f32)], axis=1))

    b1 = np.asarray(inputs["L0_ffn_b1"], dtype=f32)
    b3 = np.asarray(inputs["L0_ffn_b3"], dtype=f32)
    b2 = c_(inputs["L0_ffn_b2"])
    sb1 = np.asarray(inputs["L0_ffn_sb1"], dtype=f32)
    sb3 = np.asarray(inputs["L0_ffn_sb3"], dtype=f32)
    sb2 = np.asarray(inputs["L0_ffn_sb2"], dtype=f32)
    bo = np.asarray(inputs["L0_attn_bo"], dtype=f32)
    biases = np.zeros((128, 160), dtype=f32)
    biases[:, 0:64] = b1.reshape(8, 8, 128).transpose(2, 0, 1).reshape(128, 64)
    biases[:, 64:128] = b3.reshape(8, 8, 128).transpose(2, 0, 1).reshape(128, 64)
    biases[:, 128:136] = sb1.reshape(8, 128).T
    biases[:, 136:144] = sb3.reshape(8, 128).T
    biases[:, 144:152] = bo.reshape(8, 128).T
    biases[:, 152:160] = sb2.reshape(8, 128).T
    biases = c_(biases)

    gatet = c_(np.asarray(inputs["L0_ffn_gate_w"], dtype=f32).T
               .reshape(8, 128, 8).transpose(1, 0, 2).reshape(128, 64))

    allones = np.ones((128, 128), dtype=f32)
    blockones = np.zeros((128, 128), dtype=f32)
    blockones[0:64, 0:64] = 1.0
    blockones[64:128, 64:128] = 1.0
    ident = np.eye(128, dtype=f32)
    maskd = np.triu(np.full((128, 128), -1e30, dtype=f32), 1)
    sel2 = np.zeros((2, 128), dtype=f32)
    sel2[0, 0:64] = 1.0
    sel2[1, 64:128] = 1.0
    esel = np.zeros((8, 1024), dtype=f32)
    for e in range(8):
        esel[e, e * 128:(e + 1) * 128] = 1.0
    ones1 = np.ones((1, 128), dtype=f32)
    epsr = np.zeros((1, 1024), dtype=f32)
    epsr[0, 0:512] = 64 * 1e-5
    epsr[0, 512:1024] = 1024 * 1e-5

    if "nc" not in _CACHE:
        _CACHE["nc"] = build()
    nc = _CACHE["nc"]

    in_maps = []
    for i in range(N):
        in_maps.append({
            "x": c_(x[i].T),
            "wq": wq, "wk": wk, "wv": wv, "wo": wo,
            "w1": w1, "w3": w3, "w2": w2,
            "sw1": sw1, "sw3": sw3, "sw2": sw2,
            "wmod": wmod,
            "cc": c_(cfull[i].reshape(8, 128).T),
            "qkp": qkp, "biases": biases, "gatet": gatet, "b2r": b2,
            "callones": allones, "cblockones": blockones, "cident": ident,
            "cmask": maskd, "csel2": sel2, "cesel": esel,
            "cones1": ones1, "cepsr": epsr, "cidentf": ident,
        })
    res = run_bass_kernel_spmd(nc, in_maps, core_ids=list(range(N)))
    global _LAST_RES
    _LAST_RES = res
    out = np.stack([np.ascontiguousarray(r["out"].T) for r in res.results], axis=0)
    return out.astype(np.float32)


# version 31
# speedup vs baseline: 1.0752x; 1.0043x over previous
import math
import numpy as np
import ml_dtypes
import concourse.bass as bass
import concourse.mybir as mybir
from concourse.bass_utils import run_bass_kernel_spmd

F32 = mybir.dt.float32
BF16 = mybir.dt.bfloat16
AF = mybir.ActivationFunctionType
OP = mybir.AluOpType
AX = mybir.AxisListType

T = 2048
D = 1024
NDC = 8
NFC = 8
NE = 8
DH = 64
EPS = 1e-5

DEBUG = False

ENGS = ("pe", "v", "s", "g", "y")
QUEUES = ("qy", "qg")


class Sch:

    def __init__(self):
        self.seq = {k: [] for k in ENGS}
        self.n = {k: 0 for k in ENGS}
        self.nd = {k: 0 for k in QUEUES}
        self.wm = {k: {} for k in ENGS}

    def w(self, eng, sem, val):
        if not val or sem == eng or val <= self.wm[eng].get(sem, 0):
            return
        self.wm[eng][sem] = val
        self.seq[eng].append(("w", sem, val))

    def we(self, eng, et):
        if et is not None:
            self.w(eng, et[0], et[1])

    def o(self, eng, fn, chk=False):
        self.seq[eng].append(("o", fn, chk))
        if chk:
            self.n[eng] += 1
            return self.n[eng]
        return None

    def chk(self, eng):
        for idx in range(len(self.seq[eng]) - 1, -1, -1):
            it = self.seq[eng][idx]
            if it[0] == "o":
                if not it[2]:
                    self.seq[eng][idx] = ("o", it[1], True)
                    self.n[eng] += 1
                return self.n[eng]
            if it[0] == "d":
                raise RuntimeError("chk would land on a dma")
        raise RuntimeError("no op to chk")

    def d(self, eng, q, fn):
        self.seq[eng].append(("d", fn, q))
        self.nd[q] += 1
        return self.nd[q] * 16

    def replay(self, eng_obj, name, sems):
        for it in self.seq[name]:
            if it[0] == "w":
                eng_obj.wait_ge(sems[it[1]], it[2])
            elif it[0] == "o":
                inst = it[1](eng_obj)
                if it[2]:
                    inst.then_inc(sems[name], 1)
            else:
                it[1](eng_obj).then_inc(sems[it[2]], 16)


def build():
    nc = bass.Bass("TRN2", target_bir_lowering=False)

    dx = nc.dram_tensor("x", [D, T], F32, kind="ExternalInput").ap()
    dwq = nc.dram_tensor("wq", [D, D], F32, kind="ExternalInput").ap()
    dwk = nc.dram_tensor("wk", [D, D], F32, kind="ExternalInput").ap()
    dwv = nc.dram_tensor("wv", [D, D], F32, kind="ExternalInput").ap()
    dwo = nc.dram_tensor("wo", [D, D], F32, kind="ExternalInput").ap()
    dw1 = nc.dram_tensor("w1", [NE, D, D], F32, kind="ExternalInput").ap()
    dw3 = nc.dram_tensor("w3", [NE, D, D], F32, kind="ExternalInput").ap()
    dw2 = nc.dram_tensor("w2", [NE, D, D], F32, kind="ExternalInput").ap()
    dsw1 = nc.dram_tensor("sw1", [D, D], F32, kind="ExternalInput").ap()
    dsw3 = nc.dram_tensor("sw3", [D, D], F32, kind="ExternalInput").ap()
    dsw2 = nc.dram_tensor("sw2", [D, D], F32, kind="ExternalInput").ap()
    dwmod = nc.dram_tensor("wmod", [D, 6144], F32, kind="ExternalInput").ap()
    dcc = nc.dram_tensor("cc", [128, 8], F32, kind="ExternalInput").ap()
    dqkp = nc.dram_tensor("qkp", [128, 6], F32, kind="ExternalInput").ap()
    dbias = nc.dram_tensor("biases", [128, 160], F32, kind="ExternalInput").ap()
    dgatet = nc.dram_tensor("gatet", [128, 64], F32, kind="ExternalInput").ap()
    db2r = nc.dram_tensor("b2r", [8, 1024], F32, kind="ExternalInput").ap()
    dallones = nc.dram_tensor("callones", [128, 128], F32, kind="ExternalInput").ap()
    dblockones = nc.dram_tensor("cblockones", [128, 128], F32, kind="ExternalInput").ap()
    dident = nc.dram_tensor("cident", [128, 128], F32, kind="ExternalInput").ap()
    dmask = nc.dram_tensor("cmask", [128, 128], F32, kind="ExternalInput").ap()
    dsel2 = nc.dram_tensor("csel2", [2, 128], F32, kind="ExternalInput").ap()
    dones1 = nc.dram_tensor("cones1", [1, 128], F32, kind="ExternalInput").ap()
    didentf = nc.dram_tensor("cidentf", [128, 128], F32, kind="ExternalInput").ap()
    depsr = nc.dram_tensor("cepsr", [1, 1024], F32, kind="ExternalInput").ap()
    desel = nc.dram_tensor("cesel", [8, 1024], F32, kind="ExternalInput").ap()
    dout = nc.dram_tensor("out", [D, T], F32, kind="ExternalOutput").ap()
    if DEBUG:
        dbg_par = nc.dram_tensor("dbg_par", [128, 64], F32, kind="ExternalOutput").ap()
        dbg_h = nc.dram_tensor("dbg_h", [128, NDC * T], BF16, kind="ExternalOutput").ap()
        dbg_qkv = nc.dram_tensor("dbg_qkv", [128, 8192], BF16, kind="ExternalOutput").ap()
        dbg_ao = nc.dram_tensor("dbg_ao", [128, NDC * T], BF16, kind="ExternalOutput").ap()
        dbg_x1 = nc.dram_tensor("dbg_x1", [128, NDC * T], F32, kind="ExternalOutput").ap()
        dbg_crow = nc.dram_tensor("dbg_crow", [8, T], BF16, kind="ExternalOutput").ap()
        dbg_h2 = nc.dram_tensor("dbg_h2", [128, NDC * T], BF16, kind="ExternalOutput").ap()
        dbg_ao2 = nc.dram_tensor("dbg_ao2", [128, T], BF16, kind="ExternalOutput").ap()
        dbg_lc = nc.dram_tensor("dbg_lc", [128, 32], F32, kind="ExternalOutput").ap()
        dbg_pt = nc.dram_tensor("dbg_pt", [128, T], BF16, kind="ExternalOutput").ap()
        dbg_isge = nc.dram_tensor("dbg_isge", [128, 16], F32, kind="ExternalOutput").ap()
        dbg_sc = nc.dram_tensor("dbg_sc", [128, 2048], F32, kind="ExternalOutput").ap()
        dbg_nm = nc.dram_tensor("dbg_nm", [128, 4], F32, kind="ExternalOutput").ap()
        dbg_pr = nc.dram_tensor("dbg_pr", [128, 2048], BF16, kind="ExternalOutput").ap()

    X = nc.alloc_sbuf_tensor("X", [128, NDC * T], F32)
    HH = nc.alloc_sbuf_tensor("HH", [128, NDC * T], BF16)
    AO = nc.alloc_sbuf_tensor("AO", [128, NDC * T], BF16)
    WB = nc.alloc_sbuf_tensor("WB", [128, 2 * 8192], BF16)
    WORK = nc.alloc_sbuf_tensor("WORK", [128, 8192], BF16)
    SCA = nc.alloc_sbuf_tensor("SCA", [128, T], BF16)
    SCB = nc.alloc_sbuf_tensor("SCB", [128, T], BF16)
    WVS = nc.alloc_sbuf_tensor("WVS", [128, 2048], BF16)
    KP1 = nc.alloc_sbuf_tensor("KP1", [128, 2048], BF16)
    CROW = nc.alloc_sbuf_tensor("CROW", [8, T], BF16)
    QKP = nc.alloc_sbuf_tensor("QKP", [128, 6], F32)
    CC = nc.alloc_sbuf_tensor("CC", [128, 8], F32)
    SCS = nc.alloc_sbuf_tensor("SCS", [128, 8], F32)
    BIA = nc.alloc_sbuf_tensor("BIA", [128, 160], F32)
    PAR = nc.alloc_sbuf_tensor("PAR", [128, 64], F32)
    GATET = nc.alloc_sbuf_tensor("GATET", [128, 64], BF16)
    B2R = nc.alloc_sbuf_tensor("B2R", [8, 1024], BF16)
    ONES = nc.alloc_sbuf_tensor("ONES", [128, 128], BF16)
    BONES = nc.alloc_sbuf_tensor("BONES", [128, 128], BF16)
    IDENT = nc.alloc_sbuf_tensor("IDENT", [128, 128], BF16)
    MASK = nc.alloc_sbuf_tensor("MASK", [128, 128], BF16)
    SEL2 = nc.alloc_sbuf_tensor("SEL2", [2, 128], BF16)
    ONES1 = nc.alloc_sbuf_tensor("ONES1", [1, 128], BF16)
    IDENTF = nc.alloc_sbuf_tensor("IDENTF", [128, 128], F32)
    EPSR = nc.alloc_sbuf_tensor("EPSR", [1, 1024], BF16)
    ESEL = nc.alloc_sbuf_tensor("ESEL", [8, 1024], BF16)
    NM = nc.alloc_sbuf_tensor("NM", [128, 8], F32)
    LC = nc.alloc_sbuf_tensor("LC", [128, 32], F32)
    GST = nc.alloc_sbuf_tensor("GST", [128, 64], F32)
    JNK = nc.alloc_sbuf_tensor("JNK", [128, 4], F32)

    PA = nc.alloc_psum_tensor("PA", [128, 2048], F32)
    PB = nc.alloc_psum_tensor("PB", [128, 2048], F32)

    S = Sch()
    aof = AO.ap().bitcast(F32)

    def xs(dc, lo=0, hi=T):
        return X[:, dc * T + lo: dc * T + hi]

    def hs(dc, lo=0, hi=T):
        return HH[:, dc * T + lo: dc * T + hi]

    def aos(dc, lo=0, hi=T):
        return AO[:, dc * T + lo: dc * T + hi]

    tok_x = None
    tok_smalls = None
    for sb_t, dr in ((QKP, dqkp), (CC, dcc), (BIA, dbias), (IDENTF, didentf)):
        tok_smalls = S.d("y", "qy", lambda e, a=sb_t, b=dr: e.dma_start(out=a[:], in_=b[:]))
    tok_gatet = S.d("g", "qg", lambda e: e.dma_start(out=GATET[:], in_=dgatet[:]))
    tok_consts = None
    for sb_t, dr in ((ONES, dallones), (BONES, dblockones), (IDENT, dident),
                     (MASK, dmask), (SEL2, dsel2), (ESEL, desel), (B2R, db2r),
                     (ONES1, dones1), (EPSR, depsr)):
        tok_consts = S.d("g", "qg", lambda e, a=sb_t, b=dr: e.dma_start(out=a[:], in_=b[:]))

    def wload(dst_ap, dram_ap):
        return lambda e: e.dma_start(
            out=dst_ap.rearrange("p (c o) -> p c o", c=8),
            in_=dram_ap.rearrange("(c p) o -> p c o", p=128))

    R0 = WB[:, 0:8192]
    R1 = WB[:, 8192:16384]
    RW = WORK[:, 0:8192]
    tok_wq = S.d("g", "qg", wload(R0, dwq))
    tok_wk = S.d("g", "qg", wload(R1, dwk))

    def wv_piece(fc):
        pc = fc % 2
        return S.d("g", "qg", lambda e, fc=fc, pc=pc: e.dma_start(
            out=WVS[:, pc * 1024:(pc + 1) * 1024].rearrange("p (c o) -> p c o", c=8),
            in_=dwv.rearrange("(c p) o -> p c o", p=128)[:, :, fc * 128:(fc + 1) * 128]))

    tok_wv = [wv_piece(0), wv_piece(1)]

    S.w("s", "qy", tok_smalls)
    chk_scs = S.o("s", lambda e: e.activation(SCS[:], CC[:], AF.Silu), chk=True)

    NPC = 12
    PW = 6144 // NPC
    modp = PB[:, 2000:2048]
    ROWB = nc.alloc_sbuf_tensor("ROWB", [1, 512], F32)
    piece_tok = [None] * NPC
    piece_done = [None] * NPC
    ev_tok = None
    ev_toks = []
    tr_toks = []
    first_tr = True
    for p in range(NPC):
        if p >= 2:
            S.w("y", "pe", piece_done[p - 2])
        piece_tok[p] = S.d("y", "qy", lambda e, p=p: e.dma_start(
            out=aof[:, (p % 2) * 4096:(p % 2) * 4096 + 4096].rearrange(
                "p (c o) -> p c o", c=8),
            in_=dwmod.rearrange("(c p) o -> p c o", p=128)[:, :, p * PW:(p + 1) * PW]))
        if p == 3:
            tok_x = S.d("y", "qy", lambda e: e.dma_start(
                out=X[:].rearrange("p (c t) -> p c t", c=NDC),
                in_=dx.rearrange("(c p) t -> p c t", p=128)))
        S.w("pe", "qy", piece_tok[p])
        if p == 0:
            S.w("pe", "s", chk_scs)
        modrow = PB[0:1, (p % 2) * 512:(p % 2) * 512 + 512]
        if len(ev_toks) >= 2:
            S.w("pe", "v", ev_toks[-2])
        for kc in range(8):
            S.o("pe", lambda e, p=p, kc=kc, modrow=modrow: e.matmul(
                modrow, SCS[:, kc:kc + 1],
                aof[:, (p % 2) * 4096 + kc * PW: (p % 2) * 4096 + kc * PW + PW],
                start=(kc == 0), stop=(kc == 7)))
        cmm = S.chk("pe")
        piece_done[p] = cmm
        S.w("v", "pe", cmm)
        if tr_toks:
            S.w("v", "pe", tr_toks[-1])
        ev_tok = S.o("v", lambda e, modrow=modrow: e.tensor_copy(ROWB[:], modrow),
                     chk=True)
        ev_toks.append(ev_tok)
        S.w("pe", "v", ev_tok)
        for cb in range(4):
            col = p * 4 + cb
            S.o("pe", lambda e, cb=cb, col=col, first=first_tr: e.matmul(
                modp[:, col:col + 1], ROWB[0:1, cb * 128:(cb + 1) * 128],
                IDENTF[0:1, 0:1], start=first, stop=(col == 47),
                is_transpose=True))
            first_tr = False
        tr_toks.append(S.chk("pe"))
    piece_done[NPC - 1] = tr_toks[-1]
    chk_modmm = piece_done[-1]

    S.w("v", "pe", tr_toks[3])
    S.w("v", "qy", tok_smalls)

    def asm1(e):
        e.tensor_copy(PAR[:, 0:8], modp[:, 0:8])
        return e.tensor_scalar_add(PAR[:, 8:16], modp[:, 8:16], 1.0)
    chk_par1 = S.o("v", asm1, chk=True)
    S.w("v", "pe", chk_modmm)

    def asm(e):
        e.tensor_copy(PAR[:, 24:32], modp[:, 24:32])
        e.tensor_scalar_add(PAR[:, 32:40], modp[:, 32:40], 1.0)
        A, B = 0, 32
        for base, src in ((A, 16), (B, 40)):
            e.tensor_single_scalar(GST[:, base:base + 8], modp[:, src:src + 8],
                                   0.0, OP.is_ge)
        for base in (A, B):
            e.tensor_scalar_mul(GST[:, base + 8:base + 16], GST[:, base:base + 8], 2.0)
        for base in (A, B):
            e.tensor_single_scalar(GST[:, base + 8:base + 16],
                                   GST[:, base + 8:base + 16], 1.0, OP.subtract)
        for base, src in ((A, 16), (B, 40)):
            e.tensor_scalar_mul(GST[:, base + 16:base + 24], modp[:, src:src + 8], -1.0)
        for base in (A, B):
            e.tensor_scalar_max(GST[:, base + 16:base + 24],
                                GST[:, base + 16:base + 24], 0.25)
        for base, src in ((A, 16), (B, 40)):
            e.tensor_scalar_max(GST[:, base + 24:base + 32], modp[:, src:src + 8], 0.25)
        for base in (A, B):
            e.tensor_max(GST[:, base + 16:base + 24], GST[:, base + 16:base + 24],
                         GST[:, base + 24:base + 32])
        for base, dst in ((A, 16), (B, 40)):
            e.tensor_mul(PAR[:, dst:dst + 8], GST[:, base + 8:base + 16],
                         GST[:, base + 16:base + 24])
        e.tensor_mul(PAR[:, 48:56], PAR[:, 16:24], BIA[:, 144:152])
        if DEBUG:
            e.tensor_single_scalar(GST[:, 32:40], modp[:, 16:24], 0.0, OP.is_ge)
            e.tensor_scalar_mul(GST[:, 40:48], GST[:, 32:40], 2.0)
            e.tensor_single_scalar(GST[:, 40:48], GST[:, 40:48], 1.0, OP.subtract)
        return e.tensor_mul(PAR[:, 56:64], PAR[:, 40:48], BIA[:, 152:160])
    chk_par = S.o("v", asm, chk=True)
    S.w("s", "v", chk_par)

    ln_stats_free = [None]

    def layernorm(sc_col, bi_col, t_slice, h2f32_slice, h2_cb, pre_waits, lnscr):
        for eng in ("pe", "v", "s"):
            for et in pre_waits:
                S.we(eng, et)
        S.w("v", "qy", tok_x)
        S.w("s", "qy", tok_x)
        last_mm = None
        for dc in range(NDC):
            if last_mm is not None:
                S.w("v", "pe", last_mm)
                S.w("s", "pe", last_mm)
            ccp = S.o("v", lambda e, dc=dc: e.tensor_copy(SCA[:], xs(dc)), chk=True)
            csq = S.o("s", lambda e, dc=dc: e.activation(SCB[:], xs(dc), AF.Square),
                      chk=True)
            S.w("pe", "v", ccp)
            S.w("pe", "s", csq)
            S.w("pe", "qg", tok_consts)
            for tc in range(4):
                S.o("pe", lambda e, dc=dc, tc=tc: e.matmul(
                    PA[:, tc * 512:(tc + 1) * 512], ONES[:],
                    SCA[:, tc * 512:(tc + 1) * 512],
                    start=(dc == 0), stop=(dc == NDC - 1)))
                S.o("pe", lambda e, dc=dc, tc=tc: e.matmul(
                    PB[:, tc * 512:(tc + 1) * 512], ONES[:],
                    SCB[:, tc * 512:(tc + 1) * 512],
                    start=(dc == 0), stop=False))
            last_mm = S.chk("pe")
        for tc in range(4):
            S.o("pe", lambda e, tc=tc: e.matmul(
                PB[:, tc * 512:(tc + 1) * 512], ONES1[0:1, :],
                EPSR[0:1, 512:1024], start=False, stop=True))
        last_mm = S.chk("pe")
        S.w("v", "pe", last_mm)
        S.o("v", lambda e: e.tensor_scalar_mul(SCA[:], PA[:], 1.0 / D))
        ct1 = S.o("v", lambda e: e.tensor_tensor(SCB[:], SCA[:], PA[:], OP.mult),
                  chk=True)
        crstd = None
        cln_prev = None
        for h in (0, 1):
            if cln_prev is not None:
                S.w("v", "s", cln_prev)
            ct2 = S.o("v", lambda e, h=h: e.scalar_tensor_tensor(
                lnscr[:, 0:1024], SCB[:, h * 1024:(h + 1) * 1024], -1.0,
                PB[:, h * 1024:(h + 1) * 1024], OP.mult, OP.add), chk=True)
            S.w("s", "v", ct2)
            cln_prev = S.o("s", lambda e: e.activation(
                lnscr[:, 1024:2048], lnscr[:, 0:1024], AF.Ln), chk=True)
            crstd = S.o("s", lambda e, h=h: e.activation(
                SCB[:, h * 1024:(h + 1) * 1024], lnscr[:, 1024:2048], AF.Exp,
                scale=-0.5, bias=QKP[:, 5:6]), chk=True)
            ln_stats_free[0] = ("v", ct2)
        h_chks = []
        S.w("v", "s", crstd)
        cu = None
        prev_gate_tok = None
        for dc in range(NDC):
            if h_chks:
                S.w("v", "s", h_chks[-1][1] or h_chks[-1][0])
            S.o("v", lambda e, dc=dc: e.scalar_tensor_tensor(
                SCA[:], PA[:], -1.0 / D, xs(dc), OP.mult, OP.add))
            cu = S.o("v", lambda e, t=t_slice: e.tensor_tensor(t, SCA[:], SCB[:],
                                                               OP.mult), chk=True)
            S.w("s", "v", cu)
            ch = S.o("s", lambda e, dc=dc, t=t_slice: e.activation(
                hs(dc), t, AF.Identity, bias=PAR[:, bi_col + dc:bi_col + dc + 1],
                scale=PAR[:, sc_col + dc:sc_col + dc + 1]), chk=True)
            ch2 = None
            if h2f32_slice is not None:
                if prev_gate_tok is not None:
                    S.w("s", "pe", prev_gate_tok)
                ch2 = S.o("s", lambda e, dc=dc, t=t_slice: e.activation(
                    h2f32_slice, t, AF.Identity, bias=PAR[:, bi_col + dc:bi_col + dc + 1],
                    scale=PAR[:, sc_col + dc:sc_col + dc + 1]), chk=True)
                prev_gate_tok = h2_cb(dc, ch2)
            h_chks.append((ch, ch2, dc))
        return h_chks, cu

    if DEBUG:
        S.w("y", "v", chk_par)
        S.d("y", "qy", lambda e: e.dma_start(out=dbg_par, in_=PAR[:]))
        S.d("y", "qy", lambda e: e.dma_start(out=dbg_isge, in_=GST[:, 32:48]))

    workf = WORK.ap().bitcast(F32)

    def ln1_halved():
        chks = [[None, None] for _ in range(NDC)]
        S.w("v", "qy", tok_x)
        S.w("s", "qy", tok_x)
        S.w("s", "v", chk_par1)
        last_mm = None
        last_u = None
        prev_chs = [None, None]
        for th in (0, 1):
            tlo = th * 1024
            for dc in range(NDC):
                if last_mm is not None:
                    S.w("v", "pe", last_mm)
                    S.w("s", "pe", last_mm)
                ccp = S.o("v", lambda e, dc=dc, tlo=tlo: e.tensor_copy(
                    SCA[:, 0:1024], xs(dc, tlo, tlo + 1024)), chk=True)
                csq = S.o("s", lambda e, dc=dc, tlo=tlo: e.activation(
                    SCB[:, 0:1024], xs(dc, tlo, tlo + 1024), AF.Square), chk=True)
                S.w("pe", "v", ccp)
                S.w("pe", "s", csq)
                S.w("pe", "qg", tok_consts)
                if th == 1 and dc == 0:
                    S.w("pe", "v", last_u)
                for tc in (0, 1):
                    S.o("pe", lambda e, tc=tc, dc=dc: e.matmul(
                        PA[:, tc * 512:(tc + 1) * 512], ONES[:],
                        SCA[:, tc * 512:(tc + 1) * 512],
                        start=(dc == 0), stop=(dc == NDC - 1)))
                    S.o("pe", lambda e, tc=tc, dc=dc: e.matmul(
                        PA[:, 1024 + tc * 512:1024 + (tc + 1) * 512], ONES[:],
                        SCB[:, tc * 512:(tc + 1) * 512],
                        start=(dc == 0), stop=False))
                last_mm = S.chk("pe")
            for tc in (0, 1):
                S.o("pe", lambda e, tc=tc: e.matmul(
                    PA[:, 1024 + tc * 512:1024 + (tc + 1) * 512], ONES1[0:1, :],
                    EPSR[0:1, 512:1024], start=False, stop=True))
            last_mm = S.chk("pe")
            S.w("v", "pe", last_mm)
            S.o("v", lambda e: e.tensor_scalar_mul(SCA[:, 0:1024], PA[:, 0:1024],
                                                   1.0 / D))
            S.o("v", lambda e: e.tensor_tensor(SCB[:, 0:1024], SCA[:, 0:1024],
                                               PA[:, 0:1024], OP.mult))
            ct2 = S.o("v", lambda e: e.scalar_tensor_tensor(
                workf[:, 1024:2048], SCB[:, 0:1024], -1.0, PA[:, 1024:2048],
                OP.mult, OP.add), chk=True)
            S.w("s", "v", ct2)
            S.o("s", lambda e: e.activation(workf[:, 2048:3072], workf[:, 1024:2048],
                                            AF.Ln))
            crstd = S.o("s", lambda e: e.activation(
                SCB[:, 0:1024], workf[:, 2048:3072], AF.Exp,
                scale=-0.5, bias=QKP[:, 5:6]), chk=True)
            S.w("v", "s", crstd)
            for dc in range(NDC):
                b = (dc % 2) * 1024
                if prev_chs[dc % 2] is not None:
                    S.w("v", "s", prev_chs[dc % 2])
                S.o("v", lambda e, dc=dc, tlo=tlo, b=b: e.scalar_tensor_tensor(
                    SCA[:, b:b + 1024], PA[:, 0:1024], -1.0 / D,
                    xs(dc, tlo, tlo + 1024), OP.mult, OP.add))
                cu = S.o("v", lambda e, b=b: e.tensor_tensor(
                    WORK[:, b:b + 1024], SCA[:, b:b + 1024], SCB[:, 0:1024],
                    OP.mult), chk=True)
                last_u = cu
                S.w("s", "v", cu)
                prev_chs[dc % 2] = S.o("s", lambda e, dc=dc, tlo=tlo, b=b:
                    e.activation(
                        hs(dc, tlo, tlo + 1024), WORK[:, b:b + 1024], AF.Identity,
                        bias=PAR[:, 0 + dc:0 + dc + 1],
                        scale=PAR[:, 8 + dc:8 + dc + 1]), chk=True)
                chks[dc][th] = prev_chs[dc % 2]
        return chks, last_u

    ln1_hchks, ln1_pa_free = ln1_halved()

    QR, KR, VR, VT = 0, 2048, 4096, 6144
    TRA = PB[:, 1024:1536].bitcast(BF16)
    TRB = PB[:, 1536:2048].bitcast(BF16)
    AVP = PB[:, 1024:1152]
    AVP2 = PB[:, 1536:1664]
    trfree = [None, None]

    halves = [(PB, 0), (PA, 0), (PA, 1024)]
    half_free = [None, None, None]
    hctr = [0]
    tr_last = [None]
    av_last = [None]
    norm_tok = [None]
    ctm_k = [None]
    prev_cbc = [None]
    prev_ctl = [None]
    scb_pe_free = [None]
    kp1_free = [None]
    ao_guard = [False]
    last_q_mm = [None]
    last_k_mm = [None]

    ckp1z = S.o("v", lambda e: e.memset(KP1.ap()[0:64, 0:2048], 0.0), chk=True)

    for fc in range(NFC):
        chk_evac = {}
        for mat, psum, wreg, wtok, dst in (("q", PA, R0, tok_wq, QR),
                                           ("k", PB, R1, tok_wk, KR)):
            S.w("pe", "qg", wtok)
            if fc == 0:
                S.w("pe", "v", ln1_pa_free)
            else:
                if mat == "q":
                    S.we("pe", half_free[1])
                    S.we("pe", half_free[2])
                else:
                    S.we("pe", half_free[0])
            for kc in range(8):
                if fc == 0:
                    S.w("pe", "s", ln1_hchks[kc][0])
                    S.w("pe", "s", ln1_hchks[kc][1])
                for tc in range(4):
                    S.o("pe", lambda e, psum=psum, wreg=wreg, tc=tc, kc=kc, fc=fc:
                        e.matmul(
                            psum[:, tc * 512:(tc + 1) * 512],
                            wreg[:, kc * 1024 + fc * 128: kc * 1024 + fc * 128 + 128],
                            hs(kc, tc * 512, (tc + 1) * 512),
                            start=(kc == 0), stop=(kc == 7)))
            cm = S.chk("pe")
            if fc == NFC - 1:
                (last_q_mm if mat == "q" else last_k_mm)[0] = cm
            S.w("v", "pe", cm)
            chk_evac[mat] = S.o("v", lambda e, psum=psum, dst=dst: e.tensor_copy(
                WORK[:, dst:dst + 2048], psum[:]), chk=True)
        S.w("pe", "v", chk_evac["q"])
        S.w("pe", "qg", tok_wv[fc])
        pc = fc % 2
        for kc in range(8):
            for tc in range(4):
                S.o("pe", lambda e, tc=tc, kc=kc, pc=pc: e.matmul(
                    PA[:, tc * 512:(tc + 1) * 512],
                    WVS[:, pc * 1024 + kc * 128: pc * 1024 + kc * 128 + 128],
                    hs(kc, tc * 512, (tc + 1) * 512),
                    start=(kc == 0), stop=(kc == 7)))
        cmv = S.chk("pe")
        if fc + 2 < NFC:
            S.w("g", "pe", cmv)
            tok_wv.append(wv_piece(fc + 2))
        S.w("v", "pe", cmv)
        cev0 = S.o("v", lambda e: e.tensor_copy(
            WORK[:, VR:VR + 1024], PA[:, 0:1024]), chk=True)
        chk_evac["v"] = S.o("v", lambda e: e.tensor_copy(
            WORK[:, VR + 1024:VR + 2048], PA[:, 1024:2048]), chk=True)

        vt_half = [None, None]
        vt_done = None
        for rnd in range(2):
            trv = TRA if rnd == 0 else TRB
            S.w("pe", "v", cev0 if rnd == 0 else chk_evac["v"])
            S.we("pe", trfree[rnd])
            for j8 in range(8):
                j = rnd * 8 + j8
                S.o("pe", lambda e, j=j, j8=j8, trv=trv: e.matmul(
                    trv[:, j8 * 128:(j8 + 1) * 128],
                    WORK[:, VR + j * 128: VR + (j + 1) * 128], IDENT[:],
                    start=(j8 == 0), stop=(j8 == 7), is_transpose=True))
            ct = S.chk("pe")
            vt_done = ct
            S.w("v", "pe", ct)
            tok = ("v", S.o("v", lambda e, rnd=rnd, trv=trv: e.tensor_copy(
                WORK[:, VT + rnd * 1024: VT + (rnd + 1) * 1024], trv[:]), chk=True))
            trfree[rnd] = tok
            vt_half[rnd] = tok

        pa_free = ("v", chk_evac["v"])
        pb_free = ("v", chk_evac["k"])
        qraw = WORK[:, QR:QR + 2048]
        kraw = WORK[:, KR:KR + 2048]
        scaf = SCA.ap().bitcast(F32)
        S.we("pe", pa_free)
        for tc in range(4):
            S.o("pe", lambda e, tc=tc: e.matmul(
                PA[:, tc * 512:(tc + 1) * 512], BONES[:],
                qraw[:, tc * 512:(tc + 1) * 512], start=True, stop=True))
        csq_ = S.chk("pe")
        S.we("pe", pb_free)
        S.we("pe", vt_half[0])
        S.we("pe", vt_half[1])
        for tc in range(4):
            S.o("pe", lambda e, tc=tc: e.matmul(
                PB[:, tc * 512:(tc + 1) * 512], BONES[:],
                kraw[:, tc * 512:(tc + 1) * 512], start=True, stop=True))
        csk_ = S.chk("pe")
        S.w("v", "pe", csq_)
        cuq = S.o("v", lambda e: e.scalar_tensor_tensor(
            qraw, PA[:], -1.0 / DH, qraw, OP.mult, OP.add), chk=True)
        S.w("v", "pe", csk_)
        cuk = S.o("v", lambda e: e.scalar_tensor_tensor(
            kraw, PB[:], -1.0 / DH, kraw, OP.mult, OP.add), chk=True)
        S.w("s", "v", cuq)
        S.we("s", prev_cbc[0])
        cxq = S.o("s", lambda e: e.activation(SCA[:], qraw, AF.Square), chk=True)
        S.w("s", "v", cuk)
        S.we("s", prev_ctl[0])
        S.we("s", scb_pe_free[0])
        cxk = S.o("s", lambda e: e.activation(SCB[:], kraw, AF.Square), chk=True)
        S.w("pe", "s", cxq)
        S.w("pe", "v", cuq)
        for tc in range(4):
            S.o("pe", lambda e, tc=tc: e.matmul(
                PA[:, tc * 512:(tc + 1) * 512], BONES[:],
                SCA[:, tc * 512:(tc + 1) * 512], start=True, stop=False))
            S.o("pe", lambda e, tc=tc: e.matmul(
                PA[:, tc * 512:(tc + 1) * 512], ONES1[0:1, :],
                EPSR[0:1, 0:512], start=False, stop=True))
        cvq = S.chk("pe")
        S.w("pe", "s", cxk)
        S.w("pe", "v", cuk)
        S.we("pe", vt_half[0])
        S.we("pe", vt_half[1])
        for tc in range(4):
            S.o("pe", lambda e, tc=tc: e.matmul(
                PB[:, tc * 512:(tc + 1) * 512], BONES[:],
                SCB[:, tc * 512:(tc + 1) * 512], start=True, stop=False))
            S.o("pe", lambda e, tc=tc: e.matmul(
                PB[:, tc * 512:(tc + 1) * 512], ONES1[0:1, :],
                EPSR[0:1, 0:512], start=False, stop=True))
        cvk = S.chk("pe")
        qact_half = [None, None]
        kact_half = [None, None]
        kz_half = [None, None]
        scb_user = None
        ln_k = [None, None]
        ln_q = [None, None]
        S.w("s", "pe", cvk)
        for h in (0, 1):
            lo, hi = h * 1024, (h + 1) * 1024
            ln_k[h] = S.o("s", lambda e, lo=lo, hi=hi: e.activation(
                scaf[:, 0:1024], PB[:, lo:hi], AF.Ln), chk=True)
            ce = S.o("s", lambda e, lo=lo, hi=hi: e.activation(
                SCB[:, lo:hi], scaf[:, 0:1024], AF.Exp,
                scale=-0.5, bias=QKP[:, 4:5]), chk=True)
            S.w("v", "s", ce)
            ctmk = S.o("v", lambda e, lo=lo, hi=hi: e.tensor_tensor(
                WORK[:, KR + lo:KR + hi], WORK[:, KR + lo:KR + hi],
                SCB[:, lo:hi], OP.mult), chk=True)
            if h == 1:
                ctm_k[0] = ctmk
            S.w("s", "v", ctmk)
            S.w("s", "v", ckp1z)
            S.we("s", kp1_free[0])
            ck1 = S.o("s", lambda e, lo=lo, hi=hi: e.activation(
                KP1.ap()[64:128, lo:hi], WORK.ap()[64:128, KR + lo:KR + hi],
                AF.Identity, bias=QKP.ap()[64:128, 3:4],
                scale=QKP.ap()[64:128, 2:3]), chk=True)
            ck0 = S.o("s", lambda e, lo=lo, hi=hi: e.activation(
                WORK.ap()[0:64, KR + lo:KR + hi], WORK.ap()[0:64, KR + lo:KR + hi],
                AF.Identity, bias=QKP.ap()[0:64, 3:4],
                scale=QKP.ap()[0:64, 2:3]), chk=True)
            kact_half[h] = ("s", ck0)
            S.w("v", "s", ck1)
            kz_half[h] = ("v", S.o("v", lambda e, lo=lo, hi=hi: e.memset(
                WORK.ap()[64:128, KR + lo:KR + hi], 0.0), chk=True))
            ln_q[h] = S.o("s", lambda e, lo=lo, hi=hi: e.activation(
                scaf[:, 0:1024], PA[:, lo:hi], AF.Ln), chk=True)
            ce = S.o("s", lambda e, lo=lo, hi=hi: e.activation(
                SCB[:, lo:hi], scaf[:, 0:1024], AF.Exp,
                scale=-0.5, bias=QKP[:, 4:5]), chk=True)
            S.w("v", "s", ce)
            ctmq = S.o("v", lambda e, lo=lo, hi=hi: e.tensor_tensor(
                WORK[:, QR + lo:QR + hi], WORK[:, QR + lo:QR + hi],
                SCB[:, lo:hi], OP.mult), chk=True)
            scb_user = ("v", ctmq)
            S.w("s", "v", ctmq)
            cq = S.o("s", lambda e, lo=lo, hi=hi: e.activation(
                WORK[:, QR + lo:QR + hi], WORK[:, QR + lo:QR + hi], AF.Identity,
                bias=QKP[:, 1:2], scale=QKP[:, 0:1]), chk=True)
            qact_half[h] = ("s", cq)
        half_free[0] = ("s", ln_k[0])
        half_free[1] = ("s", ln_q[0])
        half_free[2] = ("s", ln_q[1])
        hctr[0] += (3 - hctr[0] % 3) % 3
        PRB = VR

        if DEBUG and fc == 0:
            S.we("y", qact_half[1])
            S.we("y", kact_half[1])
            S.we("y", tr_last[0])
            _tok = S.d("y", "qy", lambda e: e.dma_start(out=dbg_qkv, in_=WORK[:]))
            S.w("s", "qy", _tok)

        NT = 32
        pbufs = [WORK[:, VR:VR + 2048], SCB[:, 0:2048], SCA[:, 0:2048]]
        probs_free = [("pe", vt_done), scb_user, None]
        lc_free = [None, None]
        st = [dict() for _ in range(NT)]

        def emit_A(t):
            i, hh = divmod(t, 2)
            qoff = 64 * hh
            kv = (i + 1) * 128
            nh = 1 if kv <= 1024 else 2
            par = t % 2
            hf = i // 8
            S.we("pe", qact_half[hf])
            for h2 in range(hf + 1):
                S.we("pe", kact_half[h2])
                S.we("pe", kz_half[h2])
            hids = []
            csc_h = []
            for hx in range(nh):
                hid = hctr[0] % 3
                hctr[0] += 1
                hids.append(hid)
                S.we("pe", half_free[hid])
                psum, poff = halves[hid]
                lo = hx * 1024
                hi = min(kv, lo + 1024)
                off = lo
                while off < hi:
                    n = min(512, hi - off)
                    kop = (WORK[:, KR + off: KR + off + n] if hh == 0
                           else KP1[:, off:off + n])
                    S.o("pe", lambda e, psum=psum, poff=poff, off=off, lo=lo,
                        n=n, i=i, kop=kop: e.matmul(
                        psum[:, poff + off - lo: poff + off - lo + n],
                        WORK[:, QR + i * 128: QR + (i + 1) * 128],
                        kop, start=True, stop=True))
                    off += n
                csc_h.append(S.chk("pe"))
            st[t].update(csc=csc_h[-1], csc_h=csc_h, hids=hids, nh=nh, hh=hh, i=i)

        def emit_B(t):
            i, hh = st[t]["i"], st[t]["hh"]
            kv = (i + 1) * 128
            nh = st[t]["nh"]
            hids = st[t]["hids"]
            par = t % 2
            csc_h = st[t]["csc_h"]
            if t >= 2:
                S.w("v", "s", st[t - 2]["exp"])
            if nh == 2:
                psum, poff = halves[hids[0]]
                S.w("v", "pe", csc_h[0])
                S.o("v", lambda e, psum=psum, poff=poff, par=par: e.reduce_max(
                    NM[:, 4 * par:4 * par + 1], psum[:, poff:poff + 1024],
                    axis=AX.X, negate=True))
            S.w("v", "pe", csc_h[-1])
            dpsum, dpoff = halves[hids[-1]]
            drel = (i * 128) % 1024
            S.o("v", lambda e, dpsum=dpsum, dpoff=dpoff, drel=drel: e.tensor_tensor(
                dpsum[:, dpoff + drel: dpoff + drel + 128],
                dpsum[:, dpoff + drel: dpoff + drel + 128], MASK[:], OP.add))
            hx = nh - 1
            psum, poff = halves[hids[-1]]
            w = min(kv - hx * 1024, 1024)
            S.o("v", lambda e, psum=psum, poff=poff, w=w, hx=hx, par=par:
                e.reduce_max(
                    NM[:, 4 * par + hx:4 * par + hx + 1],
                    psum[:, poff:poff + w], axis=AX.X, negate=True))
            S.o("v", lambda e: e.tensor_copy(JNK[:, 0:2], QKP[:, 0:2]))
            if nh == 2:
                cnm = S.o("v", lambda e, par=par: e.tensor_tensor(
                    NM[:, 4 * par:4 * par + 1], NM[:, 4 * par:4 * par + 1],
                    NM[:, 4 * par + 1:4 * par + 2], OP.min), chk=True)
            else:
                cnm = S.chk("v")
            st[t]["cnm"] = cnm

        def emit_C(t):
            i, hh = st[t]["i"], st[t]["hh"]
            kv = (i + 1) * 128
            nh = st[t]["nh"]
            hids = st[t]["hids"]
            par = t % 2
            pbi = t % 3
            cnm = st[t]["cnm"]
            S.w("s", "v", cnm)
            S.we("s", probs_free[pbi])
            S.we("s", lc_free[par])
            chk_exp = None
            for hx, hid in enumerate(hids):
                psum, poff = halves[hid]
                w = min(kv - hx * 1024, 1024)
                chk_exp = S.o("s", lambda e, psum=psum, poff=poff, w=w, hx=hx,
                              par=par, pbi=pbi: e.activation(
                                  pbufs[pbi][:, hx * 1024: hx * 1024 + w],
                                  psum[:, poff:poff + w], AF.Exp,
                                  bias=NM[:, 4 * par:4 * par + 1],
                                  accum_out=NM[:, 4 * par + 2 + hx:4 * par + 3 + hx]),
                              chk=True)
                half_free[hid] = ("s", chk_exp)
            st[t]["exp"] = chk_exp

        def emit_D(tq, rnd):
            hh1, i1 = st[tq]["hh"], st[tq]["i"]
            prb = pbufs[tq % 3]
            ntr = i1 + 1
            nrnd = (ntr + 7) // 8
            if rnd >= nrnd:
                return
            if rnd == 0:
                S.w("pe", "s", st[tq]["exp"])
                S.we("pe", trfree[0])
                st[tq]["tr_evac"] = [None, None]
            else:
                S.we("pe", st[tq]["tr_evac"][0])
            lo8, hi8 = rnd * 8, min(ntr, rnd * 8 + 8)
            for j in range(lo8, hi8):
                S.o("pe", lambda e, j=j, lo8=lo8, hi8=hi8, prb=prb:
                    e.matmul(
                        TRA[:, (j - lo8) * 128:(j - lo8 + 1) * 128],
                        prb[:, j * 128:(j + 1) * 128], IDENT[:],
                        start=(j == lo8), stop=(j == hi8 - 1),
                        is_transpose=True))
            ctr = S.chk("pe")
            S.w("s", "pe", ctr)
            tok = ("s", S.o("s", lambda e, lo8=lo8, hi8=hi8, prb=prb: e.copy(
                prb[:, lo8 * 128:hi8 * 128],
                TRA[:, 0:(hi8 - lo8) * 128]), chk=True))
            st[tq]["tr_evac"][rnd] = tok
            if rnd == nrnd - 1:
                trfree[0] = tok

        def emit_EF(tq):
            hh1, i1 = st[tq]["hh"], st[tq]["i"]
            qoff1 = 64 * hh1
            par1 = tq % 2
            pbi = tq % 3
            prb = pbufs[pbi]
            ntr = i1 + 1
            nrnd = (ntr + 7) // 8
            avp = AVP2
            tr_evac = st[tq]["tr_evac"]
            S.we("pe", av_last[0])
            S.we("pe", vt_half[1])
            tp = (0, 64) if hh1 == 1 else None
            for rnd in range(nrnd):
                S.we("pe", tr_evac[rnd])
                lo8, hi8 = rnd * 8, min(ntr, rnd * 8 + 8)
                for j in range(lo8, hi8):
                    S.o("pe", lambda e, j=j, i1=i1, qoff1=qoff1, tp=tp, avp=avp,
                        prb=prb: e.matmul(
                            avp[qoff1:qoff1 + 64, 0:128],
                            WORK[:, VT + j * 128 + qoff1: VT + j * 128 + qoff1 + 64],
                            prb[:, j * 128:(j + 1) * 128],
                            start=(j == 0), stop=(j == i1), tile_position=tp))
            cav = S.chk("pe")
            st[tq]["cav"] = cav
            probs_free[pbi] = ("pe", cav)
            S.w("v", "s", st[tq]["exp"])
            if st[tq]["nh"] == 2:
                clc = S.o("v", lambda e, i1=i1, hh1=hh1, par1=par1: e.tensor_tensor(
                    LC[:, 2 * i1 + hh1: 2 * i1 + hh1 + 1],
                    NM[:, 4 * par1 + 2:4 * par1 + 3],
                    NM[:, 4 * par1 + 3:4 * par1 + 4], OP.add), chk=True)
            else:
                clc = S.o("v", lambda e, i1=i1, hh1=hh1, par1=par1: e.tensor_copy(
                    LC[:, 2 * i1 + hh1: 2 * i1 + hh1 + 1],
                    NM[:, 4 * par1 + 2:4 * par1 + 3]), chk=True)
            lc_free[par1] = ("v", clc)
            S.w("v", "pe", cav)
            av_last[0] = ("v", S.o("v", lambda e, fc=fc, i1=i1, qoff1=qoff1,
                                   avp=avp: e.tensor_copy(
                                       AO.ap()[qoff1:qoff1 + 64,
                                               fc * T + i1 * 128: fc * T + (i1 + 1) * 128],
                                       avp[qoff1:qoff1 + 64, 0:128]), chk=True))

        emit_A(0)
        emit_B(0)
        emit_C(0)
        for t in range(NT):
            if t + 1 < NT:
                emit_A(t + 1)
                emit_B(t + 1)
            emit_D(t, 0)
            if t >= 1:
                emit_EF(t - 1)
            if t + 1 < NT:
                emit_C(t + 1)
            emit_D(t, 1)
        emit_EF(NT - 1)
        scb_pe_free[0] = probs_free[1]
        kp1_free[0] = ("pe", st[NT - 1]["csc"])

        S.o("v", lambda e: e.reciprocal(LC[:], LC[:]))
        S.o("v", lambda e: e.tensor_copy(JNK[:, 0:2], QKP[:, 0:2]))
        clb = S.o("v", lambda e: e.tensor_copy(SCB[:, 0:32], LC[:]), chk=True)
        S.w("pe", "v", clb)
        S.we("pe", av_last[0])
        S.we("pe", half_free[0])
        S.we("pe", trfree[0])
        S.we("pe", trfree[1])
        pbbf = PB.ap().bitcast(BF16)
        for i in range(16):
            S.o("pe", lambda e, i=i, pbbf=pbbf: e.matmul(
                pbbf[0:2, i * 128:(i + 1) * 128], SCB[:, 2 * i:2 * i + 2], IDENT[:],
                start=(i % 8 == 0), stop=(i % 8 == 7), is_transpose=True))
        ctl = S.chk("pe")
        S.w("v", "pe", ctl)
        cle = S.o("v", lambda e, pbbf=pbbf: e.tensor_copy(
            SCA.ap()[0:2, 0:2048], pbbf[0:2, 0:2048]), chk=True)
        S.w("pe", "v", cle)
        for t16 in range(16):
            S.o("pe", lambda e, t16=t16: e.matmul(
                PB[:, t16 * 128:(t16 + 1) * 128], SEL2[:],
                SCA.ap()[0:2, t16 * 128:(t16 + 1) * 128],
                start=(t16 % 4 == 0), stop=(t16 % 4 == 3)))
        cbc = S.chk("pe")
        S.w("v", "pe", cbc)
        cnorm = S.o("v", lambda e, fc=fc: e.tensor_tensor(
            aos(fc), aos(fc), PB[:], OP.mult), chk=True)
        norm_tok[0] = ("v", cnorm)
        half_free[0] = ("v", cnorm)
        trfree[0] = ("v", cnorm)
        trfree[1] = ("v", cnorm)
        prev_cbc[0] = ("pe", cbc)
        prev_ctl[0] = ("pe", ctl)

    S.w("g", "pe", last_q_mm[0])
    tok_wo = S.d("g", "qg", wload(R0, dwo))

    S.w("pe", "qg", tok_wo)
    S.we("pe", norm_tok[0])
    wo_cadd = []
    wo_cts = []
    last_wo_mm = None
    wo_tbufs = [SCA, SCB, None]
    for dc in range(NDC):
        psum = PA if dc % 2 == 0 else PB
        if dc >= 2:
            S.w("pe", "s", wo_cts[dc - 2])
        for kc in range(8):
            for tc in range(4):
                S.o("pe", lambda e, psum=psum, dc=dc, tc=tc, kc=kc: e.matmul(
                    psum[:, tc * 512:(tc + 1) * 512],
                    R0[:, kc * 1024 + dc * 128: kc * 1024 + dc * 128 + 128],
                    aos(kc, tc * 512, (tc + 1) * 512),
                    start=(kc == 0), stop=(kc == 7)))
        cwo = S.chk("pe")
        last_wo_mm = cwo
        S.w("s", "pe", cwo)
        if dc >= 3:
            S.w("s", "v", wo_cadd[dc - 3])
        tbuf = [SCA[:, 0:2048], SCB[:, 0:2048], WORK[:, 0:2048]][dc % 3]
        cts = S.o("s", lambda e, psum=psum, dc=dc, tbuf=tbuf: e.activation(
            tbuf, psum[:], AF.Identity,
            bias=PAR[:, 48 + dc:48 + dc + 1],
            scale=PAR[:, 16 + dc:16 + dc + 1]), chk=True)
        wo_cts.append(cts)
        S.w("v", "s", cts)
        cadd = S.o("v", lambda e, dc=dc, tbuf=tbuf: e.tensor_tensor(
            xs(dc), xs(dc), tbuf, OP.add), chk=True)
        wo_cadd.append(cadd)
    wo_done = wo_cadd[-1]

    if DEBUG:
        S.we("y", norm_tok[0])
        _tok = S.d("y", "qy", lambda e: e.dma_start(out=dbg_ao, in_=AO[:]))
        S.w("v", "qy", _tok)
        S.w("s", "qy", _tok)
        S.w("y", "v", wo_done)
        _tok = S.d("y", "qy", lambda e: e.dma_start(out=dbg_x1, in_=X[:]))
        S.w("v", "qy", _tok)

    ln2_chks, ln2_pa_free = layernorm(32, 24, AO[:, 0:2048], None, None,
                                      pre_waits=[("v", wo_done),
                                                 ("s", wo_cts[-1])],
                                      lnscr=aof[:, 3072:5120])

    S.we("pe", ln_stats_free[0])
    for dc in range(NDC):
        S.w("pe", "s", ln2_chks[dc][0])
        S.w("pe", "qg", tok_gatet)
        for tc in range(4):
            S.o("pe", lambda e, dc=dc, tc=tc: e.matmul(
                PB[0:8, tc * 512:(tc + 1) * 512],
                GATET[:, dc * 8:(dc + 1) * 8],
                hs(dc, tc * 512, (tc + 1) * 512),
                start=(dc == 0), stop=(dc == NDC - 1)))
    cgm = S.chk("pe")
    S.w("v", "pe", cgm)
    cgev = S.o("v", lambda e: e.tensor_copy(SCA.ap()[0:8, 0:2048], PB[0:8, 0:2048]),
               chk=True)
    S.w("pe", "v", cgev)
    S.w("pe", "v", ln2_pa_free)
    pabf = PA.ap().bitcast(BF16)
    for t16 in range(16):
        S.o("pe", lambda e, t16=t16, pabf=pabf: e.matmul(
            pabf[0:128, t16 * 8:(t16 + 1) * 8],
            SCA.ap()[0:8, t16 * 128:(t16 + 1) * 128], IDENT.ap()[0:8, 0:8],
            start=(t16 == 0), stop=(t16 == 15), is_transpose=True))
    gate_done = S.chk("pe")

    sbf = SCB.ap().bitcast(F32)
    gex = sbf[:, 128:256]
    g3 = pabf[0:128, 0:128].rearrange("p (g k) -> p g k", k=8)
    ge3 = gex.rearrange("p (g k) -> p g k", k=8)

    def bc16(ap):
        return ap.rearrange("p (g o) -> p g o", o=1).broadcast_to([128, 16, 8])

    S.w("v", "pe", gate_done)
    S.o("v", lambda e: e.reduce_max(
        GST[:, 0:16].rearrange("p (g o) -> p g o", o=1), g3, axis=AX.X))
    S.o("v", lambda e: e.tensor_copy(JNK[:, 0:2], QKP[:, 0:2]))
    cgs = S.o("v", lambda e: e.tensor_tensor(ge3, g3, bc16(GST[:, 0:16]), OP.subtract),
              chk=True)
    S.w("s", "v", cgs)
    cge = S.o("s", lambda e: e.activation(gex, gex, AF.Exp), chk=True)
    S.w("v", "s", cge)
    S.o("v", lambda e: e.reduce_sum(
        GST[:, 16:32].rearrange("p (g o) -> p g o", o=1), ge3, axis=AX.X))
    S.o("v", lambda e: e.reduce_max(
        GST[:, 0:16].rearrange("p (g o) -> p g o", o=1), ge3, axis=AX.X))
    S.o("v", lambda e: e.tensor_copy(JNK[:, 0:2], QKP[:, 0:2]))
    S.o("v", lambda e: e.tensor_tensor(
        sbf[:, 256:384].rearrange("p (g k) -> p g k", k=8), ge3, bc16(GST[:, 0:16]),
        OP.is_ge))
    S.o("v", lambda e: e.scalar_tensor_tensor(
        sbf[:, 256:384], sbf[:, 256:384], -1e30, gex, OP.mult, OP.add))
    S.o("v", lambda e: e.reduce_max(
        GST[:, 32:48].rearrange("p (g o) -> p g o", o=1),
        sbf[:, 256:384].rearrange("p (g k) -> p g k", k=8), axis=AX.X))
    S.o("v", lambda e: e.tensor_copy(JNK[:, 0:2], QKP[:, 0:2]))
    S.o("v", lambda e: e.tensor_tensor(
        sbf[:, 256:384].rearrange("p (g k) -> p g k", k=8), ge3, bc16(GST[:, 32:48]),
        OP.is_ge))
    S.o("v", lambda e: e.tensor_tensor(gex, gex, sbf[:, 256:384], OP.mult))
    S.o("v", lambda e: e.reciprocal(GST[:, 48:64], GST[:, 16:32]))
    S.o("v", lambda e: e.tensor_copy(JNK[:, 0:2], QKP[:, 0:2]))
    S.o("v", lambda e: e.tensor_tensor(ge3, ge3, bc16(GST[:, 48:64]), OP.mult))
    ccb = S.o("v", lambda e: e.tensor_copy(SCA[:, 0:128], gex), chk=True)
    S.w("pe", "v", ccb)
    for t16 in range(16):
        S.o("pe", lambda e, t16=t16, pabf=pabf: e.matmul(
            pabf[0:8, t16 * 128:(t16 + 1) * 128], SCA[:, t16 * 8:(t16 + 1) * 8],
            IDENT[:], start=(t16 % 8 == 0), stop=(t16 % 8 == 7), is_transpose=True))
    ctc = S.chk("pe")
    S.w("v", "pe", ctc)
    crow_done = S.o("v", lambda e, pabf=pabf: e.tensor_copy(
        CROW[:], pabf[0:8, 0:2048]), chk=True)

    regs = [R1, RW, R0]
    reg_free = [("pe", last_k_mm[0]), ("v", wo_cadd[5]), ("s", wo_cts[-1])]
    wtoks = {}
    drams = {"w1": dw1, "w3": dw3, "w2": dw2}
    sdrams = {"w1": dsw1, "w3": dsw3, "w2": dsw2}

    def sched_wload(m_idx, e_idx, mat):
        ridx = (3 * e_idx + m_idx) % 3
        reg = regs[ridx]
        S.we("g", reg_free[ridx])
        if e_idx < NE:
            src = drams[mat].rearrange("e (c p) o -> e p c o", p=128)[e_idx]
        else:
            src = sdrams[mat].rearrange("(c p) o -> p c o", p=128)
        wtoks[(e_idx, mat)] = S.d("g", "qg", lambda e, reg=reg, src=src: e.dma_start(
            out=reg.rearrange("p (c o) -> p c o", c=8), in_=src))

    for m_idx, mat in enumerate(("w1", "w3", "w2")):
        sched_wload(m_idx, 0, mat)

    pa_free_ffn = ("v", crow_done)
    pb_free_ffn = ("v", cgev)
    he_read = ("s", ln2_chks[-1][0])
    prev_csil = None
    cw2 = None
    stts = []

    for e_idx in range(NE + 1):
        is_sh = e_idx == NE
        rW1 = regs[(3 * e_idx + 0) % 3]
        rW3 = regs[(3 * e_idx + 1) % 3]
        rW2 = regs[(3 * e_idx + 2) % 3]
        if not is_sh:
            S.w("pe", "v", crow_done)
            S.we("pe", pa_free_ffn)
            for tc in range(4):
                S.o("pe", lambda e, e_idx=e_idx, tc=tc: e.matmul(
                    PA[:, tc * 512:(tc + 1) * 512],
                    ESEL[:, e_idx * 128:(e_idx + 1) * 128],
                    CROW[:, tc * 512:(tc + 1) * 512], start=True, stop=True))
            ccb2 = S.chk("pe")
            S.w("v", "pe", ccb2)
            cbev0 = S.o("v", lambda e: e.tensor_copy(
                SCB[:, 0:1024], PA[:, 0:1024]), chk=True)
            cbev1 = S.o("v", lambda e: e.tensor_copy(
                SCB[:, 1024:2048], PA[:, 1024:2048]), chk=True)
            cb_half = [("v", cbev0), ("v", cbev1)]
            pa_free_ffn = ("v", cbev1)
        he_done = None
        he_toks = [None] * 8
        for ic in range(8):
            bcol1 = (e_idx * 8 + ic) if not is_sh else (128 + ic)
            bcol3 = (64 + e_idx * 8 + ic) if not is_sh else (136 + ic)
            S.w("pe", "qg", wtoks[(e_idx, "w1")])
            split_cb = (not is_sh) and ic == 0
            if not split_cb:
                S.we("pe", pa_free_ffn)
            for tc in range(4):
                if split_cb:
                    S.we("pe", cb_half[0] if tc < 2 else cb_half[1])
                for kc in range(8):
                    if e_idx == 0 and ic == 0:
                        S.w("pe", "s", ln2_chks[kc][0])
                    S.o("pe", lambda e, rW1=rW1, ic=ic, tc=tc, kc=kc: e.matmul(
                        PA[:, tc * 512:(tc + 1) * 512],
                        rW1[:, kc * 1024 + ic * 128: kc * 1024 + ic * 128 + 128],
                        hs(kc, tc * 512, (tc + 1) * 512),
                        start=(kc == 0), stop=(kc == 7)))
            ch1 = S.chk("pe")
            S.w("s", "pe", ch1)
            if prev_csil is not None and he_done is not None:
                S.w("s", "v", he_done)
            csil = S.o("s", lambda e, bcol1=bcol1: e.activation(
                SCA[:], PA[:], AF.Silu, bias=BIA[:, bcol1:bcol1 + 1]), chk=True)
            prev_csil = csil
            pa_free_ffn = ("s", csil)
            S.w("pe", "qg", wtoks[(e_idx, "w3")])
            S.we("pe", pb_free_ffn)
            for kc in range(8):
                for tc in range(4):
                    S.o("pe", lambda e, rW3=rW3, ic=ic, tc=tc, kc=kc: e.matmul(
                        PB[:, tc * 512:(tc + 1) * 512],
                        rW3[:, kc * 1024 + ic * 128: kc * 1024 + ic * 128 + 128],
                        hs(kc, tc * 512, (tc + 1) * 512),
                        start=(kc == 0), stop=(kc == 7)))
            ch3 = S.chk("pe")
            S.w("v", "pe", ch3)
            S.w("v", "s", csil)
            S.we("v", he_read)
            chm = S.o("v", lambda e, ic=ic, bcol3=bcol3: e.scalar_tensor_tensor(
                aos(ic), PB[:], BIA[:, bcol3:bcol3 + 1], SCA[:], OP.add, OP.mult),
                chk=True)
            pb_free_ffn = ("v", chm)
            if not is_sh:
                chm = S.o("v", lambda e, ic=ic: e.tensor_tensor(
                    aos(ic), aos(ic), SCB[:], OP.mult), chk=True)
            he_done = chm
            he_toks[ic] = ("v", chm)
        psfree = {0: pa_free_ffn, 1: pb_free_ffn}
        sh_cts = []
        for dc in range(NDC):
            par = dc % 2
            psum = PA if par == 0 else PB
            S.w("pe", "qg", wtoks[(e_idx, "w2")])
            S.we("pe", psfree[par])
            for kc in range(8):
                S.we("pe", he_toks[kc])
                for tc in range(4):
                    S.o("pe", lambda e, rW2=rW2, psum=psum, dc=dc, tc=tc, kc=kc:
                        e.matmul(
                            psum[:, tc * 512:(tc + 1) * 512],
                            rW2[:, kc * 1024 + dc * 128: kc * 1024 + dc * 128 + 128],
                            aos(kc, tc * 512, (tc + 1) * 512),
                            start=(kc == 0), stop=(kc == 7 and not is_sh)))
            if is_sh:
                S.w("pe", "qg", tok_consts)
                S.w("pe", "v", crow_done)
                for tc in range(4):
                    S.o("pe", lambda e, psum=psum, dc=dc, tc=tc: e.matmul(
                        psum[:, tc * 512:(tc + 1) * 512],
                        B2R[0:8, dc * 128:(dc + 1) * 128],
                        CROW[0:8, tc * 512:(tc + 1) * 512],
                        start=False, stop=True))
            cw2 = S.chk("pe")
            if dc == NDC - 1:
                he_read = ("pe", cw2)
            if not is_sh:
                S.w("v", "pe", cw2)
                cst = S.o("v", lambda e, psum=psum, dc=dc: e.scalar_tensor_tensor(
                    xs(dc), psum[:], PAR[:, 40 + dc:40 + dc + 1], xs(dc),
                    OP.mult, OP.add), chk=True)
            else:
                S.w("s", "pe", cw2)
                if dc >= 2:
                    S.w("s", "v", stts[-2])
                tbuf = SCA if par == 0 else SCB
                ct = S.o("s", lambda e, psum=psum, dc=dc, tbuf=tbuf: e.activation(
                    tbuf[:], psum[:], AF.Identity,
                    bias=PAR[:, 56 + dc:56 + dc + 1],
                    scale=PAR[:, 40 + dc:40 + dc + 1]), chk=True)
                sh_cts.append(ct)
                S.w("v", "s", ct)
                cst = S.o("v", lambda e, dc=dc, tbuf=tbuf: e.tensor_tensor(
                    xs(dc), xs(dc), tbuf[:], OP.add), chk=True)
            psfree[par] = ("v", cst)
            stts.append(cst)
        pa_free_ffn = psfree[0]
        pb_free_ffn = psfree[1]
        reg_free[(3 * e_idx + 0) % 3] = ("pe", ch1)
        reg_free[(3 * e_idx + 1) % 3] = ("pe", ch3)
        reg_free[(3 * e_idx + 2) % 3] = ("pe", cw2)
        if e_idx < NE:
            for m_idx, mat in enumerate(("w1", "w3", "w2")):
                sched_wload(m_idx, e_idx + 1, mat)

    for t in stts[-NDC:]:
        S.w("y", "v", t)
    tok_store = S.d("y", "qy", lambda e: e.dma_start(
        out=dout.rearrange("(c p) t -> p c t", p=128),
        in_=X[:].rearrange("p (c t) -> p c t", c=NDC)))
    S.w("y", "qy", tok_store)

    with (
        nc.allow_low_precision("bf16 intermediates are deliberate; fp32 where it matters"),
        nc.semaphore("sem_pe") as sem_pe,
        nc.semaphore("sem_v") as sem_v,
        nc.semaphore("sem_s") as sem_s,
        nc.semaphore("sem_g") as sem_g,
        nc.semaphore("sem_y") as sem_y,
        nc.semaphore("sem_qy") as sem_qy,
        nc.semaphore("sem_qg") as sem_qg,
        nc.Block() as block,
    ):
        sems = {"pe": sem_pe, "v": sem_v, "s": sem_s, "g": sem_g, "y": sem_y,
                "qy": sem_qy, "qg": sem_qg}

        @block.tensor
        def _(eng):
            S.replay(eng, "pe", sems)

        @block.vector
        def _(eng):
            S.replay(eng, "v", sems)

        @block.scalar
        def _(eng):
            S.replay(eng, "s", sems)

        @block.gpsimd
        def _(eng):
            S.replay(eng, "g", sems)

        @block.sync
        def _(eng):
            S.replay(eng, "y", sems)

    return nc


_CACHE = {}
_LAST_RES = None


def kernel(**inputs):
    N = 8
    x = np.asarray(inputs["L0_attn_x"])
    f32 = np.float32

    def c_(a):
        return np.ascontiguousarray(a, dtype=f32)

    wq = c_(np.asarray(inputs["L0_attn_Wq"]).T)
    wk = c_(np.asarray(inputs["L0_attn_Wk"]).T)
    wv = c_(np.asarray(inputs["L0_attn_Wv"]).T)
    wo = c_(np.asarray(inputs["L0_attn_Wo"]).T)
    w1 = c_(inputs["L0_ffn_w1"])
    w3 = c_(inputs["L0_ffn_w3"])
    w2 = c_(inputs["L0_ffn_w2"])
    sw1 = c_(inputs["L0_ffn_sw1"])
    sw3 = c_(inputs["L0_ffn_sw3"])
    sw2 = c_(inputs["L0_ffn_sw2"])
    wmod = c_(inputs["L0_wmod"])
    cfull = np.asarray(inputs["L0_c"], dtype=f32)

    gq = np.asarray(inputs["L0_attn_gq"], dtype=f32)
    bq = np.asarray(inputs["L0_attn_bq"], dtype=f32)
    gk = np.asarray(inputs["L0_attn_gk"], dtype=f32)
    bk = np.asarray(inputs["L0_attn_bk"], dtype=f32)
    qkp = c_(np.stack([np.tile(gq, 2), np.tile(bq, 2),
                       np.tile(gk, 2), np.tile(bk, 2),
                       np.full(128, 0.5 * np.log(64.0), dtype=f32),
                       np.full(128, 0.5 * np.log(1024.0), dtype=f32)], axis=1))

    b1 = np.asarray(inputs["L0_ffn_b1"], dtype=f32)
    b3 = np.asarray(inputs["L0_ffn_b3"], dtype=f32)
    b2 = c_(inputs["L0_ffn_b2"])
    sb1 = np.asarray(inputs["L0_ffn_sb1"], dtype=f32)
    sb3 = np.asarray(inputs["L0_ffn_sb3"], dtype=f32)
    sb2 = np.asarray(inputs["L0_ffn_sb2"], dtype=f32)
    bo = np.asarray(inputs["L0_attn_bo"], dtype=f32)
    biases = np.zeros((128, 160), dtype=f32)
    biases[:, 0:64] = b1.reshape(8, 8, 128).transpose(2, 0, 1).reshape(128, 64)
    biases[:, 64:128] = b3.reshape(8, 8, 128).transpose(2, 0, 1).reshape(128, 64)
    biases[:, 128:136] = sb1.reshape(8, 128).T
    biases[:, 136:144] = sb3.reshape(8, 128).T
    biases[:, 144:152] = bo.reshape(8, 128).T
    biases[:, 152:160] = sb2.reshape(8, 128).T
    biases = c_(biases)

    gatet = c_(np.asarray(inputs["L0_ffn_gate_w"], dtype=f32).T
               .reshape(8, 128, 8).transpose(1, 0, 2).reshape(128, 64))

    allones = np.ones((128, 128), dtype=f32)
    blockones = np.zeros((128, 128), dtype=f32)
    blockones[0:64, 0:64] = 1.0
    blockones[64:128, 64:128] = 1.0
    ident = np.eye(128, dtype=f32)
    maskd = np.triu(np.full((128, 128), -1e30, dtype=f32), 1)
    sel2 = np.zeros((2, 128), dtype=f32)
    sel2[0, 0:64] = 1.0
    sel2[1, 64:128] = 1.0
    esel = np.zeros((8, 1024), dtype=f32)
    for e in range(8):
        esel[e, e * 128:(e + 1) * 128] = 1.0
    ones1 = np.ones((1, 128), dtype=f32)
    epsr = np.zeros((1, 1024), dtype=f32)
    epsr[0, 0:512] = 64 * 1e-5
    epsr[0, 512:1024] = 1024 * 1e-5

    if "nc" not in _CACHE:
        _CACHE["nc"] = build()
    nc = _CACHE["nc"]

    in_maps = []
    for i in range(N):
        in_maps.append({
            "x": c_(x[i].T),
            "wq": wq, "wk": wk, "wv": wv, "wo": wo,
            "w1": w1, "w3": w3, "w2": w2,
            "sw1": sw1, "sw3": sw3, "sw2": sw2,
            "wmod": wmod,
            "cc": c_(cfull[i].reshape(8, 128).T),
            "qkp": qkp, "biases": biases, "gatet": gatet, "b2r": b2,
            "callones": allones, "cblockones": blockones, "cident": ident,
            "cmask": maskd, "csel2": sel2, "cesel": esel,
            "cones1": ones1, "cepsr": epsr, "cidentf": ident,
        })
    res = run_bass_kernel_spmd(nc, in_maps, core_ids=list(range(N)))
    global _LAST_RES
    _LAST_RES = res
    out = np.stack([np.ascontiguousarray(r["out"].T) for r in res.results], axis=0)
    return out.astype(np.float32)


# version 32 (final)
# speedup vs baseline: 1.0846x; 1.0087x over previous
import math
import numpy as np
import ml_dtypes
import concourse.bass as bass
import concourse.mybir as mybir
from concourse.bass_utils import run_bass_kernel_spmd

F32 = mybir.dt.float32
BF16 = mybir.dt.bfloat16
AF = mybir.ActivationFunctionType
OP = mybir.AluOpType
AX = mybir.AxisListType

T = 2048
D = 1024
NDC = 8
NFC = 8
NE = 8
DH = 64
EPS = 1e-5

DEBUG = False

ENGS = ("pe", "v", "s", "g", "y")
QUEUES = ("qy", "qg")


class Sch:

    def __init__(self):
        self.seq = {k: [] for k in ENGS}
        self.n = {k: 0 for k in ENGS}
        self.nd = {k: 0 for k in QUEUES}
        self.wm = {k: {} for k in ENGS}

    def w(self, eng, sem, val):
        if not val or sem == eng or val <= self.wm[eng].get(sem, 0):
            return
        self.wm[eng][sem] = val
        self.seq[eng].append(("w", sem, val))

    def we(self, eng, et):
        if et is not None:
            self.w(eng, et[0], et[1])

    def o(self, eng, fn, chk=False):
        self.seq[eng].append(("o", fn, chk))
        if chk:
            self.n[eng] += 1
            return self.n[eng]
        return None

    def chk(self, eng):
        for idx in range(len(self.seq[eng]) - 1, -1, -1):
            it = self.seq[eng][idx]
            if it[0] == "o":
                if not it[2]:
                    self.seq[eng][idx] = ("o", it[1], True)
                    self.n[eng] += 1
                return self.n[eng]
            if it[0] == "d":
                raise RuntimeError("chk would land on a dma")
        raise RuntimeError("no op to chk")

    def d(self, eng, q, fn):
        self.seq[eng].append(("d", fn, q))
        self.nd[q] += 1
        return self.nd[q] * 16

    def replay(self, eng_obj, name, sems):
        for it in self.seq[name]:
            if it[0] == "w":
                eng_obj.wait_ge(sems[it[1]], it[2])
            elif it[0] == "o":
                inst = it[1](eng_obj)
                if it[2]:
                    inst.then_inc(sems[name], 1)
            else:
                it[1](eng_obj).then_inc(sems[it[2]], 16)


def build():
    nc = bass.Bass("TRN2", target_bir_lowering=False)

    dx = nc.dram_tensor("x", [D, T], F32, kind="ExternalInput").ap()
    dwq = nc.dram_tensor("wq", [D, D], F32, kind="ExternalInput").ap()
    dwk = nc.dram_tensor("wk", [D, D], F32, kind="ExternalInput").ap()
    dwv = nc.dram_tensor("wv", [D, D], F32, kind="ExternalInput").ap()
    dwo = nc.dram_tensor("wo", [D, D], F32, kind="ExternalInput").ap()
    dw1 = nc.dram_tensor("w1", [NE, D, D], F32, kind="ExternalInput").ap()
    dw3 = nc.dram_tensor("w3", [NE, D, D], F32, kind="ExternalInput").ap()
    dw2 = nc.dram_tensor("w2", [NE, D, D], F32, kind="ExternalInput").ap()
    dsw1 = nc.dram_tensor("sw1", [D, D], F32, kind="ExternalInput").ap()
    dsw3 = nc.dram_tensor("sw3", [D, D], F32, kind="ExternalInput").ap()
    dsw2 = nc.dram_tensor("sw2", [D, D], F32, kind="ExternalInput").ap()
    dwmod = nc.dram_tensor("wmod", [D, 6144], F32, kind="ExternalInput").ap()
    dcc = nc.dram_tensor("cc", [128, 8], F32, kind="ExternalInput").ap()
    dqkp = nc.dram_tensor("qkp", [128, 6], F32, kind="ExternalInput").ap()
    dbias = nc.dram_tensor("biases", [128, 160], F32, kind="ExternalInput").ap()
    dgatet = nc.dram_tensor("gatet", [128, 64], F32, kind="ExternalInput").ap()
    db2r = nc.dram_tensor("b2r", [8, 1024], F32, kind="ExternalInput").ap()
    dallones = nc.dram_tensor("callones", [128, 128], F32, kind="ExternalInput").ap()
    dblockones = nc.dram_tensor("cblockones", [128, 128], F32, kind="ExternalInput").ap()
    dident = nc.dram_tensor("cident", [128, 128], F32, kind="ExternalInput").ap()
    dmask = nc.dram_tensor("cmask", [128, 128], F32, kind="ExternalInput").ap()
    dsel2 = nc.dram_tensor("csel2", [2, 128], F32, kind="ExternalInput").ap()
    dones1 = nc.dram_tensor("cones1", [1, 128], F32, kind="ExternalInput").ap()
    didentf = nc.dram_tensor("cidentf", [128, 128], F32, kind="ExternalInput").ap()
    depsr = nc.dram_tensor("cepsr", [1, 1024], F32, kind="ExternalInput").ap()
    desel = nc.dram_tensor("cesel", [8, 1024], F32, kind="ExternalInput").ap()
    dout = nc.dram_tensor("out", [D, T], F32, kind="ExternalOutput").ap()
    if DEBUG:
        dbg_par = nc.dram_tensor("dbg_par", [128, 64], F32, kind="ExternalOutput").ap()
        dbg_h = nc.dram_tensor("dbg_h", [128, NDC * T], BF16, kind="ExternalOutput").ap()
        dbg_qkv = nc.dram_tensor("dbg_qkv", [128, 8192], BF16, kind="ExternalOutput").ap()
        dbg_ao = nc.dram_tensor("dbg_ao", [128, NDC * T], BF16, kind="ExternalOutput").ap()
        dbg_x1 = nc.dram_tensor("dbg_x1", [128, NDC * T], F32, kind="ExternalOutput").ap()
        dbg_crow = nc.dram_tensor("dbg_crow", [8, T], BF16, kind="ExternalOutput").ap()
        dbg_h2 = nc.dram_tensor("dbg_h2", [128, NDC * T], BF16, kind="ExternalOutput").ap()
        dbg_ao2 = nc.dram_tensor("dbg_ao2", [128, T], BF16, kind="ExternalOutput").ap()
        dbg_lc = nc.dram_tensor("dbg_lc", [128, 32], F32, kind="ExternalOutput").ap()
        dbg_pt = nc.dram_tensor("dbg_pt", [128, T], BF16, kind="ExternalOutput").ap()
        dbg_isge = nc.dram_tensor("dbg_isge", [128, 16], F32, kind="ExternalOutput").ap()
        dbg_sc = nc.dram_tensor("dbg_sc", [128, 2048], F32, kind="ExternalOutput").ap()
        dbg_nm = nc.dram_tensor("dbg_nm", [128, 4], F32, kind="ExternalOutput").ap()
        dbg_pr = nc.dram_tensor("dbg_pr", [128, 2048], BF16, kind="ExternalOutput").ap()

    X = nc.alloc_sbuf_tensor("X", [128, NDC * T], F32)
    HH = nc.alloc_sbuf_tensor("HH", [128, NDC * T], BF16)
    AO = nc.alloc_sbuf_tensor("AO", [128, NDC * T], BF16)
    WB = nc.alloc_sbuf_tensor("WB", [128, 2 * 8192], BF16)
    WORK = nc.alloc_sbuf_tensor("WORK", [128, 8192], BF16)
    SCA = nc.alloc_sbuf_tensor("SCA", [128, T], BF16)
    SCB = nc.alloc_sbuf_tensor("SCB", [128, T], BF16)
    WVS = nc.alloc_sbuf_tensor("WVS", [128, 2048], BF16)
    KP1 = nc.alloc_sbuf_tensor("KP1", [128, 2048], BF16)
    CROW = nc.alloc_sbuf_tensor("CROW", [8, T], BF16)
    QKP = nc.alloc_sbuf_tensor("QKP", [128, 6], F32)
    CC = nc.alloc_sbuf_tensor("CC", [128, 8], F32)
    SCS = nc.alloc_sbuf_tensor("SCS", [128, 8], F32)
    BIA = nc.alloc_sbuf_tensor("BIA", [128, 160], F32)
    PAR = nc.alloc_sbuf_tensor("PAR", [128, 64], F32)
    GATET = nc.alloc_sbuf_tensor("GATET", [128, 64], BF16)
    B2R = nc.alloc_sbuf_tensor("B2R", [8, 1024], BF16)
    ONES = nc.alloc_sbuf_tensor("ONES", [128, 128], BF16)
    BONES = nc.alloc_sbuf_tensor("BONES", [128, 128], BF16)
    IDENT = nc.alloc_sbuf_tensor("IDENT", [128, 128], BF16)
    MASK = nc.alloc_sbuf_tensor("MASK", [128, 128], BF16)
    SEL2 = nc.alloc_sbuf_tensor("SEL2", [2, 128], BF16)
    ONES1 = nc.alloc_sbuf_tensor("ONES1", [1, 128], BF16)
    IDENTF = nc.alloc_sbuf_tensor("IDENTF", [128, 128], F32)
    EPSR = nc.alloc_sbuf_tensor("EPSR", [1, 1024], BF16)
    ESEL = nc.alloc_sbuf_tensor("ESEL", [8, 1024], BF16)
    NM = nc.alloc_sbuf_tensor("NM", [128, 8], F32)
    LC = nc.alloc_sbuf_tensor("LC", [128, 32], F32)
    GST = nc.alloc_sbuf_tensor("GST", [128, 64], F32)
    JNK = nc.alloc_sbuf_tensor("JNK", [128, 4], F32)

    PA = nc.alloc_psum_tensor("PA", [128, 2048], F32)
    PB = nc.alloc_psum_tensor("PB", [128, 2048], F32)

    S = Sch()
    aof = AO.ap().bitcast(F32)

    def xs(dc, lo=0, hi=T):
        return X[:, dc * T + lo: dc * T + hi]

    def hs(dc, lo=0, hi=T):
        return HH[:, dc * T + lo: dc * T + hi]

    def aos(dc, lo=0, hi=T):
        return AO[:, dc * T + lo: dc * T + hi]

    tok_x = None
    tok_smalls = None
    for sb_t, dr in ((QKP, dqkp), (CC, dcc), (BIA, dbias), (IDENTF, didentf)):
        tok_smalls = S.d("y", "qy", lambda e, a=sb_t, b=dr: e.dma_start(out=a[:], in_=b[:]))
    tok_gatet = S.d("g", "qg", lambda e: e.dma_start(out=GATET[:], in_=dgatet[:]))
    tok_consts = None
    for sb_t, dr in ((ONES, dallones), (BONES, dblockones), (IDENT, dident),
                     (MASK, dmask), (SEL2, dsel2), (ESEL, desel), (B2R, db2r),
                     (ONES1, dones1), (EPSR, depsr)):
        tok_consts = S.d("g", "qg", lambda e, a=sb_t, b=dr: e.dma_start(out=a[:], in_=b[:]))

    def wload(dst_ap, dram_ap):
        return lambda e: e.dma_start(
            out=dst_ap.rearrange("p (c o) -> p c o", c=8),
            in_=dram_ap.rearrange("(c p) o -> p c o", p=128))

    R0 = WB[:, 0:8192]
    R1 = WB[:, 8192:16384]
    RW = WORK[:, 0:8192]
    tok_wq = S.d("g", "qg", wload(R0, dwq))
    tok_wk = S.d("g", "qg", wload(R1, dwk))

    def wv_piece(fc):
        pc = fc % 2
        return S.d("g", "qg", lambda e, fc=fc, pc=pc: e.dma_start(
            out=WVS[:, pc * 1024:(pc + 1) * 1024].rearrange("p (c o) -> p c o", c=8),
            in_=dwv.rearrange("(c p) o -> p c o", p=128)[:, :, fc * 128:(fc + 1) * 128]))

    tok_wv = [wv_piece(0), wv_piece(1)]

    S.w("s", "qy", tok_smalls)
    chk_scs = S.o("s", lambda e: e.activation(SCS[:], CC[:], AF.Silu), chk=True)

    NPC = 12
    PW = 6144 // NPC
    modp = PB[:, 2000:2048]
    ROWB = nc.alloc_sbuf_tensor("ROWB", [1, 512], F32)
    piece_tok = [None] * NPC
    piece_done = [None] * NPC
    ev_tok = None
    ev_toks = []
    tr_toks = []
    first_tr = True
    for p in range(NPC):
        if p >= 2:
            S.w("y", "pe", piece_done[p - 2])
        piece_tok[p] = S.d("y", "qy", lambda e, p=p: e.dma_start(
            out=aof[:, (p % 2) * 4096:(p % 2) * 4096 + 4096].rearrange(
                "p (c o) -> p c o", c=8),
            in_=dwmod.rearrange("(c p) o -> p c o", p=128)[:, :, p * PW:(p + 1) * PW]))
        if p == 3:
            tok_x = S.d("y", "qy", lambda e: e.dma_start(
                out=X[:].rearrange("p (c t) -> p c t", c=NDC),
                in_=dx.rearrange("(c p) t -> p c t", p=128)))
        S.w("pe", "qy", piece_tok[p])
        if p == 0:
            S.w("pe", "s", chk_scs)
        modrow = PB[0:1, (p % 2) * 512:(p % 2) * 512 + 512]
        if len(ev_toks) >= 2:
            S.w("pe", "v", ev_toks[-2])
        for kc in range(8):
            S.o("pe", lambda e, p=p, kc=kc, modrow=modrow: e.matmul(
                modrow, SCS[:, kc:kc + 1],
                aof[:, (p % 2) * 4096 + kc * PW: (p % 2) * 4096 + kc * PW + PW],
                start=(kc == 0), stop=(kc == 7)))
        cmm = S.chk("pe")
        piece_done[p] = cmm
        S.w("v", "pe", cmm)
        if tr_toks:
            S.w("v", "pe", tr_toks[-1])
        ev_tok = S.o("v", lambda e, modrow=modrow: e.tensor_copy(ROWB[:], modrow),
                     chk=True)
        ev_toks.append(ev_tok)
        S.w("pe", "v", ev_tok)
        for cb in range(4):
            col = p * 4 + cb
            S.o("pe", lambda e, cb=cb, col=col, first=first_tr: e.matmul(
                modp[:, col:col + 1], ROWB[0:1, cb * 128:(cb + 1) * 128],
                IDENTF[0:1, 0:1], start=first, stop=(col == 47),
                is_transpose=True))
            first_tr = False
        tr_toks.append(S.chk("pe"))
    piece_done[NPC - 1] = tr_toks[-1]
    chk_modmm = piece_done[-1]

    S.w("v", "pe", tr_toks[3])
    S.w("v", "qy", tok_smalls)

    def asm1(e):
        e.tensor_copy(PAR[:, 0:8], modp[:, 0:8])
        return e.tensor_scalar_add(PAR[:, 8:16], modp[:, 8:16], 1.0)
    chk_par1 = S.o("v", asm1, chk=True)
    S.w("v", "pe", chk_modmm)

    def asm(e):
        e.tensor_copy(PAR[:, 24:32], modp[:, 24:32])
        e.tensor_scalar_add(PAR[:, 32:40], modp[:, 32:40], 1.0)
        A, B = 0, 32
        for base, src in ((A, 16), (B, 40)):
            e.tensor_single_scalar(GST[:, base:base + 8], modp[:, src:src + 8],
                                   0.0, OP.is_ge)
        for base in (A, B):
            e.tensor_scalar_mul(GST[:, base + 8:base + 16], GST[:, base:base + 8], 2.0)
        for base in (A, B):
            e.tensor_single_scalar(GST[:, base + 8:base + 16],
                                   GST[:, base + 8:base + 16], 1.0, OP.subtract)
        for base, src in ((A, 16), (B, 40)):
            e.tensor_scalar_mul(GST[:, base + 16:base + 24], modp[:, src:src + 8], -1.0)
        for base in (A, B):
            e.tensor_scalar_max(GST[:, base + 16:base + 24],
                                GST[:, base + 16:base + 24], 0.25)
        for base, src in ((A, 16), (B, 40)):
            e.tensor_scalar_max(GST[:, base + 24:base + 32], modp[:, src:src + 8], 0.25)
        for base in (A, B):
            e.tensor_max(GST[:, base + 16:base + 24], GST[:, base + 16:base + 24],
                         GST[:, base + 24:base + 32])
        for base, dst in ((A, 16), (B, 40)):
            e.tensor_mul(PAR[:, dst:dst + 8], GST[:, base + 8:base + 16],
                         GST[:, base + 16:base + 24])
        e.tensor_mul(PAR[:, 48:56], PAR[:, 16:24], BIA[:, 144:152])
        if DEBUG:
            e.tensor_single_scalar(GST[:, 32:40], modp[:, 16:24], 0.0, OP.is_ge)
            e.tensor_scalar_mul(GST[:, 40:48], GST[:, 32:40], 2.0)
            e.tensor_single_scalar(GST[:, 40:48], GST[:, 40:48], 1.0, OP.subtract)
        return e.tensor_mul(PAR[:, 56:64], PAR[:, 40:48], BIA[:, 152:160])
    chk_par = S.o("v", asm, chk=True)
    S.w("s", "v", chk_par)

    ln_stats_free = [None]

    def layernorm(sc_col, bi_col, t_slice, h2f32_slice, h2_cb, pre_waits, lnscr):
        for eng in ("pe", "v", "s"):
            for et in pre_waits:
                S.we(eng, et)
        S.w("v", "qy", tok_x)
        S.w("s", "qy", tok_x)
        last_mm = None
        for dc in range(NDC):
            if last_mm is not None:
                S.w("v", "pe", last_mm)
                S.w("s", "pe", last_mm)
            ccp = S.o("v", lambda e, dc=dc: e.tensor_copy(SCA[:], xs(dc)), chk=True)
            csq = S.o("s", lambda e, dc=dc: e.activation(SCB[:], xs(dc), AF.Square),
                      chk=True)
            S.w("pe", "v", ccp)
            S.w("pe", "s", csq)
            S.w("pe", "qg", tok_consts)
            for tc in range(4):
                S.o("pe", lambda e, dc=dc, tc=tc: e.matmul(
                    PA[:, tc * 512:(tc + 1) * 512], ONES[:],
                    SCA[:, tc * 512:(tc + 1) * 512],
                    start=(dc == 0), stop=(dc == NDC - 1)))
                S.o("pe", lambda e, dc=dc, tc=tc: e.matmul(
                    PB[:, tc * 512:(tc + 1) * 512], ONES[:],
                    SCB[:, tc * 512:(tc + 1) * 512],
                    start=(dc == 0), stop=False))
            last_mm = S.chk("pe")
        for tc in range(4):
            S.o("pe", lambda e, tc=tc: e.matmul(
                PB[:, tc * 512:(tc + 1) * 512], ONES1[0:1, :],
                EPSR[0:1, 512:1024], start=False, stop=True))
        last_mm = S.chk("pe")
        S.w("v", "pe", last_mm)
        S.o("v", lambda e: e.tensor_scalar_mul(SCA[:], PA[:], 1.0 / D))
        ct1 = S.o("v", lambda e: e.tensor_tensor(SCB[:], SCA[:], PA[:], OP.mult),
                  chk=True)
        crstd = None
        cln_prev = None
        for h in (0, 1):
            if cln_prev is not None:
                S.w("v", "s", cln_prev)
            ct2 = S.o("v", lambda e, h=h: e.scalar_tensor_tensor(
                lnscr[:, 0:1024], SCB[:, h * 1024:(h + 1) * 1024], -1.0,
                PB[:, h * 1024:(h + 1) * 1024], OP.mult, OP.add), chk=True)
            S.w("s", "v", ct2)
            cln_prev = S.o("s", lambda e: e.activation(
                lnscr[:, 1024:2048], lnscr[:, 0:1024], AF.Ln), chk=True)
            crstd = S.o("s", lambda e, h=h: e.activation(
                SCB[:, h * 1024:(h + 1) * 1024], lnscr[:, 1024:2048], AF.Exp,
                scale=-0.5, bias=QKP[:, 5:6]), chk=True)
            ln_stats_free[0] = ("v", ct2)
        h_chks = []
        S.w("v", "s", crstd)
        cu = None
        prev_gate_tok = None
        for dc in range(NDC):
            if h_chks:
                S.w("v", "s", h_chks[-1][1] or h_chks[-1][0])
            S.o("v", lambda e, dc=dc: e.scalar_tensor_tensor(
                SCA[:], PA[:], -1.0 / D, xs(dc), OP.mult, OP.add))
            cu = S.o("v", lambda e, t=t_slice: e.tensor_tensor(t, SCA[:], SCB[:],
                                                               OP.mult), chk=True)
            S.w("s", "v", cu)
            ch = S.o("s", lambda e, dc=dc, t=t_slice: e.activation(
                hs(dc), t, AF.Identity, bias=PAR[:, bi_col + dc:bi_col + dc + 1],
                scale=PAR[:, sc_col + dc:sc_col + dc + 1]), chk=True)
            ch2 = None
            if h2f32_slice is not None:
                if prev_gate_tok is not None:
                    S.w("s", "pe", prev_gate_tok)
                ch2 = S.o("s", lambda e, dc=dc, t=t_slice: e.activation(
                    h2f32_slice, t, AF.Identity, bias=PAR[:, bi_col + dc:bi_col + dc + 1],
                    scale=PAR[:, sc_col + dc:sc_col + dc + 1]), chk=True)
                prev_gate_tok = h2_cb(dc, ch2)
            h_chks.append((ch, ch2, dc))
        return h_chks, cu

    if DEBUG:
        S.w("y", "v", chk_par)
        S.d("y", "qy", lambda e: e.dma_start(out=dbg_par, in_=PAR[:]))
        S.d("y", "qy", lambda e: e.dma_start(out=dbg_isge, in_=GST[:, 32:48]))

    workf = WORK.ap().bitcast(F32)

    def ln1_halved():
        chks = [[None, None] for _ in range(NDC)]
        S.w("v", "qy", tok_x)
        S.w("s", "qy", tok_x)
        S.w("s", "v", chk_par1)
        last_mm = None
        last_u = None
        prev_chs = [None, None]
        last_mms = [None, None]
        for th in (0, 1):
            tlo = th * 1024
            for dc in range(NDC):
                sb = (dc % 2) * 1024
                if last_mms[dc % 2] is not None:
                    S.w("v", "pe", last_mms[dc % 2])
                    S.w("s", "pe", last_mms[dc % 2])
                ccp = S.o("v", lambda e, dc=dc, tlo=tlo, sb=sb: e.tensor_copy(
                    SCA[:, sb:sb + 1024], xs(dc, tlo, tlo + 1024)), chk=True)
                csq = S.o("s", lambda e, dc=dc, tlo=tlo, sb=sb: e.activation(
                    SCB[:, sb:sb + 1024], xs(dc, tlo, tlo + 1024), AF.Square),
                    chk=True)
                S.w("pe", "v", ccp)
                S.w("pe", "s", csq)
                S.w("pe", "qg", tok_consts)
                if th == 1 and dc == 0:
                    S.w("pe", "v", last_u)
                for tc in (0, 1):
                    S.o("pe", lambda e, tc=tc, dc=dc, sb=sb: e.matmul(
                        PA[:, tc * 512:(tc + 1) * 512], ONES[:],
                        SCA[:, sb + tc * 512:sb + (tc + 1) * 512],
                        start=(dc == 0), stop=(dc == NDC - 1)))
                    S.o("pe", lambda e, tc=tc, dc=dc, sb=sb: e.matmul(
                        PA[:, 1024 + tc * 512:1024 + (tc + 1) * 512], ONES[:],
                        SCB[:, sb + tc * 512:sb + (tc + 1) * 512],
                        start=(dc == 0), stop=False))
                last_mms[dc % 2] = S.chk("pe")
            for tc in (0, 1):
                S.o("pe", lambda e, tc=tc: e.matmul(
                    PA[:, 1024 + tc * 512:1024 + (tc + 1) * 512], ONES1[0:1, :],
                    EPSR[0:1, 512:1024], start=False, stop=True))
            last_mm = S.chk("pe")
            S.w("v", "pe", last_mm)
            S.o("v", lambda e: e.tensor_scalar_mul(SCA[:, 0:1024], PA[:, 0:1024],
                                                   1.0 / D))
            S.o("v", lambda e: e.tensor_tensor(SCB[:, 0:1024], SCA[:, 0:1024],
                                               PA[:, 0:1024], OP.mult))
            ct2 = S.o("v", lambda e: e.scalar_tensor_tensor(
                workf[:, 1024:2048], SCB[:, 0:1024], -1.0, PA[:, 1024:2048],
                OP.mult, OP.add), chk=True)
            S.w("s", "v", ct2)
            S.o("s", lambda e: e.activation(workf[:, 2048:3072], workf[:, 1024:2048],
                                            AF.Ln))
            crstd = S.o("s", lambda e: e.activation(
                SCB[:, 0:1024], workf[:, 2048:3072], AF.Exp,
                scale=-0.5, bias=QKP[:, 5:6]), chk=True)
            S.w("v", "s", crstd)
            for dc in range(NDC):
                b = (dc % 2) * 1024
                if prev_chs[dc % 2] is not None:
                    S.w("v", "s", prev_chs[dc % 2])
                S.o("v", lambda e, dc=dc, tlo=tlo, b=b: e.scalar_tensor_tensor(
                    SCA[:, b:b + 1024], PA[:, 0:1024], -1.0 / D,
                    xs(dc, tlo, tlo + 1024), OP.mult, OP.add))
                cu = S.o("v", lambda e, b=b: e.tensor_tensor(
                    WORK[:, b:b + 1024], SCA[:, b:b + 1024], SCB[:, 0:1024],
                    OP.mult), chk=True)
                last_u = cu
                S.w("s", "v", cu)
                prev_chs[dc % 2] = S.o("s", lambda e, dc=dc, tlo=tlo, b=b:
                    e.activation(
                        hs(dc, tlo, tlo + 1024), WORK[:, b:b + 1024], AF.Identity,
                        bias=PAR[:, 0 + dc:0 + dc + 1],
                        scale=PAR[:, 8 + dc:8 + dc + 1]), chk=True)
                chks[dc][th] = prev_chs[dc % 2]
        return chks, last_u

    ln1_hchks, ln1_pa_free = ln1_halved()

    QR, KR, VR, VT = 0, 2048, 4096, 6144
    TRA = PB[:, 1024:1536].bitcast(BF16)
    TRB = PB[:, 1536:2048].bitcast(BF16)
    AVP = PB[:, 1024:1152]
    AVP2 = PB[:, 1536:1664]
    trfree = [None, None]

    halves = [(PB, 0), (PA, 0), (PA, 1024)]
    half_free = [None, None, None]
    hctr = [0]
    tr_last = [None]
    av_last = [None]
    norm_tok = [None]
    ctm_k = [None]
    prev_cbc = [None]
    prev_ctl = [None]
    scb_pe_free = [None]
    kp1_free = [None]
    ao_guard = [False]
    last_q_mm = [None]
    last_k_mm = [None]

    ckp1z = S.o("v", lambda e: e.memset(KP1.ap()[0:64, 0:2048], 0.0), chk=True)

    for fc in range(NFC):
        chk_evac = {}
        for mat, psum, wreg, wtok, dst in (("q", PA, R0, tok_wq, QR),
                                           ("k", PB, R1, tok_wk, KR)):
            S.w("pe", "qg", wtok)
            if fc == 0:
                S.w("pe", "v", ln1_pa_free)
            else:
                if mat == "q":
                    S.we("pe", half_free[1])
                    S.we("pe", half_free[2])
                else:
                    S.we("pe", half_free[0])
            for kc in range(8):
                if fc == 0:
                    S.w("pe", "s", ln1_hchks[kc][0])
                    S.w("pe", "s", ln1_hchks[kc][1])
                for tc in range(4):
                    S.o("pe", lambda e, psum=psum, wreg=wreg, tc=tc, kc=kc, fc=fc:
                        e.matmul(
                            psum[:, tc * 512:(tc + 1) * 512],
                            wreg[:, kc * 1024 + fc * 128: kc * 1024 + fc * 128 + 128],
                            hs(kc, tc * 512, (tc + 1) * 512),
                            start=(kc == 0), stop=(kc == 7)))
            cm = S.chk("pe")
            if fc == NFC - 1:
                (last_q_mm if mat == "q" else last_k_mm)[0] = cm
            S.w("v", "pe", cm)
            chk_evac[mat] = S.o("v", lambda e, psum=psum, dst=dst: e.tensor_copy(
                WORK[:, dst:dst + 2048], psum[:]), chk=True)
        S.w("pe", "v", chk_evac["q"])
        S.w("pe", "qg", tok_wv[fc])
        pc = fc % 2
        for kc in range(8):
            for tc in range(4):
                S.o("pe", lambda e, tc=tc, kc=kc, pc=pc: e.matmul(
                    PA[:, tc * 512:(tc + 1) * 512],
                    WVS[:, pc * 1024 + kc * 128: pc * 1024 + kc * 128 + 128],
                    hs(kc, tc * 512, (tc + 1) * 512),
                    start=(kc == 0), stop=(kc == 7)))
        cmv = S.chk("pe")
        if fc + 2 < NFC:
            S.w("g", "pe", cmv)
            tok_wv.append(wv_piece(fc + 2))
        S.w("v", "pe", cmv)
        cev0 = S.o("v", lambda e: e.tensor_copy(
            WORK[:, VR:VR + 1024], PA[:, 0:1024]), chk=True)
        chk_evac["v"] = S.o("v", lambda e: e.tensor_copy(
            WORK[:, VR + 1024:VR + 2048], PA[:, 1024:2048]), chk=True)

        vt_half = [None, None]
        vt_done = None
        for rnd in range(2):
            trv = TRA if rnd == 0 else TRB
            S.w("pe", "v", cev0 if rnd == 0 else chk_evac["v"])
            S.we("pe", trfree[rnd])
            for j8 in range(8):
                j = rnd * 8 + j8
                S.o("pe", lambda e, j=j, j8=j8, trv=trv: e.matmul(
                    trv[:, j8 * 128:(j8 + 1) * 128],
                    WORK[:, VR + j * 128: VR + (j + 1) * 128], IDENT[:],
                    start=(j8 == 0), stop=(j8 == 7), is_transpose=True))
            ct = S.chk("pe")
            vt_done = ct
            S.w("v", "pe", ct)
            tok = ("v", S.o("v", lambda e, rnd=rnd, trv=trv: e.tensor_copy(
                WORK[:, VT + rnd * 1024: VT + (rnd + 1) * 1024], trv[:]), chk=True))
            trfree[rnd] = tok
            vt_half[rnd] = tok

        pa_free = ("v", chk_evac["v"])
        pb_free = ("v", chk_evac["k"])
        qraw = WORK[:, QR:QR + 2048]
        kraw = WORK[:, KR:KR + 2048]
        scaf = SCA.ap().bitcast(F32)
        S.we("pe", pa_free)
        for tc in range(4):
            S.o("pe", lambda e, tc=tc: e.matmul(
                PA[:, tc * 512:(tc + 1) * 512], BONES[:],
                qraw[:, tc * 512:(tc + 1) * 512], start=True, stop=True))
        csq_ = S.chk("pe")
        S.we("pe", pb_free)
        S.we("pe", vt_half[0])
        S.we("pe", vt_half[1])
        for tc in range(4):
            S.o("pe", lambda e, tc=tc: e.matmul(
                PB[:, tc * 512:(tc + 1) * 512], BONES[:],
                kraw[:, tc * 512:(tc + 1) * 512], start=True, stop=True))
        csk_ = S.chk("pe")
        S.w("v", "pe", csq_)
        cuq = S.o("v", lambda e: e.scalar_tensor_tensor(
            qraw, PA[:], -1.0 / DH, qraw, OP.mult, OP.add), chk=True)
        S.w("v", "pe", csk_)
        cuk = S.o("v", lambda e: e.scalar_tensor_tensor(
            kraw, PB[:], -1.0 / DH, kraw, OP.mult, OP.add), chk=True)
        S.w("s", "v", cuq)
        S.we("s", prev_cbc[0])
        cxq = S.o("s", lambda e: e.activation(SCA[:], qraw, AF.Square), chk=True)
        S.w("s", "v", cuk)
        S.we("s", prev_ctl[0])
        S.we("s", scb_pe_free[0])
        cxk = S.o("s", lambda e: e.activation(SCB[:], kraw, AF.Square), chk=True)
        S.w("pe", "s", cxq)
        S.w("pe", "v", cuq)
        for tc in range(4):
            S.o("pe", lambda e, tc=tc: e.matmul(
                PA[:, tc * 512:(tc + 1) * 512], BONES[:],
                SCA[:, tc * 512:(tc + 1) * 512], start=True, stop=False))
            S.o("pe", lambda e, tc=tc: e.matmul(
                PA[:, tc * 512:(tc + 1) * 512], ONES1[0:1, :],
                EPSR[0:1, 0:512], start=False, stop=True))
        cvq = S.chk("pe")
        S.w("pe", "s", cxk)
        S.w("pe", "v", cuk)
        S.we("pe", vt_half[0])
        S.we("pe", vt_half[1])
        for tc in range(4):
            S.o("pe", lambda e, tc=tc: e.matmul(
                PB[:, tc * 512:(tc + 1) * 512], BONES[:],
                SCB[:, tc * 512:(tc + 1) * 512], start=True, stop=False))
            S.o("pe", lambda e, tc=tc: e.matmul(
                PB[:, tc * 512:(tc + 1) * 512], ONES1[0:1, :],
                EPSR[0:1, 0:512], start=False, stop=True))
        cvk = S.chk("pe")
        qact_half = [None, None]
        kact_half = [None, None]
        kz_half = [None, None]
        scb_user = None
        ln_k = [None, None]
        ln_q = [None, None]
        S.w("s", "pe", cvk)
        for h in (0, 1):
            lo, hi = h * 1024, (h + 1) * 1024
            ln_k[h] = S.o("s", lambda e, lo=lo, hi=hi: e.activation(
                scaf[:, 0:1024], PB[:, lo:hi], AF.Ln), chk=True)
            ce = S.o("s", lambda e, lo=lo, hi=hi: e.activation(
                SCB[:, lo:hi], scaf[:, 0:1024], AF.Exp,
                scale=-0.5, bias=QKP[:, 4:5]), chk=True)
            S.w("v", "s", ce)
            ctmk = S.o("v", lambda e, lo=lo, hi=hi: e.tensor_tensor(
                WORK[:, KR + lo:KR + hi], WORK[:, KR + lo:KR + hi],
                SCB[:, lo:hi], OP.mult), chk=True)
            if h == 1:
                ctm_k[0] = ctmk
            S.w("s", "v", ctmk)
            S.w("s", "v", ckp1z)
            S.we("s", kp1_free[0])
            ck1 = S.o("s", lambda e, lo=lo, hi=hi: e.activation(
                KP1.ap()[64:128, lo:hi], WORK.ap()[64:128, KR + lo:KR + hi],
                AF.Identity, bias=QKP.ap()[64:128, 3:4],
                scale=QKP.ap()[64:128, 2:3]), chk=True)
            ck0 = S.o("s", lambda e, lo=lo, hi=hi: e.activation(
                WORK.ap()[0:64, KR + lo:KR + hi], WORK.ap()[0:64, KR + lo:KR + hi],
                AF.Identity, bias=QKP.ap()[0:64, 3:4],
                scale=QKP.ap()[0:64, 2:3]), chk=True)
            kact_half[h] = ("s", ck0)
            S.w("v", "s", ck1)
            kz_half[h] = ("v", S.o("v", lambda e, lo=lo, hi=hi: e.memset(
                WORK.ap()[64:128, KR + lo:KR + hi], 0.0), chk=True))
            ln_q[h] = S.o("s", lambda e, lo=lo, hi=hi: e.activation(
                scaf[:, 0:1024], PA[:, lo:hi], AF.Ln), chk=True)
            ce = S.o("s", lambda e, lo=lo, hi=hi: e.activation(
                SCB[:, lo:hi], scaf[:, 0:1024], AF.Exp,
                scale=-0.5, bias=QKP[:, 4:5]), chk=True)
            S.w("v", "s", ce)
            ctmq = S.o("v", lambda e, lo=lo, hi=hi: e.tensor_tensor(
                WORK[:, QR + lo:QR + hi], WORK[:, QR + lo:QR + hi],
                SCB[:, lo:hi], OP.mult), chk=True)
            scb_user = ("v", ctmq)
            S.w("s", "v", ctmq)
            cq = S.o("s", lambda e, lo=lo, hi=hi: e.activation(
                WORK[:, QR + lo:QR + hi], WORK[:, QR + lo:QR + hi], AF.Identity,
                bias=QKP[:, 1:2], scale=QKP[:, 0:1]), chk=True)
            qact_half[h] = ("s", cq)
        half_free[0] = ("s", ln_k[0])
        half_free[1] = ("s", ln_q[0])
        half_free[2] = ("s", ln_q[1])
        hctr[0] += (3 - hctr[0] % 3) % 3
        PRB = VR

        if DEBUG and fc == 0:
            S.we("y", qact_half[1])
            S.we("y", kact_half[1])
            S.we("y", tr_last[0])
            _tok = S.d("y", "qy", lambda e: e.dma_start(out=dbg_qkv, in_=WORK[:]))
            S.w("s", "qy", _tok)

        NT = 32
        pbufs = [WORK[:, VR:VR + 2048], SCB[:, 0:2048], SCA[:, 0:2048]]
        probs_free = [("pe", vt_done), scb_user, None]
        lc_free = [None, None]
        st = [dict() for _ in range(NT)]

        def emit_A(t):
            i, hh = divmod(t, 2)
            qoff = 64 * hh
            kv = (i + 1) * 128
            nh = 1 if kv <= 1024 else 2
            par = t % 2
            hf = i // 8
            S.we("pe", qact_half[hf])
            for h2 in range(hf + 1):
                S.we("pe", kact_half[h2])
                S.we("pe", kz_half[h2])
            hids = []
            csc_h = []
            for hx in range(nh):
                hid = hctr[0] % 3
                hctr[0] += 1
                hids.append(hid)
                S.we("pe", half_free[hid])
                psum, poff = halves[hid]
                lo = hx * 1024
                hi = min(kv, lo + 1024)
                off = lo
                while off < hi:
                    n = min(512, hi - off)
                    kop = (WORK[:, KR + off: KR + off + n] if hh == 0
                           else KP1[:, off:off + n])
                    S.o("pe", lambda e, psum=psum, poff=poff, off=off, lo=lo,
                        n=n, i=i, kop=kop: e.matmul(
                        psum[:, poff + off - lo: poff + off - lo + n],
                        WORK[:, QR + i * 128: QR + (i + 1) * 128],
                        kop, start=True, stop=True))
                    off += n
                csc_h.append(S.chk("pe"))
            st[t].update(csc=csc_h[-1], csc_h=csc_h, hids=hids, nh=nh, hh=hh, i=i)

        def emit_B(t):
            i, hh = st[t]["i"], st[t]["hh"]
            kv = (i + 1) * 128
            nh = st[t]["nh"]
            hids = st[t]["hids"]
            par = t % 2
            csc_h = st[t]["csc_h"]
            if t >= 2:
                S.w("v", "s", st[t - 2]["exp"])
            if nh == 2:
                psum, poff = halves[hids[0]]
                S.w("v", "pe", csc_h[0])
                S.o("v", lambda e, psum=psum, poff=poff, par=par: e.reduce_max(
                    NM[:, 4 * par:4 * par + 1], psum[:, poff:poff + 1024],
                    axis=AX.X, negate=True))
            S.w("v", "pe", csc_h[-1])
            dpsum, dpoff = halves[hids[-1]]
            drel = (i * 128) % 1024
            S.o("v", lambda e, dpsum=dpsum, dpoff=dpoff, drel=drel: e.tensor_tensor(
                dpsum[:, dpoff + drel: dpoff + drel + 128],
                dpsum[:, dpoff + drel: dpoff + drel + 128], MASK[:], OP.add))
            hx = nh - 1
            psum, poff = halves[hids[-1]]
            w = min(kv - hx * 1024, 1024)
            S.o("v", lambda e, psum=psum, poff=poff, w=w, hx=hx, par=par:
                e.reduce_max(
                    NM[:, 4 * par + hx:4 * par + hx + 1],
                    psum[:, poff:poff + w], axis=AX.X, negate=True))
            S.o("v", lambda e: e.tensor_copy(JNK[:, 0:2], QKP[:, 0:2]))
            if nh == 2:
                cnm = S.o("v", lambda e, par=par: e.tensor_tensor(
                    NM[:, 4 * par:4 * par + 1], NM[:, 4 * par:4 * par + 1],
                    NM[:, 4 * par + 1:4 * par + 2], OP.min), chk=True)
            else:
                cnm = S.chk("v")
            st[t]["cnm"] = cnm

        def emit_C(t):
            i, hh = st[t]["i"], st[t]["hh"]
            kv = (i + 1) * 128
            nh = st[t]["nh"]
            hids = st[t]["hids"]
            par = t % 2
            pbi = t % 3
            cnm = st[t]["cnm"]
            S.w("s", "v", cnm)
            S.we("s", probs_free[pbi])
            S.we("s", lc_free[par])
            chk_exp = None
            for hx, hid in enumerate(hids):
                psum, poff = halves[hid]
                w = min(kv - hx * 1024, 1024)
                chk_exp = S.o("s", lambda e, psum=psum, poff=poff, w=w, hx=hx,
                              par=par, pbi=pbi: e.activation(
                                  pbufs[pbi][:, hx * 1024: hx * 1024 + w],
                                  psum[:, poff:poff + w], AF.Exp,
                                  bias=NM[:, 4 * par:4 * par + 1],
                                  accum_out=NM[:, 4 * par + 2 + hx:4 * par + 3 + hx]),
                              chk=True)
                half_free[hid] = ("s", chk_exp)
            st[t]["exp"] = chk_exp

        def emit_D(tq, rnd):
            hh1, i1 = st[tq]["hh"], st[tq]["i"]
            prb = pbufs[tq % 3]
            ntr = i1 + 1
            nrnd = (ntr + 7) // 8
            if rnd >= nrnd:
                return
            if rnd == 0:
                S.w("pe", "s", st[tq]["exp"])
                S.we("pe", trfree[0])
                st[tq]["tr_evac"] = [None, None]
            else:
                S.we("pe", st[tq]["tr_evac"][0])
            lo8, hi8 = rnd * 8, min(ntr, rnd * 8 + 8)
            for j in range(lo8, hi8):
                S.o("pe", lambda e, j=j, lo8=lo8, hi8=hi8, prb=prb:
                    e.matmul(
                        TRA[:, (j - lo8) * 128:(j - lo8 + 1) * 128],
                        prb[:, j * 128:(j + 1) * 128], IDENT[:],
                        start=(j == lo8), stop=(j == hi8 - 1),
                        is_transpose=True))
            ctr = S.chk("pe")
            S.w("s", "pe", ctr)
            tok = ("s", S.o("s", lambda e, lo8=lo8, hi8=hi8, prb=prb: e.copy(
                prb[:, lo8 * 128:hi8 * 128],
                TRA[:, 0:(hi8 - lo8) * 128]), chk=True))
            st[tq]["tr_evac"][rnd] = tok
            if rnd == nrnd - 1:
                trfree[0] = tok

        def emit_EF(tq):
            hh1, i1 = st[tq]["hh"], st[tq]["i"]
            qoff1 = 64 * hh1
            par1 = tq % 2
            pbi = tq % 3
            prb = pbufs[pbi]
            ntr = i1 + 1
            nrnd = (ntr + 7) // 8
            avp = AVP2
            tr_evac = st[tq]["tr_evac"]
            S.we("pe", av_last[0])
            S.we("pe", vt_half[1])
            tp = (0, 64) if hh1 == 1 else None
            for rnd in range(nrnd):
                S.we("pe", tr_evac[rnd])
                lo8, hi8 = rnd * 8, min(ntr, rnd * 8 + 8)
                for j in range(lo8, hi8):
                    S.o("pe", lambda e, j=j, i1=i1, qoff1=qoff1, tp=tp, avp=avp,
                        prb=prb: e.matmul(
                            avp[qoff1:qoff1 + 64, 0:128],
                            WORK[:, VT + j * 128 + qoff1: VT + j * 128 + qoff1 + 64],
                            prb[:, j * 128:(j + 1) * 128],
                            start=(j == 0), stop=(j == i1), tile_position=tp))
            cav = S.chk("pe")
            st[tq]["cav"] = cav
            probs_free[pbi] = ("pe", cav)
            S.w("v", "s", st[tq]["exp"])
            if st[tq]["nh"] == 2:
                clc = S.o("v", lambda e, i1=i1, hh1=hh1, par1=par1: e.tensor_tensor(
                    LC[:, 2 * i1 + hh1: 2 * i1 + hh1 + 1],
                    NM[:, 4 * par1 + 2:4 * par1 + 3],
                    NM[:, 4 * par1 + 3:4 * par1 + 4], OP.add), chk=True)
            else:
                clc = S.o("v", lambda e, i1=i1, hh1=hh1, par1=par1: e.tensor_copy(
                    LC[:, 2 * i1 + hh1: 2 * i1 + hh1 + 1],
                    NM[:, 4 * par1 + 2:4 * par1 + 3]), chk=True)
            lc_free[par1] = ("v", clc)
            S.w("v", "pe", cav)
            av_last[0] = ("v", S.o("v", lambda e, fc=fc, i1=i1, qoff1=qoff1,
                                   avp=avp: e.tensor_copy(
                                       AO.ap()[qoff1:qoff1 + 64,
                                               fc * T + i1 * 128: fc * T + (i1 + 1) * 128],
                                       avp[qoff1:qoff1 + 64, 0:128]), chk=True))

        emit_A(0)
        emit_B(0)
        emit_C(0)
        for t in range(NT):
            if t + 1 < NT:
                emit_A(t + 1)
                emit_B(t + 1)
            emit_D(t, 0)
            if t >= 1:
                emit_EF(t - 1)
            if t + 1 < NT:
                emit_C(t + 1)
            emit_D(t, 1)
        emit_EF(NT - 1)
        scb_pe_free[0] = probs_free[1]
        kp1_free[0] = ("pe", st[NT - 1]["csc"])

        S.o("v", lambda e: e.reciprocal(LC[:], LC[:]))
        S.o("v", lambda e: e.tensor_copy(JNK[:, 0:2], QKP[:, 0:2]))
        clb = S.o("v", lambda e: e.tensor_copy(SCB[:, 0:32], LC[:]), chk=True)
        S.w("pe", "v", clb)
        S.we("pe", av_last[0])
        S.we("pe", half_free[0])
        S.we("pe", trfree[0])
        S.we("pe", trfree[1])
        pbbf = PB.ap().bitcast(BF16)
        for i in range(16):
            S.o("pe", lambda e, i=i, pbbf=pbbf: e.matmul(
                pbbf[0:2, i * 128:(i + 1) * 128], SCB[:, 2 * i:2 * i + 2], IDENT[:],
                start=(i % 8 == 0), stop=(i % 8 == 7), is_transpose=True))
        ctl = S.chk("pe")
        S.w("v", "pe", ctl)
        cle = S.o("v", lambda e, pbbf=pbbf: e.tensor_copy(
            SCA.ap()[0:2, 0:2048], pbbf[0:2, 0:2048]), chk=True)
        S.w("pe", "v", cle)
        for t16 in range(16):
            S.o("pe", lambda e, t16=t16: e.matmul(
                PB[:, t16 * 128:(t16 + 1) * 128], SEL2[:],
                SCA.ap()[0:2, t16 * 128:(t16 + 1) * 128],
                start=(t16 % 4 == 0), stop=(t16 % 4 == 3)))
        cbc = S.chk("pe")
        S.w("v", "pe", cbc)
        cnorm = S.o("v", lambda e, fc=fc: e.tensor_tensor(
            aos(fc), aos(fc), PB[:], OP.mult), chk=True)
        norm_tok[0] = ("v", cnorm)
        half_free[0] = ("v", cnorm)
        trfree[0] = ("v", cnorm)
        trfree[1] = ("v", cnorm)
        prev_cbc[0] = ("pe", cbc)
        prev_ctl[0] = ("pe", ctl)

    S.w("g", "pe", last_q_mm[0])
    tok_wo = S.d("g", "qg", wload(R0, dwo))

    S.w("pe", "qg", tok_wo)
    S.we("pe", norm_tok[0])
    wo_cadd = []
    wo_cts = []
    last_wo_mm = None
    wo_tbufs = [SCA, SCB, None]
    for dc in range(NDC):
        psum = PA if dc % 2 == 0 else PB
        if dc >= 2:
            S.w("pe", "s", wo_cts[dc - 2])
        for kc in range(8):
            for tc in range(4):
                S.o("pe", lambda e, psum=psum, dc=dc, tc=tc, kc=kc: e.matmul(
                    psum[:, tc * 512:(tc + 1) * 512],
                    R0[:, kc * 1024 + dc * 128: kc * 1024 + dc * 128 + 128],
                    aos(kc, tc * 512, (tc + 1) * 512),
                    start=(kc == 0), stop=(kc == 7)))
        cwo = S.chk("pe")
        last_wo_mm = cwo
        S.w("s", "pe", cwo)
        if dc >= 3:
            S.w("s", "v", wo_cadd[dc - 3])
        tbuf = [SCA[:, 0:2048], SCB[:, 0:2048], WORK[:, 0:2048]][dc % 3]
        cts = S.o("s", lambda e, psum=psum, dc=dc, tbuf=tbuf: e.activation(
            tbuf, psum[:], AF.Identity,
            bias=PAR[:, 48 + dc:48 + dc + 1],
            scale=PAR[:, 16 + dc:16 + dc + 1]), chk=True)
        wo_cts.append(cts)
        S.w("v", "s", cts)
        cadd = S.o("v", lambda e, dc=dc, tbuf=tbuf: e.tensor_tensor(
            xs(dc), xs(dc), tbuf, OP.add), chk=True)
        wo_cadd.append(cadd)
    wo_done = wo_cadd[-1]

    if DEBUG:
        S.we("y", norm_tok[0])
        _tok = S.d("y", "qy", lambda e: e.dma_start(out=dbg_ao, in_=AO[:]))
        S.w("v", "qy", _tok)
        S.w("s", "qy", _tok)
        S.w("y", "v", wo_done)
        _tok = S.d("y", "qy", lambda e: e.dma_start(out=dbg_x1, in_=X[:]))
        S.w("v", "qy", _tok)

    ln2_chks, ln2_pa_free = layernorm(32, 24, AO[:, 0:2048], None, None,
                                      pre_waits=[("v", wo_done),
                                                 ("s", wo_cts[-1])],
                                      lnscr=aof[:, 3072:5120])

    S.we("pe", ln_stats_free[0])
    for dc in range(NDC):
        S.w("pe", "s", ln2_chks[dc][0])
        S.w("pe", "qg", tok_gatet)
        for tc in range(4):
            S.o("pe", lambda e, dc=dc, tc=tc: e.matmul(
                PB[0:8, tc * 512:(tc + 1) * 512],
                GATET[:, dc * 8:(dc + 1) * 8],
                hs(dc, tc * 512, (tc + 1) * 512),
                start=(dc == 0), stop=(dc == NDC - 1)))
    cgm = S.chk("pe")
    S.w("v", "pe", cgm)
    cgev = S.o("v", lambda e: e.tensor_copy(SCA.ap()[0:8, 0:2048], PB[0:8, 0:2048]),
               chk=True)
    S.w("pe", "v", cgev)
    S.w("pe", "v", ln2_pa_free)
    pabf = PA.ap().bitcast(BF16)
    for t16 in range(16):
        S.o("pe", lambda e, t16=t16, pabf=pabf: e.matmul(
            pabf[0:128, t16 * 8:(t16 + 1) * 8],
            SCA.ap()[0:8, t16 * 128:(t16 + 1) * 128], IDENT.ap()[0:8, 0:8],
            start=(t16 == 0), stop=(t16 == 15), is_transpose=True))
    gate_done = S.chk("pe")

    sbf = SCB.ap().bitcast(F32)
    gex = sbf[:, 128:256]
    g3 = pabf[0:128, 0:128].rearrange("p (g k) -> p g k", k=8)
    ge3 = gex.rearrange("p (g k) -> p g k", k=8)

    def bc16(ap):
        return ap.rearrange("p (g o) -> p g o", o=1).broadcast_to([128, 16, 8])

    S.w("v", "pe", gate_done)
    S.o("v", lambda e: e.reduce_max(
        GST[:, 0:16].rearrange("p (g o) -> p g o", o=1), g3, axis=AX.X))
    S.o("v", lambda e: e.tensor_copy(JNK[:, 0:2], QKP[:, 0:2]))
    cgs = S.o("v", lambda e: e.tensor_tensor(ge3, g3, bc16(GST[:, 0:16]), OP.subtract),
              chk=True)
    S.w("s", "v", cgs)
    cge = S.o("s", lambda e: e.activation(gex, gex, AF.Exp), chk=True)
    S.w("v", "s", cge)
    S.o("v", lambda e: e.reduce_sum(
        GST[:, 16:32].rearrange("p (g o) -> p g o", o=1), ge3, axis=AX.X))
    S.o("v", lambda e: e.reduce_max(
        GST[:, 0:16].rearrange("p (g o) -> p g o", o=1), ge3, axis=AX.X))
    S.o("v", lambda e: e.tensor_copy(JNK[:, 0:2], QKP[:, 0:2]))
    S.o("v", lambda e: e.tensor_tensor(
        sbf[:, 256:384].rearrange("p (g k) -> p g k", k=8), ge3, bc16(GST[:, 0:16]),
        OP.is_ge))
    S.o("v", lambda e: e.scalar_tensor_tensor(
        sbf[:, 256:384], sbf[:, 256:384], -1e30, gex, OP.mult, OP.add))
    S.o("v", lambda e: e.reduce_max(
        GST[:, 32:48].rearrange("p (g o) -> p g o", o=1),
        sbf[:, 256:384].rearrange("p (g k) -> p g k", k=8), axis=AX.X))
    S.o("v", lambda e: e.tensor_copy(JNK[:, 0:2], QKP[:, 0:2]))
    S.o("v", lambda e: e.tensor_tensor(
        sbf[:, 256:384].rearrange("p (g k) -> p g k", k=8), ge3, bc16(GST[:, 32:48]),
        OP.is_ge))
    S.o("v", lambda e: e.tensor_tensor(gex, gex, sbf[:, 256:384], OP.mult))
    S.o("v", lambda e: e.reciprocal(GST[:, 48:64], GST[:, 16:32]))
    S.o("v", lambda e: e.tensor_copy(JNK[:, 0:2], QKP[:, 0:2]))
    S.o("v", lambda e: e.tensor_tensor(ge3, ge3, bc16(GST[:, 48:64]), OP.mult))
    ccb = S.o("v", lambda e: e.tensor_copy(SCA[:, 0:128], gex), chk=True)
    S.w("pe", "v", ccb)
    for t16 in range(16):
        S.o("pe", lambda e, t16=t16, pabf=pabf: e.matmul(
            pabf[0:8, t16 * 128:(t16 + 1) * 128], SCA[:, t16 * 8:(t16 + 1) * 8],
            IDENT[:], start=(t16 % 8 == 0), stop=(t16 % 8 == 7), is_transpose=True))
    ctc = S.chk("pe")
    S.w("v", "pe", ctc)
    crow_done = S.o("v", lambda e, pabf=pabf: e.tensor_copy(
        CROW[:], pabf[0:8, 0:2048]), chk=True)

    regs = [R1, RW, R0]
    reg_free = [("pe", last_k_mm[0]), ("v", wo_cadd[5]), ("s", wo_cts[-1])]
    wtoks = {}
    drams = {"w1": dw1, "w3": dw3, "w2": dw2}
    sdrams = {"w1": dsw1, "w3": dsw3, "w2": dsw2}

    def sched_wload(m_idx, e_idx, mat):
        ridx = (3 * e_idx + m_idx) % 3
        reg = regs[ridx]
        S.we("g", reg_free[ridx])
        if e_idx < NE:
            src = drams[mat].rearrange("e (c p) o -> e p c o", p=128)[e_idx]
        else:
            src = sdrams[mat].rearrange("(c p) o -> p c o", p=128)
        wtoks[(e_idx, mat)] = S.d("g", "qg", lambda e, reg=reg, src=src: e.dma_start(
            out=reg.rearrange("p (c o) -> p c o", c=8), in_=src))

    for m_idx, mat in enumerate(("w1", "w3", "w2")):
        sched_wload(m_idx, 0, mat)

    pa_free_ffn = ("v", crow_done)
    pb_free_ffn = ("v", cgev)
    he_read = ("s", ln2_chks[-1][0])
    prev_csil = None
    cw2 = None
    stts = []

    for e_idx in range(NE + 1):
        is_sh = e_idx == NE
        rW1 = regs[(3 * e_idx + 0) % 3]
        rW3 = regs[(3 * e_idx + 1) % 3]
        rW2 = regs[(3 * e_idx + 2) % 3]
        if not is_sh:
            S.w("pe", "v", crow_done)
            S.we("pe", pa_free_ffn)
            for tc in range(4):
                S.o("pe", lambda e, e_idx=e_idx, tc=tc: e.matmul(
                    PA[:, tc * 512:(tc + 1) * 512],
                    ESEL[:, e_idx * 128:(e_idx + 1) * 128],
                    CROW[:, tc * 512:(tc + 1) * 512], start=True, stop=True))
            ccb2 = S.chk("pe")
            S.w("v", "pe", ccb2)
            cbev0 = S.o("v", lambda e: e.tensor_copy(
                SCB[:, 0:1024], PA[:, 0:1024]), chk=True)
            cbev1 = S.o("v", lambda e: e.tensor_copy(
                SCB[:, 1024:2048], PA[:, 1024:2048]), chk=True)
            cb_half = [("v", cbev0), ("v", cbev1)]
            pa_free_ffn = ("v", cbev1)
        he_done = None
        he_toks = [None] * 8
        for ic in range(8):
            bcol1 = (e_idx * 8 + ic) if not is_sh else (128 + ic)
            bcol3 = (64 + e_idx * 8 + ic) if not is_sh else (136 + ic)
            S.w("pe", "qg", wtoks[(e_idx, "w1")])
            split_cb = (not is_sh) and ic == 0
            if not split_cb:
                S.we("pe", pa_free_ffn)
            for tc in range(4):
                if split_cb:
                    S.we("pe", cb_half[0] if tc < 2 else cb_half[1])
                for kc in range(8):
                    if e_idx == 0 and ic == 0:
                        S.w("pe", "s", ln2_chks[kc][0])
                    S.o("pe", lambda e, rW1=rW1, ic=ic, tc=tc, kc=kc: e.matmul(
                        PA[:, tc * 512:(tc + 1) * 512],
                        rW1[:, kc * 1024 + ic * 128: kc * 1024 + ic * 128 + 128],
                        hs(kc, tc * 512, (tc + 1) * 512),
                        start=(kc == 0), stop=(kc == 7)))
            ch1 = S.chk("pe")
            S.w("s", "pe", ch1)
            if prev_csil is not None and he_done is not None:
                S.w("s", "v", he_done)
            csil = S.o("s", lambda e, bcol1=bcol1: e.activation(
                SCA[:], PA[:], AF.Silu, bias=BIA[:, bcol1:bcol1 + 1]), chk=True)
            prev_csil = csil
            pa_free_ffn = ("s", csil)
            S.w("pe", "qg", wtoks[(e_idx, "w3")])
            S.we("pe", pb_free_ffn)
            for kc in range(8):
                for tc in range(4):
                    S.o("pe", lambda e, rW3=rW3, ic=ic, tc=tc, kc=kc: e.matmul(
                        PB[:, tc * 512:(tc + 1) * 512],
                        rW3[:, kc * 1024 + ic * 128: kc * 1024 + ic * 128 + 128],
                        hs(kc, tc * 512, (tc + 1) * 512),
                        start=(kc == 0), stop=(kc == 7)))
            ch3 = S.chk("pe")
            S.w("v", "pe", ch3)
            S.w("v", "s", csil)
            S.we("v", he_read)
            chm = S.o("v", lambda e, ic=ic, bcol3=bcol3: e.scalar_tensor_tensor(
                aos(ic), PB[:], BIA[:, bcol3:bcol3 + 1], SCA[:], OP.add, OP.mult),
                chk=True)
            pb_free_ffn = ("v", chm)
            if not is_sh:
                chm = S.o("v", lambda e, ic=ic: e.tensor_tensor(
                    aos(ic), aos(ic), SCB[:], OP.mult), chk=True)
            he_done = chm
            he_toks[ic] = ("v", chm)
        psfree = {0: pa_free_ffn, 1: pb_free_ffn}
        sh_cts = []
        for dc in range(NDC):
            par = dc % 2
            psum = PA if par == 0 else PB
            S.w("pe", "qg", wtoks[(e_idx, "w2")])
            S.we("pe", psfree[par])
            for kc in range(8):
                S.we("pe", he_toks[kc])
                for tc in range(4):
                    S.o("pe", lambda e, rW2=rW2, psum=psum, dc=dc, tc=tc, kc=kc:
                        e.matmul(
                            psum[:, tc * 512:(tc + 1) * 512],
                            rW2[:, kc * 1024 + dc * 128: kc * 1024 + dc * 128 + 128],
                            aos(kc, tc * 512, (tc + 1) * 512),
                            start=(kc == 0), stop=(kc == 7 and not is_sh)))
            if is_sh:
                S.w("pe", "qg", tok_consts)
                S.w("pe", "v", crow_done)
                for tc in range(4):
                    S.o("pe", lambda e, psum=psum, dc=dc, tc=tc: e.matmul(
                        psum[:, tc * 512:(tc + 1) * 512],
                        B2R[0:8, dc * 128:(dc + 1) * 128],
                        CROW[0:8, tc * 512:(tc + 1) * 512],
                        start=False, stop=True))
            cw2 = S.chk("pe")
            if dc == NDC - 1:
                he_read = ("pe", cw2)
            if not is_sh:
                S.w("v", "pe", cw2)
                cst = S.o("v", lambda e, psum=psum, dc=dc: e.scalar_tensor_tensor(
                    xs(dc), psum[:], PAR[:, 40 + dc:40 + dc + 1], xs(dc),
                    OP.mult, OP.add), chk=True)
            else:
                S.w("s", "pe", cw2)
                if dc >= 2:
                    S.w("s", "v", stts[-2])
                tbuf = SCA if par == 0 else SCB
                ct = S.o("s", lambda e, psum=psum, dc=dc, tbuf=tbuf: e.activation(
                    tbuf[:], psum[:], AF.Identity,
                    bias=PAR[:, 56 + dc:56 + dc + 1],
                    scale=PAR[:, 40 + dc:40 + dc + 1]), chk=True)
                sh_cts.append(ct)
                S.w("v", "s", ct)
                cst = S.o("v", lambda e, dc=dc, tbuf=tbuf: e.tensor_tensor(
                    xs(dc), xs(dc), tbuf[:], OP.add), chk=True)
            psfree[par] = ("v", cst)
            stts.append(cst)
        pa_free_ffn = psfree[0]
        pb_free_ffn = psfree[1]
        reg_free[(3 * e_idx + 0) % 3] = ("pe", ch1)
        reg_free[(3 * e_idx + 1) % 3] = ("pe", ch3)
        reg_free[(3 * e_idx + 2) % 3] = ("pe", cw2)
        if e_idx < NE:
            for m_idx, mat in enumerate(("w1", "w3", "w2")):
                sched_wload(m_idx, e_idx + 1, mat)

    for t in stts[-NDC:]:
        S.w("y", "v", t)
    tok_store = S.d("y", "qy", lambda e: e.dma_start(
        out=dout.rearrange("(c p) t -> p c t", p=128),
        in_=X[:].rearrange("p (c t) -> p c t", c=NDC)))
    S.w("y", "qy", tok_store)

    with (
        nc.allow_low_precision("bf16 intermediates are deliberate; fp32 where it matters"),
        nc.semaphore("sem_pe") as sem_pe,
        nc.semaphore("sem_v") as sem_v,
        nc.semaphore("sem_s") as sem_s,
        nc.semaphore("sem_g") as sem_g,
        nc.semaphore("sem_y") as sem_y,
        nc.semaphore("sem_qy") as sem_qy,
        nc.semaphore("sem_qg") as sem_qg,
        nc.Block() as block,
    ):
        sems = {"pe": sem_pe, "v": sem_v, "s": sem_s, "g": sem_g, "y": sem_y,
                "qy": sem_qy, "qg": sem_qg}

        @block.tensor
        def _(eng):
            S.replay(eng, "pe", sems)

        @block.vector
        def _(eng):
            S.replay(eng, "v", sems)

        @block.scalar
        def _(eng):
            S.replay(eng, "s", sems)

        @block.gpsimd
        def _(eng):
            S.replay(eng, "g", sems)

        @block.sync
        def _(eng):
            S.replay(eng, "y", sems)

    return nc


_CACHE = {}
_LAST_RES = None


def kernel(**inputs):
    N = 8
    x = np.asarray(inputs["L0_attn_x"])
    f32 = np.float32

    def c_(a):
        return np.ascontiguousarray(a, dtype=f32)

    wq = c_(np.asarray(inputs["L0_attn_Wq"]).T)
    wk = c_(np.asarray(inputs["L0_attn_Wk"]).T)
    wv = c_(np.asarray(inputs["L0_attn_Wv"]).T)
    wo = c_(np.asarray(inputs["L0_attn_Wo"]).T)
    w1 = c_(inputs["L0_ffn_w1"])
    w3 = c_(inputs["L0_ffn_w3"])
    w2 = c_(inputs["L0_ffn_w2"])
    sw1 = c_(inputs["L0_ffn_sw1"])
    sw3 = c_(inputs["L0_ffn_sw3"])
    sw2 = c_(inputs["L0_ffn_sw2"])
    wmod = c_(inputs["L0_wmod"])
    cfull = np.asarray(inputs["L0_c"], dtype=f32)

    gq = np.asarray(inputs["L0_attn_gq"], dtype=f32)
    bq = np.asarray(inputs["L0_attn_bq"], dtype=f32)
    gk = np.asarray(inputs["L0_attn_gk"], dtype=f32)
    bk = np.asarray(inputs["L0_attn_bk"], dtype=f32)
    qkp = c_(np.stack([np.tile(gq, 2), np.tile(bq, 2),
                       np.tile(gk, 2), np.tile(bk, 2),
                       np.full(128, 0.5 * np.log(64.0), dtype=f32),
                       np.full(128, 0.5 * np.log(1024.0), dtype=f32)], axis=1))

    b1 = np.asarray(inputs["L0_ffn_b1"], dtype=f32)
    b3 = np.asarray(inputs["L0_ffn_b3"], dtype=f32)
    b2 = c_(inputs["L0_ffn_b2"])
    sb1 = np.asarray(inputs["L0_ffn_sb1"], dtype=f32)
    sb3 = np.asarray(inputs["L0_ffn_sb3"], dtype=f32)
    sb2 = np.asarray(inputs["L0_ffn_sb2"], dtype=f32)
    bo = np.asarray(inputs["L0_attn_bo"], dtype=f32)
    biases = np.zeros((128, 160), dtype=f32)
    biases[:, 0:64] = b1.reshape(8, 8, 128).transpose(2, 0, 1).reshape(128, 64)
    biases[:, 64:128] = b3.reshape(8, 8, 128).transpose(2, 0, 1).reshape(128, 64)
    biases[:, 128:136] = sb1.reshape(8, 128).T
    biases[:, 136:144] = sb3.reshape(8, 128).T
    biases[:, 144:152] = bo.reshape(8, 128).T
    biases[:, 152:160] = sb2.reshape(8, 128).T
    biases = c_(biases)

    gatet = c_(np.asarray(inputs["L0_ffn_gate_w"], dtype=f32).T
               .reshape(8, 128, 8).transpose(1, 0, 2).reshape(128, 64))

    allones = np.ones((128, 128), dtype=f32)
    blockones = np.zeros((128, 128), dtype=f32)
    blockones[0:64, 0:64] = 1.0
    blockones[64:128, 64:128] = 1.0
    ident = np.eye(128, dtype=f32)
    maskd = np.triu(np.full((128, 128), -1e30, dtype=f32), 1)
    sel2 = np.zeros((2, 128), dtype=f32)
    sel2[0, 0:64] = 1.0
    sel2[1, 64:128] = 1.0
    esel = np.zeros((8, 1024), dtype=f32)
    for e in range(8):
        esel[e, e * 128:(e + 1) * 128] = 1.0
    ones1 = np.ones((1, 128), dtype=f32)
    epsr = np.zeros((1, 1024), dtype=f32)
    epsr[0, 0:512] = 64 * 1e-5
    epsr[0, 512:1024] = 1024 * 1e-5

    if "nc" not in _CACHE:
        _CACHE["nc"] = build()
    nc = _CACHE["nc"]

    in_maps = []
    for i in range(N):
        in_maps.append({
            "x": c_(x[i].T),
            "wq": wq, "wk": wk, "wv": wv, "wo": wo,
            "w1": w1, "w3": w3, "w2": w2,
            "sw1": sw1, "sw3": sw3, "sw2": sw2,
            "wmod": wmod,
            "cc": c_(cfull[i].reshape(8, 128).T),
            "qkp": qkp, "biases": biases, "gatet": gatet, "b2r": b2,
            "callones": allones, "cblockones": blockones, "cident": ident,
            "cmask": maskd, "csel2": sel2, "cesel": esel,
            "cones1": ones1, "cepsr": epsr, "cidentf": ident,
        })
    res = run_bass_kernel_spmd(nc, in_maps, core_ids=list(range(N)))
    global _LAST_RES
    _LAST_RES = res
    out = np.stack([np.ascontiguousarray(r["out"].T) for r in res.results], axis=0)
    return out.astype(np.float32)


# version 33
# speedup vs baseline: 1.0891x; 1.0042x over previous
import math
import numpy as np
import ml_dtypes
import concourse.bass as bass
import concourse.mybir as mybir
from concourse.bass_utils import run_bass_kernel_spmd

F32 = mybir.dt.float32
BF16 = mybir.dt.bfloat16
AF = mybir.ActivationFunctionType
OP = mybir.AluOpType
AX = mybir.AxisListType

T = 2048
D = 1024
NDC = 8
NFC = 8
NE = 8
DH = 64
EPS = 1e-5

DEBUG = False

ENGS = ("pe", "v", "s", "g", "y")
QUEUES = ("qy", "qg")


class Sch:

    def __init__(self):
        self.seq = {k: [] for k in ENGS}
        self.n = {k: 0 for k in ENGS}
        self.nd = {k: 0 for k in QUEUES}
        self.wm = {k: {} for k in ENGS}

    def w(self, eng, sem, val):
        if not val or sem == eng or val <= self.wm[eng].get(sem, 0):
            return
        self.wm[eng][sem] = val
        self.seq[eng].append(("w", sem, val))

    def we(self, eng, et):
        if et is not None:
            self.w(eng, et[0], et[1])

    def o(self, eng, fn, chk=False):
        self.seq[eng].append(("o", fn, chk))
        if chk:
            self.n[eng] += 1
            return self.n[eng]
        return None

    def chk(self, eng):
        for idx in range(len(self.seq[eng]) - 1, -1, -1):
            it = self.seq[eng][idx]
            if it[0] == "o":
                if not it[2]:
                    self.seq[eng][idx] = ("o", it[1], True)
                    self.n[eng] += 1
                return self.n[eng]
            if it[0] == "d":
                raise RuntimeError("chk would land on a dma")
        raise RuntimeError("no op to chk")

    def d(self, eng, q, fn):
        self.seq[eng].append(("d", fn, q))
        self.nd[q] += 1
        return self.nd[q] * 16

    def replay(self, eng_obj, name, sems):
        for it in self.seq[name]:
            if it[0] == "w":
                eng_obj.wait_ge(sems[it[1]], it[2])
            elif it[0] == "o":
                inst = it[1](eng_obj)
                if it[2]:
                    inst.then_inc(sems[name], 1)
            else:
                it[1](eng_obj).then_inc(sems[it[2]], 16)


def build():
    nc = bass.Bass("TRN2", target_bir_lowering=False)

    dx = nc.dram_tensor("x", [D, T], F32, kind="ExternalInput").ap()
    dwq = nc.dram_tensor("wq", [D, D], F32, kind="ExternalInput").ap()
    dwk = nc.dram_tensor("wk", [D, D], F32, kind="ExternalInput").ap()
    dwv = nc.dram_tensor("wv", [D, D], F32, kind="ExternalInput").ap()
    dwo = nc.dram_tensor("wo", [D, D], F32, kind="ExternalInput").ap()
    dw1 = nc.dram_tensor("w1", [NE, D, D], F32, kind="ExternalInput").ap()
    dw3 = nc.dram_tensor("w3", [NE, D, D], F32, kind="ExternalInput").ap()
    dw2 = nc.dram_tensor("w2", [NE, D, D], F32, kind="ExternalInput").ap()
    dsw1 = nc.dram_tensor("sw1", [D, D], F32, kind="ExternalInput").ap()
    dsw3 = nc.dram_tensor("sw3", [D, D], F32, kind="ExternalInput").ap()
    dsw2 = nc.dram_tensor("sw2", [D, D], F32, kind="ExternalInput").ap()
    dwmod = nc.dram_tensor("wmod", [D, 6144], F32, kind="ExternalInput").ap()
    dcc = nc.dram_tensor("cc", [128, 8], F32, kind="ExternalInput").ap()
    dqkp = nc.dram_tensor("qkp", [128, 6], F32, kind="ExternalInput").ap()
    dbias = nc.dram_tensor("biases", [128, 160], F32, kind="ExternalInput").ap()
    dgatet = nc.dram_tensor("gatet", [128, 64], F32, kind="ExternalInput").ap()
    db2r = nc.dram_tensor("b2r", [8, 1024], F32, kind="ExternalInput").ap()
    dallones = nc.dram_tensor("callones", [128, 128], F32, kind="ExternalInput").ap()
    dblockones = nc.dram_tensor("cblockones", [128, 128], F32, kind="ExternalInput").ap()
    dident = nc.dram_tensor("cident", [128, 128], F32, kind="ExternalInput").ap()
    dmask = nc.dram_tensor("cmask", [128, 128], F32, kind="ExternalInput").ap()
    dsel2 = nc.dram_tensor("csel2", [2, 128], F32, kind="ExternalInput").ap()
    dones1 = nc.dram_tensor("cones1", [1, 128], F32, kind="ExternalInput").ap()
    didentf = nc.dram_tensor("cidentf", [128, 128], F32, kind="ExternalInput").ap()
    depsr = nc.dram_tensor("cepsr", [1, 1024], F32, kind="ExternalInput").ap()
    desel = nc.dram_tensor("cesel", [8, 1024], F32, kind="ExternalInput").ap()
    dout = nc.dram_tensor("out", [D, T], F32, kind="ExternalOutput").ap()
    if DEBUG:
        dbg_par = nc.dram_tensor("dbg_par", [128, 64], F32, kind="ExternalOutput").ap()
        dbg_h = nc.dram_tensor("dbg_h", [128, NDC * T], BF16, kind="ExternalOutput").ap()
        dbg_qkv = nc.dram_tensor("dbg_qkv", [128, 8192], BF16, kind="ExternalOutput").ap()
        dbg_ao = nc.dram_tensor("dbg_ao", [128, NDC * T], BF16, kind="ExternalOutput").ap()
        dbg_x1 = nc.dram_tensor("dbg_x1", [128, NDC * T], F32, kind="ExternalOutput").ap()
        dbg_crow = nc.dram_tensor("dbg_crow", [8, T], BF16, kind="ExternalOutput").ap()
        dbg_h2 = nc.dram_tensor("dbg_h2", [128, NDC * T], BF16, kind="ExternalOutput").ap()
        dbg_ao2 = nc.dram_tensor("dbg_ao2", [128, T], BF16, kind="ExternalOutput").ap()
        dbg_lc = nc.dram_tensor("dbg_lc", [128, 32], F32, kind="ExternalOutput").ap()
        dbg_pt = nc.dram_tensor("dbg_pt", [128, T], BF16, kind="ExternalOutput").ap()
        dbg_isge = nc.dram_tensor("dbg_isge", [128, 16], F32, kind="ExternalOutput").ap()
        dbg_sc = nc.dram_tensor("dbg_sc", [128, 2048], F32, kind="ExternalOutput").ap()
        dbg_nm = nc.dram_tensor("dbg_nm", [128, 4], F32, kind="ExternalOutput").ap()
        dbg_pr = nc.dram_tensor("dbg_pr", [128, 2048], BF16, kind="ExternalOutput").ap()

    X = nc.alloc_sbuf_tensor("X", [128, NDC * T], F32)
    HH = nc.alloc_sbuf_tensor("HH", [128, NDC * T], BF16)
    AO = nc.alloc_sbuf_tensor("AO", [128, NDC * T], BF16)
    WB = nc.alloc_sbuf_tensor("WB", [128, 2 * 8192], BF16)
    WORK = nc.alloc_sbuf_tensor("WORK", [128, 8192], BF16)
    SCA = nc.alloc_sbuf_tensor("SCA", [128, T], BF16)
    SCB = nc.alloc_sbuf_tensor("SCB", [128, T], BF16)
    WVS = nc.alloc_sbuf_tensor("WVS", [128, 2048], BF16)
    KP1 = nc.alloc_sbuf_tensor("KP1", [128, 2048], BF16)
    CROW = nc.alloc_sbuf_tensor("CROW", [8, T], BF16)
    QKP = nc.alloc_sbuf_tensor("QKP", [128, 6], F32)
    CC = nc.alloc_sbuf_tensor("CC", [128, 8], F32)
    SCS = nc.alloc_sbuf_tensor("SCS", [128, 8], F32)
    BIA = nc.alloc_sbuf_tensor("BIA", [128, 160], F32)
    PAR = nc.alloc_sbuf_tensor("PAR", [128, 64], F32)
    GATET = nc.alloc_sbuf_tensor("GATET", [128, 64], BF16)
    B2R = nc.alloc_sbuf_tensor("B2R", [8, 1024], BF16)
    ONES = nc.alloc_sbuf_tensor("ONES", [128, 128], BF16)
    BONES = nc.alloc_sbuf_tensor("BONES", [128, 128], BF16)
    IDENT = nc.alloc_sbuf_tensor("IDENT", [128, 128], BF16)
    MASK = nc.alloc_sbuf_tensor("MASK", [128, 128], BF16)
    SEL2 = nc.alloc_sbuf_tensor("SEL2", [2, 128], BF16)
    ONES1 = nc.alloc_sbuf_tensor("ONES1", [1, 128], BF16)
    IDENTF = nc.alloc_sbuf_tensor("IDENTF", [128, 128], F32)
    EPSR = nc.alloc_sbuf_tensor("EPSR", [1, 1024], BF16)
    ESEL = nc.alloc_sbuf_tensor("ESEL", [8, 1024], BF16)
    NM = nc.alloc_sbuf_tensor("NM", [128, 8], F32)
    LC = nc.alloc_sbuf_tensor("LC", [128, 32], F32)
    GST = nc.alloc_sbuf_tensor("GST", [128, 64], F32)
    JNK = nc.alloc_sbuf_tensor("JNK", [128, 4], F32)

    PA = nc.alloc_psum_tensor("PA", [128, 2048], F32)
    PB = nc.alloc_psum_tensor("PB", [128, 2048], F32)

    S = Sch()
    aof = AO.ap().bitcast(F32)

    def xs(dc, lo=0, hi=T):
        return X[:, dc * T + lo: dc * T + hi]

    def hs(dc, lo=0, hi=T):
        return HH[:, dc * T + lo: dc * T + hi]

    def aos(dc, lo=0, hi=T):
        return AO[:, dc * T + lo: dc * T + hi]

    tok_x = None
    tok_smalls = None
    for sb_t, dr in ((QKP, dqkp), (CC, dcc), (BIA, dbias), (IDENTF, didentf)):
        tok_smalls = S.d("y", "qy", lambda e, a=sb_t, b=dr: e.dma_start(out=a[:], in_=b[:]))
    tok_gatet = S.d("g", "qg", lambda e: e.dma_start(out=GATET[:], in_=dgatet[:]))
    tok_consts = None
    for sb_t, dr in ((ONES, dallones), (BONES, dblockones), (IDENT, dident),
                     (MASK, dmask), (SEL2, dsel2), (ESEL, desel), (B2R, db2r),
                     (ONES1, dones1), (EPSR, depsr)):
        tok_consts = S.d("g", "qg", lambda e, a=sb_t, b=dr: e.dma_start(out=a[:], in_=b[:]))

    def wload(dst_ap, dram_ap):
        return lambda e: e.dma_start(
            out=dst_ap.rearrange("p (c o) -> p c o", c=8),
            in_=dram_ap.rearrange("(c p) o -> p c o", p=128))

    R0 = WB[:, 0:8192]
    R1 = WB[:, 8192:16384]
    RW = WORK[:, 0:8192]
    tok_wq = S.d("g", "qg", wload(R0, dwq))
    tok_wk = S.d("g", "qg", wload(R1, dwk))

    def wv_piece(fc):
        pc = fc % 2
        return S.d("g", "qg", lambda e, fc=fc, pc=pc: e.dma_start(
            out=WVS[:, pc * 1024:(pc + 1) * 1024].rearrange("p (c o) -> p c o", c=8),
            in_=dwv.rearrange("(c p) o -> p c o", p=128)[:, :, fc * 128:(fc + 1) * 128]))

    tok_wv = [wv_piece(0), wv_piece(1)]

    S.w("s", "qy", tok_smalls)
    chk_scs = S.o("s", lambda e: e.activation(SCS[:], CC[:], AF.Silu), chk=True)

    NPC = 12
    PW = 6144 // NPC
    modp = PB[:, 2000:2048]
    ROWB = nc.alloc_sbuf_tensor("ROWB", [1, 512], F32)
    piece_tok = [None] * NPC
    piece_done = [None] * NPC
    ev_tok = None
    ev_toks = []
    tr_toks = []
    first_tr = True
    for p in range(NPC):
        if p >= 2:
            S.w("y", "pe", piece_done[p - 2])
        piece_tok[p] = S.d("y", "qy", lambda e, p=p: e.dma_start(
            out=aof[:, (p % 2) * 4096:(p % 2) * 4096 + 4096].rearrange(
                "p (c o) -> p c o", c=8),
            in_=dwmod.rearrange("(c p) o -> p c o", p=128)[:, :, p * PW:(p + 1) * PW]))
        if p == 3:
            tok_x = S.d("y", "qy", lambda e: e.dma_start(
                out=X[:].rearrange("p (c t) -> p c t", c=NDC),
                in_=dx.rearrange("(c p) t -> p c t", p=128)))
        S.w("pe", "qy", piece_tok[p])
        if p == 0:
            S.w("pe", "s", chk_scs)
        modrow = PB[0:1, (p % 2) * 512:(p % 2) * 512 + 512]
        if len(ev_toks) >= 2:
            S.w("pe", "v", ev_toks[-2])
        for kc in range(8):
            S.o("pe", lambda e, p=p, kc=kc, modrow=modrow: e.matmul(
                modrow, SCS[:, kc:kc + 1],
                aof[:, (p % 2) * 4096 + kc * PW: (p % 2) * 4096 + kc * PW + PW],
                start=(kc == 0), stop=(kc == 7)))
        cmm = S.chk("pe")
        piece_done[p] = cmm
        S.w("v", "pe", cmm)
        if tr_toks:
            S.w("v", "pe", tr_toks[-1])
        ev_tok = S.o("v", lambda e, modrow=modrow: e.tensor_copy(ROWB[:], modrow),
                     chk=True)
        ev_toks.append(ev_tok)
        S.w("pe", "v", ev_tok)
        for cb in range(4):
            col = p * 4 + cb
            S.o("pe", lambda e, cb=cb, col=col, first=first_tr: e.matmul(
                modp[:, col:col + 1], ROWB[0:1, cb * 128:(cb + 1) * 128],
                IDENTF[0:1, 0:1], start=first, stop=(col == 47),
                is_transpose=True))
            first_tr = False
        tr_toks.append(S.chk("pe"))
    piece_done[NPC - 1] = tr_toks[-1]
    chk_modmm = piece_done[-1]

    S.w("v", "pe", tr_toks[3])
    S.w("v", "qy", tok_smalls)

    def asm1(e):
        e.tensor_copy(PAR[:, 0:8], modp[:, 0:8])
        return e.tensor_scalar_add(PAR[:, 8:16], modp[:, 8:16], 1.0)
    chk_par1 = S.o("v", asm1, chk=True)
    S.w("v", "pe", chk_modmm)

    def asm(e):
        e.tensor_copy(PAR[:, 24:32], modp[:, 24:32])
        e.tensor_scalar_add(PAR[:, 32:40], modp[:, 32:40], 1.0)
        A, B = 0, 32
        for base, src in ((A, 16), (B, 40)):
            e.tensor_single_scalar(GST[:, base:base + 8], modp[:, src:src + 8],
                                   0.0, OP.is_ge)
        for base in (A, B):
            e.tensor_scalar_mul(GST[:, base + 8:base + 16], GST[:, base:base + 8], 2.0)
        for base in (A, B):
            e.tensor_single_scalar(GST[:, base + 8:base + 16],
                                   GST[:, base + 8:base + 16], 1.0, OP.subtract)
        for base, src in ((A, 16), (B, 40)):
            e.tensor_scalar_mul(GST[:, base + 16:base + 24], modp[:, src:src + 8], -1.0)
        for base in (A, B):
            e.tensor_scalar_max(GST[:, base + 16:base + 24],
                                GST[:, base + 16:base + 24], 0.25)
        for base, src in ((A, 16), (B, 40)):
            e.tensor_scalar_max(GST[:, base + 24:base + 32], modp[:, src:src + 8], 0.25)
        for base in (A, B):
            e.tensor_max(GST[:, base + 16:base + 24], GST[:, base + 16:base + 24],
                         GST[:, base + 24:base + 32])
        for base, dst in ((A, 16), (B, 40)):
            e.tensor_mul(PAR[:, dst:dst + 8], GST[:, base + 8:base + 16],
                         GST[:, base + 16:base + 24])
        e.tensor_mul(PAR[:, 48:56], PAR[:, 16:24], BIA[:, 144:152])
        if DEBUG:
            e.tensor_single_scalar(GST[:, 32:40], modp[:, 16:24], 0.0, OP.is_ge)
            e.tensor_scalar_mul(GST[:, 40:48], GST[:, 32:40], 2.0)
            e.tensor_single_scalar(GST[:, 40:48], GST[:, 40:48], 1.0, OP.subtract)
        return e.tensor_mul(PAR[:, 56:64], PAR[:, 40:48], BIA[:, 152:160])
    chk_par = S.o("v", asm, chk=True)
    S.w("s", "v", chk_par)

    ln_stats_free = [None]

    def layernorm(sc_col, bi_col, t_slice, h2f32_slice, h2_cb, pre_waits, lnscr):
        for eng in ("pe", "v", "s"):
            for et in pre_waits:
                S.we(eng, et)
        S.w("v", "qy", tok_x)
        S.w("s", "qy", tok_x)
        last_mm = None
        for dc in range(NDC):
            if last_mm is not None:
                S.w("v", "pe", last_mm)
                S.w("s", "pe", last_mm)
            ccp = S.o("v", lambda e, dc=dc: e.tensor_copy(SCA[:], xs(dc)), chk=True)
            csq = S.o("s", lambda e, dc=dc: e.activation(SCB[:], xs(dc), AF.Square),
                      chk=True)
            S.w("pe", "v", ccp)
            S.w("pe", "s", csq)
            S.w("pe", "qg", tok_consts)
            for tc in range(4):
                S.o("pe", lambda e, dc=dc, tc=tc: e.matmul(
                    PA[:, tc * 512:(tc + 1) * 512], ONES[:],
                    SCA[:, tc * 512:(tc + 1) * 512],
                    start=(dc == 0), stop=(dc == NDC - 1)))
                S.o("pe", lambda e, dc=dc, tc=tc: e.matmul(
                    PB[:, tc * 512:(tc + 1) * 512], ONES[:],
                    SCB[:, tc * 512:(tc + 1) * 512],
                    start=(dc == 0), stop=False))
            last_mm = S.chk("pe")
        for tc in range(4):
            S.o("pe", lambda e, tc=tc: e.matmul(
                PB[:, tc * 512:(tc + 1) * 512], ONES1[0:1, :],
                EPSR[0:1, 512:1024], start=False, stop=True))
        last_mm = S.chk("pe")
        S.w("v", "pe", last_mm)
        S.o("v", lambda e: e.tensor_scalar_mul(SCA[:], PA[:], 1.0 / D))
        ct1 = S.o("v", lambda e: e.tensor_tensor(SCB[:], SCA[:], PA[:], OP.mult),
                  chk=True)
        crstd = None
        cln_prev = None
        for h in (0, 1):
            if cln_prev is not None:
                S.w("v", "s", cln_prev)
            ct2 = S.o("v", lambda e, h=h: e.scalar_tensor_tensor(
                lnscr[:, 0:1024], SCB[:, h * 1024:(h + 1) * 1024], -1.0,
                PB[:, h * 1024:(h + 1) * 1024], OP.mult, OP.add), chk=True)
            S.w("s", "v", ct2)
            cln_prev = S.o("s", lambda e: e.activation(
                lnscr[:, 1024:2048], lnscr[:, 0:1024], AF.Ln), chk=True)
            crstd = S.o("s", lambda e, h=h: e.activation(
                SCB[:, h * 1024:(h + 1) * 1024], lnscr[:, 1024:2048], AF.Exp,
                scale=-0.5, bias=QKP[:, 5:6]), chk=True)
            ln_stats_free[0] = ("v", ct2)
        h_chks = []
        S.w("v", "s", crstd)
        cu = None
        prev_gate_tok = None
        ubufs = [SCA.ap()[:, 0:2048], KP1.ap()[:, 0:2048]]
        tbufs = [t_slice, WVS.ap()[:, 0:2048]]
        for dc in range(NDC):
            p = dc % 2
            if len(h_chks) >= 2:
                S.w("v", "s", h_chks[-2][1] or h_chks[-2][0])
            S.o("v", lambda e, dc=dc, p=p: e.scalar_tensor_tensor(
                ubufs[p], PA[:], -1.0 / D, xs(dc), OP.mult, OP.add))
            cu = S.o("v", lambda e, p=p: e.tensor_tensor(tbufs[p], ubufs[p], SCB[:],
                                                         OP.mult), chk=True)
            S.w("s", "v", cu)
            ch = S.o("s", lambda e, dc=dc, p=p: e.activation(
                hs(dc), tbufs[p], AF.Identity,
                bias=PAR[:, bi_col + dc:bi_col + dc + 1],
                scale=PAR[:, sc_col + dc:sc_col + dc + 1]), chk=True)
            ch2 = None
            if h2f32_slice is not None:
                if prev_gate_tok is not None:
                    S.w("s", "pe", prev_gate_tok)
                ch2 = S.o("s", lambda e, dc=dc, p=p: e.activation(
                    h2f32_slice, tbufs[p], AF.Identity,
                    bias=PAR[:, bi_col + dc:bi_col + dc + 1],
                    scale=PAR[:, sc_col + dc:sc_col + dc + 1]), chk=True)
                prev_gate_tok = h2_cb(dc, ch2)
            h_chks.append((ch, ch2, dc))
        return h_chks, cu

    if DEBUG:
        S.w("y", "v", chk_par)
        S.d("y", "qy", lambda e: e.dma_start(out=dbg_par, in_=PAR[:]))
        S.d("y", "qy", lambda e: e.dma_start(out=dbg_isge, in_=GST[:, 32:48]))

    workf = WORK.ap().bitcast(F32)

    def ln1_halved():
        chks = [[None, None] for _ in range(NDC)]
        S.w("v", "qy", tok_x)
        S.w("s", "qy", tok_x)
        S.w("s", "v", chk_par1)
        last_mm = None
        last_u = None
        prev_chs = [None, None]
        last_mms = [None, None]
        for th in (0, 1):
            tlo = th * 1024
            for dc in range(NDC):
                sb = (dc % 2) * 1024
                if last_mms[dc % 2] is not None:
                    S.w("v", "pe", last_mms[dc % 2])
                    S.w("s", "pe", last_mms[dc % 2])
                ccp = S.o("v", lambda e, dc=dc, tlo=tlo, sb=sb: e.tensor_copy(
                    SCA[:, sb:sb + 1024], xs(dc, tlo, tlo + 1024)), chk=True)
                csq = S.o("s", lambda e, dc=dc, tlo=tlo, sb=sb: e.activation(
                    SCB[:, sb:sb + 1024], xs(dc, tlo, tlo + 1024), AF.Square),
                    chk=True)
                S.w("pe", "v", ccp)
                S.w("pe", "s", csq)
                S.w("pe", "qg", tok_consts)
                if th == 1 and dc == 0:
                    S.w("pe", "v", last_u)
                for tc in (0, 1):
                    S.o("pe", lambda e, tc=tc, dc=dc, sb=sb: e.matmul(
                        PA[:, tc * 512:(tc + 1) * 512], ONES[:],
                        SCA[:, sb + tc * 512:sb + (tc + 1) * 512],
                        start=(dc == 0), stop=(dc == NDC - 1)))
                    S.o("pe", lambda e, tc=tc, dc=dc, sb=sb: e.matmul(
                        PA[:, 1024 + tc * 512:1024 + (tc + 1) * 512], ONES[:],
                        SCB[:, sb + tc * 512:sb + (tc + 1) * 512],
                        start=(dc == 0), stop=False))
                last_mms[dc % 2] = S.chk("pe")
            for tc in (0, 1):
                S.o("pe", lambda e, tc=tc: e.matmul(
                    PA[:, 1024 + tc * 512:1024 + (tc + 1) * 512], ONES1[0:1, :],
                    EPSR[0:1, 512:1024], start=False, stop=True))
            last_mm = S.chk("pe")
            S.w("v", "pe", last_mm)
            S.o("v", lambda e: e.tensor_scalar_mul(SCA[:, 0:1024], PA[:, 0:1024],
                                                   1.0 / D))
            S.o("v", lambda e: e.tensor_tensor(SCB[:, 0:1024], SCA[:, 0:1024],
                                               PA[:, 0:1024], OP.mult))
            ct2 = S.o("v", lambda e: e.scalar_tensor_tensor(
                workf[:, 1024:2048], SCB[:, 0:1024], -1.0, PA[:, 1024:2048],
                OP.mult, OP.add), chk=True)
            S.w("s", "v", ct2)
            S.o("s", lambda e: e.activation(workf[:, 2048:3072], workf[:, 1024:2048],
                                            AF.Ln))
            crstd = S.o("s", lambda e: e.activation(
                SCB[:, 0:1024], workf[:, 2048:3072], AF.Exp,
                scale=-0.5, bias=QKP[:, 5:6]), chk=True)
            S.w("v", "s", crstd)
            for dc in range(NDC):
                b = (dc % 2) * 1024
                if prev_chs[dc % 2] is not None:
                    S.w("v", "s", prev_chs[dc % 2])
                S.o("v", lambda e, dc=dc, tlo=tlo, b=b: e.scalar_tensor_tensor(
                    SCA[:, b:b + 1024], PA[:, 0:1024], -1.0 / D,
                    xs(dc, tlo, tlo + 1024), OP.mult, OP.add))
                cu = S.o("v", lambda e, b=b: e.tensor_tensor(
                    WORK[:, b:b + 1024], SCA[:, b:b + 1024], SCB[:, 0:1024],
                    OP.mult), chk=True)
                last_u = cu
                S.w("s", "v", cu)
                prev_chs[dc % 2] = S.o("s", lambda e, dc=dc, tlo=tlo, b=b:
                    e.activation(
                        hs(dc, tlo, tlo + 1024), WORK[:, b:b + 1024], AF.Identity,
                        bias=PAR[:, 0 + dc:0 + dc + 1],
                        scale=PAR[:, 8 + dc:8 + dc + 1]), chk=True)
                chks[dc][th] = prev_chs[dc % 2]
        return chks, last_u

    ln1_hchks, ln1_pa_free = ln1_halved()

    QR, KR, VR, VT = 0, 2048, 4096, 6144
    TRA = PB[:, 1024:1536].bitcast(BF16)
    TRB = PB[:, 1536:2048].bitcast(BF16)
    AVP = PB[:, 1024:1152]
    AVP2 = PB[:, 1536:1664]
    trfree = [None, None]

    halves = [(PB, 0), (PA, 0), (PA, 1024)]
    half_free = [None, None, None]
    hctr = [0]
    tr_last = [None]
    av_last = [None]
    norm_tok = [None]
    ctm_k = [None]
    prev_cbc = [None]
    prev_ctl = [None]
    scb_pe_free = [None]
    kp1_free = [None]
    ao_guard = [False]
    last_q_mm = [None]
    last_k_mm = [None]

    ckp1z = S.o("v", lambda e: e.memset(KP1.ap()[0:64, 0:2048], 0.0), chk=True)

    for fc in range(NFC):
        chk_evac = {}
        for mat, psum, wreg, wtok, dst in (("q", PA, R0, tok_wq, QR),
                                           ("k", PB, R1, tok_wk, KR)):
            S.w("pe", "qg", wtok)
            if fc == 0:
                S.w("pe", "v", ln1_pa_free)
            else:
                if mat == "q":
                    S.we("pe", half_free[1])
                    S.we("pe", half_free[2])
                else:
                    S.we("pe", half_free[0])
            for kc in range(8):
                if fc == 0:
                    S.w("pe", "s", ln1_hchks[kc][0])
                    S.w("pe", "s", ln1_hchks[kc][1])
                for tc in range(4):
                    S.o("pe", lambda e, psum=psum, wreg=wreg, tc=tc, kc=kc, fc=fc:
                        e.matmul(
                            psum[:, tc * 512:(tc + 1) * 512],
                            wreg[:, kc * 1024 + fc * 128: kc * 1024 + fc * 128 + 128],
                            hs(kc, tc * 512, (tc + 1) * 512),
                            start=(kc == 0), stop=(kc == 7)))
            cm = S.chk("pe")
            if fc == NFC - 1:
                (last_q_mm if mat == "q" else last_k_mm)[0] = cm
            S.w("v", "pe", cm)
            chk_evac[mat] = S.o("v", lambda e, psum=psum, dst=dst: e.tensor_copy(
                WORK[:, dst:dst + 2048], psum[:]), chk=True)
        S.w("pe", "v", chk_evac["q"])
        S.w("pe", "qg", tok_wv[fc])
        pc = fc % 2
        for kc in range(8):
            for tc in range(4):
                S.o("pe", lambda e, tc=tc, kc=kc, pc=pc: e.matmul(
                    PA[:, tc * 512:(tc + 1) * 512],
                    WVS[:, pc * 1024 + kc * 128: pc * 1024 + kc * 128 + 128],
                    hs(kc, tc * 512, (tc + 1) * 512),
                    start=(kc == 0), stop=(kc == 7)))
        cmv = S.chk("pe")
        if fc + 2 < NFC:
            S.w("g", "pe", cmv)
            tok_wv.append(wv_piece(fc + 2))
        S.w("v", "pe", cmv)
        cev0 = S.o("v", lambda e: e.tensor_copy(
            WORK[:, VR:VR + 1024], PA[:, 0:1024]), chk=True)
        chk_evac["v"] = S.o("v", lambda e: e.tensor_copy(
            WORK[:, VR + 1024:VR + 2048], PA[:, 1024:2048]), chk=True)

        vt_half = [None, None]
        vt_done = None
        for rnd in range(2):
            trv = TRA if rnd == 0 else TRB
            S.w("pe", "v", cev0 if rnd == 0 else chk_evac["v"])
            S.we("pe", trfree[rnd])
            for j8 in range(8):
                j = rnd * 8 + j8
                S.o("pe", lambda e, j=j, j8=j8, trv=trv: e.matmul(
                    trv[:, j8 * 128:(j8 + 1) * 128],
                    WORK[:, VR + j * 128: VR + (j + 1) * 128], IDENT[:],
                    start=(j8 == 0), stop=(j8 == 7), is_transpose=True))
            ct = S.chk("pe")
            vt_done = ct
            S.w("v", "pe", ct)
            tok = ("v", S.o("v", lambda e, rnd=rnd, trv=trv: e.tensor_copy(
                WORK[:, VT + rnd * 1024: VT + (rnd + 1) * 1024], trv[:]), chk=True))
            trfree[rnd] = tok
            vt_half[rnd] = tok

        pa_free = ("v", chk_evac["v"])
        pb_free = ("v", chk_evac["k"])
        qraw = WORK[:, QR:QR + 2048]
        kraw = WORK[:, KR:KR + 2048]
        scaf = SCA.ap().bitcast(F32)
        S.we("pe", pa_free)
        for tc in range(4):
            S.o("pe", lambda e, tc=tc: e.matmul(
                PA[:, tc * 512:(tc + 1) * 512], BONES[:],
                qraw[:, tc * 512:(tc + 1) * 512], start=True, stop=True))
        csq_ = S.chk("pe")
        S.we("pe", pb_free)
        S.we("pe", vt_half[0])
        S.we("pe", vt_half[1])
        for tc in range(4):
            S.o("pe", lambda e, tc=tc: e.matmul(
                PB[:, tc * 512:(tc + 1) * 512], BONES[:],
                kraw[:, tc * 512:(tc + 1) * 512], start=True, stop=True))
        csk_ = S.chk("pe")
        S.w("v", "pe", csq_)
        cuq = S.o("v", lambda e: e.scalar_tensor_tensor(
            qraw, PA[:], -1.0 / DH, qraw, OP.mult, OP.add), chk=True)
        S.w("v", "pe", csk_)
        cuk = S.o("v", lambda e: e.scalar_tensor_tensor(
            kraw, PB[:], -1.0 / DH, kraw, OP.mult, OP.add), chk=True)
        S.w("s", "v", cuq)
        S.we("s", prev_cbc[0])
        cxq = S.o("s", lambda e: e.activation(SCA[:], qraw, AF.Square), chk=True)
        S.w("s", "v", cuk)
        S.we("s", prev_ctl[0])
        S.we("s", scb_pe_free[0])
        cxk = S.o("s", lambda e: e.activation(SCB[:], kraw, AF.Square), chk=True)
        S.w("pe", "s", cxq)
        S.w("pe", "v", cuq)
        for tc in range(4):
            S.o("pe", lambda e, tc=tc: e.matmul(
                PA[:, tc * 512:(tc + 1) * 512], BONES[:],
                SCA[:, tc * 512:(tc + 1) * 512], start=True, stop=False))
            S.o("pe", lambda e, tc=tc: e.matmul(
                PA[:, tc * 512:(tc + 1) * 512], ONES1[0:1, :],
                EPSR[0:1, 0:512], start=False, stop=True))
        cvq = S.chk("pe")
        S.w("pe", "s", cxk)
        S.w("pe", "v", cuk)
        S.we("pe", vt_half[0])
        S.we("pe", vt_half[1])
        for tc in range(4):
            S.o("pe", lambda e, tc=tc: e.matmul(
                PB[:, tc * 512:(tc + 1) * 512], BONES[:],
                SCB[:, tc * 512:(tc + 1) * 512], start=True, stop=False))
            S.o("pe", lambda e, tc=tc: e.matmul(
                PB[:, tc * 512:(tc + 1) * 512], ONES1[0:1, :],
                EPSR[0:1, 0:512], start=False, stop=True))
        cvk = S.chk("pe")
        qact_half = [None, None]
        kact_half = [None, None]
        kz_half = [None, None]
        scb_user = None
        ln_k = [None, None]
        ln_q = [None, None]
        S.w("s", "pe", cvk)
        for h in (0, 1):
            lo, hi = h * 1024, (h + 1) * 1024
            ln_k[h] = S.o("s", lambda e, lo=lo, hi=hi: e.activation(
                scaf[:, 0:1024], PB[:, lo:hi], AF.Ln), chk=True)
            ce = S.o("s", lambda e, lo=lo, hi=hi: e.activation(
                SCB[:, lo:hi], scaf[:, 0:1024], AF.Exp,
                scale=-0.5, bias=QKP[:, 4:5]), chk=True)
            S.w("v", "s", ce)
            ctmk = S.o("v", lambda e, lo=lo, hi=hi: e.tensor_tensor(
                WORK[:, KR + lo:KR + hi], WORK[:, KR + lo:KR + hi],
                SCB[:, lo:hi], OP.mult), chk=True)
            if h == 1:
                ctm_k[0] = ctmk
            S.w("s", "v", ctmk)
            S.w("s", "v", ckp1z)
            S.we("s", kp1_free[0])
            ck1 = S.o("s", lambda e, lo=lo, hi=hi: e.activation(
                KP1.ap()[64:128, lo:hi], WORK.ap()[64:128, KR + lo:KR + hi],
                AF.Identity, bias=QKP.ap()[64:128, 3:4],
                scale=QKP.ap()[64:128, 2:3]), chk=True)
            ck0 = S.o("s", lambda e, lo=lo, hi=hi: e.activation(
                WORK.ap()[0:64, KR + lo:KR + hi], WORK.ap()[0:64, KR + lo:KR + hi],
                AF.Identity, bias=QKP.ap()[0:64, 3:4],
                scale=QKP.ap()[0:64, 2:3]), chk=True)
            kact_half[h] = ("s", ck0)
            S.w("v", "s", ck1)
            kz_half[h] = ("v", S.o("v", lambda e, lo=lo, hi=hi: e.memset(
                WORK.ap()[64:128, KR + lo:KR + hi], 0.0), chk=True))
            ln_q[h] = S.o("s", lambda e, lo=lo, hi=hi: e.activation(
                scaf[:, 0:1024], PA[:, lo:hi], AF.Ln), chk=True)
            ce = S.o("s", lambda e, lo=lo, hi=hi: e.activation(
                SCB[:, lo:hi], scaf[:, 0:1024], AF.Exp,
                scale=-0.5, bias=QKP[:, 4:5]), chk=True)
            S.w("v", "s", ce)
            ctmq = S.o("v", lambda e, lo=lo, hi=hi: e.tensor_tensor(
                WORK[:, QR + lo:QR + hi], WORK[:, QR + lo:QR + hi],
                SCB[:, lo:hi], OP.mult), chk=True)
            scb_user = ("v", ctmq)
            S.w("s", "v", ctmq)
            cq = S.o("s", lambda e, lo=lo, hi=hi: e.activation(
                WORK[:, QR + lo:QR + hi], WORK[:, QR + lo:QR + hi], AF.Identity,
                bias=QKP[:, 1:2], scale=QKP[:, 0:1]), chk=True)
            qact_half[h] = ("s", cq)
        half_free[0] = ("s", ln_k[0])
        half_free[1] = ("s", ln_q[0])
        half_free[2] = ("s", ln_q[1])
        hctr[0] += (3 - hctr[0] % 3) % 3
        PRB = VR

        if DEBUG and fc == 0:
            S.we("y", qact_half[1])
            S.we("y", kact_half[1])
            S.we("y", tr_last[0])
            _tok = S.d("y", "qy", lambda e: e.dma_start(out=dbg_qkv, in_=WORK[:]))
            S.w("s", "qy", _tok)

        NT = 32
        pbufs = [WORK[:, VR:VR + 2048], SCB[:, 0:2048], SCA[:, 0:2048]]
        probs_free = [("pe", vt_done), scb_user, None]
        lc_free = [None, None]
        st = [dict() for _ in range(NT)]

        def emit_A(t):
            i, hh = divmod(t, 2)
            qoff = 64 * hh
            kv = (i + 1) * 128
            nh = 1 if kv <= 1024 else 2
            par = t % 2
            hf = i // 8
            S.we("pe", qact_half[hf])
            for h2 in range(hf + 1):
                S.we("pe", kact_half[h2])
                S.we("pe", kz_half[h2])
            hids = []
            csc_h = []
            for hx in range(nh):
                hid = hctr[0] % 3
                hctr[0] += 1
                hids.append(hid)
                S.we("pe", half_free[hid])
                psum, poff = halves[hid]
                lo = hx * 1024
                hi = min(kv, lo + 1024)
                off = lo
                while off < hi:
                    n = min(512, hi - off)
                    kop = (WORK[:, KR + off: KR + off + n] if hh == 0
                           else KP1[:, off:off + n])
                    S.o("pe", lambda e, psum=psum, poff=poff, off=off, lo=lo,
                        n=n, i=i, kop=kop: e.matmul(
                        psum[:, poff + off - lo: poff + off - lo + n],
                        WORK[:, QR + i * 128: QR + (i + 1) * 128],
                        kop, start=True, stop=True))
                    off += n
                csc_h.append(S.chk("pe"))
            st[t].update(csc=csc_h[-1], csc_h=csc_h, hids=hids, nh=nh, hh=hh, i=i)

        def emit_B(t):
            i, hh = st[t]["i"], st[t]["hh"]
            kv = (i + 1) * 128
            nh = st[t]["nh"]
            hids = st[t]["hids"]
            par = t % 2
            csc_h = st[t]["csc_h"]
            if t >= 2:
                S.w("v", "s", st[t - 2]["exp"])
            if nh == 2:
                psum, poff = halves[hids[0]]
                S.w("v", "pe", csc_h[0])
                S.o("v", lambda e, psum=psum, poff=poff, par=par: e.reduce_max(
                    NM[:, 4 * par:4 * par + 1], psum[:, poff:poff + 1024],
                    axis=AX.X, negate=True))
            S.w("v", "pe", csc_h[-1])
            dpsum, dpoff = halves[hids[-1]]
            drel = (i * 128) % 1024
            S.o("v", lambda e, dpsum=dpsum, dpoff=dpoff, drel=drel: e.tensor_tensor(
                dpsum[:, dpoff + drel: dpoff + drel + 128],
                dpsum[:, dpoff + drel: dpoff + drel + 128], MASK[:], OP.add))
            hx = nh - 1
            psum, poff = halves[hids[-1]]
            w = min(kv - hx * 1024, 1024)
            S.o("v", lambda e, psum=psum, poff=poff, w=w, hx=hx, par=par:
                e.reduce_max(
                    NM[:, 4 * par + hx:4 * par + hx + 1],
                    psum[:, poff:poff + w], axis=AX.X, negate=True))
            S.o("v", lambda e: e.tensor_copy(JNK[:, 0:2], QKP[:, 0:2]))
            if nh == 2:
                cnm = S.o("v", lambda e, par=par: e.tensor_tensor(
                    NM[:, 4 * par:4 * par + 1], NM[:, 4 * par:4 * par + 1],
                    NM[:, 4 * par + 1:4 * par + 2], OP.min), chk=True)
            else:
                cnm = S.chk("v")
            st[t]["cnm"] = cnm

        def emit_C(t):
            i, hh = st[t]["i"], st[t]["hh"]
            kv = (i + 1) * 128
            nh = st[t]["nh"]
            hids = st[t]["hids"]
            par = t % 2
            pbi = t % 3
            cnm = st[t]["cnm"]
            S.w("s", "v", cnm)
            S.we("s", probs_free[pbi])
            S.we("s", lc_free[par])
            chk_exp = None
            for hx, hid in enumerate(hids):
                psum, poff = halves[hid]
                w = min(kv - hx * 1024, 1024)
                chk_exp = S.o("s", lambda e, psum=psum, poff=poff, w=w, hx=hx,
                              par=par, pbi=pbi: e.activation(
                                  pbufs[pbi][:, hx * 1024: hx * 1024 + w],
                                  psum[:, poff:poff + w], AF.Exp,
                                  bias=NM[:, 4 * par:4 * par + 1],
                                  accum_out=NM[:, 4 * par + 2 + hx:4 * par + 3 + hx]),
                              chk=True)
                half_free[hid] = ("s", chk_exp)
            st[t]["exp"] = chk_exp

        def emit_D(tq, rnd):
            hh1, i1 = st[tq]["hh"], st[tq]["i"]
            prb = pbufs[tq % 3]
            ntr = i1 + 1
            nrnd = (ntr + 7) // 8
            if rnd >= nrnd:
                return
            if rnd == 0:
                S.w("pe", "s", st[tq]["exp"])
                S.we("pe", trfree[0])
                st[tq]["tr_evac"] = [None, None]
            else:
                S.we("pe", st[tq]["tr_evac"][0])
            lo8, hi8 = rnd * 8, min(ntr, rnd * 8 + 8)
            for j in range(lo8, hi8):
                S.o("pe", lambda e, j=j, lo8=lo8, hi8=hi8, prb=prb:
                    e.matmul(
                        TRA[:, (j - lo8) * 128:(j - lo8 + 1) * 128],
                        prb[:, j * 128:(j + 1) * 128], IDENT[:],
                        start=(j == lo8), stop=(j == hi8 - 1),
                        is_transpose=True))
            ctr = S.chk("pe")
            S.w("s", "pe", ctr)
            tok = ("s", S.o("s", lambda e, lo8=lo8, hi8=hi8, prb=prb: e.copy(
                prb[:, lo8 * 128:hi8 * 128],
                TRA[:, 0:(hi8 - lo8) * 128]), chk=True))
            st[tq]["tr_evac"][rnd] = tok
            if rnd == nrnd - 1:
                trfree[0] = tok

        def emit_EF(tq):
            hh1, i1 = st[tq]["hh"], st[tq]["i"]
            qoff1 = 64 * hh1
            par1 = tq % 2
            pbi = tq % 3
            prb = pbufs[pbi]
            ntr = i1 + 1
            nrnd = (ntr + 7) // 8
            avp = AVP2
            tr_evac = st[tq]["tr_evac"]
            S.we("pe", av_last[0])
            S.we("pe", vt_half[1])
            tp = (0, 64) if hh1 == 1 else None
            for rnd in range(nrnd):
                S.we("pe", tr_evac[rnd])
                lo8, hi8 = rnd * 8, min(ntr, rnd * 8 + 8)
                for j in range(lo8, hi8):
                    S.o("pe", lambda e, j=j, i1=i1, qoff1=qoff1, tp=tp, avp=avp,
                        prb=prb: e.matmul(
                            avp[qoff1:qoff1 + 64, 0:128],
                            WORK[:, VT + j * 128 + qoff1: VT + j * 128 + qoff1 + 64],
                            prb[:, j * 128:(j + 1) * 128],
                            start=(j == 0), stop=(j == i1), tile_position=tp))
            cav = S.chk("pe")
            st[tq]["cav"] = cav
            probs_free[pbi] = ("pe", cav)
            S.w("v", "s", st[tq]["exp"])
            if st[tq]["nh"] == 2:
                clc = S.o("v", lambda e, i1=i1, hh1=hh1, par1=par1: e.tensor_tensor(
                    LC[:, 2 * i1 + hh1: 2 * i1 + hh1 + 1],
                    NM[:, 4 * par1 + 2:4 * par1 + 3],
                    NM[:, 4 * par1 + 3:4 * par1 + 4], OP.add), chk=True)
            else:
                clc = S.o("v", lambda e, i1=i1, hh1=hh1, par1=par1: e.tensor_copy(
                    LC[:, 2 * i1 + hh1: 2 * i1 + hh1 + 1],
                    NM[:, 4 * par1 + 2:4 * par1 + 3]), chk=True)
            lc_free[par1] = ("v", clc)
            S.w("v", "pe", cav)
            av_last[0] = ("v", S.o("v", lambda e, fc=fc, i1=i1, qoff1=qoff1,
                                   avp=avp: e.tensor_copy(
                                       AO.ap()[qoff1:qoff1 + 64,
                                               fc * T + i1 * 128: fc * T + (i1 + 1) * 128],
                                       avp[qoff1:qoff1 + 64, 0:128]), chk=True))

        emit_A(0)
        emit_B(0)
        emit_C(0)
        for t in range(NT):
            if t + 1 < NT:
                emit_A(t + 1)
                emit_B(t + 1)
            emit_D(t, 0)
            if t >= 1:
                emit_EF(t - 1)
            if t + 1 < NT:
                emit_C(t + 1)
            emit_D(t, 1)
        emit_EF(NT - 1)
        scb_pe_free[0] = probs_free[1]
        kp1_free[0] = ("pe", st[NT - 1]["csc"])

        S.o("v", lambda e: e.reciprocal(LC[:], LC[:]))
        S.o("v", lambda e: e.tensor_copy(JNK[:, 0:2], QKP[:, 0:2]))
        clb = S.o("v", lambda e: e.tensor_copy(SCB[:, 0:32], LC[:]), chk=True)
        S.w("pe", "v", clb)
        S.we("pe", av_last[0])
        S.we("pe", half_free[0])
        S.we("pe", trfree[0])
        S.we("pe", trfree[1])
        pbbf = PB.ap().bitcast(BF16)
        for i in range(16):
            S.o("pe", lambda e, i=i, pbbf=pbbf: e.matmul(
                pbbf[0:2, i * 128:(i + 1) * 128], SCB[:, 2 * i:2 * i + 2], IDENT[:],
                start=(i % 8 == 0), stop=(i % 8 == 7), is_transpose=True))
        ctl = S.chk("pe")
        S.w("v", "pe", ctl)
        cle = S.o("v", lambda e, pbbf=pbbf: e.tensor_copy(
            SCA.ap()[0:2, 0:2048], pbbf[0:2, 0:2048]), chk=True)
        S.w("pe", "v", cle)
        for t16 in range(16):
            S.o("pe", lambda e, t16=t16: e.matmul(
                PB[:, t16 * 128:(t16 + 1) * 128], SEL2[:],
                SCA.ap()[0:2, t16 * 128:(t16 + 1) * 128],
                start=(t16 % 4 == 0), stop=(t16 % 4 == 3)))
        cbc = S.chk("pe")
        S.w("v", "pe", cbc)
        cnorm = S.o("v", lambda e, fc=fc: e.tensor_tensor(
            aos(fc), aos(fc), PB[:], OP.mult), chk=True)
        norm_tok[0] = ("v", cnorm)
        half_free[0] = ("v", cnorm)
        trfree[0] = ("v", cnorm)
        trfree[1] = ("v", cnorm)
        prev_cbc[0] = ("pe", cbc)
        prev_ctl[0] = ("pe", ctl)

    S.w("g", "pe", last_q_mm[0])
    tok_wo = S.d("g", "qg", wload(R0, dwo))

    S.w("pe", "qg", tok_wo)
    S.we("pe", norm_tok[0])
    wo_cadd = []
    wo_cts = []
    last_wo_mm = None
    wo_tbufs = [SCA, SCB, None]
    for dc in range(NDC):
        psum = PA if dc % 2 == 0 else PB
        if dc >= 2:
            S.w("pe", "s", wo_cts[dc - 2])
        for kc in range(8):
            for tc in range(4):
                S.o("pe", lambda e, psum=psum, dc=dc, tc=tc, kc=kc: e.matmul(
                    psum[:, tc * 512:(tc + 1) * 512],
                    R0[:, kc * 1024 + dc * 128: kc * 1024 + dc * 128 + 128],
                    aos(kc, tc * 512, (tc + 1) * 512),
                    start=(kc == 0), stop=(kc == 7)))
        cwo = S.chk("pe")
        last_wo_mm = cwo
        S.w("s", "pe", cwo)
        if dc >= 3:
            S.w("s", "v", wo_cadd[dc - 3])
        tbuf = [SCA[:, 0:2048], SCB[:, 0:2048], WORK[:, 0:2048]][dc % 3]
        cts = S.o("s", lambda e, psum=psum, dc=dc, tbuf=tbuf: e.activation(
            tbuf, psum[:], AF.Identity,
            bias=PAR[:, 48 + dc:48 + dc + 1],
            scale=PAR[:, 16 + dc:16 + dc + 1]), chk=True)
        wo_cts.append(cts)
        S.w("v", "s", cts)
        cadd = S.o("v", lambda e, dc=dc, tbuf=tbuf: e.tensor_tensor(
            xs(dc), xs(dc), tbuf, OP.add), chk=True)
        wo_cadd.append(cadd)
    wo_done = wo_cadd[-1]

    if DEBUG:
        S.we("y", norm_tok[0])
        _tok = S.d("y", "qy", lambda e: e.dma_start(out=dbg_ao, in_=AO[:]))
        S.w("v", "qy", _tok)
        S.w("s", "qy", _tok)
        S.w("y", "v", wo_done)
        _tok = S.d("y", "qy", lambda e: e.dma_start(out=dbg_x1, in_=X[:]))
        S.w("v", "qy", _tok)

    ln2_chks, ln2_pa_free = layernorm(32, 24, AO[:, 0:2048], None, None,
                                      pre_waits=[("v", wo_done),
                                                 ("s", wo_cts[-1])],
                                      lnscr=aof[:, 3072:5120])

    S.we("pe", ln_stats_free[0])
    for dc in range(NDC):
        S.w("pe", "s", ln2_chks[dc][0])
        S.w("pe", "qg", tok_gatet)
        for tc in range(4):
            S.o("pe", lambda e, dc=dc, tc=tc: e.matmul(
                PB[0:8, tc * 512:(tc + 1) * 512],
                GATET[:, dc * 8:(dc + 1) * 8],
                hs(dc, tc * 512, (tc + 1) * 512),
                start=(dc == 0), stop=(dc == NDC - 1)))
    cgm = S.chk("pe")
    S.w("v", "pe", cgm)
    cgev = S.o("v", lambda e: e.tensor_copy(SCA.ap()[0:8, 0:2048], PB[0:8, 0:2048]),
               chk=True)
    S.w("pe", "v", cgev)
    S.w("pe", "v", ln2_pa_free)
    pabf = PA.ap().bitcast(BF16)
    for t16 in range(16):
        S.o("pe", lambda e, t16=t16, pabf=pabf: e.matmul(
            pabf[0:128, t16 * 8:(t16 + 1) * 8],
            SCA.ap()[0:8, t16 * 128:(t16 + 1) * 128], IDENT.ap()[0:8, 0:8],
            start=(t16 == 0), stop=(t16 == 15), is_transpose=True))
    gate_done = S.chk("pe")

    sbf = SCB.ap().bitcast(F32)
    gex = sbf[:, 128:256]
    g3 = pabf[0:128, 0:128].rearrange("p (g k) -> p g k", k=8)
    ge3 = gex.rearrange("p (g k) -> p g k", k=8)

    def bc16(ap):
        return ap.rearrange("p (g o) -> p g o", o=1).broadcast_to([128, 16, 8])

    S.w("v", "pe", gate_done)
    S.o("v", lambda e: e.reduce_max(
        GST[:, 0:16].rearrange("p (g o) -> p g o", o=1), g3, axis=AX.X))
    S.o("v", lambda e: e.tensor_copy(JNK[:, 0:2], QKP[:, 0:2]))
    cgs = S.o("v", lambda e: e.tensor_tensor(ge3, g3, bc16(GST[:, 0:16]), OP.subtract),
              chk=True)
    S.w("s", "v", cgs)
    cge = S.o("s", lambda e: e.activation(gex, gex, AF.Exp), chk=True)
    S.w("v", "s", cge)
    S.o("v", lambda e: e.reduce_sum(
        GST[:, 16:32].rearrange("p (g o) -> p g o", o=1), ge3, axis=AX.X))
    S.o("v", lambda e: e.reduce_max(
        GST[:, 0:16].rearrange("p (g o) -> p g o", o=1), ge3, axis=AX.X))
    S.o("v", lambda e: e.tensor_copy(JNK[:, 0:2], QKP[:, 0:2]))
    S.o("v", lambda e: e.tensor_tensor(
        sbf[:, 256:384].rearrange("p (g k) -> p g k", k=8), ge3, bc16(GST[:, 0:16]),
        OP.is_ge))
    S.o("v", lambda e: e.scalar_tensor_tensor(
        sbf[:, 256:384], sbf[:, 256:384], -1e30, gex, OP.mult, OP.add))
    S.o("v", lambda e: e.reduce_max(
        GST[:, 32:48].rearrange("p (g o) -> p g o", o=1),
        sbf[:, 256:384].rearrange("p (g k) -> p g k", k=8), axis=AX.X))
    S.o("v", lambda e: e.tensor_copy(JNK[:, 0:2], QKP[:, 0:2]))
    S.o("v", lambda e: e.tensor_tensor(
        sbf[:, 256:384].rearrange("p (g k) -> p g k", k=8), ge3, bc16(GST[:, 32:48]),
        OP.is_ge))
    S.o("v", lambda e: e.tensor_tensor(gex, gex, sbf[:, 256:384], OP.mult))
    S.o("v", lambda e: e.reciprocal(GST[:, 48:64], GST[:, 16:32]))
    S.o("v", lambda e: e.tensor_copy(JNK[:, 0:2], QKP[:, 0:2]))
    S.o("v", lambda e: e.tensor_tensor(ge3, ge3, bc16(GST[:, 48:64]), OP.mult))
    ccb = S.o("v", lambda e: e.tensor_copy(SCA[:, 0:128], gex), chk=True)
    S.w("pe", "v", ccb)
    for t16 in range(16):
        S.o("pe", lambda e, t16=t16, pabf=pabf: e.matmul(
            pabf[0:8, t16 * 128:(t16 + 1) * 128], SCA[:, t16 * 8:(t16 + 1) * 8],
            IDENT[:], start=(t16 % 8 == 0), stop=(t16 % 8 == 7), is_transpose=True))
    ctc = S.chk("pe")
    S.w("v", "pe", ctc)
    crow_done = S.o("v", lambda e, pabf=pabf: e.tensor_copy(
        CROW[:], pabf[0:8, 0:2048]), chk=True)

    regs = [R1, RW, R0]
    reg_free = [("pe", last_k_mm[0]), ("v", wo_cadd[5]), ("s", wo_cts[-1])]
    wtoks = {}
    drams = {"w1": dw1, "w3": dw3, "w2": dw2}
    sdrams = {"w1": dsw1, "w3": dsw3, "w2": dsw2}

    def sched_wload(m_idx, e_idx, mat):
        ridx = (3 * e_idx + m_idx) % 3
        reg = regs[ridx]
        S.we("g", reg_free[ridx])
        if e_idx < NE:
            src = drams[mat].rearrange("e (c p) o -> e p c o", p=128)[e_idx]
        else:
            src = sdrams[mat].rearrange("(c p) o -> p c o", p=128)
        wtoks[(e_idx, mat)] = S.d("g", "qg", lambda e, reg=reg, src=src: e.dma_start(
            out=reg.rearrange("p (c o) -> p c o", c=8), in_=src))

    for m_idx, mat in enumerate(("w1", "w3", "w2")):
        sched_wload(m_idx, 0, mat)

    pa_free_ffn = ("v", crow_done)
    pb_free_ffn = ("v", cgev)
    he_read = ("s", ln2_chks[-1][0])
    prev_csil = None
    cw2 = None
    stts = []

    for e_idx in range(NE + 1):
        is_sh = e_idx == NE
        rW1 = regs[(3 * e_idx + 0) % 3]
        rW3 = regs[(3 * e_idx + 1) % 3]
        rW2 = regs[(3 * e_idx + 2) % 3]
        if not is_sh:
            S.w("pe", "v", crow_done)
            S.we("pe", pa_free_ffn)
            for tc in range(4):
                S.o("pe", lambda e, e_idx=e_idx, tc=tc: e.matmul(
                    PA[:, tc * 512:(tc + 1) * 512],
                    ESEL[:, e_idx * 128:(e_idx + 1) * 128],
                    CROW[:, tc * 512:(tc + 1) * 512], start=True, stop=True))
            ccb2 = S.chk("pe")
            S.w("v", "pe", ccb2)
            cbev0 = S.o("v", lambda e: e.tensor_copy(
                SCB[:, 0:1024], PA[:, 0:1024]), chk=True)
            cbev1 = S.o("v", lambda e: e.tensor_copy(
                SCB[:, 1024:2048], PA[:, 1024:2048]), chk=True)
            cb_half = [("v", cbev0), ("v", cbev1)]
            pa_free_ffn = ("v", cbev1)
        he_done = None
        he_toks = [None] * 8
        for ic in range(8):
            bcol1 = (e_idx * 8 + ic) if not is_sh else (128 + ic)
            bcol3 = (64 + e_idx * 8 + ic) if not is_sh else (136 + ic)
            S.w("pe", "qg", wtoks[(e_idx, "w1")])
            split_cb = (not is_sh) and ic == 0
            if not split_cb:
                S.we("pe", pa_free_ffn)
            for tc in range(4):
                if split_cb:
                    S.we("pe", cb_half[0] if tc < 2 else cb_half[1])
                for kc in range(8):
                    if e_idx == 0 and ic == 0:
                        S.w("pe", "s", ln2_chks[kc][0])
                    S.o("pe", lambda e, rW1=rW1, ic=ic, tc=tc, kc=kc: e.matmul(
                        PA[:, tc * 512:(tc + 1) * 512],
                        rW1[:, kc * 1024 + ic * 128: kc * 1024 + ic * 128 + 128],
                        hs(kc, tc * 512, (tc + 1) * 512),
                        start=(kc == 0), stop=(kc == 7)))
            ch1 = S.chk("pe")
            S.w("s", "pe", ch1)
            if prev_csil is not None and he_done is not None:
                S.w("s", "v", he_done)
            csil = S.o("s", lambda e, bcol1=bcol1: e.activation(
                SCA[:], PA[:], AF.Silu, bias=BIA[:, bcol1:bcol1 + 1]), chk=True)
            prev_csil = csil
            pa_free_ffn = ("s", csil)
            S.w("pe", "qg", wtoks[(e_idx, "w3")])
            S.we("pe", pb_free_ffn)
            for kc in range(8):
                for tc in range(4):
                    S.o("pe", lambda e, rW3=rW3, ic=ic, tc=tc, kc=kc: e.matmul(
                        PB[:, tc * 512:(tc + 1) * 512],
                        rW3[:, kc * 1024 + ic * 128: kc * 1024 + ic * 128 + 128],
                        hs(kc, tc * 512, (tc + 1) * 512),
                        start=(kc == 0), stop=(kc == 7)))
            ch3 = S.chk("pe")
            S.w("v", "pe", ch3)
            S.w("v", "s", csil)
            S.we("v", he_read)
            chm = S.o("v", lambda e, ic=ic, bcol3=bcol3: e.scalar_tensor_tensor(
                aos(ic), PB[:], BIA[:, bcol3:bcol3 + 1], SCA[:], OP.add, OP.mult),
                chk=True)
            pb_free_ffn = ("v", chm)
            if not is_sh:
                chm = S.o("v", lambda e, ic=ic: e.tensor_tensor(
                    aos(ic), aos(ic), SCB[:], OP.mult), chk=True)
            he_done = chm
            he_toks[ic] = ("v", chm)
        psfree = {0: pa_free_ffn, 1: pb_free_ffn}
        sh_cts = []
        for dc in range(NDC):
            par = dc % 2
            psum = PA if par == 0 else PB
            S.w("pe", "qg", wtoks[(e_idx, "w2")])
            S.we("pe", psfree[par])
            for kc in range(8):
                S.we("pe", he_toks[kc])
                for tc in range(4):
                    S.o("pe", lambda e, rW2=rW2, psum=psum, dc=dc, tc=tc, kc=kc:
                        e.matmul(
                            psum[:, tc * 512:(tc + 1) * 512],
                            rW2[:, kc * 1024 + dc * 128: kc * 1024 + dc * 128 + 128],
                            aos(kc, tc * 512, (tc + 1) * 512),
                            start=(kc == 0), stop=(kc == 7 and not is_sh)))
            if is_sh:
                S.w("pe", "qg", tok_consts)
                S.w("pe", "v", crow_done)
                for tc in range(4):
                    S.o("pe", lambda e, psum=psum, dc=dc, tc=tc: e.matmul(
                        psum[:, tc * 512:(tc + 1) * 512],
                        B2R[0:8, dc * 128:(dc + 1) * 128],
                        CROW[0:8, tc * 512:(tc + 1) * 512],
                        start=False, stop=True))
            cw2 = S.chk("pe")
            if dc == NDC - 1:
                he_read = ("pe", cw2)
            if not is_sh:
                S.w("v", "pe", cw2)
                cst = S.o("v", lambda e, psum=psum, dc=dc: e.scalar_tensor_tensor(
                    xs(dc), psum[:], PAR[:, 40 + dc:40 + dc + 1], xs(dc),
                    OP.mult, OP.add), chk=True)
            else:
                S.w("s", "pe", cw2)
                if dc >= 2:
                    S.w("s", "v", stts[-2])
                tbuf = SCA if par == 0 else SCB
                ct = S.o("s", lambda e, psum=psum, dc=dc, tbuf=tbuf: e.activation(
                    tbuf[:], psum[:], AF.Identity,
                    bias=PAR[:, 56 + dc:56 + dc + 1],
                    scale=PAR[:, 40 + dc:40 + dc + 1]), chk=True)
                sh_cts.append(ct)
                S.w("v", "s", ct)
                cst = S.o("v", lambda e, dc=dc, tbuf=tbuf: e.tensor_tensor(
                    xs(dc), xs(dc), tbuf[:], OP.add), chk=True)
            psfree[par] = ("v", cst)
            stts.append(cst)
        pa_free_ffn = psfree[0]
        pb_free_ffn = psfree[1]
        reg_free[(3 * e_idx + 0) % 3] = ("pe", ch1)
        reg_free[(3 * e_idx + 1) % 3] = ("pe", ch3)
        reg_free[(3 * e_idx + 2) % 3] = ("pe", cw2)
        if e_idx < NE:
            for m_idx, mat in enumerate(("w1", "w3", "w2")):
                sched_wload(m_idx, e_idx + 1, mat)

    for t in stts[-NDC:]:
        S.w("y", "v", t)
    tok_store = S.d("y", "qy", lambda e: e.dma_start(
        out=dout.rearrange("(c p) t -> p c t", p=128),
        in_=X[:].rearrange("p (c t) -> p c t", c=NDC)))
    S.w("y", "qy", tok_store)

    with (
        nc.allow_low_precision("bf16 intermediates are deliberate; fp32 where it matters"),
        nc.semaphore("sem_pe") as sem_pe,
        nc.semaphore("sem_v") as sem_v,
        nc.semaphore("sem_s") as sem_s,
        nc.semaphore("sem_g") as sem_g,
        nc.semaphore("sem_y") as sem_y,
        nc.semaphore("sem_qy") as sem_qy,
        nc.semaphore("sem_qg") as sem_qg,
        nc.Block() as block,
    ):
        sems = {"pe": sem_pe, "v": sem_v, "s": sem_s, "g": sem_g, "y": sem_y,
                "qy": sem_qy, "qg": sem_qg}

        @block.tensor
        def _(eng):
            S.replay(eng, "pe", sems)

        @block.vector
        def _(eng):
            S.replay(eng, "v", sems)

        @block.scalar
        def _(eng):
            S.replay(eng, "s", sems)

        @block.gpsimd
        def _(eng):
            S.replay(eng, "g", sems)

        @block.sync
        def _(eng):
            S.replay(eng, "y", sems)

    return nc


_CACHE = {}
_LAST_RES = None


def kernel(**inputs):
    N = 8
    x = np.asarray(inputs["L0_attn_x"])
    f32 = np.float32

    def c_(a):
        return np.ascontiguousarray(a, dtype=f32)

    wq = c_(np.asarray(inputs["L0_attn_Wq"]).T)
    wk = c_(np.asarray(inputs["L0_attn_Wk"]).T)
    wv = c_(np.asarray(inputs["L0_attn_Wv"]).T)
    wo = c_(np.asarray(inputs["L0_attn_Wo"]).T)
    w1 = c_(inputs["L0_ffn_w1"])
    w3 = c_(inputs["L0_ffn_w3"])
    w2 = c_(inputs["L0_ffn_w2"])
    sw1 = c_(inputs["L0_ffn_sw1"])
    sw3 = c_(inputs["L0_ffn_sw3"])
    sw2 = c_(inputs["L0_ffn_sw2"])
    wmod = c_(inputs["L0_wmod"])
    cfull = np.asarray(inputs["L0_c"], dtype=f32)

    gq = np.asarray(inputs["L0_attn_gq"], dtype=f32)
    bq = np.asarray(inputs["L0_attn_bq"], dtype=f32)
    gk = np.asarray(inputs["L0_attn_gk"], dtype=f32)
    bk = np.asarray(inputs["L0_attn_bk"], dtype=f32)
    qkp = c_(np.stack([np.tile(gq, 2), np.tile(bq, 2),
                       np.tile(gk, 2), np.tile(bk, 2),
                       np.full(128, 0.5 * np.log(64.0), dtype=f32),
                       np.full(128, 0.5 * np.log(1024.0), dtype=f32)], axis=1))

    b1 = np.asarray(inputs["L0_ffn_b1"], dtype=f32)
    b3 = np.asarray(inputs["L0_ffn_b3"], dtype=f32)
    b2 = c_(inputs["L0_ffn_b2"])
    sb1 = np.asarray(inputs["L0_ffn_sb1"], dtype=f32)
    sb3 = np.asarray(inputs["L0_ffn_sb3"], dtype=f32)
    sb2 = np.asarray(inputs["L0_ffn_sb2"], dtype=f32)
    bo = np.asarray(inputs["L0_attn_bo"], dtype=f32)
    biases = np.zeros((128, 160), dtype=f32)
    biases[:, 0:64] = b1.reshape(8, 8, 128).transpose(2, 0, 1).reshape(128, 64)
    biases[:, 64:128] = b3.reshape(8, 8, 128).transpose(2, 0, 1).reshape(128, 64)
    biases[:, 128:136] = sb1.reshape(8, 128).T
    biases[:, 136:144] = sb3.reshape(8, 128).T
    biases[:, 144:152] = bo.reshape(8, 128).T
    biases[:, 152:160] = sb2.reshape(8, 128).T
    biases = c_(biases)

    gatet = c_(np.asarray(inputs["L0_ffn_gate_w"], dtype=f32).T
               .reshape(8, 128, 8).transpose(1, 0, 2).reshape(128, 64))

    allones = np.ones((128, 128), dtype=f32)
    blockones = np.zeros((128, 128), dtype=f32)
    blockones[0:64, 0:64] = 1.0
    blockones[64:128, 64:128] = 1.0
    ident = np.eye(128, dtype=f32)
    maskd = np.triu(np.full((128, 128), -1e30, dtype=f32), 1)
    sel2 = np.zeros((2, 128), dtype=f32)
    sel2[0, 0:64] = 1.0
    sel2[1, 64:128] = 1.0
    esel = np.zeros((8, 1024), dtype=f32)
    for e in range(8):
        esel[e, e * 128:(e + 1) * 128] = 1.0
    ones1 = np.ones((1, 128), dtype=f32)
    epsr = np.zeros((1, 1024), dtype=f32)
    epsr[0, 0:512] = 64 * 1e-5
    epsr[0, 512:1024] = 1024 * 1e-5

    if "nc" not in _CACHE:
        _CACHE["nc"] = build()
    nc = _CACHE["nc"]

    in_maps = []
    for i in range(N):
        in_maps.append({
            "x": c_(x[i].T),
            "wq": wq, "wk": wk, "wv": wv, "wo": wo,
            "w1": w1, "w3": w3, "w2": w2,
            "sw1": sw1, "sw3": sw3, "sw2": sw2,
            "wmod": wmod,
            "cc": c_(cfull[i].reshape(8, 128).T),
            "qkp": qkp, "biases": biases, "gatet": gatet, "b2r": b2,
            "callones": allones, "cblockones": blockones, "cident": ident,
            "cmask": maskd, "csel2": sel2, "cesel": esel,
            "cones1": ones1, "cepsr": epsr, "cidentf": ident,
        })
    res = run_bass_kernel_spmd(nc, in_maps, core_ids=list(range(N)))
    global _LAST_RES
    _LAST_RES = res
    out = np.stack([np.ascontiguousarray(r["out"].T) for r in res.results], axis=0)
    return out.astype(np.float32)
